# version 03
# baseline (speedup 1.0000x reference)
import dataclasses
import numpy as np
import ml_dtypes
import concourse.bass as bass
import concourse.mybir as mybir
from concourse.bass_utils import run_bass_kernel_spmd
from contextlib import ExitStack

F32 = mybir.dt.float32
BF16 = mybir.dt.bfloat16
I32 = mybir.dt.int32
U8 = mybir.dt.uint8
ALU = mybir.AluOpType
AF = mybir.ActivationFunctionType
AX = mybir.AxisListType
BF = ml_dtypes.bfloat16

D = 1024
NG = 8
TOK = 2048
COLS = 2052
SEQ = 8192
HEADS = 8
DH = 64
WIN = 512
INNER = 512
HID = 2730
NHC = 22
HC_LAST = 42
QK_SCALE = 8.0
NEG = -1e30
LNEPS = 1e-5

DEBUG = []
PHASES = "ABC"
ALIAS_CHANNELS = False


class Chan:
    __slots__ = ("sem", "n", "name")
    def __init__(self, sem, name):
        self.sem = sem
        self.n = 0
        self.name = name


class Prog:
    ENGINES = ("tensor", "vector", "scalar", "gpsimd", "sync")

    def __init__(self, nc, es):
        self.nc = nc
        self.es = es
        self.ops = {e: [] for e in self.ENGINES}
        self.dumps = []

    def chan(self, name):
        sem = self.es.enter_context(self.nc.semaphore(name))
        return Chan(sem, name)

    def op(self, engine, fn, waits=()):
        self.ops[engine].append([tuple(waits), fn, None, 0])

    def tick(self, engine, ch, by=1):
        rec = self.ops[engine][-1]
        assert rec[2] is None, "op already ticked"
        rec[2] = ch
        rec[3] = by
        ch.n += by
        return (ch, ch.n)

    def dma(self, engine, out, in_, ch, waits=(), **kw):
        self.op(engine, lambda e, out=out, in_=in_, kw=kw: e.dma_start(out=out, in_=in_, **kw),
                waits=waits)
        return self.tick(engine, ch, by=16)

    def emit(self):
        block = self.es.enter_context(self.nc.Block())
        for ename in self.ENGINES:
            ops = self.ops[ename]
            if not ops:
                continue

            def body(eng, ops=ops):
                last_wait = {}
                for waits, fn, inc_ch, inc_by in ops:
                    for (ch, cnt) in waits:
                        if cnt > last_wait.get(ch.name, -1):
                            eng.wait_ge(ch.sem, cnt)
                            last_wait[ch.name] = cnt
                    ins = fn(eng)
                    if inc_ch is not None:
                        ins.then_inc(inc_ch.sem, inc_by)
            getattr(block, ename)(body)


class Arena:
    def __init__(self, nc, name, nbytes):
        nbytes = (nbytes + 63) & ~63
        self.h = nc.alloc_sbuf_tensor(name, [128, nbytes], U8)
        self.nbytes = nbytes

    @staticmethod
    def _shape(ap, shape):
        if len(shape) <= 1:
            return ap
        names = "abcdefg"[: len(shape)]
        spec = "p (" + " ".join(names) + ") -> p " + " ".join(names)
        kw = {names[i]: shape[i] for i in range(1, len(shape))}
        return ap.rearrange(spec, **kw)

    def f32(self, off, *shape):
        n = int(np.prod(shape))
        assert off % 64 == 0 and off + 4 * n <= self.nbytes, (off, shape, self.nbytes)
        return self._shape(self.h.ap().bitcast(F32)[:, off // 4: off // 4 + n], shape)

    def bf16(self, off, *shape):
        n = int(np.prod(shape))
        assert off % 64 == 0 and off + 2 * n <= self.nbytes, (off, shape, self.nbytes)
        return self._shape(self.h.ap().bitcast(BF16)[:, off // 2: off // 2 + n], shape)


def _rot_tables():
    half = np.arange(0, DH, 2, dtype=np.float64)
    inv_freq = 1.0 / 10000.0 ** (half / DH)
    t = np.arange(SEQ, dtype=np.float64)
    fr = t[:, None] * inv_freq[None, :]
    fr = np.concatenate([fr, fr], axis=-1)
    base = (half + 0.4 * DH) / (1.4 * DH)
    power = (t - SEQ // 2) / float(WIN // 2)
    scale = np.concatenate([base[None, :] ** power[:, None]] * 2, axis=-1)
    cos = np.cos(fr)
    sin = np.sin(fr)
    sign = np.concatenate([-np.ones(DH // 2), np.ones(DH // 2)])[None, :]
    Aq = (cos * scale).astype(np.float32)
    Bq = (sign * sin * scale).astype(np.float32)
    Ak = (cos / scale).astype(np.float32)
    Bk = (sign * sin / scale).astype(np.float32)
    return Aq, Bq, Ak, Bk


def _edge_masks():
    i = np.arange(128)[:, None]
    j = np.arange(128)[None, :]
    d0 = np.where(j < i, NEG, 0.0).astype(BF)
    d4 = np.where(j > i, NEG, 0.0).astype(BF)
    return d0, d4


def _w0_masks(is_s0):
    i = np.arange(128)[:, None]
    j = np.arange(128)[None, :]
    tiles = np.zeros((4, 5, 128, 128), np.float32)
    for qi in range(4):
        for bs in range(5):
            jc = qi + bs
            if bs == 0:
                m = np.where(j < i, NEG, 0.0)
            elif bs == 4:
                m = np.where(j > i, NEG, 0.0)
            else:
                m = np.zeros((128, 128), np.float32)
            if is_s0 and jc < 4:
                m = np.full((128, 128), NEG, np.float32)
            tiles[qi, bs] = m
    return tiles.astype(BF)


def stage_inputs(inputs):
    f = np.float32
    x = np.ascontiguousarray(inputs["L0_attn_x"], dtype=f)
    Wt = np.ascontiguousarray(np.asarray(inputs["L0_attn_W"], dtype=f).T)
    dec = np.ascontiguousarray(np.asarray(inputs["L0_attn_decay_logit"], f).reshape(NG, 128).T)
    bias_b = np.ascontiguousarray(np.asarray(inputs["L0_attn_b"], f).reshape(NG, 128).T)
    conv_b = np.ascontiguousarray(np.asarray(inputs["L0_attn_conv_b"], f).reshape(NG, 128).T)
    conv_w = np.ascontiguousarray(
        np.asarray(inputs["L0_attn_conv_w"], f).reshape(NG, 128, 5).transpose(1, 0, 2))
    wqkv = np.ascontiguousarray(np.asarray(inputs["L0_ffn_wqkv"], f))
    wo = np.ascontiguousarray(np.asarray(inputs["L0_ffn_wo"], f))
    w1p = np.zeros((2, D, 2 * NHC * 128), f)
    w1p[:, :, :HID] = np.asarray(inputs["L0_ffn_w1"], f)[:, :, :HID]
    w1p[:, :, NHC * 128: NHC * 128 + HID] = np.asarray(inputs["L0_ffn_w1"], f)[:, :, HID:]
    w2p = np.zeros((2, NHC * 128, 1024), f)
    w2p[:, :HID] = np.asarray(inputs["L0_ffn_w2"], f)
    qsr = np.ascontiguousarray(np.repeat(np.asarray(inputs["L0_ffn_q_scale"], f)[:, None, :], 128, 1))
    ksr = np.ascontiguousarray(np.repeat(np.asarray(inputs["L0_ffn_k_scale"], f)[:, None, :], 128, 1))
    ln_g = np.ascontiguousarray(np.asarray(inputs["L0_ffn_ln_g"], f).reshape(2, NG, 128).transpose(2, 0, 1))
    ln_b = np.ascontiguousarray(np.asarray(inputs["L0_ffn_ln_b"], f).reshape(2, NG, 128).transpose(2, 0, 1))
    b1p = np.zeros((2, 2, NHC * 128), f)
    b1p[:, 0, :HID] = np.asarray(inputs["L0_ffn_b1"], f)[:, :HID]
    b1p[:, 1, :HID] = np.asarray(inputs["L0_ffn_b1"], f)[:, HID:]
    b1s = np.ascontiguousarray(b1p.reshape(2, 2, NHC, 128).transpose(3, 0, 1, 2))
    b2s = np.ascontiguousarray(np.asarray(inputs["L0_ffn_b2"], f).reshape(2, NG, 128).transpose(2, 0, 1))

    Aq, Bq, Ak, Bk = _rot_tables()
    d0, d4 = _edge_masks()
    shared = dict(
        Wt=Wt, dec=dec, bias_b=bias_b, conv_b=conv_b, conv_w=conv_w,
        wqkv=wqkv, wo=wo, w1=w1p, w2=w2p, qsr=qsr, ksr=ksr,
        qsr_shuf=np.ascontiguousarray(qsr[:, :, [32 + i for i in range(32)] + list(range(32))]),
        ksr_shuf=np.ascontiguousarray(ksr[:, :, [32 + i for i in range(32)] + list(range(32))]),
        identf=np.eye(128).astype(f),
        bndmask3=np.ascontiguousarray(np.broadcast_to(
            np.where(np.arange(128)[None, :] < 127, NEG, 0.0).astype(BF), (128, 128))),
        ln_g=ln_g, ln_b=ln_b, b1=b1s, b2=b2s,
        mask_d0=np.ascontiguousarray(d0), mask_d4=np.ascontiguousarray(d4),
        ident=np.eye(128).astype(BF),
        ones_col=np.ones((128, 1), BF),
        ones1_128=np.ones((1, 128), BF),
        ones1_64=np.ones((1, 64), BF),
        on64_d=np.ones((65, 64), BF),
    )

    in_maps = []
    for c in range(8):
        b, s = c // 4, c % 4
        t0 = TOK * s
        xs = np.zeros((D, COLS), f)
        lo, hi = t0 - 2, t0 + TOK + 2
        slo, shi = max(lo, 0), min(hi, SEQ)
        xs[:, slo - lo: shi - lo] = x[b, slo:shi, :].T
        xs = np.ascontiguousarray(xs)

        def tt(tab):
            return np.ascontiguousarray(
                tab[t0:t0 + TOK].reshape(16, 128, DH).transpose(1, 0, 2))
        m_sel = np.zeros((128, 4), f)
        e_sel = np.zeros((128, 4), f)
        for r in range(4):
            if r < s:
                m_sel[:, r] = 1.0
                e_sel[:, r] = float(s - 1 - r)
        gate = np.full((128, 1), 0.0 if s == 0 else 1.0, f)
        m = dict(shared)
        m.update(
            x_in=xs,
            tabAq=tt(Aq), tabBq=tt(Bq), tabAk=tt(Ak), tabBk=tt(Bk),
            m_sel=m_sel, e_sel=e_sel, gate=gate,
            s0neg=np.full((1, 512), NEG if s == 0 else 0.0, BF),
            prev_off=np.array([[(s - 1) % 4]], np.int32),
        )
        in_maps.append(m)
    return in_maps


def build_program(debug=(), phases="ABC"):
    nc = bass.Bass("TRN2", target_bir_lowering=False)
    es = ExitStack()
    P = Prog(nc, es)

    def din(name, shape, dt=F32):
        return nc.dram_tensor(name, shape, dt, kind="ExternalInput").ap()

    x_in = din("x_in", [D, COLS])
    Wt_d = din("Wt", [D, D])
    dec_d = din("dec", [128, NG])
    bias_b_d = din("bias_b", [128, NG])
    conv_b_d = din("conv_b", [128, NG])
    conv_w_d = din("conv_w", [128, NG, 5])
    wqkv_d = din("wqkv", [2, D, 3 * INNER])
    wo_d = din("wo", [2, INNER, D])
    w1_d = din("w1", [2, D, 2 * NHC * 128])
    w2_d = din("w2", [2, NHC * 128, D])
    qsr_d = din("qsr", [2, 128, DH])
    ksr_d = din("ksr", [2, 128, DH])
    ln_g_d = din("ln_g", [128, 2, NG])
    ln_b_d = din("ln_b", [128, 2, NG])
    b1_d = din("b1", [128, 2, 2, NHC])
    b2_d = din("b2", [128, 2, NG])
    qsr_shuf_d = din("qsr_shuf", [2, 128, DH])
    ksr_shuf_d = din("ksr_shuf", [2, 128, DH])
    identf_d = din("identf", [128, 128])
    bndmask3_d = din("bndmask3", [128, 128], BF16)
    s0neg_d = din("s0neg", [1, 512], BF16)
    mask_d0_d = din("mask_d0", [128, 128], BF16)
    mask_d4_d = din("mask_d4", [128, 128], BF16)
    ident_d = din("ident", [128, 128], BF16)
    ones_col_d = din("ones_col", [128, 1], BF16)
    ones1_128_d = din("ones1_128", [1, 128], BF16)
    ones1_64_d = din("ones1_64", [1, 64], BF16)
    on64_d = din("on64_d", [65, 64], BF16)
    tabAq_d = din("tabAq", [128, 16, DH])
    tabBq_d = din("tabBq", [128, 16, DH])
    tabAk_d = din("tabAk", [128, 16, DH])
    tabBk_d = din("tabBk", [128, 16, DH])
    m_sel_d = din("m_sel", [128, 4])
    e_sel_d = din("e_sel", [128, 4])
    gate_d = din("gate", [128, 1])
    prev_off_d = din("prev_off", [1, 1], I32)

    y_out = nc.dram_tensor("y", [D, TOK], F32, kind="ExternalOutput").ap()

    x2_spill = nc.dram_tensor("x2_spill", [D, TOK], F32, kind="Internal").ap()
    cc1_in = nc.dram_tensor("cc1_in", [128, NG], F32, kind="Internal").ap()
    cc1_out = nc.dram_tensor("cc1_out", [4 * 128, NG], F32, kind="Internal").ap()
    cc2_in = nc.dram_tensor("cc2_in", [128, NG], F32, kind="Internal").ap()
    cc2_out = nc.dram_tensor("cc2_out", [4 * 128, NG], F32, kind="Internal").ap()
    R3 = 1552
    cc3_in = [nc.dram_tensor(f"cc3_in{l}", [R3, 1040], BF16, kind="Internal").ap() for l in range(2)]
    cc3_out = [nc.dram_tensor(f"cc3_out{l}", [4 * R3, 1040], BF16, kind="Internal").ap() for l in range(2)]
    cc4_in = nc.dram_tensor("cc4_in", [128, NG], F32, kind="Internal").ap()
    cc4_out = nc.dram_tensor("cc4_out", [4 * 128, NG], F32, kind="Internal").ap()
    RG = [[0, 1, 2, 3], [4, 5, 6, 7]]

    AR_S = Arena(nc, "arS", NG * COLS * 4)
    AR_L = Arena(nc, "arL", NG * COLS * 4)
    AR_C = Arena(nc, "arC", 50496)

    def sb(name, shape, dt=F32):
        return es.enter_context(nc.sbuf_tensor(name, shape, dt))

    ident = sb("t_ident", [128, 128], BF16)
    mask_d0 = sb("t_mask_d0", [128, 128], BF16)
    mask_d4 = sb("t_mask_d4", [128, 128], BF16)
    ones_col = sb("t_ones_col", [128, 1], BF16)
    ones1_128 = sb("t_ones1_128", [1, 128], BF16)
    ones1_64 = sb("t_ones1_64", [1, 64], BF16)
    dec = sb("t_dec", [128, NG])
    smalla = sb("smalla", [128, NG])
    smallb = sb("smallb", [128, NG])
    a_col = sb("a_col", [128, NG])
    om_a = sb("om_a", [128, NG])
    a2k = sb("a2k", [128, NG])
    lnA = sb("lnA", [128, NG])
    coef = sb("coef", [128, 4, NG])
    bcb = sb("bcb", [128, NG])
    conv_w = sb("t_conv_w", [128, NG, 5])
    qsr = sb("t_qsr", [128, 2, DH])
    ksr = sb("t_ksr", [128, 2, DH])
    ln_g = sb("t_ln_g", [128, 2, NG])
    ln_b = sb("t_ln_b", [128, 2, NG])
    b1 = sb("t_b1", [128, 2, 2, NHC])
    b2 = sb("t_b2", [128, 2, NG])
    m_sel = sb("t_m_sel", [128, 4])
    e_sel = sb("t_e_sel", [128, 4])
    gate = sb("t_gate", [128, 1])
    poff = sb("poff", [1, 1], I32)
    hb1 = sb("hb1", [128, NG])
    hb2 = sb("hb2", [128, NG])
    wc_all = sb("wc_all", [128, 4, NG])
    sinit = sb("sinit", [128, NG])
    qsr_shuf = sb("t_qsr_shuf", [128, 2, DH])
    ksr_shuf = sb("t_ksr_shuf", [128, 2, DH])
    identf = sb("t_identf", [128, 128])
    bndmask3 = sb("t_bndmask3", [128, 128], BF16)
    s0neg = sb("t_s0neg", [1, 512], BF16)
    negm = sb("negm", [1, 512], BF16)
    r64 = sb("r64", [65, 1024])
    r64h = sb("r64h", [65, 1024], BF16)
    on64 = sb("on64", [65, 64], BF16)
    rb64 = sb("rb64", [64, 512])
    mst = sb("mst", [128, 4])
    qbndt = sb("qbndt", [128, 4], BF16)
    qfk = sb("qfk", [128, 4, 128], BF16)

    nrm = sb("nrm", [128, 8])
    nrm2 = sb("nrm2", [128, 8])
    rqv = sb("rqv", [128, 8])
    qbf = AR_C.bf16(42368, 512)
    ub = [AR_C.bf16(40256, 8, 129), AR_C.bf16(44352, 8, 129)]
    r_mu = sb("r_mu", [1, 512])
    r_var = sb("r_var", [1, 512])
    r_rstd = sb("r_rstd", [1, 512])
    r_mrs = sb("r_mrs", [1, 512])
    rb_hi = sb("rb_hi", [1, 512], BF16)
    rb_lo = sb("rb_lo", [1, 512], BF16)
    mb_hi = sb("mb_hi", [1, 512], BF16)
    mb_lo = sb("mb_lo", [1, 512], BF16)


    PB = [es.enter_context(nc.psum_tensor(f"PB{i}", [128, 512], F32)) for i in range(4)]
    PS640 = es.enter_context(nc.psum_tensor("PS640", [128, 640], F32))
    PW = es.enter_context(nc.psum_tensor("PW", [65, 512], F32))
    PT = es.enter_context(nc.psum_tensor("PT", [128, 512], BF16))

    cLD = P.chan("cLD")
    cSM = P.chan("cSM")
    cLX = P.chan("cLX")
    cSP = P.chan("cSP")
    cTB = [P.chan("cTB0"), P.chan("cTB1")]
    cCCP = P.chan("cCCP")
    cCCR = P.chan("cCCR")
    cWQ = [P.chan("cWQ0"), P.chan("cWQ1")]
    cWT = [P.chan("cWT0"), P.chan("cWT1")]
    cWO = [P.chan("cWO0"), P.chan("cWO1")]
    cW1 = [P.chan("cW10"), P.chan("cW11")]
    cW2 = [P.chan("cW20"), P.chan("cW21")]
    cLG = P.chan("cLG")
    cST = P.chan("cST")
    if ALIAS_CHANNELS:
        cSM = cLX = cSP = cCCP = cCCR = cLD
        cTB = [cLD, cLD]
        cWQ = [cLG, cLG]
        cWT = [cLD, cLD]
        cWO = [cLG, cLG]
        cW1 = [cLG, cLG]
        cW2 = [cLG, cLG]
    cPE = P.chan("cPE")
    cDVE = P.chan("cDVE")
    cACT = P.chan("cACT")
    cPOOL = P.chan("cPOOL")
    cCC = P.chan("cCC")

    dump_toks = []

    def dbg_guard():
        return [(cST, cST.n)] if debug else []

    def dump(name, view3d, ncols, waits):
        if name not in debug:
            return
        dd = nc.dram_tensor("dbg_" + name, [D, ncols], F32, kind="ExternalOutput").ap()
        t = P.dma("sync", dd.rearrange("(g p) t -> p g t", p=128), view3d, cST, waits=waits)
        dump_toks.append(t)

    vS = AR_S.f32(0, NG, COLS)
    vL = AR_L.f32(0, NG, COLS)

    tok_x = P.dma("sync", vS, x_in.rearrange("(g p) t -> p g t", p=128), cLX)
    small_loads = [
        (dec, dec_d), (bcb, bias_b_d), (conv_w, conv_w_d),
        (qsr, qsr_d.rearrange("l p d -> p l d")), (ksr, ksr_d.rearrange("l p d -> p l d")),
        (ln_g, ln_g_d), (ln_b, ln_b_d), (b1, b1_d), (b2, b2_d),
        (m_sel, m_sel_d), (e_sel, e_sel_d), (gate, gate_d), (poff, prev_off_d),
        (smalla, conv_b_d),
        (ident, ident_d), (mask_d0, mask_d0_d), (mask_d4, mask_d4_d),
        (qsr_shuf, qsr_shuf_d.rearrange("l p d -> p l d")),
        (ksr_shuf, ksr_shuf_d.rearrange("l p d -> p l d")),
        (identf, identf_d), (bndmask3, bndmask3_d), (s0neg, s0neg_d),
        (ones_col, ones_col_d), (ones1_128, ones1_128_d), (ones1_64, ones1_64_d),
    ]
    for dst, src in small_loads:
        tok_small = P.dma("sync", dst[:], src, cSM)
    tok_on64 = P.dma("sync", on64[:, :], on64_d, cSM)
    tok_small = tok_on64
    P.op("vector", lambda e: e.tensor_add(out=bcb[:], in0=bcb[:], in1=smalla[:]),
         waits=[tok_small])
    tok_bcb = P.tick("vector", cDVE)

    def chain(engine, fn, waits, ch):
        P.op(engine, fn, waits=waits)
        return P.tick(engine, ch)

    t = chain("scalar", lambda e: e.activation(out=smalla[:], in_=dec[:], func=AF.Exp,
                                               scale=-1.0), [tok_small, tok_bcb], cACT)
    t = chain("vector", lambda e: e.tensor_scalar(out=smalla[:], in0=smalla[:], scalar1=1.0,
                                                  scalar2=None, op0=ALU.add), [t], cDVE)
    t = chain("vector", lambda e: e.reciprocal(out=a_col[:], in_=smalla[:]), [t], cDVE)
    tok_a = t
    t = chain("vector", lambda e: e.tensor_scalar(out=om_a[:], in0=a_col[:], scalar1=-1.0,
                                                  scalar2=-1.0, op0=ALU.mult,
                                                  op1=ALU.subtract), [t], cDVE)
    tok_oma = t
    cur, oth = a_col, None
    t = chain("vector", lambda e: e.tensor_mul(out=smalla[:], in0=a_col[:], in1=a_col[:]),
              [t], cDVE)
    cur, oth = smalla, smallb
    for i in range(10):
        t = chain("vector", lambda e, cur=cur, oth=oth: e.tensor_mul(
            out=oth[:], in0=cur[:], in1=cur[:]), [t], cDVE)
        cur, oth = oth, cur
    t = chain("vector", lambda e, cur=cur: e.tensor_scalar_max(out=a2k[:], in0=cur[:],
                                                               scalar1=1e-30), [t], cDVE)
    t = chain("scalar", lambda e: e.activation(out=lnA[:], in_=a2k[:], func=AF.Ln), [t], cACT)
    for r in range(4):
        t = chain("vector", lambda e, r=r: e.tensor_scalar(
            out=coef[:, r, :], in0=lnA[:], scalar1=e_sel[:, r:r + 1], scalar2=None,
            op0=ALU.mult), [t], cDVE)
    t = chain("scalar", lambda e: e.activation(out=coef[:], in_=coef[:], func=AF.Exp), [t], cACT)
    for r in range(4):
        t = chain("vector", lambda e, r=r: e.tensor_scalar(
            out=coef[:, r, :], in0=coef[:, r, :], scalar1=m_sel[:, r:r + 1], scalar2=None,
            op0=ALU.mult), [t], cDVE)
    tok_coef = t

    stgA = AR_C.bf16(16384, NG, 512)
    stgB = AR_C.bf16(24768, NG, 512)
    scr1 = AR_C.f32(0, 2048)
    scr2 = AR_C.f32(8192, 2048)
    bc_rb = AR_C.f32(44352, 512)
    bc_mb = AR_C.f32(46400, 512)
    bc_t1 = AR_C.f32(48448, 512)
    pr_s1 = AR_C.f32(16704, 512)
    pr_s2 = AR_C.f32(18752, 512)
    pr_s3 = AR_C.f32(20800, 512)

    ln_state = {"mm": None, "apply": None, "rows": None}

    def layernorm(src3, tiles, apply_ops, waits_src):
        for (c0, w) in tiles:
            cast_waits = list(waits_src)
            if ln_state["mm"] is not None:
                cast_waits.append(ln_state["mm"])
            for g in range(NG):
                P.op("scalar", lambda e, g=g, c0=c0, w=w: e.activation(
                    out=stgA[:, g, 0:w], in_=src3(g, c0, w), func=AF.Identity),
                    waits=cast_waits)
                P.op("scalar", lambda e, g=g, c0=c0, w=w: e.activation(
                    out=stgB[:, g, 0:w], in_=src3(g, c0, w), func=AF.Square))
            tok_cast = P.tick("scalar", cACT)
            mm_waits = [tok_cast]
            if ln_state["rows"] is not None:
                mm_waits.append(ln_state["rows"])
            for g in range(NG):
                P.op("tensor", lambda e, g=g, w=w: e.matmul(
                    PB[0].ap()[0:1, 0:w], ones_col[:, 0:1], stgA[:, g, 0:w],
                    start=(g == 0), stop=(g == NG - 1)), waits=mm_waits)
            for g in range(NG):
                P.op("tensor", lambda e, g=g, w=w: e.matmul(
                    PB[1].ap()[0:1, 0:w], ones_col[:, 0:1], stgB[:, g, 0:w],
                    start=(g == 0), stop=(g == NG - 1)))
            tok_mm = P.tick("tensor", cPE)
            ln_state["mm"] = tok_mm
            P.op("vector", lambda e, w=w: e.tensor_scalar(
                out=r_mu[0:1, 0:w], in0=PB[0].ap()[0:1, 0:w], scalar1=1.0 / D, scalar2=None,
                op0=ALU.mult), waits=[tok_mm])
            P.op("vector", lambda e, w=w: e.tensor_scalar(
                out=r_var[0:1, 0:w], in0=PB[1].ap()[0:1, 0:w], scalar1=1.0 / D, scalar2=None,
                op0=ALU.mult))
            tok_rows = P.tick("vector", cDVE)
            ln_state["rows"] = tok_rows
            P.op("vector", lambda e, w=w: e.tensor_mul(
                out=r_mrs[0:1, 0:w], in0=r_mu[0:1, 0:w], in1=r_mu[0:1, 0:w]))
            P.op("vector", lambda e, w=w: e.scalar_tensor_tensor(
                out=r_var[0:1, 0:w], in0=r_var[0:1, 0:w], scalar=LNEPS, in1=r_mrs[0:1, 0:w],
                op0=ALU.add, op1=ALU.subtract))
            tok_var = P.tick("vector", cDVE)
            P.op("scalar", lambda e, w=w: e.activation(
                out=r_mrs[0:1, 0:w], in_=r_var[0:1, 0:w], func=AF.Ln), waits=[tok_var])
            P.op("scalar", lambda e, w=w: e.activation(
                out=r_rstd[0:1, 0:w], in_=r_mrs[0:1, 0:w], func=AF.Exp, scale=-0.5))
            tok_rstd = P.tick("scalar", cACT)
            P.op("vector", lambda e, w=w: e.tensor_mul(
                out=r_mrs[0:1, 0:w], in0=r_mu[0:1, 0:w], in1=r_rstd[0:1, 0:w]), waits=[tok_rstd])
            tok_mrs = P.tick("vector", cDVE)
            P.op("scalar", lambda e, w=w: e.activation(
                out=rb_hi[0:1, 0:w], in_=r_rstd[0:1, 0:w], func=AF.Identity), waits=[tok_mrs])
            P.op("scalar", lambda e, w=w: e.activation(
                out=mb_hi[0:1, 0:w], in_=r_mrs[0:1, 0:w], func=AF.Identity))
            P.op("scalar", lambda e, w=w: e.activation(
                out=r_mu[0:1, 0:w], in_=rb_hi[0:1, 0:w], func=AF.Identity))
            tok_hi = P.tick("scalar", cACT)
            P.op("vector", lambda e, w=w: e.tensor_sub(
                out=r_mu[0:1, 0:w], in0=r_rstd[0:1, 0:w], in1=r_mu[0:1, 0:w]), waits=[tok_hi])
            tok_lof1 = P.tick("vector", cDVE)
            P.op("scalar", lambda e, w=w: e.activation(
                out=rb_lo[0:1, 0:w], in_=r_mu[0:1, 0:w], func=AF.Identity), waits=[tok_lof1])
            P.op("scalar", lambda e, w=w: e.activation(
                out=r_mu[0:1, 0:w], in_=mb_hi[0:1, 0:w], func=AF.Identity))
            tok_hi2 = P.tick("scalar", cACT)
            P.op("vector", lambda e, w=w: e.tensor_sub(
                out=r_mu[0:1, 0:w], in0=r_mrs[0:1, 0:w], in1=r_mu[0:1, 0:w]), waits=[tok_hi2])
            tok_lof2 = P.tick("vector", cDVE)
            P.op("scalar", lambda e, w=w: e.activation(
                out=mb_lo[0:1, 0:w], in_=r_mu[0:1, 0:w], func=AF.Identity), waits=[tok_lof2])
            tok_lo = P.tick("scalar", cACT)
            bc_waits = [tok_lo]
            if ln_state["apply"] is not None:
                bc_waits.append(ln_state["apply"])
            P.op("tensor", lambda e, w=w: e.matmul(
                PB[2].ap()[:, 0:w], ones1_128[0:1, :], rb_hi[0:1, 0:w], start=True, stop=False),
                waits=bc_waits)
            P.op("tensor", lambda e, w=w: e.matmul(
                PB[2].ap()[:, 0:w], ones1_128[0:1, :], rb_lo[0:1, 0:w], start=False, stop=True))
            P.op("tensor", lambda e, w=w: e.matmul(
                PB[3].ap()[:, 0:w], ones1_128[0:1, :], mb_hi[0:1, 0:w], start=True, stop=False))
            P.op("tensor", lambda e, w=w: e.matmul(
                PB[3].ap()[:, 0:w], ones1_128[0:1, :], mb_lo[0:1, 0:w], start=False, stop=True))
            tok_bc = P.tick("tensor", cPE)
            P.op("scalar", lambda e, w=w: e.activation(
                out=bc_rb[:, 0:w], in_=PB[2].ap()[:, 0:w], func=AF.Identity), waits=[tok_bc])
            P.op("scalar", lambda e, w=w: e.activation(
                out=bc_mb[:, 0:w], in_=PB[3].ap()[:, 0:w], func=AF.Identity))
            tok_bcs = P.tick("scalar", cACT)
            apply_ops(c0, w, tok_bcs)
            ln_state["apply"] = P.tick("vector", cDVE)
        return ln_state["apply"]

    tok_attn_out_done = None
    tok_u0 = None
    tok_spill = None
    if "A" in phases:
        tiles_full = [(0, 512), (512, 512), (1024, 512), (1536, 512), (1540, 512)]

        def apply_ln1(c0, w, tok_bcs):
            for g in range(NG):
                P.op("vector", lambda e, g=g, c0=c0, w=w: e.tensor_mul(
                    out=bc_t1[:, 0:w], in0=vS[:, g, c0:c0 + w], in1=bc_rb[:, 0:w]),
                    waits=[tok_bcs])
                P.op("vector", lambda e, g=g, c0=c0, w=w: e.tensor_sub(
                    out=vL[:, g, c0:c0 + w], in0=bc_t1[:, 0:w], in1=bc_mb[:, 0:w]))

        tok_ln1 = layernorm(lambda g, c0, w: vS[:, g, c0:c0 + w], tiles_full, apply_ln1, [tok_x])
        dump("ln1", vL[:, :, 2:2050], 2048, [tok_ln1])
        if "bcdbg" in debug:
            dbg1 = nc.dram_tensor("dbg_bcdbg", [128, 1024], F32, kind="ExternalOutput").ap()
            dbg2 = nc.dram_tensor("dbg_rows", [1, 2048], F32, kind="ExternalOutput").ap()
            dump_toks.append(P.dma("sync", dbg1[:, 0:512], bc_rb[:], cST, waits=[tok_ln1]))
            dump_toks.append(P.dma("sync", dbg1[:, 512:1024], bc_mb[:], cST))
            dump_toks.append(P.dma("sync", dbg2[0:1, 0:512], r_mu[:], cST))
            dump_toks.append(P.dma("sync", dbg2[0:1, 512:1024], r_var[:], cST))
            dump_toks.append(P.dma("sync", dbg2[0:1, 1024:1536], r_rstd[:], cST))
            dump_toks.append(P.dma("sync", dbg2[0:1, 1536:2048], r_mrs[:], cST))

        P.op("gpsimd", lambda e: e.memset(scr2[:], 1.0))
        tok_ones = P.tick("gpsimd", cPOOL)
        scan_toks = []
        for g in range(NG):
            P.op("vector", lambda e, g=g: e.tensor_scalar(
                out=scr1[:], in0=scr2[:], scalar1=a_col[:, g:g + 1], scalar2=None, op0=ALU.mult),
                waits=[tok_ones, tok_a, tok_ln1])
            tok_ab = P.tick("vector", cDVE)
            P.op("vector", lambda e, g=g: e.tensor_tensor_scan(
                out=vS[:, g, 2:2050], data0=scr1[:], data1=vL[:, g, 2:2050],
                initial=0.0, op0=ALU.mult, op1=ALU.add), waits=[tok_ab])
            scan_toks.append(P.tick("vector", cDVE))
        P.op("vector", lambda e: e.tensor_copy(out=hb1[:], in_=vS[:, :, 2049]), waits=[scan_toks[-1]])
        tok_wc = P.tick("vector", cDVE)
        t1 = P.dma("sync", cc1_in, hb1[:], cCCP, waits=[tok_wc])
        P.op("gpsimd", lambda e: e.collective_compute(
            "AllGather", ALU.bypass, replica_groups=RG,
            ins=[cc1_in.opt()], outs=[cc1_out.opt()]), waits=[t1])
        tok_cc1 = P.tick("gpsimd", cCC)
        t2 = P.dma("gpsimd", wc_all[:], cc1_out.rearrange("(r p) g -> p r g", p=128), cCCR,
                   waits=[tok_cc1])
        ts = chain("vector", lambda e: e.tensor_mul(out=sinit[:], in0=coef[:, 0, :],
                                                    in1=wc_all[:, 0, :]), [t2, tok_coef], cDVE)
        for r in range(1, 4):
            ts = chain("vector", lambda e, r=r: e.tensor_mul(out=smalla[:], in0=coef[:, r, :],
                                                             in1=wc_all[:, r, :]), [ts], cDVE)
            ts = chain("vector", lambda e, r=r: e.tensor_add(out=sinit[:], in0=sinit[:],
                                                             in1=smalla[:]), [ts], cDVE)
        tok_sinit = ts
        if "emadbg" in debug:
            dbge = nc.dram_tensor("dbg_emadbg", [128, 512], F32, kind="ExternalOutput").ap()
            emadump = bc_rb
            P.op("vector", lambda e: e.tensor_copy(out=emadump[:, 0:32], in_=wc_all[:].rearrange("p a b -> p (a b)")), waits=[tok_sinit])
            P.op("vector", lambda e: e.tensor_copy(out=emadump[:, 32:40], in_=sinit[:]))
            P.op("vector", lambda e: e.tensor_copy(out=emadump[:, 40:72], in_=coef[:].rearrange("p a b -> p (a b)")))
            P.op("vector", lambda e: e.tensor_copy(out=emadump[:, 72:80], in_=a2k[:]))
            P.op("vector", lambda e: e.tensor_copy(out=emadump[:, 80:88], in_=a_col[:]))
            P.op("vector", lambda e: e.tensor_copy(out=emadump[:, 88:96], in_=lnA[:]))
            tok_ed = P.tick("vector", cDVE)
            dump_toks.append(P.dma("sync", dbge, emadump[:], cST, waits=[tok_ed]))
        for g in range(NG):
            P.op("vector", lambda e, g=g: e.tensor_scalar(
                out=scr1[:], in0=scr2[:], scalar1=a_col[:, g:g + 1], scalar2=None, op0=ALU.mult),
                waits=[tok_sinit])
            tok_ab = P.tick("vector", cDVE)
            P.op("vector", lambda e, g=g: e.tensor_tensor_scan(
                out=vS[:, g, 2:2050], data0=scr1[:], data1=vL[:, g, 2:2050],
                initial=sinit[:, g:g + 1], op0=ALU.mult, op1=ALU.add), waits=[tok_ab])
            P.tick("vector", cDVE)
        tok_scan2 = (cDVE, cDVE.n)
        dump("S", vS[:, :, 2:2050], 2048, [tok_scan2])

        wtb = [AR_C.bf16(33088, NG, 128), AR_C.bf16(35136, NG, 128)]
        tok_prev_evac = [tok_scan2, tok_scan2]
        tok_wtb_rd = [None, None]
        tok_prev_mm = None
        last_conv = None
        for tt in range(4):
            c0 = 2 + 512 * tt
            cast_w = [tok_scan2] + dbg_guard()
            if tok_prev_mm is not None:
                cast_w.append(tok_prev_mm)
            P.op("scalar", lambda e, c0=c0: e.activation(
                out=stgA[:, :, 0:512], in_=vS[:, :, c0:c0 + 512], func=AF.Identity),
                waits=cast_w)
            tok_scast = P.tick("scalar", cACT)
            for ec in range(NG):
                bi = ec % 2
                tw_w = [] if tok_wtb_rd[bi] is None else [tok_wtb_rd[bi]]
                tw = P.dma("gpsimd", wtb[bi][:],
                           Wt_d.rearrange("(g p) e -> p g e", p=128)[:, :, 128 * ec:128 * (ec + 1)],
                           cWT[bi], waits=tw_w)
                for g in range(NG):
                    P.op("vector", lambda e, bi=bi, g=g: e.tensor_scalar(
                        out=wtb[bi][:, g, :], in0=wtb[bi][:, g, :], scalar1=om_a[:, g:g + 1],
                        scalar2=None, op0=ALU.mult), waits=[tw])
                tok_fold = P.tick("vector", cDVE)
                pb = PB[ec % 2]
                for g in range(NG):
                    P.op("tensor", lambda e, pb=pb, bi=bi, g=g: e.matmul(
                        pb.ap()[:, :], wtb[bi][:, g, :], stgA[:, g, 0:512],
                        start=(g == 0), stop=(g == NG - 1)),
                        waits=[tok_fold, tok_scast, tok_prev_evac[ec % 2]])
                tok_ymm = P.tick("tensor", cPE)
                tok_wtb_rd[bi] = tok_ymm
                cbase = c0 - 2
                P.op("vector", lambda e, ec=ec, cbase=cbase: e.tensor_scalar(
                    out=scr1[0:128, 0:512], in0=vL[:, ec, cbase:cbase + 512],
                    scalar1=conv_w[:, ec, 0:1], scalar2=None, op0=ALU.mult),
                    waits=[tok_ln1])
                srcs = [scr1, scr2, scr1, scr2, scr1]
                for kk in range(1, 5):
                    P.op("vector", lambda e, ec=ec, cbase=cbase, kk=kk, si=srcs[kk - 1], so=srcs[kk]:
                         e.scalar_tensor_tensor(
                             out=so[0:128, 0:512], in0=vL[:, ec, cbase + kk:cbase + kk + 512],
                             scalar=conv_w[:, ec, kk:kk + 1], in1=si[0:128, 0:512],
                             op0=ALU.mult, op1=ALU.add))
                tok_conv = P.tick("vector", cDVE)
                last_conv = tok_conv
                P.op("vector", lambda e, pb=pb, ec=ec, c0=c0: e.scalar_tensor_tensor(
                    out=vS[:, ec, c0:c0 + 512], in0=pb.ap()[:, :], scalar=bcb[:, ec:ec + 1],
                    in1=scr1[0:128, 0:512], op0=ALU.add, op1=ALU.add),
                    waits=[tok_ymm, tok_bcb, tok_conv])
                tok_prev_evac[ec % 2] = P.tick("vector", cDVE)
            tok_prev_mm = tok_ymm
        tok_attn_out_done = tok_prev_evac[1]
        dump("attn_out", vS[:, :, 2:2050], 2048, [tok_attn_out_done])

        tok_xre = P.dma("sync", vL, x_in.rearrange("(g p) t -> p g t", p=128), cLX,
                        waits=[last_conv] + dbg_guard())
        tiles_own = [(2, 512), (514, 512), (1026, 512), (1538, 512)]

        def apply_x2(c0, w, tok_bcs):
            for g in range(NG):
                P.op("vector", lambda e, g=g, c0=c0, w=w: e.tensor_mul(
                    out=bc_t1[:, 0:w], in0=vS[:, g, c0:c0 + w], in1=bc_rb[:, 0:w]),
                    waits=[tok_bcs, tok_xre] + dbg_guard())
                P.op("vector", lambda e, g=g, c0=c0, w=w: e.tensor_sub(
                    out=scr2[0:128, 0:w], in0=bc_t1[:, 0:w], in1=bc_mb[:, 0:w]))
                P.op("vector", lambda e, g=g, c0=c0, w=w: e.tensor_add(
                    out=vS[:, g, c0:c0 + w], in0=scr2[0:128, 0:w], in1=vL[:, g, c0:c0 + w]))

        tok_x2 = layernorm(lambda g, c0, w: vS[:, g, c0:c0 + w], tiles_own, apply_x2, [tok_attn_out_done])
        dump("x2", vS[:, :, 2:2050], 2048, [tok_x2])
        tok_spill = P.dma("sync", x2_spill.rearrange("(g p) t -> p g t", p=128),
                          vS[:, :, 2:2050], cSP, waits=[tok_x2])

        def apply_u0(c0, w, tok_bcs):
            for g in range(NG):
                P.op("vector", lambda e, g=g, c0=c0, w=w: e.tensor_mul(
                    out=bc_t1[:, 0:w], in0=vS[:, g, c0:c0 + w], in1=bc_rb[:, 0:w]),
                    waits=[tok_bcs])
                P.op("vector", lambda e, g=g, c0=c0, w=w: e.tensor_sub(
                    out=vL[:, g, c0:c0 + w], in0=bc_t1[:, 0:w], in1=bc_mb[:, 0:w]))

        tok_u0 = layernorm(lambda g, c0, w: vS[:, g, c0:c0 + w], tiles_own, apply_u0, [tok_x2])
        P.op("gpsimd", lambda e: e.memset(vL[:, :, 0:2], 0.0), waits=[tok_xre])
        P.op("gpsimd", lambda e: e.memset(vL[:, :, 2050:2052], 0.0))
        tok_hz = P.tick("gpsimd", cPOOL)
        P.op("vector", lambda e: e.tensor_copy(out=hb1[:], in_=vL[:, :, 2049]), waits=[tok_u0])
        tok_b2 = P.tick("vector", cDVE)
        t3 = P.dma("sync", cc2_in, hb1[:], cCCP, waits=[tok_b2])
        P.op("gpsimd", lambda e: e.collective_compute(
            "AllGather", ALU.bypass, replica_groups=RG,
            ins=[cc2_in.opt()], outs=[cc2_out.opt()]), waits=[t3])
        tok_cc2 = P.tick("gpsimd", cCC)
        tok_poff = tok_small

        _poff_cache = {}

        def read_prev(e, base_ap, dst_ap, block_elems):
            if id(e) not in _poff_cache:
                _poff_cache[id(e)] = e.value_load(poff[0:1, 0:1])
            v = _poff_cache[id(e)]
            srcp = dataclasses.replace(base_ap, offset=v * block_elems + base_ap.offset)
            return e.dma_start(out=dst_ap, in_=srcp)

        P.op("gpsimd", lambda e: read_prev(e, cc2_out[0:128, :], hb2[:], 128 * NG),
             waits=[tok_cc2, tok_poff])
        tok_hb = P.tick("gpsimd", cCCR, by=16)
        P.op("vector", lambda e: e.tensor_scalar(
            out=vL[:, :, 1], in0=hb2[:], scalar1=gate[:, 0:1], scalar2=None, op0=ALU.mult),
            waits=[tok_hb, tok_hz])
        tok_halo = P.tick("vector", cDVE)
        dump("u0", vL[:, :, 2:2050], 2048, [tok_u0, tok_halo])

    import dataclasses as _dc

    def apx(ap, dims):
        part = list(ap.ap[0])
        return _dc.replace(ap, ap=[part] + [list(d) for d in dims])

    def offx(ap, delta):
        return _dc.replace(ap, offset=ap.offset + delta)

    def qp_ap(fc):
        return AR_S.bf16(4096 * fc, 2048)
    def kp_ap(fc):
        return AR_S.bf16(16384 + 4096 * fc, 2048)
    vqk_flat = AR_S.bf16(0, 32832)
    hln = AR_S.bf16(0, NG, 2048)
    h1t = AR_S.bf16(32768, NHC, 512)
    x2r = AR_S.f32(0, NG, 2048)

    c_v = AR_C.bf16(0, 16, 520)
    c_oa = AR_C.bf16(16704, 8, 512)
    obnd = AR_C.bf16(16704, 8, 128)
    c_aT = AR_C.bf16(24896, 3, 512)
    c_st = [AR_C.f32(27968, 512), AR_C.f32(30016, 512)]
    c_tab = AR_C.f32(32064, 2, 4, 64)
    c_wq = AR_C.bf16(34112, 2, 1536)
    c_wo = AR_C.bf16(40256, 2, 8, 128)
    def khalo_ap(fc):
        return AR_C.bf16(27968 + 2048 * fc, 1024)
    c_w1 = AR_C.bf16(0, 2, 2, NG, 128)
    c_w2 = AR_C.bf16(16384, 2, 1024)

    wqkv3 = wqkv_d.rearrange("l (g p) c -> l p g c", p=128)
    w1r = w1_d.rearrange("l (g p) c -> l p g c", p=128)
    wo_r = wo_d.rearrange("l (h dh) e -> l dh h e", dh=64)

    def exchange_boundary(cc_in_t, cc_out_t, tok_ready):
        P.op("vector", lambda e: e.tensor_copy(out=hb1[:], in_=vL[:, :, 2049]),
             waits=[tok_ready])
        twc = P.tick("vector", cDVE)
        td = P.dma("sync", cc_in_t, hb1[:], cCCP, waits=[twc])
        P.op("gpsimd", lambda e: e.collective_compute(
            "AllGather", ALU.bypass, replica_groups=RG,
            ins=[cc_in_t.opt()], outs=[cc_out_t.opt()]), waits=[td])
        tcc = P.tick("gpsimd", cCC)
        P.op("gpsimd", lambda e: read_prev(e, cc_out_t[0:128, :], hb2[:], 128 * NG),
             waits=[tcc])
        th = P.tick("gpsimd", cCCR, by=16)
        P.op("vector", lambda e: e.tensor_scalar(
            out=vL[:, :, 1], in0=hb2[:], scalar1=gate[:, 0:1], scalar2=None, op0=ALU.mult),
            waits=[th])
        return P.tick("vector", cDVE)

    def _slice_cols(ap2, so, n):
        return ap2[:, so:so + n]

    def mm_bank_group(ops, waits):
        n = len(ops)
        for i, f in enumerate(ops):
            P.op("tensor",
                 (lambda e, f=f, s=(i == 0), t=(i == n - 1): f(e, s, t)),
                 waits=(waits if i == 0 else ()))

    sub = set(c for c in phases if c.isdigit()) or {"1", "2", "3", "4", "5"}
    if "B" in phases:
        tok_u = tok_halo

        for l in range(2):
            st = dict(
                ub_rd=[None, None], qkv_ev=[None, None, None], vmemset=None,
                tab_rd=[None, None], bct1=None, qbf_rd=None, pt=None,
                wq=[None, None], stats_rd=None, mrow_rd=None,
                st_rd=[None, None], aT_w=[None, None, None], aT_rd=[None, None, None],
                pw_rd=None, oa_rd=None, kv_dma=None,
            )
            P.op("gpsimd", lambda e: (e.memset(c_v[:, :, :], 1.0) if "v" not in phases
                                      else e.engine_nop()), waits=[tok_u])
            st["vmemset"] = P.tick("gpsimd", cPOOL)

            NTC = 2 if "x" in phases else 16
            for tc in range(NTC):
                bi = tc % 2
                pass
                uw = [tok_u] + ([st["ub_rd"][bi]] if st["ub_rd"][bi] else [])
                P.op("vector", lambda e, tc=tc, bi=bi: (e.tensor_copy(
                    out=ub[bi][:, :, 0:129], in_=vL[:, :, 1 + 128 * tc: 130 + 128 * tc])
                    if "u" not in phases else e.engine_nop()), waits=uw)
                tok_ub = P.tick("vector", cDVE)
                tbi = tc % 2
                for ti, tabd in enumerate([] if "t" in phases else [tabAq_d, tabBq_d, tabAk_d, tabBk_d]):
                    tok_tab = P.dma(
                        "sync", c_tab[:, tbi, ti, :], tabd[:, tc, :], cTB[tbi],
                        waits=([st["tab_rd"][tbi]] if (ti == 0 and st["tab_rd"][tbi]) else []))
                pw0 = [tok_ub]
                for sec in range(3):
                    if st["qkv_ev"][sec] is not None:
                        pw0.append(st["qkv_ev"][sec])
                for g in range(NG):
                    wbuf = g % 2
                    wv = [] if st["wq"][wbuf] is None else [st["wq"][wbuf]]
                    twq = (P.dma("gpsimd", c_wq[:, wbuf, :], wqkv3[l, :, g, :], cWQ[wbuf], waits=wv)
                           if "w" not in phases else (tok_ub))
                    off = 1 if g < 4 else 0
                    if "m" in phases:
                        st["wq"][wbuf] = twq
                        continue
                    for sec, pb in ((0, PB[0]), (1, PB[1]), (2, PB[2])):
                        P.op("tensor", lambda e, g=g, off=off, sec=sec, pb=pb, bi=bi,
                             wbuf=wbuf: e.matmul(
                                 pb.ap()[:, :], ub[bi][:, g, off:off + 128],
                                 c_wq[:, wbuf, 512 * sec: 512 * (sec + 1)],
                                 start=(g == 0), stop=(g == NG - 1)),
                             waits=(pw0 + [twq]) if True else [])
                    tok_mm = P.tick("tensor", cPE)
                    st["wq"][wbuf] = tok_mm
                if "m" in phases:
                    tok_mm = tok_ub
                tok_qkvmm = tok_mm
                st["ub_rd"][bi] = tok_qkvmm
                if "e" in phases:
                    st["qkv_ev"] = [tok_qkvmm] * 3
                    continue
                vdst = apx(c_v[:, tc, 0:520], [[65, 8], [1, 64]])
                P.op("scalar", lambda e, vdst=vdst: e.activation(
                    out=vdst, in_=PB[2].ap()[:, :], func=AF.Identity),
                    waits=[tok_qkvmm, st["vmemset"]])
                P.op("scalar", lambda e: e.activation(out=c_st[0][:, :], in_=PB[0].ap()[:, :],
                                                      func=AF.Identity))
                P.op("scalar", lambda e: e.activation(out=c_st[1][:, :], in_=PB[1].ap()[:, :],
                                                      func=AF.Identity))
                tok_qkev = P.tick("scalar", cACT)
                st["qkv_ev"] = [tok_qkev, tok_qkev, tok_qkev]

                for which in ((0, 1) if "p" not in phases else ()):
                    stg = c_st[which]
                    tabA = c_tab[:, tbi, 2 * which, :]
                    tabB = c_tab[:, tbi, 2 * which + 1, :]
                    scl = (qsr if which == 0 else ksr)[:, l, :]
                    scl_sh = (qsr_shuf if which == 0 else ksr_shuf)[:, l, :]
                    sqw = [tok_qkev]
                    if st["bct1"] is not None:
                        sqw.append(st["bct1"])
                    P.op("scalar", lambda e, stg=stg: e.activation(
                        out=pr_s3[:, :], in_=stg[:, :], func=AF.Square), waits=sqw)
                    tok_sq = P.tick("scalar", cACT)
                    P.op("vector", lambda e: e.reduce_sum(
                        out=nrm[:, :], in_=apx(pr_s3[:, :], [[64, 8], [1, 64]]), axis=AX.X),
                        waits=[tok_sq])
                    t = P.tick("vector", cDVE)
                    t = chain("scalar", lambda e: e.activation(out=nrm2[:], in_=nrm[:],
                                                               func=AF.Ln), [t], cACT)
                    t = chain("scalar", lambda e: e.activation(out=nrm2[:], in_=nrm2[:],
                                                               func=AF.Exp, scale=0.5),
                              [t], cACT)
                    t = chain("vector", lambda e: e.tensor_scalar_max(
                        out=nrm2[:], in0=nrm2[:], scalar1=1e-12), [t], cDVE)
                    t = chain("vector", lambda e: e.reciprocal(out=rqv[:], in_=nrm2[:]),
                              [t], cDVE)
                    t = chain("vector", lambda e, tabA=tabA, scl=scl: e.tensor_tensor(
                        out=pr_s1[:, 0:64], in0=tabA, in1=scl, op=ALU.mult),
                        [t, tok_tab], cDVE)
                    t = chain("vector", lambda e, tabB=tabB, scl_sh=scl_sh: e.tensor_tensor(
                        out=pr_s1[:, 64:128], in0=tabB, in1=scl_sh, op=ALU.mult), [t], cDVE)
                    rq_b = apx(rqv[:, :], [[1, 8], [0, 64]])
                    P.op("vector", lambda e, stg=stg, rq_b=rq_b: e.tensor_tensor(
                        out=apx(pr_s2[:, :], [[64, 8], [1, 64]]),
                        in0=apx(stg[:, :], [[64, 8], [1, 64]]), in1=rq_b, op=ALU.mult),
                        waits=[t])
                    P.op("vector", lambda e: e.tensor_copy(
                        out=apx(pr_s3[:, :], [[64, 8], [1, 32]]),
                        in_=apx(offx(pr_s2[:, :], 32), [[64, 8], [1, 32]])))
                    P.op("vector", lambda e: e.tensor_copy(
                        out=apx(offx(pr_s3[:, :], 32), [[64, 8], [1, 32]]),
                        in_=apx(pr_s2[:, :], [[64, 8], [1, 32]])))
                    tok_pp = P.tick("vector", cDVE)
                    tA_rep = apx(pr_s1[:, 0:64], [[0, 8], [1, 64]])
                    tB_rep = apx(pr_s1[:, 64:128], [[0, 8], [1, 64]])
                    P.op("vector", lambda e, stg=stg, tA_rep=tA_rep: e.tensor_tensor(
                        out=apx(stg[:, :], [[64, 8], [1, 64]]),
                        in0=apx(pr_s2[:, :], [[64, 8], [1, 64]]), in1=tA_rep, op=ALU.mult),
                        waits=[tok_pp])
                    P.op("vector", lambda e, tB_rep=tB_rep: e.tensor_tensor(
                        out=apx(pr_s2[:, :], [[64, 8], [1, 64]]),
                        in0=apx(pr_s3[:, :], [[64, 8], [1, 64]]), in1=tB_rep, op=ALU.mult))
                    P.op("vector", lambda e, stg=stg: e.tensor_add(
                        out=pr_s3[:, :], in0=stg[:, :], in1=pr_s2[:, :]))
                    tok_rot = P.tick("vector", cDVE)
                    st["bct1"] = tok_rot
                    cw = [tok_rot]
                    if st["qbf_rd"] is not None:
                        cw.append(st["qbf_rd"])
                    P.op("scalar", lambda e, which=which: e.activation(
                        out=qbf[:, :], in_=pr_s3[:, :], func=AF.Identity,
                        scale=(QK_SCALE if which == 0 else 1.0)), waits=cw)
                    tok_cast = P.tick("scalar", cACT)
                    tw = [tok_cast]
                    if st["pt"] is not None:
                        tw.append(st["pt"])
                    for fc in range(4):
                        P.op("tensor", lambda e, fc=fc: e.transpose(
                            PT.ap()[:, 128 * fc: 128 * (fc + 1)],
                            qbf[:, 128 * fc: 128 * (fc + 1)], ident[:, :]),
                            waits=tw if fc == 0 else ())
                    tok_tp = P.tick("tensor", cPE)
                    st["qbf_rd"] = tok_tp
                    base = 0 if which == 0 else 8192
                    dst = apx(offx(vqk_flat, base + 128 * tc), [[2048, 4], [1, 128]])
                    P.op("vector", lambda e, dst=dst: e.tensor_copy(
                        out=dst, in_=apx(PT.ap()[:, :], [[128, 4], [1, 128]])),
                        waits=[tok_tp])
                    st["pt"] = P.tick("vector", cDVE)
                st["tab_rd"][tbi] = st["bct1"]
            tok_prep_done = st["pt"] or st["qkv_ev"][0]
            if "2" not in sub:
                tok_u = tok_prep_done
                continue

            pk_toks = []
            for fc in range(4):
                pk_toks.append(P.dma(
                    "sync", cc3_in[l][128 * fc: 128 * (fc + 1), 0:1024],
                    kp_ap(fc)[:, 1024:2048], cCCP, waits=[tok_prep_done]))
            for tcv in range(8, 16):
                pk_toks.append(P.dma(
                    "sync", cc3_in[l][520 + 128 * (tcv - 8): 520 + 128 * (tcv - 7), 0:520],
                    c_v[:, tcv, :], cCCP, waits=[tok_prep_done]))
            for fc in range(4):
                pk_toks.append(P.dma(
                    "sync", cc3_in[l][1544 + fc: 1545 + fc, 0:128],
                    qp_ap(fc)[:, 2047:2048], cCCP, waits=[tok_prep_done]))
            P.op("gpsimd", lambda e, l=l: e.collective_compute(
                "AllGather", ALU.bypass, replica_groups=RG,
                ins=[cc3_in[l].opt()], outs=[cc3_out[l].opt()]), waits=[pk_toks[-1]])
            tok_cc3 = P.tick("gpsimd", cCC)
            blk = R3 * 1040
            for fc in range(4):
                P.op("gpsimd", lambda e, fc=fc, l=l: read_prev(
                    e, cc3_out[l][128 * fc: 128 * (fc + 1), 0:1024], khalo_ap(fc)[:, :], blk),
                    waits=[tok_cc3, tok_prep_done])
                P.tick("gpsimd", cCCR, by=16)
            for fc in range(4):
                P.op("gpsimd", lambda e, fc=fc, l=l: read_prev(
                    e, cc3_out[l][1544 + fc: 1545 + fc, 0:128], qbndt[:, fc:fc + 1], blk),
                    waits=[tok_cc3])
                P.tick("gpsimd", cCCR, by=16)
            tok_halo_kv = (cCCR, cCCR.n)

            if "3" not in sub:
                tok_u = tok_halo_kv
                continue
            def kjc_src(w, jc, fc, ro):
                if w == 0 and jc < 4:
                    return khalo_ap(fc)[ro:ro + 64, 512 + 128 * jc: 640 + 128 * jc]
                base = 512 * (w - 1) if w >= 1 else -512
                return kp_ap(fc)[ro:ro + 64, base + 128 * jc: base + 128 * jc + 128]

            def vjc_src(w, jc):
                if w == 0 and jc < 4:
                    return 12 + jc
                return (4 * (w - 1) + jc) if w >= 1 else (jc - 4)

            def stats_segs(w, qi, fc, ro):
                j0, j1 = 128 * qi, 128 * qi + 640
                segs = []
                if w == 0:
                    if j0 < 512:
                        segs.append(khalo_ap(fc)[ro:ro + 64, 512 + j0: 1024])
                    segs.append(kp_ap(fc)[ro:ro + 64, max(0, j0 - 512): j1 - 512])
                else:
                    b = 512 * (w - 1)
                    segs.append(kp_ap(fc)[ro:ro + 64, b + j0: b + j1])
                return segs

            def attn_evac(h, iw, dst_ap, tok_pv, st, extra=()):
                t = chain("vector", lambda e, iw=iw: e.reciprocal(
                    out=r64[64:65, 0:iw], in_=PW.ap()[64:65, 0:iw]),
                    [tok_pv] + list(extra), cDVE)
                t = chain("scalar", lambda e, iw=iw: e.activation(
                    out=r64h[64:65, 0:iw], in_=r64[64:65, 0:iw], func=AF.Identity), [t], cACT)
                t = chain("scalar", lambda e, iw=iw: e.activation(
                    out=r64[64:65, 512:512 + iw], in_=r64h[64:65, 0:iw], func=AF.Identity),
                    [t], cACT)
                t = chain("vector", lambda e, iw=iw: e.tensor_sub(
                    out=pr_s3[64:65, 0:iw], in0=r64[64:65, 0:iw],
                    in1=r64[64:65, 512:512 + iw]), [t], cDVE)
                t = chain("scalar", lambda e, iw=iw: e.activation(
                    out=r64h[64:65, 512:512 + iw], in_=pr_s3[64:65, 0:iw],
                    func=AF.Identity), [t], cACT)
                bw = [t]
                if st["rb_rd"] is not None:
                    bw.append(st["rb_rd"])
                mm_bank_group([
                    lambda e, s, p, iw=iw: e.matmul(PB[3].ap()[0:64, 0:iw],
                                                    on64[64:65, 0:64],
                                                    r64h[64:65, 0:iw], start=s, stop=p),
                    lambda e, s, p, iw=iw: e.matmul(PB[3].ap()[0:64, 0:iw],
                                                    on64[64:65, 0:64],
                                                    r64h[64:65, 512:512 + iw],
                                                    start=s, stop=p),
                ], bw)
                tok_rb = P.tick("tensor", cPE)
                P.op("scalar", lambda e, iw=iw: e.activation(
                    out=rb64[0:64, 0:iw], in_=PB[3].ap()[0:64, 0:iw], func=AF.Identity),
                    waits=[tok_rb])
                tok_rbs = P.tick("scalar", cACT)
                st["pb3_rd"] = tok_rbs
                P.op("vector", lambda e, iw=iw, dst_ap=dst_ap: e.tensor_tensor(
                    out=dst_ap, in0=PW.ap()[0:64, 0:iw], in1=rb64[0:64, 0:iw], op=ALU.mult),
                    waits=[tok_rbs])
                tok_e = P.tick("vector", cDVE)
                st["pw_rd"] = tok_e
                st["rb_rd"] = tok_e
                return tok_e

            st["rb_rd"] = None
            tok_oa_w = {}
            tok_vhalo = None
            windows = [1, 2, 3, 0]
            for w in windows:
                if w == 0:
                    for tcv in range(8):
                        P.op("gpsimd", lambda e, tcv=tcv, l=l: read_prev(
                            e, cc3_out[l][520 + 128 * tcv: 520 + 128 * (tcv + 1), 0:520],
                            c_v[:, 8 + tcv, :], R3 * 1040),
                            waits=[tok_cc3, st["pw_rd"]])
                        P.tick("gpsimd", cCCR, by=16)
                    tok_vhalo = (cCCR, cCCR.n)
                tok_kv = tok_prep_done if w != 0 else tok_halo_kv
                for h in range(HEADS):
                    fc, ro = h // 2, (h % 2) * 64
                    qw_full = qp_ap(fc)
                    for qi in range(4):
                        pw = [tok_kv]
                        if st["stats_rd"] is not None:
                            pw.append(st["stats_rd"])
                        qslice = qp_ap(fc)[ro:ro + 64, 512 * w + 128 * qi: 512 * w + 128 * qi + 128]
                        segs = stats_segs(w, qi, fc, ro)
                        b4ops, b5ops = [], []
                        col = 0
                        for sg in segs:
                            wid = sg.shape[-1]
                            while wid > 0:
                                n = min(wid, 512 - (col % 512)) if col < 512 else min(wid, 128)
                                n = min(n, 512 - col) if col < 512 else n
                                tgt = (b4ops if col < 512 else b5ops)
                                tgt.append((col, n, sg, sg.shape[-1] - wid))
                                col += n
                                wid -= n
                        ops4 = [
                            (lambda e, s, p, c=c, n=n, sg=sg, so=so, qslice=qslice:
                             e.matmul(PS640.ap()[:, c:c + n], qslice,
                                      _slice_cols(sg, so, n), start=s, stop=p))
                            for (c, n, sg, so) in b4ops]
                        if w == 0:
                            hw = 512 - 128 * qi
                            ops4.append(lambda e, s, p, hw=hw: e.matmul(
                                PS640.ap()[:, 0:hw], ones1_128[0:1, :], s0neg[0:1, 0:hw],
                                start=s, stop=p))
                        mm_bank_group(ops4, pw)
                        ops5 = [
                            (lambda e, s, p, c=c, n=n, sg=sg, so=so, qslice=qslice:
                             e.matmul(PS640.ap()[:, c:c + n], qslice,
                                      _slice_cols(sg, so, n), start=s, stop=p))
                            for (c, n, sg, so) in b5ops]
                        mm_bank_group(ops5, [])
                        tok_smm = P.tick("tensor", cPE)
                        P.op("vector", lambda e, qi=qi: e.tensor_reduce(
                            out=mst[:, qi:qi + 1], in_=PS640.ap()[:, 0:640], axis=AX.X,
                            op=ALU.max, negate=True), waits=[tok_smm])
                        st["stats_rd"] = P.tick("vector", cDVE)
                    mw = [st["stats_rd"]]
                    if st["mrow_rd"] is not None:
                        mw.append(st["mrow_rd"])
                    if st.get("pb3_rd") is not None:
                        mw.append(st["pb3_rd"])
                    for qi in range(4):
                        P.op("tensor", lambda e, qi=qi: e.transpose(
                            PB[3].ap()[0:1, 128 * qi: 128 * qi + 128], mst[:, qi:qi + 1],
                            identf[:, :]), waits=(mw if qi == 0 else ()))
                    tok_mtp = P.tick("tensor", cPE)
                    negw = [tok_mtp]
                    if st.get("negm_rd") is not None:
                        negw.append(st["negm_rd"])
                    P.op("scalar", lambda e: e.activation(
                        out=negm[0:1, 0:512], in_=PB[3].ap()[0:1, 0:512], func=AF.Identity),
                        waits=negw)
                    tok_negm = P.tick("scalar", cACT)
                    st["mrow_rd"] = tok_negm
                    for jc in range(8):
                        i0 = max(0, 128 * (jc - 4))
                        i1 = min(512, 128 * jc + 128)
                        iw = i1 - i0
                        pb = PB[jc % 2]
                        pw = [tok_negm, tok_kv]
                        if st["st_rd"][jc % 2] is not None:
                            pw.append(st["st_rd"][jc % 2])
                        kap = kjc_src(w, jc, fc, ro)
                        qap = qp_ap(fc)[ro:ro + 64, 512 * w + i0: 512 * w + i1]
                        ops = [lambda e, s, p, kap=kap, qap=qap, pb=pb, i0=i0, iw=iw:
                               e.matmul(pb.ap()[:, i0:i0 + iw], kap, qap, start=s, stop=p)]
                        for qi in range(i0 // 128, i1 // 128):
                            ops.append(lambda e, s, p, pb=pb, qi=qi: e.matmul(
                                pb.ap()[:, 128 * qi: 128 * qi + 128], ones1_128[0:1, :],
                                negm[0:1, 128 * qi: 128 * qi + 128], start=s, stop=p))
                        if jc <= 3:
                            qe = jc
                            if i0 <= 128 * qe < i1:
                                ops.append(lambda e, s, p, pb=pb, qe=qe: e.matmul(
                                    pb.ap()[:, 128 * qe: 128 * qe + 128], mask_d0[:, :],
                                    ident[:, :], start=s, stop=p))
                        else:
                            qe = jc - 4
                            if i0 <= 128 * qe < i1:
                                ops.append(lambda e, s, p, pb=pb, qe=qe: e.matmul(
                                    pb.ap()[:, 128 * qe: 128 * qe + 128], mask_d4[:, :],
                                    ident[:, :], start=s, stop=p))
                        if w == 0 and jc < 4:
                            ops.append(lambda e, s, p, pb=pb, i0=i0, iw=iw: e.matmul(
                                pb.ap()[:, i0:i0 + iw], ones1_128[0:1, :],
                                s0neg[0:1, 0:iw], start=s, stop=p))
                        mm_bank_group(ops, pw)
                        tok_stmm = P.tick("tensor", cPE)
                        st["negm_rd"] = tok_stmm
                        slot = jc % 3
                        pw2 = [tok_stmm]
                        if st["aT_rd"][slot] is not None:
                            pw2.append(st["aT_rd"][slot])
                        P.op("scalar", lambda e, pb=pb, i0=i0, iw=iw, slot=slot: e.activation(
                            out=c_aT[:, slot, i0:i0 + iw], in_=pb.ap()[:, i0:i0 + iw],
                            func=AF.Exp), waits=pw2)
                        tok_exp = P.tick("scalar", cACT)
                        st["st_rd"][jc % 2] = tok_exp
                        vt = vjc_src(w, jc)
                        pvw = [tok_exp]
                        if jc == 0 and st["pw_rd"] is not None:
                            pvw.append(st["pw_rd"])
                        if jc == 0 and w == 0:
                            pvw.append(tok_vhalo)
                        P.op("tensor", lambda e, vt=vt, h=h, i0=i0, iw=iw, slot=slot, jc=jc:
                             e.matmul(PW.ap()[:, i0:i0 + iw], c_v[:, vt, 65 * h: 65 * h + 65],
                                      c_aT[:, slot, i0:i0 + iw], start=(jc == 0),
                                      stop=(jc == 7)), waits=pvw)
                        st["aT_rd"][slot] = P.tick("tensor", cPE)
                    tok_pv = st["aT_rd"][7 % 3]
                    dw = [st["oa_wo"]] if st.get("oa_wo") else []
                    attn_evac(h, 512, c_oa[0:64, h, 0:512], tok_pv, st, extra=dw)
                tok_attnw = st["pw_rd"]
                ecw = []
                for ec in range(NG):
                    wobuf = ec % 2
                    wv = [tok_attnw]
                    if st.get(f"wo_rd{wobuf}"):
                        wv.append(st[f"wo_rd{wobuf}"])
                    twl = P.dma("gpsimd", c_wo[0:64, wobuf, :, :],
                                wo_r[l, :, :, 128 * ec: 128 * (ec + 1)], cWO[wobuf], waits=wv)
                    pb = PB[ec % 2]
                    pw = [twl]
                    if st.get(f"wops{ec % 2}"):
                        pw.append(st[f"wops{ec % 2}"])
                    ops = [(lambda e, s, p, h=h2, pb=pb, wobuf=wobuf: e.matmul(
                        pb.ap()[:, :], c_wo[0:64, wobuf, h, :], c_oa[0:64, h, 0:512],
                        start=s, stop=p)) for h2 in range(HEADS)]
                    mm_bank_group(ops, pw)
                    tok_womm = P.tick("tensor", cPE)
                    st[f"wo_rd{wobuf}"] = tok_womm
                    c0 = 2 + 512 * w
                    P.op("vector", lambda e, pb=pb, ec=ec, c0=c0: e.tensor_add(
                        out=vL[:, ec, c0:c0 + 512], in0=pb.ap()[:, :],
                        in1=vL[:, ec, c0:c0 + 512]), waits=[tok_womm] + dbg_guard())
                    st[f"wops{ec % 2}"] = P.tick("vector", cDVE)
                tok_oa_w[w] = st[f"wops{(NG - 1) % 2}"]
                st["oa_wo"] = st[f"wops{(NG - 1) % 2}"]

            if "4" not in sub:
                tok_u = tok_oa_w[0]
                continue
            t = chain("vector", lambda e: e.tensor_copy(
                out=apx(qfk[:, :, :], [[128, 4], [1, 1]]), in_=qbndt[:, :]),
                [tok_halo_kv, tok_oa_w[0]], cDVE)
            for fc in range(4):
                t = chain("vector", lambda e, fc=fc: e.tensor_copy(
                    out=qfk[:, fc, 1:128], in_=qp_ap(fc)[:, 0:127]), [t], cDVE)
            tok_qfk = t
            for h in range(HEADS):
                fc, ro = h // 2, (h % 2) * 64
                qap = qfk[ro:ro + 64, fc, :]
                pw = [tok_qfk]
                if st["stats_rd"] is not None:
                    pw.append(st["stats_rd"])
                mm_bank_group([
                    lambda e, s, p, fc=fc, ro=ro, qap=qap: e.matmul(
                        PS640.ap()[:, 0:512], qap, khalo_ap(fc)[ro:ro + 64, 384:896],
                        start=s, stop=p)], pw)
                mm_bank_group([
                    lambda e, s, p, fc=fc, ro=ro, qap=qap: e.matmul(
                        PS640.ap()[:, 512:640], qap, khalo_ap(fc)[ro:ro + 64, 896:1024],
                        start=s, stop=p)], [])
                tok_smm = P.tick("tensor", cPE)
                P.op("vector", lambda e: e.tensor_reduce(
                    out=mst[:, 0:1], in_=PS640.ap()[:, 0:640], axis=AX.X, op=ALU.max,
                    negate=True), waits=[tok_smm])
                st["stats_rd"] = P.tick("vector", cDVE)
                mw = [st["stats_rd"], st["mrow_rd"]]
                if st.get("pb3_rd") is not None:
                    mw.append(st["pb3_rd"])
                P.op("tensor", lambda e: e.transpose(
                    PB[3].ap()[0:1, 0:128], mst[:, 0:1], identf[:, :]), waits=mw)
                tok_mtp = P.tick("tensor", cPE)
                P.op("scalar", lambda e: e.activation(
                    out=negm[0:1, 0:128], in_=PB[3].ap()[0:1, 0:128], func=AF.Identity),
                    waits=[tok_mtp, st["negm_rd"]] if st.get("negm_rd") else [tok_mtp])
                tok_negm = P.tick("scalar", cACT)
                st["mrow_rd"] = tok_negm
                for jj, jc in enumerate(range(3, 8)):
                    pb = PB[jc % 2]
                    pw = [tok_negm]
                    if st["st_rd"][jc % 2] is not None:
                        pw.append(st["st_rd"][jc % 2])
                    ops = [lambda e, s, p, pb=pb, fc=fc, ro=ro, jc=jc, qap=qap: e.matmul(
                        pb.ap()[:, 0:128], khalo_ap(fc)[ro:ro + 64, 128 * jc: 128 * jc + 128],
                        qap, start=s, stop=p)]
                    ops.append(lambda e, s, p, pb=pb: e.matmul(
                        pb.ap()[:, 0:128], ones1_128[0:1, :], negm[0:1, 0:128],
                        start=s, stop=p))
                    if jc == 3:
                        ops.append(lambda e, s, p, pb=pb: e.matmul(
                            pb.ap()[:, 0:128], bndmask3[:, :], ident[:, :], start=s, stop=p))
                    mm_bank_group(ops, pw)
                    tok_stmm = P.tick("tensor", cPE)
                    st["negm_rd"] = tok_stmm
                    slot = jc % 3
                    pw2 = [tok_stmm]
                    if st["aT_rd"][slot] is not None:
                        pw2.append(st["aT_rd"][slot])
                    P.op("scalar", lambda e, pb=pb, slot=slot: e.activation(
                        out=c_aT[:, slot, 0:128], in_=pb.ap()[:, 0:128], func=AF.Exp),
                        waits=pw2)
                    tok_exp = P.tick("scalar", cACT)
                    st["st_rd"][jc % 2] = tok_exp
                    pvw = [tok_exp]
                    if jj == 0 and st["pw_rd"] is not None:
                        pvw.append(st["pw_rd"])
                    P.op("tensor", lambda e, jc=jc, h=h, slot=slot, jj=jj: e.matmul(
                        PW.ap()[:, 0:128], c_v[:, 8 + jc, 65 * h: 65 * h + 65],
                        c_aT[:, slot, 0:128], start=(jj == 0), stop=(jj == 4)), waits=pvw)
                    st["aT_rd"][slot] = P.tick("tensor", cPE)
                tok_pv = st["aT_rd"][7 % 3]
                attn_evac(h, 128, obnd[0:64, h, 0:128], tok_pv, st)
            tok_bnd = st["pw_rd"]
            for ec in range(NG):
                wobuf = ec % 2
                wv = [tok_bnd]
                if st.get(f"wo_rd{wobuf}"):
                    wv.append(st[f"wo_rd{wobuf}"])
                twl = P.dma("gpsimd", c_wo[0:64, wobuf, :, :],
                            wo_r[l, :, :, 128 * ec: 128 * (ec + 1)], cWO[wobuf], waits=wv)
                pb = PB[ec % 2]
                pw = [twl]
                if st.get(f"wops{ec % 2}"):
                    pw.append(st[f"wops{ec % 2}"])
                ops = [(lambda e, s, p, h=h2, pb=pb, wobuf=wobuf: e.matmul(
                    pb.ap()[:, 0:128], c_wo[0:64, wobuf, h, :], obnd[0:64, h, 0:128],
                    start=s, stop=p)) for h2 in range(HEADS)]
                mm_bank_group(ops, pw)
                tok_womm = P.tick("tensor", cPE)
                st[f"wo_rd{wobuf}"] = tok_womm
                P.op("vector", lambda e, pb=pb, ec=ec: e.scalar_tensor_tensor(
                    out=vL[:, ec, 1:2], in0=pb.ap()[:, 0:1], scalar=gate[:, 0:1],
                    in1=vL[:, ec, 1:2], op0=ALU.mult, op1=ALU.add), waits=[tok_womm] + dbg_guard())
                st[f"wops{ec % 2}"] = P.tick("vector", cDVE)
            tok_xmid = st[f"wops{(NG - 1) % 2}"]
            dump(f"xmid{l}", vL[:, :, 2:2050], 2048, [tok_xmid])

            if "5" not in sub:
                tok_u = tok_xmid
                continue
            def src_shift(g, c0, wdt):
                off = 0 if g < 4 else -1
                return vL[:, g, c0 + off: c0 + off + wdt]

            def apply_ln2(c0, wdt, tok_bcs, l=l):
                for g in range(NG):
                    off = 0 if g < 4 else -1
                    P.op("vector", lambda e, g=g, c0=c0, wdt=wdt, off=off: e.tensor_mul(
                        out=bc_t1[:, 0:wdt], in0=vL[:, g, c0 + off: c0 + off + wdt],
                        in1=bc_rb[:, 0:wdt]), waits=[tok_bcs])
                    P.op("vector", lambda e, g=g, c0=c0, wdt=wdt: e.tensor_sub(
                        out=scr1[0:128, 0:wdt], in0=bc_t1[:, 0:wdt], in1=bc_mb[:, 0:wdt]))
                    tok_s = P.tick("vector", cDVE)
                    P.op("scalar", lambda e, g=g, c0=c0, wdt=wdt, l=l: e.activation(
                        out=hln[:, g, c0 - 2: c0 - 2 + wdt], in_=scr1[0:128, 0:wdt],
                        func=AF.Identity, scale=ln_g[:, l, g:g + 1], bias=ln_b[:, l, g:g + 1]),
                        waits=[tok_s])
                P.op("vector", lambda e: e.engine_nop())

            tok_ln2 = layernorm(src_shift, tiles_own, apply_ln2, [tok_xmid])
            tok_hln = P.tick("scalar", cACT)

            stw = {"w1_rd": [None, None], "w2_rd": [None, None], "h1_rd": {}, "ps": {}}
            tok_u_upd = []
            for tt in range(4):
                tok_h1 = {}
                for hc in range(NHC):
                    wb = hc % 2
                    wv = ([stw["w1_rd"][wb]] if stw["w1_rd"][wb] else []) + [tok_hln]
                    ta = P.dma("gpsimd", c_w1[:, wb, 0, :, :],
                               w1r[l, :, :, 128 * hc: 128 * hc + 128], cW1[wb], waits=wv)
                    tg = P.dma("gpsimd", c_w1[:, wb, 1, :, :],
                               w1r[l, :, :, NHC * 128 + 128 * hc: NHC * 128 + 128 * hc + 128],
                               cW1[wb])
                    pba = PB[2 * (hc % 2)]
                    pbg = PB[2 * (hc % 2) + 1]
                    pw = [tg, tok_hln]
                    if stw["ps"].get(hc % 2):
                        pw.append(stw["ps"][hc % 2])
                    if hc < 2 and stw.get("w2ps0"):
                        pw.append(stw["w2ps0"])
                    if hc < 2 and stw.get("w2ps1"):
                        pw.append(stw["w2ps1"])
                    opsa = [(lambda e, s, p, g=g, wb=wb, pba=pba, tt=tt: e.matmul(
                        pba.ap()[:, :], c_w1[:, wb, 0, g, :],
                        hln[:, g, 512 * tt: 512 * tt + 512], start=s, stop=p))
                        for g in range(NG)]
                    mm_bank_group(opsa, pw)
                    opsg = [(lambda e, s, p, g=g, wb=wb, pbg=pbg, tt=tt: e.matmul(
                        pbg.ap()[:, :], c_w1[:, wb, 1, g, :],
                        hln[:, g, 512 * tt: 512 * tt + 512], start=s, stop=p))
                        for g in range(NG)]
                    mm_bank_group(opsg, [])
                    tok_w1mm = P.tick("tensor", cPE)
                    stw["w1_rd"][wb] = tok_w1mm
                    gw = [tok_w1mm]
                    if stw.get("gel_rd"):
                        gw.append(stw["gel_rd"])
                    P.op("scalar", lambda e, pbg=pbg, hc=hc, l=l: e.activation(
                        out=bc_t1[:, :], in_=pbg.ap()[:, :], func=AF.Gelu,
                        bias=b1[:, l, 1, hc:hc + 1]), waits=gw)
                    tok_gel = P.tick("scalar", cACT)
                    hw1 = [tok_gel]
                    if stw["h1_rd"].get(hc):
                        hw1.append(stw["h1_rd"][hc])
                    P.op("vector", lambda e, pba=pba, hc=hc, l=l: e.scalar_tensor_tensor(
                        out=h1t[:, hc, :], in0=pba.ap()[:, :], scalar=b1[:, l, 0, hc:hc + 1],
                        in1=bc_t1[:, :], op0=ALU.add, op1=ALU.mult), waits=hw1)
                    tok_h1[hc] = P.tick("vector", cDVE)
                    stw["gel_rd"] = tok_h1[hc]
                    stw["ps"][hc % 2] = tok_h1[hc]
                for ec in range(NG):
                    pb = PB[ec % 2]
                    pw0 = [tok_h1[NHC - 1]]
                    if stw.get(f"w2ps{ec % 2}"):
                        pw0.append(stw[f"w2ps{ec % 2}"])
                    first = True
                    for hc in range(NHC):
                        wb2 = hc % 2
                        wv = [stw["w2_rd"][wb2]] if stw["w2_rd"][wb2] else []
                        tl = P.dma("gpsimd", c_w2[:, wb2, :],
                                   w2_d[l, 128 * hc: 128 * hc + 128, :], cW2[wb2],
                                   waits=wv + [tok_h1[hc], tok_hln])
                        P.op("tensor", lambda e, pb=pb, wb2=wb2, hc=hc, ec=ec, first=first:
                             e.matmul(pb.ap()[:, :], c_w2[:, wb2, 128 * ec: 128 * ec + 128],
                                      h1t[:, hc, :], start=first, stop=(hc == NHC - 1)),
                             waits=[tl] + (pw0 if first else []))
                        stw["w2_rd"][wb2] = P.tick("tensor", cPE)
                        first = False
                    tok_w2mm = stw["w2_rd"][(NHC - 1) % 2]
                    c0 = 2 + 512 * tt
                    P.op("vector", lambda e, pb=pb, ec=ec, c0=c0, l=l: e.scalar_tensor_tensor(
                        out=vL[:, ec, c0:c0 + 512], in0=pb.ap()[:, :],
                        scalar=b2[:, l, ec:ec + 1], in1=vL[:, ec, c0:c0 + 512],
                        op0=ALU.add, op1=ALU.add), waits=[tok_w2mm] + dbg_guard())
                    stw[f"w2ps{ec % 2}"] = P.tick("vector", cDVE)
                    tok_u_upd.append(stw[f"w2ps{ec % 2}"])
                for hc in range(NHC):
                    stw["h1_rd"][hc] = stw["w2_rd"][(NHC - 1) % 2]
            tok_layer = tok_u_upd[-1]
            dump(f"y{l}", vL[:, :, 2:2050], 2048, [tok_layer])

            if l == 0:
                tok_u = exchange_boundary(cc4_in, cc4_out, tok_layer)
            else:
                tok_u = tok_layer
            if l == 1:
                tok_x2r = P.dma("sync", x2r, x2_spill.rearrange("(g p) t -> p g t", p=128),
                                cLX, waits=[tok_spill, tok_layer])

    if "C" in phases and "B" in phases:
        def apply_final(c0, wdt, tok_bcs):
            for g in range(NG):
                P.op("vector", lambda e, g=g, c0=c0, wdt=wdt: e.tensor_mul(
                    out=bc_t1[:, 0:wdt], in0=vL[:, g, c0:c0 + wdt], in1=bc_rb[:, 0:wdt]),
                    waits=[tok_bcs, tok_x2r] + dbg_guard())
                P.op("vector", lambda e, g=g, c0=c0, wdt=wdt: e.tensor_sub(
                    out=scr1[0:128, 0:wdt], in0=bc_t1[:, 0:wdt], in1=bc_mb[:, 0:wdt]))
                P.op("vector", lambda e, g=g, c0=c0, wdt=wdt: e.tensor_add(
                    out=vL[:, g, c0:c0 + wdt], in0=scr1[0:128, 0:wdt],
                    in1=x2r[:, g, c0 - 2: c0 - 2 + wdt]))

        tok_fin = layernorm(lambda g, c0, wdt: vL[:, g, c0:c0 + wdt], tiles_own,
                            apply_final, [tok_u])
        tok_y = P.dma("sync", y_out.rearrange("(g p) t -> p g t", p=128),
                      vL[:, :, 2:2050], cST, waits=[tok_fin])
        dump_toks.append(tok_y)
    else:
        tok_y = P.dma("sync", y_out.rearrange("(g p) t -> p g t", p=128),
                      vL[:, :, 2:2050], cST,
                      waits=[tok_u if "B" in phases else (tok_halo if "A" in phases else tok_x)])
        dump_toks.append(tok_y)

    P.op("sync", lambda e: e.nop(), waits=dump_toks + [(cST, cST.n)])
    P.emit()
    es.close()
    return nc


def kernel(**inputs):
    in_maps = stage_inputs(inputs)
    nc = build_program(debug=DEBUG, phases=PHASES)
    res = run_bass_kernel_spmd(nc, in_maps, core_ids=list(range(8)))
    out = np.zeros((2, SEQ, D), np.float32)
    for c in range(8):
        b, s = c // 4, c % 4
        y = np.asarray(res.results[c]["y"], np.float32)
        out[b, TOK * s: TOK * (s + 1), :] = y.T
    return out


# version 05
# speedup vs baseline: 1.1127x; 1.1127x over previous
import dataclasses
import numpy as np
import ml_dtypes
import concourse.bass as bass
import concourse.mybir as mybir
from concourse.bass_utils import run_bass_kernel_spmd
from contextlib import ExitStack

F32 = mybir.dt.float32
BF16 = mybir.dt.bfloat16
I32 = mybir.dt.int32
U8 = mybir.dt.uint8
ALU = mybir.AluOpType
AF = mybir.ActivationFunctionType
DR = mybir.MatmulPerfMode.DoubleRow
AX = mybir.AxisListType
BF = ml_dtypes.bfloat16

D = 1024
NG = 8
TOK = 2048
COLS = 2052
SEQ = 8192
HEADS = 8
DH = 64
WIN = 512
INNER = 512
HID = 2730
NHC = 22
HC_LAST = 42
QK_SCALE = 8.0
NEG = -1e30
LNEPS = 1e-5

DEBUG = []
PHASES = "ABC"
ALIAS_CHANNELS = False


class Chan:
    __slots__ = ("sem", "n", "name")
    def __init__(self, sem, name):
        self.sem = sem
        self.n = 0
        self.name = name


class Prog:
    ENGINES = ("tensor", "vector", "scalar", "gpsimd", "sync")

    def __init__(self, nc, es):
        self.nc = nc
        self.es = es
        self.ops = {e: [] for e in self.ENGINES}
        self.dumps = []

    def chan(self, name):
        sem = self.es.enter_context(self.nc.semaphore(name))
        return Chan(sem, name)

    def op(self, engine, fn, waits=()):
        self.ops[engine].append([tuple(waits), fn, None, 0])

    def tick(self, engine, ch, by=1):
        rec = self.ops[engine][-1]
        assert rec[2] is None, "op already ticked"
        rec[2] = ch
        rec[3] = by
        ch.n += by
        return (ch, ch.n)

    def dma(self, engine, out, in_, ch, waits=(), **kw):
        self.op(engine, lambda e, out=out, in_=in_, kw=kw: e.dma_start(out=out, in_=in_, **kw),
                waits=waits)
        return self.tick(engine, ch, by=16)

    def emit(self):
        block = self.es.enter_context(self.nc.Block())
        for ename in self.ENGINES:
            ops = self.ops[ename]
            if not ops:
                continue

            def body(eng, ops=ops):
                last_wait = {}
                for waits, fn, inc_ch, inc_by in ops:
                    for (ch, cnt) in waits:
                        if cnt > last_wait.get(ch.name, -1):
                            eng.wait_ge(ch.sem, cnt)
                            last_wait[ch.name] = cnt
                    ins = fn(eng)
                    if inc_ch is not None:
                        ins.then_inc(inc_ch.sem, inc_by)
            getattr(block, ename)(body)


class Arena:
    def __init__(self, nc, name, nbytes):
        nbytes = (nbytes + 63) & ~63
        self.h = nc.alloc_sbuf_tensor(name, [128, nbytes], U8)
        self.nbytes = nbytes

    @staticmethod
    def _shape(ap, shape):
        if len(shape) <= 1:
            return ap
        names = "abcdefg"[: len(shape)]
        spec = "p (" + " ".join(names) + ") -> p " + " ".join(names)
        kw = {names[i]: shape[i] for i in range(1, len(shape))}
        return ap.rearrange(spec, **kw)

    def f32(self, off, *shape):
        n = int(np.prod(shape))
        assert off % 64 == 0 and off + 4 * n <= self.nbytes, (off, shape, self.nbytes)
        return self._shape(self.h.ap().bitcast(F32)[:, off // 4: off // 4 + n], shape)

    def bf16(self, off, *shape):
        n = int(np.prod(shape))
        assert off % 64 == 0 and off + 2 * n <= self.nbytes, (off, shape, self.nbytes)
        return self._shape(self.h.ap().bitcast(BF16)[:, off // 2: off // 2 + n], shape)


def _rot_tables():
    half = np.arange(0, DH, 2, dtype=np.float64)
    inv_freq = 1.0 / 10000.0 ** (half / DH)
    t = np.arange(SEQ, dtype=np.float64)
    fr = t[:, None] * inv_freq[None, :]
    fr = np.concatenate([fr, fr], axis=-1)
    base = (half + 0.4 * DH) / (1.4 * DH)
    power = (t - SEQ // 2) / float(WIN // 2)
    scale = np.concatenate([base[None, :] ** power[:, None]] * 2, axis=-1)
    cos = np.cos(fr)
    sin = np.sin(fr)
    sign = np.concatenate([-np.ones(DH // 2), np.ones(DH // 2)])[None, :]
    Aq = (cos * scale).astype(np.float32)
    Bq = (sign * sin * scale).astype(np.float32)
    Ak = (cos / scale).astype(np.float32)
    Bk = (sign * sin / scale).astype(np.float32)
    return Aq, Bq, Ak, Bk


def _edge_masks():
    i = np.arange(128)[:, None]
    j = np.arange(128)[None, :]
    d0 = np.where(j < i, NEG, 0.0).astype(BF)
    d4 = np.where(j > i, NEG, 0.0).astype(BF)
    return d0, d4


def _w0_masks(is_s0):
    i = np.arange(128)[:, None]
    j = np.arange(128)[None, :]
    tiles = np.zeros((4, 5, 128, 128), np.float32)
    for qi in range(4):
        for bs in range(5):
            jc = qi + bs
            if bs == 0:
                m = np.where(j < i, NEG, 0.0)
            elif bs == 4:
                m = np.where(j > i, NEG, 0.0)
            else:
                m = np.zeros((128, 128), np.float32)
            if is_s0 and jc < 4:
                m = np.full((128, 128), NEG, np.float32)
            tiles[qi, bs] = m
    return tiles.astype(BF)


def stage_inputs(inputs):
    f = np.float32
    x = np.ascontiguousarray(inputs["L0_attn_x"], dtype=f)
    Wt = np.ascontiguousarray(np.asarray(inputs["L0_attn_W"], dtype=f).T)
    dec = np.ascontiguousarray(np.asarray(inputs["L0_attn_decay_logit"], f).reshape(NG, 128).T)
    bias_b = np.ascontiguousarray(np.asarray(inputs["L0_attn_b"], f).reshape(NG, 128).T)
    conv_b = np.ascontiguousarray(np.asarray(inputs["L0_attn_conv_b"], f).reshape(NG, 128).T)
    conv_w = np.ascontiguousarray(
        np.asarray(inputs["L0_attn_conv_w"], f).reshape(NG, 128, 5).transpose(1, 0, 2))
    wqkv = np.ascontiguousarray(np.asarray(inputs["L0_ffn_wqkv"], f))
    wo = np.ascontiguousarray(np.asarray(inputs["L0_ffn_wo"], f))
    w1p = np.zeros((2, D, 2 * NHC * 128), f)
    w1p[:, :, :HID] = np.asarray(inputs["L0_ffn_w1"], f)[:, :, :HID]
    w1p[:, :, NHC * 128: NHC * 128 + HID] = np.asarray(inputs["L0_ffn_w1"], f)[:, :, HID:]
    w2p = np.zeros((2, NHC * 128, 1024), f)
    w2p[:, :HID] = np.asarray(inputs["L0_ffn_w2"], f)
    qsr = np.ascontiguousarray(np.repeat(np.asarray(inputs["L0_ffn_q_scale"], f)[:, None, :], 128, 1))
    ksr = np.ascontiguousarray(np.repeat(np.asarray(inputs["L0_ffn_k_scale"], f)[:, None, :], 128, 1))
    ln_g = np.ascontiguousarray(np.asarray(inputs["L0_ffn_ln_g"], f).reshape(2, NG, 128).transpose(2, 0, 1))
    ln_b = np.ascontiguousarray(np.asarray(inputs["L0_ffn_ln_b"], f).reshape(2, NG, 128).transpose(2, 0, 1))
    b1p = np.zeros((2, 2, NHC * 128), f)
    b1p[:, 0, :HID] = np.asarray(inputs["L0_ffn_b1"], f)[:, :HID]
    b1p[:, 1, :HID] = np.asarray(inputs["L0_ffn_b1"], f)[:, HID:]
    b1s = np.ascontiguousarray(b1p.reshape(2, 2, NHC, 128).transpose(3, 0, 1, 2))
    b2s = np.ascontiguousarray(np.asarray(inputs["L0_ffn_b2"], f).reshape(2, NG, 128).transpose(2, 0, 1))

    Aq, Bq, Ak, Bk = _rot_tables()
    d0, d4 = _edge_masks()
    shared = dict(
        Wt=Wt, dec=dec, bias_b=bias_b, conv_b=conv_b, conv_w=conv_w,
        wqkv=wqkv, wo=wo, w1=w1p, w2=w2p, qsr=qsr, ksr=ksr,
        qsr_shuf=np.ascontiguousarray(qsr[:, :, [32 + i for i in range(32)] + list(range(32))]),
        ksr_shuf=np.ascontiguousarray(ksr[:, :, [32 + i for i in range(32)] + list(range(32))]),
        identf=np.eye(128).astype(f),
        bndmask3=np.ascontiguousarray(np.broadcast_to(
            np.where(np.arange(128)[None, :] < 127, NEG, 0.0).astype(BF), (128, 128))),
        ln_g=ln_g, ln_b=ln_b, b1=b1s, b2=b2s,
        mask_d0=np.ascontiguousarray(d0), mask_d4=np.ascontiguousarray(d4),
        ident=np.eye(128).astype(BF),
        ones_col=np.ones((128, 1), BF),
        ones1_128=np.ones((1, 128), BF),
        ones1_64=np.ones((1, 64), BF),
        on64_d=np.ones((65, 64), BF),
    )

    in_maps = []
    for c in range(8):
        b, s = c // 4, c % 4
        t0 = TOK * s
        xs = np.zeros((D, COLS), f)
        lo, hi = t0 - 2, t0 + TOK + 2
        slo, shi = max(lo, 0), min(hi, SEQ)
        xs[:, slo - lo: shi - lo] = x[b, slo:shi, :].T
        xs = np.ascontiguousarray(xs)

        def tt(tab):
            return np.ascontiguousarray(
                tab[t0:t0 + TOK].reshape(16, 128, DH).transpose(1, 0, 2))
        m_sel = np.zeros((128, 4), f)
        e_sel = np.zeros((128, 4), f)
        for r in range(4):
            if r < s:
                m_sel[:, r] = 1.0
                e_sel[:, r] = float(s - 1 - r)
        gate = np.full((128, 1), 0.0 if s == 0 else 1.0, f)
        m = dict(shared)
        m.update(
            x_in=xs,
            tabAq=tt(Aq), tabBq=tt(Bq), tabAk=tt(Ak), tabBk=tt(Bk),
            m_sel=m_sel, e_sel=e_sel, gate=gate,
            s0neg=np.full((1, 512), NEG if s == 0 else 0.0, BF),
            prev_off=np.array([[(s - 1) % 4]], np.int32),
        )
        in_maps.append(m)
    return in_maps


def build_program(debug=(), phases="ABC"):
    nc = bass.Bass("TRN2", target_bir_lowering=False)
    es = ExitStack()
    P = Prog(nc, es)

    def din(name, shape, dt=F32):
        return nc.dram_tensor(name, shape, dt, kind="ExternalInput").ap()

    x_in = din("x_in", [D, COLS])
    Wt_d = din("Wt", [D, D])
    dec_d = din("dec", [128, NG])
    bias_b_d = din("bias_b", [128, NG])
    conv_b_d = din("conv_b", [128, NG])
    conv_w_d = din("conv_w", [128, NG, 5])
    wqkv_d = din("wqkv", [2, D, 3 * INNER])
    wo_d = din("wo", [2, INNER, D])
    w1_d = din("w1", [2, D, 2 * NHC * 128])
    w2_d = din("w2", [2, NHC * 128, D])
    qsr_d = din("qsr", [2, 128, DH])
    ksr_d = din("ksr", [2, 128, DH])
    ln_g_d = din("ln_g", [128, 2, NG])
    ln_b_d = din("ln_b", [128, 2, NG])
    b1_d = din("b1", [128, 2, 2, NHC])
    b2_d = din("b2", [128, 2, NG])
    qsr_shuf_d = din("qsr_shuf", [2, 128, DH])
    ksr_shuf_d = din("ksr_shuf", [2, 128, DH])
    identf_d = din("identf", [128, 128])
    bndmask3_d = din("bndmask3", [128, 128], BF16)
    s0neg_d = din("s0neg", [1, 512], BF16)
    mask_d0_d = din("mask_d0", [128, 128], BF16)
    mask_d4_d = din("mask_d4", [128, 128], BF16)
    ident_d = din("ident", [128, 128], BF16)
    ones_col_d = din("ones_col", [128, 1], BF16)
    ones1_128_d = din("ones1_128", [1, 128], BF16)
    ones1_64_d = din("ones1_64", [1, 64], BF16)
    on64_d = din("on64_d", [65, 64], BF16)
    tabAq_d = din("tabAq", [128, 16, DH])
    tabBq_d = din("tabBq", [128, 16, DH])
    tabAk_d = din("tabAk", [128, 16, DH])
    tabBk_d = din("tabBk", [128, 16, DH])
    m_sel_d = din("m_sel", [128, 4])
    e_sel_d = din("e_sel", [128, 4])
    gate_d = din("gate", [128, 1])
    prev_off_d = din("prev_off", [1, 1], I32)

    y_out = nc.dram_tensor("y", [D, TOK], F32, kind="ExternalOutput").ap()

    x2_spill = nc.dram_tensor("x2_spill", [D, TOK], F32, kind="Internal").ap()
    cc1_in = nc.dram_tensor("cc1_in", [128, NG], F32, kind="Internal").ap()
    cc1_out = nc.dram_tensor("cc1_out", [4 * 128, NG], F32, kind="Internal").ap()
    cc2_in = nc.dram_tensor("cc2_in", [128, NG], F32, kind="Internal").ap()
    cc2_out = nc.dram_tensor("cc2_out", [4 * 128, NG], F32, kind="Internal").ap()
    R3 = 1552
    cc3_in = [nc.dram_tensor(f"cc3_in{l}", [R3, 1040], BF16, kind="Internal").ap() for l in range(2)]
    cc3_out = [nc.dram_tensor(f"cc3_out{l}", [4 * R3, 1040], BF16, kind="Internal").ap() for l in range(2)]
    cc4_in = nc.dram_tensor("cc4_in", [128, NG], F32, kind="Internal").ap()
    cc4_out = nc.dram_tensor("cc4_out", [4 * 128, NG], F32, kind="Internal").ap()
    RG = [[0, 1, 2, 3], [4, 5, 6, 7]]

    AR_S = Arena(nc, "arS", NG * COLS * 4)
    AR_L = Arena(nc, "arL", NG * COLS * 4)
    AR_C = Arena(nc, "arC", 50496)

    def sb(name, shape, dt=F32):
        return es.enter_context(nc.sbuf_tensor(name, shape, dt))

    ident = sb("t_ident", [128, 128], BF16)
    mask_d0 = sb("t_mask_d0", [128, 128], BF16)
    mask_d4 = sb("t_mask_d4", [128, 128], BF16)
    ones_col = sb("t_ones_col", [128, 1], BF16)
    ones1_128 = sb("t_ones1_128", [1, 128], BF16)
    ones1_64 = sb("t_ones1_64", [1, 64], BF16)
    dec = sb("t_dec", [128, NG])
    smalla = sb("smalla", [128, NG])
    smallb = sb("smallb", [128, NG])
    a_col = sb("a_col", [128, NG])
    om_a = sb("om_a", [128, NG])
    a2k = sb("a2k", [128, NG])
    lnA = sb("lnA", [128, NG])
    coef = sb("coef", [128, 4, NG])
    bcb = sb("bcb", [128, NG])
    conv_w = sb("t_conv_w", [128, NG, 5])
    qsr = sb("t_qsr", [128, 2, DH])
    ksr = sb("t_ksr", [128, 2, DH])
    ln_g = sb("t_ln_g", [128, 2, NG])
    ln_b = sb("t_ln_b", [128, 2, NG])
    b1 = sb("t_b1", [128, 2, 2, NHC])
    b2 = sb("t_b2", [128, 2, NG])
    m_sel = sb("t_m_sel", [128, 4])
    e_sel = sb("t_e_sel", [128, 4])
    gate = sb("t_gate", [128, 1])
    poff = sb("poff", [1, 1], I32)
    hb1 = sb("hb1", [128, NG])
    hb2 = sb("hb2", [128, NG])
    wc_all = sb("wc_all", [128, 4, NG])
    sinit = sb("sinit", [128, NG])
    qsr_shuf = sb("t_qsr_shuf", [128, 2, DH])
    ksr_shuf = sb("t_ksr_shuf", [128, 2, DH])
    identf = sb("t_identf", [128, 128])
    bndmask3 = sb("t_bndmask3", [128, 128], BF16)
    s0neg = sb("t_s0neg", [1, 512], BF16)
    negm = sb("negm", [1, 512], BF16)
    r64 = sb("r64", [65, 1024])
    r64h = sb("r64h", [65, 1024], BF16)
    on64 = sb("on64", [65, 64], BF16)
    rb64 = sb("rb64", [64, 512])
    mst = sb("mst", [128, 4])
    qbndt = sb("qbndt", [128, 4], BF16)
    qfk = sb("qfk", [128, 4, 128], BF16)

    nrm = sb("nrm", [128, 8])
    nrm2 = sb("nrm2", [128, 8])
    rqv = sb("rqv", [128, 8])
    qbf = AR_C.bf16(42368, 512)
    ub = [AR_C.bf16(40256, 8, 129), AR_C.bf16(44352, 8, 129)]
    r_mu = sb("r_mu", [1, 512])
    r_var = sb("r_var", [1, 512])
    r_rstd = sb("r_rstd", [1, 512])
    r_mrs = sb("r_mrs", [1, 512])
    rb_hi = sb("rb_hi", [1, 512], BF16)
    rb_lo = sb("rb_lo", [1, 512], BF16)
    mb_hi = sb("mb_hi", [1, 512], BF16)
    mb_lo = sb("mb_lo", [1, 512], BF16)


    PB = [es.enter_context(nc.psum_tensor(f"PB{i}", [128, 512], F32)) for i in range(4)]
    PS640 = es.enter_context(nc.psum_tensor("PS640", [128, 640], F32))
    PW = es.enter_context(nc.psum_tensor("PW", [65, 512], F32))
    PT = es.enter_context(nc.psum_tensor("PT", [128, 512], BF16))

    cLD = P.chan("cLD")
    cSM = P.chan("cSM")
    cLX = P.chan("cLX")
    cSP = P.chan("cSP")
    cTB = [P.chan("cTB0"), P.chan("cTB1")]
    cCCP = P.chan("cCCP")
    cCCR = P.chan("cCCR")
    cWQ = [P.chan("cWQ0"), P.chan("cWQ1")]
    cWT = [P.chan("cWT0"), P.chan("cWT1")]
    cWO = [P.chan("cWO0"), P.chan("cWO1")]
    cW1 = [P.chan("cW10"), P.chan("cW11")]
    cW2 = [P.chan("cW20"), P.chan("cW21")]
    cLG = P.chan("cLG")
    cST = P.chan("cST")
    if ALIAS_CHANNELS:
        cSM = cLX = cSP = cCCP = cCCR = cLD
        cTB = [cLD, cLD]
        cWQ = [cLG, cLG]
        cWT = [cLD, cLD]
        cWO = [cLG, cLG]
        cW1 = [cLG, cLG]
        cW2 = [cLG, cLG]
    cPE = P.chan("cPE")
    cDVE = P.chan("cDVE")
    cACT = P.chan("cACT")
    cPOOL = P.chan("cPOOL")
    cCC = P.chan("cCC")

    dump_toks = []

    def dbg_guard():
        return [(cST, cST.n)] if debug else []

    def dump(name, view3d, ncols, waits):
        if name not in debug:
            return
        dd = nc.dram_tensor("dbg_" + name, [D, ncols], F32, kind="ExternalOutput").ap()
        t = P.dma("sync", dd.rearrange("(g p) t -> p g t", p=128), view3d, cST, waits=waits)
        dump_toks.append(t)

    vS = AR_S.f32(0, NG, COLS)
    vL = AR_L.f32(0, NG, COLS)

    tok_x = P.dma("sync", vS, x_in.rearrange("(g p) t -> p g t", p=128), cLX)
    small_loads = [
        (dec, dec_d), (bcb, bias_b_d), (conv_w, conv_w_d),
        (qsr, qsr_d.rearrange("l p d -> p l d")), (ksr, ksr_d.rearrange("l p d -> p l d")),
        (ln_g, ln_g_d), (ln_b, ln_b_d), (b1, b1_d), (b2, b2_d),
        (m_sel, m_sel_d), (e_sel, e_sel_d), (gate, gate_d), (poff, prev_off_d),
        (smalla, conv_b_d),
        (ident, ident_d), (mask_d0, mask_d0_d), (mask_d4, mask_d4_d),
        (qsr_shuf, qsr_shuf_d.rearrange("l p d -> p l d")),
        (ksr_shuf, ksr_shuf_d.rearrange("l p d -> p l d")),
        (identf, identf_d), (bndmask3, bndmask3_d), (s0neg, s0neg_d),
        (ones_col, ones_col_d), (ones1_128, ones1_128_d), (ones1_64, ones1_64_d),
    ]
    for dst, src in small_loads:
        tok_small = P.dma("sync", dst[:], src, cSM)
    tok_on64 = P.dma("sync", on64[:, :], on64_d, cSM)
    tok_small = tok_on64
    P.op("vector", lambda e: e.tensor_add(out=bcb[:], in0=bcb[:], in1=smalla[:]),
         waits=[tok_small])
    tok_bcb = P.tick("vector", cDVE)

    def chain(engine, fn, waits, ch):
        P.op(engine, fn, waits=waits)
        return P.tick(engine, ch)

    t = chain("scalar", lambda e: e.activation(out=smalla[:], in_=dec[:], func=AF.Exp,
                                               scale=-1.0), [tok_small, tok_bcb], cACT)
    t = chain("vector", lambda e: e.tensor_scalar(out=smalla[:], in0=smalla[:], scalar1=1.0,
                                                  scalar2=None, op0=ALU.add), [t], cDVE)
    t = chain("vector", lambda e: e.reciprocal(out=a_col[:], in_=smalla[:]), [t], cDVE)
    tok_a = t
    t = chain("vector", lambda e: e.tensor_scalar(out=om_a[:], in0=a_col[:], scalar1=-1.0,
                                                  scalar2=-1.0, op0=ALU.mult,
                                                  op1=ALU.subtract), [t], cDVE)
    tok_oma = t
    cur, oth = a_col, None
    t = chain("vector", lambda e: e.tensor_mul(out=smalla[:], in0=a_col[:], in1=a_col[:]),
              [t], cDVE)
    cur, oth = smalla, smallb
    for i in range(10):
        t = chain("vector", lambda e, cur=cur, oth=oth: e.tensor_mul(
            out=oth[:], in0=cur[:], in1=cur[:]), [t], cDVE)
        cur, oth = oth, cur
    t = chain("vector", lambda e, cur=cur: e.tensor_scalar_max(out=a2k[:], in0=cur[:],
                                                               scalar1=1e-30), [t], cDVE)
    t = chain("scalar", lambda e: e.activation(out=lnA[:], in_=a2k[:], func=AF.Ln), [t], cACT)
    for r in range(4):
        t = chain("vector", lambda e, r=r: e.tensor_scalar(
            out=coef[:, r, :], in0=lnA[:], scalar1=e_sel[:, r:r + 1], scalar2=None,
            op0=ALU.mult), [t], cDVE)
    t = chain("scalar", lambda e: e.activation(out=coef[:], in_=coef[:], func=AF.Exp), [t], cACT)
    for r in range(4):
        t = chain("vector", lambda e, r=r: e.tensor_scalar(
            out=coef[:, r, :], in0=coef[:, r, :], scalar1=m_sel[:, r:r + 1], scalar2=None,
            op0=ALU.mult), [t], cDVE)
    tok_coef = t

    stgA = AR_C.bf16(16384, NG, 512)
    stgB = AR_C.bf16(24768, NG, 512)
    scr1 = AR_C.f32(0, 2048)
    scr2 = AR_C.f32(8192, 2048)
    bc_rb = AR_C.f32(44352, 512)
    bc_mb = AR_C.f32(46400, 512)
    bc_t1 = AR_C.f32(48448, 512)
    pr_s1 = AR_C.f32(16704, 512)
    pr_s2 = AR_C.f32(18752, 512)
    pr_s3 = AR_C.f32(20800, 512)

    ln_state = {"mm": None, "apply": None, "rows": None}

    def layernorm(src3, tiles, apply_ops, waits_src):
        for (c0, w) in tiles:
            cast_waits = list(waits_src)
            if ln_state["mm"] is not None:
                cast_waits.append(ln_state["mm"])
            for g in range(NG):
                P.op("scalar", lambda e, g=g, c0=c0, w=w: e.activation(
                    out=stgA[:, g, 0:w], in_=src3(g, c0, w), func=AF.Identity),
                    waits=cast_waits)
                P.op("scalar", lambda e, g=g, c0=c0, w=w: e.activation(
                    out=stgB[:, g, 0:w], in_=src3(g, c0, w), func=AF.Square))
            tok_cast = P.tick("scalar", cACT)
            mm_waits = [tok_cast]
            if ln_state["rows"] is not None:
                mm_waits.append(ln_state["rows"])
            for g in range(NG):
                P.op("tensor", lambda e, g=g, w=w: e.matmul(
                    PB[0].ap()[0:1, 0:w], ones_col[:, 0:1], stgA[:, g, 0:w],
                    start=(g == 0), stop=(g == NG - 1)), waits=mm_waits)
            for g in range(NG):
                P.op("tensor", lambda e, g=g, w=w: e.matmul(
                    PB[1].ap()[0:1, 0:w], ones_col[:, 0:1], stgB[:, g, 0:w],
                    start=(g == 0), stop=(g == NG - 1)))
            tok_mm = P.tick("tensor", cPE)
            ln_state["mm"] = tok_mm
            P.op("vector", lambda e, w=w: e.tensor_scalar(
                out=r_mu[0:1, 0:w], in0=PB[0].ap()[0:1, 0:w], scalar1=1.0 / D, scalar2=None,
                op0=ALU.mult), waits=[tok_mm])
            P.op("vector", lambda e, w=w: e.tensor_scalar(
                out=r_var[0:1, 0:w], in0=PB[1].ap()[0:1, 0:w], scalar1=1.0 / D, scalar2=None,
                op0=ALU.mult))
            tok_rows = P.tick("vector", cDVE)
            ln_state["rows"] = tok_rows
            P.op("vector", lambda e, w=w: e.tensor_mul(
                out=r_mrs[0:1, 0:w], in0=r_mu[0:1, 0:w], in1=r_mu[0:1, 0:w]))
            P.op("vector", lambda e, w=w: e.scalar_tensor_tensor(
                out=r_var[0:1, 0:w], in0=r_var[0:1, 0:w], scalar=LNEPS, in1=r_mrs[0:1, 0:w],
                op0=ALU.add, op1=ALU.subtract))
            tok_var = P.tick("vector", cDVE)
            P.op("scalar", lambda e, w=w: e.activation(
                out=r_mrs[0:1, 0:w], in_=r_var[0:1, 0:w], func=AF.Ln), waits=[tok_var])
            P.op("scalar", lambda e, w=w: e.activation(
                out=r_rstd[0:1, 0:w], in_=r_mrs[0:1, 0:w], func=AF.Exp, scale=-0.5))
            tok_rstd = P.tick("scalar", cACT)
            P.op("vector", lambda e, w=w: e.tensor_mul(
                out=r_mrs[0:1, 0:w], in0=r_mu[0:1, 0:w], in1=r_rstd[0:1, 0:w]), waits=[tok_rstd])
            tok_mrs = P.tick("vector", cDVE)
            P.op("scalar", lambda e, w=w: e.activation(
                out=rb_hi[0:1, 0:w], in_=r_rstd[0:1, 0:w], func=AF.Identity), waits=[tok_mrs])
            P.op("scalar", lambda e, w=w: e.activation(
                out=mb_hi[0:1, 0:w], in_=r_mrs[0:1, 0:w], func=AF.Identity))
            P.op("scalar", lambda e, w=w: e.activation(
                out=r_mu[0:1, 0:w], in_=rb_hi[0:1, 0:w], func=AF.Identity))
            tok_hi = P.tick("scalar", cACT)
            P.op("vector", lambda e, w=w: e.tensor_sub(
                out=r_mu[0:1, 0:w], in0=r_rstd[0:1, 0:w], in1=r_mu[0:1, 0:w]), waits=[tok_hi])
            tok_lof1 = P.tick("vector", cDVE)
            P.op("scalar", lambda e, w=w: e.activation(
                out=rb_lo[0:1, 0:w], in_=r_mu[0:1, 0:w], func=AF.Identity), waits=[tok_lof1])
            P.op("scalar", lambda e, w=w: e.activation(
                out=r_mu[0:1, 0:w], in_=mb_hi[0:1, 0:w], func=AF.Identity))
            tok_hi2 = P.tick("scalar", cACT)
            P.op("vector", lambda e, w=w: e.tensor_sub(
                out=r_mu[0:1, 0:w], in0=r_mrs[0:1, 0:w], in1=r_mu[0:1, 0:w]), waits=[tok_hi2])
            tok_lof2 = P.tick("vector", cDVE)
            P.op("scalar", lambda e, w=w: e.activation(
                out=mb_lo[0:1, 0:w], in_=r_mu[0:1, 0:w], func=AF.Identity), waits=[tok_lof2])
            tok_lo = P.tick("scalar", cACT)
            bc_waits = [tok_lo]
            if ln_state["apply"] is not None:
                bc_waits.append(ln_state["apply"])
            P.op("tensor", lambda e, w=w: e.matmul(
                PB[2].ap()[:, 0:w], ones1_128[0:1, :], rb_hi[0:1, 0:w], start=True, stop=False),
                waits=bc_waits)
            P.op("tensor", lambda e, w=w: e.matmul(
                PB[2].ap()[:, 0:w], ones1_128[0:1, :], rb_lo[0:1, 0:w], start=False, stop=True))
            P.op("tensor", lambda e, w=w: e.matmul(
                PB[3].ap()[:, 0:w], ones1_128[0:1, :], mb_hi[0:1, 0:w], start=True, stop=False))
            P.op("tensor", lambda e, w=w: e.matmul(
                PB[3].ap()[:, 0:w], ones1_128[0:1, :], mb_lo[0:1, 0:w], start=False, stop=True))
            tok_bc = P.tick("tensor", cPE)
            P.op("scalar", lambda e, w=w: e.activation(
                out=bc_rb[:, 0:w], in_=PB[2].ap()[:, 0:w], func=AF.Identity), waits=[tok_bc])
            P.op("scalar", lambda e, w=w: e.activation(
                out=bc_mb[:, 0:w], in_=PB[3].ap()[:, 0:w], func=AF.Identity))
            tok_bcs = P.tick("scalar", cACT)
            apply_ops(c0, w, tok_bcs)
            ln_state["apply"] = P.tick("vector", cDVE)
        return ln_state["apply"]

    tok_attn_out_done = None
    tok_u0 = None
    tok_spill = None
    if "A" in phases:
        tiles_full = [(0, 512), (512, 512), (1024, 512), (1536, 512), (1540, 512)]

        def apply_ln1(c0, w, tok_bcs):
            for g in range(NG):
                P.op("vector", lambda e, g=g, c0=c0, w=w: e.tensor_mul(
                    out=bc_t1[:, 0:w], in0=vS[:, g, c0:c0 + w], in1=bc_rb[:, 0:w]),
                    waits=[tok_bcs])
                P.op("vector", lambda e, g=g, c0=c0, w=w: e.tensor_sub(
                    out=vL[:, g, c0:c0 + w], in0=bc_t1[:, 0:w], in1=bc_mb[:, 0:w]))

        tok_ln1 = layernorm(lambda g, c0, w: vS[:, g, c0:c0 + w], tiles_full, apply_ln1, [tok_x])
        dump("ln1", vL[:, :, 2:2050], 2048, [tok_ln1])
        if "bcdbg" in debug:
            dbg1 = nc.dram_tensor("dbg_bcdbg", [128, 1024], F32, kind="ExternalOutput").ap()
            dbg2 = nc.dram_tensor("dbg_rows", [1, 2048], F32, kind="ExternalOutput").ap()
            dump_toks.append(P.dma("sync", dbg1[:, 0:512], bc_rb[:], cST, waits=[tok_ln1]))
            dump_toks.append(P.dma("sync", dbg1[:, 512:1024], bc_mb[:], cST))
            dump_toks.append(P.dma("sync", dbg2[0:1, 0:512], r_mu[:], cST))
            dump_toks.append(P.dma("sync", dbg2[0:1, 512:1024], r_var[:], cST))
            dump_toks.append(P.dma("sync", dbg2[0:1, 1024:1536], r_rstd[:], cST))
            dump_toks.append(P.dma("sync", dbg2[0:1, 1536:2048], r_mrs[:], cST))

        P.op("gpsimd", lambda e: e.memset(scr2[:], 1.0))
        tok_ones = P.tick("gpsimd", cPOOL)
        scan_toks = []
        for g in range(NG):
            P.op("vector", lambda e, g=g: e.tensor_scalar(
                out=scr1[:], in0=scr2[:], scalar1=a_col[:, g:g + 1], scalar2=None, op0=ALU.mult),
                waits=[tok_ones, tok_a, tok_ln1])
            tok_ab = P.tick("vector", cDVE)
            P.op("vector", lambda e, g=g: e.tensor_tensor_scan(
                out=vS[:, g, 2:2050], data0=scr1[:], data1=vL[:, g, 2:2050],
                initial=0.0, op0=ALU.mult, op1=ALU.add), waits=[tok_ab])
            scan_toks.append(P.tick("vector", cDVE))
        P.op("vector", lambda e: e.tensor_copy(out=hb1[:], in_=vS[:, :, 2049]), waits=[scan_toks[-1]])
        tok_wc = P.tick("vector", cDVE)
        t1 = P.dma("sync", cc1_in, hb1[:], cCCP, waits=[tok_wc])
        P.op("gpsimd", lambda e: e.collective_compute(
            "AllGather", ALU.bypass, replica_groups=RG,
            ins=[cc1_in.opt()], outs=[cc1_out.opt()]), waits=[t1])
        tok_cc1 = P.tick("gpsimd", cCC)
        t2 = P.dma("gpsimd", wc_all[:], cc1_out.rearrange("(r p) g -> p r g", p=128), cCCR,
                   waits=[tok_cc1])
        ts = chain("vector", lambda e: e.tensor_mul(out=sinit[:], in0=coef[:, 0, :],
                                                    in1=wc_all[:, 0, :]), [t2, tok_coef], cDVE)
        for r in range(1, 4):
            ts = chain("vector", lambda e, r=r: e.tensor_mul(out=smalla[:], in0=coef[:, r, :],
                                                             in1=wc_all[:, r, :]), [ts], cDVE)
            ts = chain("vector", lambda e, r=r: e.tensor_add(out=sinit[:], in0=sinit[:],
                                                             in1=smalla[:]), [ts], cDVE)
        tok_sinit = ts
        if "emadbg" in debug:
            dbge = nc.dram_tensor("dbg_emadbg", [128, 512], F32, kind="ExternalOutput").ap()
            emadump = bc_rb
            P.op("vector", lambda e: e.tensor_copy(out=emadump[:, 0:32], in_=wc_all[:].rearrange("p a b -> p (a b)")), waits=[tok_sinit])
            P.op("vector", lambda e: e.tensor_copy(out=emadump[:, 32:40], in_=sinit[:]))
            P.op("vector", lambda e: e.tensor_copy(out=emadump[:, 40:72], in_=coef[:].rearrange("p a b -> p (a b)")))
            P.op("vector", lambda e: e.tensor_copy(out=emadump[:, 72:80], in_=a2k[:]))
            P.op("vector", lambda e: e.tensor_copy(out=emadump[:, 80:88], in_=a_col[:]))
            P.op("vector", lambda e: e.tensor_copy(out=emadump[:, 88:96], in_=lnA[:]))
            tok_ed = P.tick("vector", cDVE)
            dump_toks.append(P.dma("sync", dbge, emadump[:], cST, waits=[tok_ed]))
        for g in range(NG):
            P.op("vector", lambda e, g=g: e.tensor_scalar(
                out=scr1[:], in0=scr2[:], scalar1=a_col[:, g:g + 1], scalar2=None, op0=ALU.mult),
                waits=[tok_sinit])
            tok_ab = P.tick("vector", cDVE)
            P.op("vector", lambda e, g=g: e.tensor_tensor_scan(
                out=vS[:, g, 2:2050], data0=scr1[:], data1=vL[:, g, 2:2050],
                initial=sinit[:, g:g + 1], op0=ALU.mult, op1=ALU.add), waits=[tok_ab])
            P.tick("vector", cDVE)
        tok_scan2 = (cDVE, cDVE.n)
        dump("S", vS[:, :, 2:2050], 2048, [tok_scan2])

        wtb = [AR_C.bf16(33088, NG, 128), AR_C.bf16(35136, NG, 128)]
        tok_prev_evac = [tok_scan2, tok_scan2]
        tok_wtb_rd = [None, None]
        tok_prev_mm = None
        last_conv = None
        for tt in range(4):
            c0 = 2 + 512 * tt
            cast_w = [tok_scan2] + dbg_guard()
            if tok_prev_mm is not None:
                cast_w.append(tok_prev_mm)
            P.op("scalar", lambda e, c0=c0: e.activation(
                out=stgA[:, :, 0:512], in_=vS[:, :, c0:c0 + 512], func=AF.Identity),
                waits=cast_w)
            tok_scast = P.tick("scalar", cACT)
            for ec in range(NG):
                bi = ec % 2
                tw_w = [] if tok_wtb_rd[bi] is None else [tok_wtb_rd[bi]]
                tw = P.dma("gpsimd", wtb[bi][:],
                           Wt_d.rearrange("(g p) e -> p g e", p=128)[:, :, 128 * ec:128 * (ec + 1)],
                           cWT[bi], waits=tw_w)
                for g in range(NG):
                    P.op("vector", lambda e, bi=bi, g=g: e.tensor_scalar(
                        out=wtb[bi][:, g, :], in0=wtb[bi][:, g, :], scalar1=om_a[:, g:g + 1],
                        scalar2=None, op0=ALU.mult), waits=[tw])
                tok_fold = P.tick("vector", cDVE)
                pb = PB[ec % 2]
                for g in range(NG):
                    P.op("tensor", lambda e, pb=pb, bi=bi, g=g: e.matmul(
                        pb.ap()[:, :], wtb[bi][:, g, :], stgA[:, g, 0:512],
                        start=(g == 0), stop=(g == NG - 1)),
                        waits=[tok_fold, tok_scast, tok_prev_evac[ec % 2]])
                tok_ymm = P.tick("tensor", cPE)
                tok_wtb_rd[bi] = tok_ymm
                cbase = c0 - 2
                P.op("vector", lambda e, ec=ec, cbase=cbase: e.tensor_scalar(
                    out=scr1[0:128, 0:512], in0=vL[:, ec, cbase:cbase + 512],
                    scalar1=conv_w[:, ec, 0:1], scalar2=None, op0=ALU.mult),
                    waits=[tok_ln1])
                srcs = [scr1, scr2, scr1, scr2, scr1]
                for kk in range(1, 5):
                    P.op("vector", lambda e, ec=ec, cbase=cbase, kk=kk, si=srcs[kk - 1], so=srcs[kk]:
                         e.scalar_tensor_tensor(
                             out=so[0:128, 0:512], in0=vL[:, ec, cbase + kk:cbase + kk + 512],
                             scalar=conv_w[:, ec, kk:kk + 1], in1=si[0:128, 0:512],
                             op0=ALU.mult, op1=ALU.add))
                tok_conv = P.tick("vector", cDVE)
                last_conv = tok_conv
                P.op("vector", lambda e, pb=pb, ec=ec, c0=c0: e.scalar_tensor_tensor(
                    out=vS[:, ec, c0:c0 + 512], in0=pb.ap()[:, :], scalar=bcb[:, ec:ec + 1],
                    in1=scr1[0:128, 0:512], op0=ALU.add, op1=ALU.add),
                    waits=[tok_ymm, tok_bcb, tok_conv])
                tok_prev_evac[ec % 2] = P.tick("vector", cDVE)
            tok_prev_mm = tok_ymm
        tok_attn_out_done = tok_prev_evac[1]
        dump("attn_out", vS[:, :, 2:2050], 2048, [tok_attn_out_done])

        tok_xre = P.dma("sync", vL, x_in.rearrange("(g p) t -> p g t", p=128), cLX,
                        waits=[last_conv] + dbg_guard())
        tiles_own = [(2, 512), (514, 512), (1026, 512), (1538, 512)]

        def apply_x2(c0, w, tok_bcs):
            for g in range(NG):
                P.op("vector", lambda e, g=g, c0=c0, w=w: e.tensor_mul(
                    out=bc_t1[:, 0:w], in0=vS[:, g, c0:c0 + w], in1=bc_rb[:, 0:w]),
                    waits=[tok_bcs, tok_xre] + dbg_guard())
                P.op("vector", lambda e, g=g, c0=c0, w=w: e.tensor_sub(
                    out=scr2[0:128, 0:w], in0=bc_t1[:, 0:w], in1=bc_mb[:, 0:w]))
                P.op("vector", lambda e, g=g, c0=c0, w=w: e.tensor_add(
                    out=vS[:, g, c0:c0 + w], in0=scr2[0:128, 0:w], in1=vL[:, g, c0:c0 + w]))

        tok_x2 = layernorm(lambda g, c0, w: vS[:, g, c0:c0 + w], tiles_own, apply_x2, [tok_attn_out_done])
        dump("x2", vS[:, :, 2:2050], 2048, [tok_x2])
        tok_spill = P.dma("sync", x2_spill.rearrange("(g p) t -> p g t", p=128),
                          vS[:, :, 2:2050], cSP, waits=[tok_x2])

        def apply_u0(c0, w, tok_bcs):
            for g in range(NG):
                P.op("vector", lambda e, g=g, c0=c0, w=w: e.tensor_mul(
                    out=bc_t1[:, 0:w], in0=vS[:, g, c0:c0 + w], in1=bc_rb[:, 0:w]),
                    waits=[tok_bcs])
                P.op("vector", lambda e, g=g, c0=c0, w=w: e.tensor_sub(
                    out=vL[:, g, c0:c0 + w], in0=bc_t1[:, 0:w], in1=bc_mb[:, 0:w]))

        tok_u0 = layernorm(lambda g, c0, w: vS[:, g, c0:c0 + w], tiles_own, apply_u0, [tok_x2])
        P.op("gpsimd", lambda e: e.memset(vL[:, :, 0:2], 0.0), waits=[tok_xre])
        P.op("gpsimd", lambda e: e.memset(vL[:, :, 2050:2052], 0.0))
        tok_hz = P.tick("gpsimd", cPOOL)
        P.op("vector", lambda e: e.tensor_copy(out=hb1[:], in_=vL[:, :, 2049]), waits=[tok_u0])
        tok_b2 = P.tick("vector", cDVE)
        t3 = P.dma("sync", cc2_in, hb1[:], cCCP, waits=[tok_b2])
        P.op("gpsimd", lambda e: e.collective_compute(
            "AllGather", ALU.bypass, replica_groups=RG,
            ins=[cc2_in.opt()], outs=[cc2_out.opt()]), waits=[t3])
        tok_cc2 = P.tick("gpsimd", cCC)
        tok_poff = tok_small

        _poff_cache = {}

        def read_prev(e, base_ap, dst_ap, block_elems):
            if id(e) not in _poff_cache:
                _poff_cache[id(e)] = e.value_load(poff[0:1, 0:1])
            v = _poff_cache[id(e)]
            srcp = dataclasses.replace(base_ap, offset=v * block_elems + base_ap.offset)
            return e.dma_start(out=dst_ap, in_=srcp)

        P.op("gpsimd", lambda e: read_prev(e, cc2_out[0:128, :], hb2[:], 128 * NG),
             waits=[tok_cc2, tok_poff])
        tok_hb = P.tick("gpsimd", cCCR, by=16)
        P.op("vector", lambda e: e.tensor_scalar(
            out=vL[:, :, 1], in0=hb2[:], scalar1=gate[:, 0:1], scalar2=None, op0=ALU.mult),
            waits=[tok_hb, tok_hz])
        tok_halo = P.tick("vector", cDVE)
        dump("u0", vL[:, :, 2:2050], 2048, [tok_u0, tok_halo])

    import dataclasses as _dc

    def apx(ap, dims):
        part = list(ap.ap[0])
        return _dc.replace(ap, ap=[part] + [list(d) for d in dims])

    def offx(ap, delta):
        return _dc.replace(ap, offset=ap.offset + delta)

    def qp_ap(fc):
        return AR_S.bf16(4096 * fc, 2048)
    def kp_ap(fc):
        return AR_S.bf16(16384 + 4096 * fc, 2048)
    vqk_flat = AR_S.bf16(0, 32832)
    hln = AR_S.bf16(0, NG, 2048)
    h1t = AR_S.bf16(32768, NHC, 512)
    x2r = AR_S.f32(0, NG, 2048)

    c_v = AR_C.bf16(0, 16, 520)
    c_oa = AR_C.bf16(16704, 8, 512)
    obnd = AR_C.bf16(16704, 8, 128)
    c_aT = AR_C.bf16(24896, 3, 512)
    c_st = [AR_C.f32(27968, 512), AR_C.f32(30016, 512)]
    c_tab = AR_C.f32(32064, 2, 4, 64)
    c_wq = AR_C.bf16(34112, 2, 1536)
    c_wo = AR_C.bf16(40256, 2, 8, 128)
    def khalo_ap(fc):
        return AR_C.bf16(27968 + 2048 * fc, 1024)
    c_w1 = AR_C.bf16(0, 2, 2, NG, 128)
    c_w2 = AR_C.bf16(16384, 2, 128)

    wqkv3 = wqkv_d.rearrange("l (g p) c -> l p g c", p=128)
    w1r = w1_d.rearrange("l (g p) c -> l p g c", p=128)
    wo_r = wo_d.rearrange("l (h dh) e -> l dh h e", dh=64)

    def exchange_boundary(cc_in_t, cc_out_t, tok_ready):
        P.op("vector", lambda e: e.tensor_copy(out=hb1[:], in_=vL[:, :, 2049]),
             waits=[tok_ready])
        twc = P.tick("vector", cDVE)
        td = P.dma("sync", cc_in_t, hb1[:], cCCP, waits=[twc])
        P.op("gpsimd", lambda e: e.collective_compute(
            "AllGather", ALU.bypass, replica_groups=RG,
            ins=[cc_in_t.opt()], outs=[cc_out_t.opt()]), waits=[td])
        tcc = P.tick("gpsimd", cCC)
        P.op("gpsimd", lambda e: read_prev(e, cc_out_t[0:128, :], hb2[:], 128 * NG),
             waits=[tcc])
        th = P.tick("gpsimd", cCCR, by=16)
        P.op("vector", lambda e: e.tensor_scalar(
            out=vL[:, :, 1], in0=hb2[:], scalar1=gate[:, 0:1], scalar2=None, op0=ALU.mult),
            waits=[th])
        return P.tick("vector", cDVE)

    def _slice_cols(ap2, so, n):
        return ap2[:, so:so + n]

    def mm_bank_group(ops, waits):
        n = len(ops)
        for i, f in enumerate(ops):
            P.op("tensor",
                 (lambda e, f=f, s=(i == 0), t=(i == n - 1): f(e, s, t)),
                 waits=(waits if i == 0 else ()))

    sub = set(c for c in phases if c.isdigit()) or {"1", "2", "3", "4", "5"}
    if "B" in phases:
        tok_u = tok_halo

        for l in range(2):
            st = dict(
                ub_rd=[None, None], qkv_ev=[None, None, None], vmemset=None,
                tab_rd=[None, None], bct1=None, qbf_rd=None, pt=None,
                wq=[None, None], stats_rd=None, mrow_rd=None,
                st_rd=[None, None], aT_w=[None, None, None], aT_rd=[None, None, None],
                pw_rd=None, oa_rd=None, kv_dma=None,
            )
            P.op("gpsimd", lambda e: (e.memset(c_v[:, :, :], 1.0) if "v" not in phases
                                      else e.engine_nop()), waits=[tok_u])
            st["vmemset"] = P.tick("gpsimd", cPOOL)

            NTC = 2 if "x" in phases else 16
            for tc in range(NTC):
                bi = tc % 2
                pass
                uw = [tok_u] + ([st["ub_rd"][bi]] if st["ub_rd"][bi] else [])
                P.op("vector", lambda e, tc=tc, bi=bi: (e.tensor_copy(
                    out=ub[bi][:, :, 0:129], in_=vL[:, :, 1 + 128 * tc: 130 + 128 * tc])
                    if "u" not in phases else e.engine_nop()), waits=uw)
                tok_ub = P.tick("vector", cDVE)
                tbi = tc % 2
                for ti, tabd in enumerate([] if "t" in phases else [tabAq_d, tabBq_d, tabAk_d, tabBk_d]):
                    tok_tab = P.dma(
                        "sync", c_tab[:, tbi, ti, :], tabd[:, tc, :], cTB[tbi],
                        waits=([st["tab_rd"][tbi]] if (ti == 0 and st["tab_rd"][tbi]) else []))
                pw0 = [tok_ub]
                for sec in range(3):
                    if st["qkv_ev"][sec] is not None:
                        pw0.append(st["qkv_ev"][sec])
                for g in range(NG):
                    wbuf = g % 2
                    wv = [] if st["wq"][wbuf] is None else [st["wq"][wbuf]]
                    twq = (P.dma("gpsimd", c_wq[:, wbuf, :], wqkv3[l, :, g, :], cWQ[wbuf], waits=wv)
                           if "w" not in phases else (tok_ub))
                    off = 1 if g < 4 else 0
                    if "m" in phases:
                        st["wq"][wbuf] = twq
                        continue
                    for sec, pb in ((0, PB[0]), (1, PB[1]), (2, PB[2])):
                        P.op("tensor", lambda e, g=g, off=off, sec=sec, pb=pb, bi=bi,
                             wbuf=wbuf: e.matmul(
                                 pb.ap()[:, :], ub[bi][:, g, off:off + 128],
                                 c_wq[:, wbuf, 512 * sec: 512 * (sec + 1)],
                                 start=(g == 0), stop=(g == NG - 1)),
                             waits=(pw0 + [twq]) if True else [])
                    tok_mm = P.tick("tensor", cPE)
                    st["wq"][wbuf] = tok_mm
                if "m" in phases:
                    tok_mm = tok_ub
                tok_qkvmm = tok_mm
                st["ub_rd"][bi] = tok_qkvmm
                if "e" in phases:
                    st["qkv_ev"] = [tok_qkvmm] * 3
                    continue
                vdst = apx(c_v[:, tc, 0:520], [[65, 8], [1, 64]])
                P.op("scalar", lambda e, vdst=vdst: e.activation(
                    out=vdst, in_=PB[2].ap()[:, :], func=AF.Identity),
                    waits=[tok_qkvmm, st["vmemset"]])
                P.op("scalar", lambda e: e.activation(out=c_st[0][:, :], in_=PB[0].ap()[:, :],
                                                      func=AF.Identity))
                P.op("scalar", lambda e: e.activation(out=c_st[1][:, :], in_=PB[1].ap()[:, :],
                                                      func=AF.Identity))
                tok_qkev = P.tick("scalar", cACT)
                st["qkv_ev"] = [tok_qkev, tok_qkev, tok_qkev]

                for which in ((0, 1) if "p" not in phases else ()):
                    stg = c_st[which]
                    tabA = c_tab[:, tbi, 2 * which, :]
                    tabB = c_tab[:, tbi, 2 * which + 1, :]
                    scl = (qsr if which == 0 else ksr)[:, l, :]
                    scl_sh = (qsr_shuf if which == 0 else ksr_shuf)[:, l, :]
                    sqw = [tok_qkev]
                    if st["bct1"] is not None:
                        sqw.append(st["bct1"])
                    P.op("scalar", lambda e, stg=stg: e.activation(
                        out=pr_s3[:, :], in_=stg[:, :], func=AF.Square), waits=sqw)
                    tok_sq = P.tick("scalar", cACT)
                    P.op("vector", lambda e: e.reduce_sum(
                        out=nrm[:, :], in_=apx(pr_s3[:, :], [[64, 8], [1, 64]]), axis=AX.X),
                        waits=[tok_sq])
                    t = P.tick("vector", cDVE)
                    t = chain("scalar", lambda e: e.activation(out=nrm2[:], in_=nrm[:],
                                                               func=AF.Ln), [t], cACT)
                    t = chain("scalar", lambda e: e.activation(out=nrm2[:], in_=nrm2[:],
                                                               func=AF.Exp, scale=0.5),
                              [t], cACT)
                    t = chain("vector", lambda e: e.tensor_scalar_max(
                        out=nrm2[:], in0=nrm2[:], scalar1=1e-12), [t], cDVE)
                    t = chain("vector", lambda e: e.reciprocal(out=rqv[:], in_=nrm2[:]),
                              [t], cDVE)
                    t = chain("vector", lambda e, tabA=tabA, scl=scl: e.tensor_tensor(
                        out=pr_s1[:, 0:64], in0=tabA, in1=scl, op=ALU.mult),
                        [t, tok_tab], cDVE)
                    t = chain("vector", lambda e, tabB=tabB, scl_sh=scl_sh: e.tensor_tensor(
                        out=pr_s1[:, 64:128], in0=tabB, in1=scl_sh, op=ALU.mult), [t], cDVE)
                    rq_b = apx(rqv[:, :], [[1, 8], [0, 64]])
                    P.op("vector", lambda e, stg=stg, rq_b=rq_b: e.tensor_tensor(
                        out=apx(pr_s2[:, :], [[64, 8], [1, 64]]),
                        in0=apx(stg[:, :], [[64, 8], [1, 64]]), in1=rq_b, op=ALU.mult),
                        waits=[t])
                    P.op("vector", lambda e: e.tensor_copy(
                        out=apx(pr_s3[:, :], [[64, 8], [1, 32]]),
                        in_=apx(offx(pr_s2[:, :], 32), [[64, 8], [1, 32]])))
                    P.op("vector", lambda e: e.tensor_copy(
                        out=apx(offx(pr_s3[:, :], 32), [[64, 8], [1, 32]]),
                        in_=apx(pr_s2[:, :], [[64, 8], [1, 32]])))
                    tok_pp = P.tick("vector", cDVE)
                    tA_rep = apx(pr_s1[:, 0:64], [[0, 8], [1, 64]])
                    tB_rep = apx(pr_s1[:, 64:128], [[0, 8], [1, 64]])
                    P.op("vector", lambda e, stg=stg, tA_rep=tA_rep: e.tensor_tensor(
                        out=apx(stg[:, :], [[64, 8], [1, 64]]),
                        in0=apx(pr_s2[:, :], [[64, 8], [1, 64]]), in1=tA_rep, op=ALU.mult),
                        waits=[tok_pp])
                    P.op("vector", lambda e, tB_rep=tB_rep: e.tensor_tensor(
                        out=apx(pr_s2[:, :], [[64, 8], [1, 64]]),
                        in0=apx(pr_s3[:, :], [[64, 8], [1, 64]]), in1=tB_rep, op=ALU.mult))
                    P.op("vector", lambda e, stg=stg: e.tensor_add(
                        out=pr_s3[:, :], in0=stg[:, :], in1=pr_s2[:, :]))
                    tok_rot = P.tick("vector", cDVE)
                    st["bct1"] = tok_rot
                    cw = [tok_rot]
                    if st["qbf_rd"] is not None:
                        cw.append(st["qbf_rd"])
                    P.op("scalar", lambda e, which=which: e.activation(
                        out=qbf[:, :], in_=pr_s3[:, :], func=AF.Identity,
                        scale=(QK_SCALE if which == 0 else 1.0)), waits=cw)
                    tok_cast = P.tick("scalar", cACT)
                    tw = [tok_cast]
                    if st["pt"] is not None:
                        tw.append(st["pt"])
                    for fc in range(4):
                        P.op("tensor", lambda e, fc=fc: e.transpose(
                            PT.ap()[:, 128 * fc: 128 * (fc + 1)],
                            qbf[:, 128 * fc: 128 * (fc + 1)], ident[:, :]),
                            waits=tw if fc == 0 else ())
                    tok_tp = P.tick("tensor", cPE)
                    st["qbf_rd"] = tok_tp
                    base = 0 if which == 0 else 8192
                    dst = apx(offx(vqk_flat, base + 128 * tc), [[2048, 4], [1, 128]])
                    P.op("vector", lambda e, dst=dst: e.tensor_copy(
                        out=dst, in_=apx(PT.ap()[:, :], [[128, 4], [1, 128]])),
                        waits=[tok_tp])
                    st["pt"] = P.tick("vector", cDVE)
                st["tab_rd"][tbi] = st["bct1"]
            tok_prep_done = st["pt"] or st["qkv_ev"][0]
            if "2" not in sub:
                tok_u = tok_prep_done
                continue

            pk_toks = []
            for fc in range(4):
                pk_toks.append(P.dma(
                    "sync", cc3_in[l][128 * fc: 128 * (fc + 1), 0:1024],
                    kp_ap(fc)[:, 1024:2048], cCCP, waits=[tok_prep_done]))
            for tcv in range(8, 16):
                pk_toks.append(P.dma(
                    "sync", cc3_in[l][520 + 128 * (tcv - 8): 520 + 128 * (tcv - 7), 0:520],
                    c_v[:, tcv, :], cCCP, waits=[tok_prep_done]))
            for fc in range(4):
                pk_toks.append(P.dma(
                    "sync", cc3_in[l][1544 + fc: 1545 + fc, 0:128],
                    qp_ap(fc)[:, 2047:2048], cCCP, waits=[tok_prep_done]))
            P.op("gpsimd", lambda e, l=l: e.collective_compute(
                "AllGather", ALU.bypass, replica_groups=RG,
                ins=[cc3_in[l].opt()], outs=[cc3_out[l].opt()]), waits=[pk_toks[-1]])
            tok_cc3 = P.tick("gpsimd", cCC)
            blk = R3 * 1040
            for fc in range(4):
                P.op("gpsimd", lambda e, fc=fc, l=l: read_prev(
                    e, cc3_out[l][128 * fc: 128 * (fc + 1), 0:1024], khalo_ap(fc)[:, :], blk),
                    waits=[tok_cc3, tok_prep_done])
                P.tick("gpsimd", cCCR, by=16)
            for fc in range(4):
                P.op("gpsimd", lambda e, fc=fc, l=l: read_prev(
                    e, cc3_out[l][1544 + fc: 1545 + fc, 0:128], qbndt[:, fc:fc + 1], blk),
                    waits=[tok_cc3])
                P.tick("gpsimd", cCCR, by=16)
            tok_halo_kv = (cCCR, cCCR.n)

            if "3" not in sub:
                tok_u = tok_halo_kv
                continue
            def kjc_src(w, jc, fc, ro):
                if w == 0 and jc < 4:
                    return khalo_ap(fc)[ro:ro + 64, 512 + 128 * jc: 640 + 128 * jc]
                base = 512 * (w - 1) if w >= 1 else -512
                return kp_ap(fc)[ro:ro + 64, base + 128 * jc: base + 128 * jc + 128]

            def vjc_src(w, jc):
                if w == 0 and jc < 4:
                    return 12 + jc
                return (4 * (w - 1) + jc) if w >= 1 else (jc - 4)

            def stats_segs(w, qi, fc, ro):
                j0, j1 = 128 * qi, 128 * qi + 640
                segs = []
                if w == 0:
                    if j0 < 512:
                        segs.append(khalo_ap(fc)[ro:ro + 64, 512 + j0: 1024])
                    segs.append(kp_ap(fc)[ro:ro + 64, max(0, j0 - 512): j1 - 512])
                else:
                    b = 512 * (w - 1)
                    segs.append(kp_ap(fc)[ro:ro + 64, b + j0: b + j1])
                return segs

            def attn_evac(h, iw, dst_ap, tok_pv, st, extra=()):
                t = chain("vector", lambda e, iw=iw: e.reciprocal(
                    out=r64[64:65, 0:iw], in_=PW.ap()[64:65, 0:iw]),
                    [tok_pv] + list(extra), cDVE)
                t = chain("scalar", lambda e, iw=iw: e.activation(
                    out=r64h[64:65, 0:iw], in_=r64[64:65, 0:iw], func=AF.Identity), [t], cACT)
                t = chain("scalar", lambda e, iw=iw: e.activation(
                    out=r64[64:65, 512:512 + iw], in_=r64h[64:65, 0:iw], func=AF.Identity),
                    [t], cACT)
                t = chain("vector", lambda e, iw=iw: e.tensor_sub(
                    out=pr_s3[64:65, 0:iw], in0=r64[64:65, 0:iw],
                    in1=r64[64:65, 512:512 + iw]), [t], cDVE)
                t = chain("scalar", lambda e, iw=iw: e.activation(
                    out=r64h[64:65, 512:512 + iw], in_=pr_s3[64:65, 0:iw],
                    func=AF.Identity), [t], cACT)
                bw = [t]
                if st["rb_rd"] is not None:
                    bw.append(st["rb_rd"])
                mm_bank_group([
                    lambda e, s, p, iw=iw: e.matmul(PB[3].ap()[0:64, 0:iw],
                                                    on64[64:65, 0:64],
                                                    r64h[64:65, 0:iw], start=s, stop=p),
                    lambda e, s, p, iw=iw: e.matmul(PB[3].ap()[0:64, 0:iw],
                                                    on64[64:65, 0:64],
                                                    r64h[64:65, 512:512 + iw],
                                                    start=s, stop=p),
                ], bw)
                tok_rb = P.tick("tensor", cPE)
                P.op("scalar", lambda e, iw=iw: e.activation(
                    out=rb64[0:64, 0:iw], in_=PB[3].ap()[0:64, 0:iw], func=AF.Identity),
                    waits=[tok_rb])
                tok_rbs = P.tick("scalar", cACT)
                st["pb3_rd"] = tok_rbs
                P.op("vector", lambda e, iw=iw, dst_ap=dst_ap: e.tensor_tensor(
                    out=dst_ap, in0=PW.ap()[0:64, 0:iw], in1=rb64[0:64, 0:iw], op=ALU.mult),
                    waits=[tok_rbs])
                tok_e = P.tick("vector", cDVE)
                st["pw_rd"] = tok_e
                st["rb_rd"] = tok_e
                return tok_e

            st["rb_rd"] = None
            tok_oa_w = {}
            tok_vhalo = None
            windows = [1, 2, 3, 0]
            for w in windows:
                if w == 0:
                    for tcv in range(8):
                        P.op("gpsimd", lambda e, tcv=tcv, l=l: read_prev(
                            e, cc3_out[l][520 + 128 * tcv: 520 + 128 * (tcv + 1), 0:520],
                            c_v[:, 8 + tcv, :], R3 * 1040),
                            waits=[tok_cc3, st["pw_rd"]])
                        P.tick("gpsimd", cCCR, by=16)
                    tok_vhalo = (cCCR, cCCR.n)
                tok_kv = tok_prep_done if w != 0 else tok_halo_kv
                for h in range(HEADS):
                    fc, ro = h // 2, (h % 2) * 64
                    qw_full = qp_ap(fc)
                    for qi in range(4):
                        pw = [tok_kv]
                        if st["stats_rd"] is not None:
                            pw.append(st["stats_rd"])
                        qslice = qp_ap(fc)[ro:ro + 64, 512 * w + 128 * qi: 512 * w + 128 * qi + 128]
                        segs = stats_segs(w, qi, fc, ro)
                        b4ops, b5ops = [], []
                        col = 0
                        for sg in segs:
                            wid = sg.shape[-1]
                            while wid > 0:
                                n = min(wid, 512 - (col % 512)) if col < 512 else min(wid, 128)
                                n = min(n, 512 - col) if col < 512 else n
                                tgt = (b4ops if col < 512 else b5ops)
                                tgt.append((col, n, sg, sg.shape[-1] - wid))
                                col += n
                                wid -= n
                        ops4 = [
                            (lambda e, s, p, c=c, n=n, sg=sg, so=so, qslice=qslice:
                             e.matmul(PS640.ap()[:, c:c + n], qslice,
                                      _slice_cols(sg, so, n), start=s, stop=p))
                            for (c, n, sg, so) in b4ops]
                        if w == 0:
                            hw = 512 - 128 * qi
                            ops4.append(lambda e, s, p, hw=hw: e.matmul(
                                PS640.ap()[:, 0:hw], ones1_128[0:1, :], s0neg[0:1, 0:hw],
                                start=s, stop=p))
                        mm_bank_group(ops4, pw)
                        ops5 = [
                            (lambda e, s, p, c=c, n=n, sg=sg, so=so, qslice=qslice:
                             e.matmul(PS640.ap()[:, c:c + n], qslice,
                                      _slice_cols(sg, so, n), start=s, stop=p))
                            for (c, n, sg, so) in b5ops]
                        mm_bank_group(ops5, [])
                        tok_smm = P.tick("tensor", cPE)
                        P.op("vector", lambda e, qi=qi: e.tensor_reduce(
                            out=mst[:, qi:qi + 1], in_=PS640.ap()[:, 0:640], axis=AX.X,
                            op=ALU.max, negate=True), waits=[tok_smm])
                        st["stats_rd"] = P.tick("vector", cDVE)
                    mw = [st["stats_rd"]]
                    if st["mrow_rd"] is not None:
                        mw.append(st["mrow_rd"])
                    if st.get("pb3_rd") is not None:
                        mw.append(st["pb3_rd"])
                    for qi in range(4):
                        P.op("tensor", lambda e, qi=qi: e.transpose(
                            PB[3].ap()[0:1, 128 * qi: 128 * qi + 128], mst[:, qi:qi + 1],
                            identf[:, :]), waits=(mw if qi == 0 else ()))
                    tok_mtp = P.tick("tensor", cPE)
                    negw = [tok_mtp]
                    if st.get("negm_rd") is not None:
                        negw.append(st["negm_rd"])
                    P.op("scalar", lambda e: e.activation(
                        out=negm[0:1, 0:512], in_=PB[3].ap()[0:1, 0:512], func=AF.Identity),
                        waits=negw)
                    tok_negm = P.tick("scalar", cACT)
                    st["mrow_rd"] = tok_negm
                    for jc in range(8):
                        i0 = max(0, 128 * (jc - 4))
                        i1 = min(512, 128 * jc + 128)
                        iw = i1 - i0
                        pb = PB[jc % 2]
                        pw = [tok_negm, tok_kv]
                        if st["st_rd"][jc % 2] is not None:
                            pw.append(st["st_rd"][jc % 2])
                        kap = kjc_src(w, jc, fc, ro)
                        qap = qp_ap(fc)[ro:ro + 64, 512 * w + i0: 512 * w + i1]
                        ops = [lambda e, s, p, kap=kap, qap=qap, pb=pb, i0=i0, iw=iw:
                               e.matmul(pb.ap()[:, i0:i0 + iw], kap, qap, start=s, stop=p)]
                        for qi in range(i0 // 128, i1 // 128):
                            ops.append(lambda e, s, p, pb=pb, qi=qi: e.matmul(
                                pb.ap()[:, 128 * qi: 128 * qi + 128], ones1_128[0:1, :],
                                negm[0:1, 128 * qi: 128 * qi + 128], start=s, stop=p))
                        if jc <= 3:
                            qe = jc
                            if i0 <= 128 * qe < i1:
                                ops.append(lambda e, s, p, pb=pb, qe=qe: e.matmul(
                                    pb.ap()[:, 128 * qe: 128 * qe + 128], mask_d0[:, :],
                                    ident[:, :], start=s, stop=p))
                        else:
                            qe = jc - 4
                            if i0 <= 128 * qe < i1:
                                ops.append(lambda e, s, p, pb=pb, qe=qe: e.matmul(
                                    pb.ap()[:, 128 * qe: 128 * qe + 128], mask_d4[:, :],
                                    ident[:, :], start=s, stop=p))
                        if w == 0 and jc < 4:
                            ops.append(lambda e, s, p, pb=pb, i0=i0, iw=iw: e.matmul(
                                pb.ap()[:, i0:i0 + iw], ones1_128[0:1, :],
                                s0neg[0:1, 0:iw], start=s, stop=p))
                        mm_bank_group(ops, pw)
                        tok_stmm = P.tick("tensor", cPE)
                        st["negm_rd"] = tok_stmm
                        slot = jc % 3
                        pw2 = [tok_stmm]
                        if st["aT_rd"][slot] is not None:
                            pw2.append(st["aT_rd"][slot])
                        P.op("scalar", lambda e, pb=pb, i0=i0, iw=iw, slot=slot: e.activation(
                            out=c_aT[:, slot, i0:i0 + iw], in_=pb.ap()[:, i0:i0 + iw],
                            func=AF.Exp), waits=pw2)
                        tok_exp = P.tick("scalar", cACT)
                        st["st_rd"][jc % 2] = tok_exp
                        vt = vjc_src(w, jc)
                        pvw = [tok_exp]
                        if jc == 0 and st["pw_rd"] is not None:
                            pvw.append(st["pw_rd"])
                        if jc == 0 and w == 0:
                            pvw.append(tok_vhalo)
                        P.op("tensor", lambda e, vt=vt, h=h, i0=i0, iw=iw, slot=slot, jc=jc:
                             e.matmul(PW.ap()[:, i0:i0 + iw], c_v[:, vt, 65 * h: 65 * h + 65],
                                      c_aT[:, slot, i0:i0 + iw], start=(jc == 0),
                                      stop=(jc == 7)), waits=pvw)
                        st["aT_rd"][slot] = P.tick("tensor", cPE)
                    tok_pv = st["aT_rd"][7 % 3]
                    dw = [st["oa_wo"]] if st.get("oa_wo") else []
                    attn_evac(h, 512, c_oa[0:64, h, 0:512], tok_pv, st, extra=dw)
                tok_attnw = st["pw_rd"]
                ecw = []
                for ec in range(NG):
                    wobuf = ec % 2
                    wv = [tok_attnw]
                    if st.get(f"wo_rd{wobuf}"):
                        wv.append(st[f"wo_rd{wobuf}"])
                    twl = P.dma("gpsimd", c_wo[0:64, wobuf, :, :],
                                wo_r[l, :, :, 128 * ec: 128 * (ec + 1)], cWO[wobuf], waits=wv)
                    pb = PB[ec % 2]
                    pw = [twl]
                    if st.get(f"wops{ec % 2}"):
                        pw.append(st[f"wops{ec % 2}"])
                    ops = [(lambda e, s, p, h=h2, pb=pb, wobuf=wobuf: e.matmul(
                        pb.ap()[:, :], c_wo[0:64, wobuf, h, :], c_oa[0:64, h, 0:512],
                        start=s, stop=p)) for h2 in range(HEADS)]
                    mm_bank_group(ops, pw)
                    tok_womm = P.tick("tensor", cPE)
                    st[f"wo_rd{wobuf}"] = tok_womm
                    c0 = 2 + 512 * w
                    P.op("vector", lambda e, pb=pb, ec=ec, c0=c0: e.tensor_add(
                        out=vL[:, ec, c0:c0 + 512], in0=pb.ap()[:, :],
                        in1=vL[:, ec, c0:c0 + 512]), waits=[tok_womm] + dbg_guard())
                    st[f"wops{ec % 2}"] = P.tick("vector", cDVE)
                tok_oa_w[w] = st[f"wops{(NG - 1) % 2}"]
                st["oa_wo"] = st[f"wops{(NG - 1) % 2}"]

            if "4" not in sub:
                tok_u = tok_oa_w[0]
                continue
            t = chain("vector", lambda e: e.tensor_copy(
                out=apx(qfk[:, :, :], [[128, 4], [1, 1]]), in_=qbndt[:, :]),
                [tok_halo_kv, tok_oa_w[0]], cDVE)
            for fc in range(4):
                t = chain("vector", lambda e, fc=fc: e.tensor_copy(
                    out=qfk[:, fc, 1:128], in_=qp_ap(fc)[:, 0:127]), [t], cDVE)
            tok_qfk = t
            for h in range(HEADS):
                fc, ro = h // 2, (h % 2) * 64
                qap = qfk[ro:ro + 64, fc, :]
                pw = [tok_qfk]
                if st["stats_rd"] is not None:
                    pw.append(st["stats_rd"])
                mm_bank_group([
                    lambda e, s, p, fc=fc, ro=ro, qap=qap: e.matmul(
                        PS640.ap()[:, 0:512], qap, khalo_ap(fc)[ro:ro + 64, 384:896],
                        start=s, stop=p)], pw)
                mm_bank_group([
                    lambda e, s, p, fc=fc, ro=ro, qap=qap: e.matmul(
                        PS640.ap()[:, 512:640], qap, khalo_ap(fc)[ro:ro + 64, 896:1024],
                        start=s, stop=p)], [])
                tok_smm = P.tick("tensor", cPE)
                P.op("vector", lambda e: e.tensor_reduce(
                    out=mst[:, 0:1], in_=PS640.ap()[:, 0:640], axis=AX.X, op=ALU.max,
                    negate=True), waits=[tok_smm])
                st["stats_rd"] = P.tick("vector", cDVE)
                mw = [st["stats_rd"], st["mrow_rd"]]
                if st.get("pb3_rd") is not None:
                    mw.append(st["pb3_rd"])
                P.op("tensor", lambda e: e.transpose(
                    PB[3].ap()[0:1, 0:128], mst[:, 0:1], identf[:, :]), waits=mw)
                tok_mtp = P.tick("tensor", cPE)
                P.op("scalar", lambda e: e.activation(
                    out=negm[0:1, 0:128], in_=PB[3].ap()[0:1, 0:128], func=AF.Identity),
                    waits=[tok_mtp, st["negm_rd"]] if st.get("negm_rd") else [tok_mtp])
                tok_negm = P.tick("scalar", cACT)
                st["mrow_rd"] = tok_negm
                for jj, jc in enumerate(range(3, 8)):
                    pb = PB[jc % 2]
                    pw = [tok_negm]
                    if st["st_rd"][jc % 2] is not None:
                        pw.append(st["st_rd"][jc % 2])
                    ops = [lambda e, s, p, pb=pb, fc=fc, ro=ro, jc=jc, qap=qap: e.matmul(
                        pb.ap()[:, 0:128], khalo_ap(fc)[ro:ro + 64, 128 * jc: 128 * jc + 128],
                        qap, start=s, stop=p)]
                    ops.append(lambda e, s, p, pb=pb: e.matmul(
                        pb.ap()[:, 0:128], ones1_128[0:1, :], negm[0:1, 0:128],
                        start=s, stop=p))
                    if jc == 3:
                        ops.append(lambda e, s, p, pb=pb: e.matmul(
                            pb.ap()[:, 0:128], bndmask3[:, :], ident[:, :], start=s, stop=p))
                    mm_bank_group(ops, pw)
                    tok_stmm = P.tick("tensor", cPE)
                    st["negm_rd"] = tok_stmm
                    slot = jc % 3
                    pw2 = [tok_stmm]
                    if st["aT_rd"][slot] is not None:
                        pw2.append(st["aT_rd"][slot])
                    P.op("scalar", lambda e, pb=pb, slot=slot: e.activation(
                        out=c_aT[:, slot, 0:128], in_=pb.ap()[:, 0:128], func=AF.Exp),
                        waits=pw2)
                    tok_exp = P.tick("scalar", cACT)
                    st["st_rd"][jc % 2] = tok_exp
                    pvw = [tok_exp]
                    if jj == 0 and st["pw_rd"] is not None:
                        pvw.append(st["pw_rd"])
                    P.op("tensor", lambda e, jc=jc, h=h, slot=slot, jj=jj: e.matmul(
                        PW.ap()[:, 0:128], c_v[:, 8 + jc, 65 * h: 65 * h + 65],
                        c_aT[:, slot, 0:128], start=(jj == 0), stop=(jj == 4)), waits=pvw)
                    st["aT_rd"][slot] = P.tick("tensor", cPE)
                tok_pv = st["aT_rd"][7 % 3]
                attn_evac(h, 128, obnd[0:64, h, 0:128], tok_pv, st)
            tok_bnd = st["pw_rd"]
            for ec in range(NG):
                wobuf = ec % 2
                wv = [tok_bnd]
                if st.get(f"wo_rd{wobuf}"):
                    wv.append(st[f"wo_rd{wobuf}"])
                twl = P.dma("gpsimd", c_wo[0:64, wobuf, :, :],
                            wo_r[l, :, :, 128 * ec: 128 * (ec + 1)], cWO[wobuf], waits=wv)
                pb = PB[ec % 2]
                pw = [twl]
                if st.get(f"wops{ec % 2}"):
                    pw.append(st[f"wops{ec % 2}"])
                ops = [(lambda e, s, p, h=h2, pb=pb, wobuf=wobuf: e.matmul(
                    pb.ap()[:, 0:128], c_wo[0:64, wobuf, h, :], obnd[0:64, h, 0:128],
                    start=s, stop=p)) for h2 in range(HEADS)]
                mm_bank_group(ops, pw)
                tok_womm = P.tick("tensor", cPE)
                st[f"wo_rd{wobuf}"] = tok_womm
                P.op("vector", lambda e, pb=pb, ec=ec: e.scalar_tensor_tensor(
                    out=vL[:, ec, 1:2], in0=pb.ap()[:, 0:1], scalar=gate[:, 0:1],
                    in1=vL[:, ec, 1:2], op0=ALU.mult, op1=ALU.add), waits=[tok_womm] + dbg_guard())
                st[f"wops{ec % 2}"] = P.tick("vector", cDVE)
            tok_xmid = st[f"wops{(NG - 1) % 2}"]
            dump(f"xmid{l}", vL[:, :, 2:2050], 2048, [tok_xmid])

            if "5" not in sub:
                tok_u = tok_xmid
                continue
            def src_shift(g, c0, wdt):
                off = 0 if g < 4 else -1
                return vL[:, g, c0 + off: c0 + off + wdt]

            def apply_ln2(c0, wdt, tok_bcs, l=l):
                for g in range(NG):
                    off = 0 if g < 4 else -1
                    P.op("vector", lambda e, g=g, c0=c0, wdt=wdt, off=off: e.tensor_mul(
                        out=bc_t1[:, 0:wdt], in0=vL[:, g, c0 + off: c0 + off + wdt],
                        in1=bc_rb[:, 0:wdt]), waits=[tok_bcs])
                    P.op("vector", lambda e, g=g, c0=c0, wdt=wdt: e.tensor_sub(
                        out=scr1[0:128, 0:wdt], in0=bc_t1[:, 0:wdt], in1=bc_mb[:, 0:wdt]))
                    tok_s = P.tick("vector", cDVE)
                    P.op("scalar", lambda e, g=g, c0=c0, wdt=wdt, l=l: e.activation(
                        out=hln[:, g, c0 - 2: c0 - 2 + wdt], in_=scr1[0:128, 0:wdt],
                        func=AF.Identity, scale=ln_g[:, l, g:g + 1], bias=ln_b[:, l, g:g + 1]),
                        waits=[tok_s])
                P.op("vector", lambda e: e.engine_nop())

            tok_ln2 = layernorm(src_shift, tiles_own, apply_ln2, [tok_xmid])
            tok_hln = P.tick("scalar", cACT)

            stw = {"w1_rd": [None, None], "w2_rd": [None, None], "h1_rd": {}, "ps": {}}
            tok_u_upd = []
            for tt in range(4):
                tok_h1 = {}
                for hc in range(NHC):
                    wb = hc % 2
                    wv = ([stw["w1_rd"][wb]] if stw["w1_rd"][wb] else []) + [tok_hln]
                    ta = P.dma("gpsimd", c_w1[:, wb, 0, :, :],
                               w1r[l, :, :, 128 * hc: 128 * hc + 128], cW1[wb], waits=wv)
                    tg = P.dma("gpsimd", c_w1[:, wb, 1, :, :],
                               w1r[l, :, :, NHC * 128 + 128 * hc: NHC * 128 + 128 * hc + 128],
                               cW1[wb])
                    pba = PB[2 * (hc % 2)]
                    pbg = PB[2 * (hc % 2) + 1]
                    pw = [tg, tok_hln]
                    if stw["ps"].get(hc % 2):
                        pw.append(stw["ps"][hc % 2])
                    if hc < 2 and stw.get("w2ps0"):
                        pw.append(stw["w2ps0"])
                    if hc < 2 and stw.get("w2ps1"):
                        pw.append(stw["w2ps1"])
                    opsa = [(lambda e, s, p, g=g, wb=wb, pba=pba, tt=tt: e.matmul(
                        pba.ap()[:, :], c_w1[:, wb, 0, g, :],
                        hln[:, g, 512 * tt: 512 * tt + 512], start=s, stop=p))
                        for g in range(NG)]
                    mm_bank_group(opsa, pw)
                    opsg = [(lambda e, s, p, g=g, wb=wb, pbg=pbg, tt=tt: e.matmul(
                        pbg.ap()[:, :], c_w1[:, wb, 1, g, :],
                        hln[:, g, 512 * tt: 512 * tt + 512], start=s, stop=p))
                        for g in range(NG)]
                    mm_bank_group(opsg, [])
                    tok_w1mm = P.tick("tensor", cPE)
                    stw["w1_rd"][wb] = tok_w1mm
                    gw = [tok_w1mm]
                    if stw.get("gel_rd"):
                        gw.append(stw["gel_rd"])
                    P.op("scalar", lambda e, pbg=pbg, hc=hc, l=l: e.activation(
                        out=bc_t1[:, :], in_=pbg.ap()[:, :], func=AF.Gelu,
                        bias=b1[:, l, 1, hc:hc + 1]), waits=gw)
                    tok_gel = P.tick("scalar", cACT)
                    hw1 = [tok_gel]
                    if stw["h1_rd"].get(hc):
                        hw1.append(stw["h1_rd"][hc])
                    P.op("vector", lambda e, pba=pba, hc=hc, l=l: e.scalar_tensor_tensor(
                        out=h1t[:, hc, :], in0=pba.ap()[:, :], scalar=b1[:, l, 0, hc:hc + 1],
                        in1=bc_t1[:, :], op0=ALU.add, op1=ALU.mult), waits=hw1)
                    tok_h1[hc] = P.tick("vector", cDVE)
                    stw["gel_rd"] = tok_h1[hc]
                    stw["ps"][hc % 2] = tok_h1[hc]
                for ec in range(NG):
                    pb = PB[ec % 2]
                    pw0 = [tok_h1[NHC - 1]]
                    if stw.get(f"w2ps{ec % 2}"):
                        pw0.append(stw[f"w2ps{ec % 2}"])
                    for hc in range(NHC):
                        wb2 = hc % 2
                        wv = [stw["w2_rd"][wb2]] if stw["w2_rd"][wb2] else []
                        tl = P.dma("gpsimd", c_w2[:, wb2, :],
                                   w2_d[l, 128 * hc: 128 * hc + 128,
                                        128 * ec: 128 * ec + 128], cW2[wb2],
                                   waits=wv + [tok_h1[hc], tok_hln])
                        P.op("tensor", lambda e, pb=pb, wb2=wb2, hc=hc:
                             e.matmul(pb.ap()[:, :], c_w2[:, wb2, :],
                                      h1t[:, hc, :], start=(hc == 0),
                                      stop=(hc == NHC - 1)),
                             waits=[tl] + (pw0 if hc == 0 else []))
                        stw["w2_rd"][wb2] = P.tick("tensor", cPE)
                    tok_w2mm = stw["w2_rd"][(NHC - 1) % 2]
                    c0 = 2 + 512 * tt
                    P.op("vector", lambda e, pb=pb, ec=ec, c0=c0, l=l: e.scalar_tensor_tensor(
                        out=vL[:, ec, c0:c0 + 512], in0=pb.ap()[:, :],
                        scalar=b2[:, l, ec:ec + 1], in1=vL[:, ec, c0:c0 + 512],
                        op0=ALU.add, op1=ALU.add), waits=[tok_w2mm] + dbg_guard())
                    stw[f"w2ps{ec % 2}"] = P.tick("vector", cDVE)
                    tok_u_upd.append(stw[f"w2ps{ec % 2}"])
                for hc in range(NHC):
                    stw["h1_rd"][hc] = stw["w2_rd"][(NHC - 1) % 2]
            tok_layer = tok_u_upd[-1]
            dump(f"y{l}", vL[:, :, 2:2050], 2048, [tok_layer])

            if l == 0:
                tok_u = exchange_boundary(cc4_in, cc4_out, tok_layer)
            else:
                tok_u = tok_layer
            if l == 1:
                tok_x2r = P.dma("sync", x2r, x2_spill.rearrange("(g p) t -> p g t", p=128),
                                cLX, waits=[tok_spill, tok_layer])

    if "C" in phases and "B" in phases:
        def apply_final(c0, wdt, tok_bcs):
            for g in range(NG):
                P.op("vector", lambda e, g=g, c0=c0, wdt=wdt: e.tensor_mul(
                    out=bc_t1[:, 0:wdt], in0=vL[:, g, c0:c0 + wdt], in1=bc_rb[:, 0:wdt]),
                    waits=[tok_bcs, tok_x2r] + dbg_guard())
                P.op("vector", lambda e, g=g, c0=c0, wdt=wdt: e.tensor_sub(
                    out=scr1[0:128, 0:wdt], in0=bc_t1[:, 0:wdt], in1=bc_mb[:, 0:wdt]))
                P.op("vector", lambda e, g=g, c0=c0, wdt=wdt: e.tensor_add(
                    out=vL[:, g, c0:c0 + wdt], in0=scr1[0:128, 0:wdt],
                    in1=x2r[:, g, c0 - 2: c0 - 2 + wdt]))

        tok_fin = layernorm(lambda g, c0, wdt: vL[:, g, c0:c0 + wdt], tiles_own,
                            apply_final, [tok_u])
        tok_y = P.dma("sync", y_out.rearrange("(g p) t -> p g t", p=128),
                      vL[:, :, 2:2050], cST, waits=[tok_fin])
        dump_toks.append(tok_y)
    else:
        tok_y = P.dma("sync", y_out.rearrange("(g p) t -> p g t", p=128),
                      vL[:, :, 2:2050], cST,
                      waits=[tok_u if "B" in phases else (tok_halo if "A" in phases else tok_x)])
        dump_toks.append(tok_y)

    P.op("sync", lambda e: e.nop(), waits=dump_toks + [(cST, cST.n)])
    P.emit()
    es.close()
    return nc


def kernel(**inputs):
    in_maps = stage_inputs(inputs)
    nc = build_program(debug=DEBUG, phases=PHASES)
    res = run_bass_kernel_spmd(nc, in_maps, core_ids=list(range(8)))
    out = np.zeros((2, SEQ, D), np.float32)
    for c in range(8):
        b, s = c // 4, c % 4
        y = np.asarray(res.results[c]["y"], np.float32)
        out[b, TOK * s: TOK * (s + 1), :] = y.T
    return out


# version 06
# speedup vs baseline: 1.3417x; 1.2059x over previous
import dataclasses
import numpy as np
import ml_dtypes
import concourse.bass as bass
import concourse.mybir as mybir
from concourse.bass_utils import run_bass_kernel_spmd
from contextlib import ExitStack

F32 = mybir.dt.float32
BF16 = mybir.dt.bfloat16
I32 = mybir.dt.int32
U8 = mybir.dt.uint8
ALU = mybir.AluOpType
AF = mybir.ActivationFunctionType
DR = mybir.MatmulPerfMode.DoubleRow
AX = mybir.AxisListType
BF = ml_dtypes.bfloat16

D = 1024
NG = 8
TOK = 2048
COLS = 2052
SEQ = 8192
HEADS = 8
DH = 64
WIN = 512
INNER = 512
HID = 2730
NHC = 22
HC_LAST = 42
QK_SCALE = 8.0
NEG = -1e30
LNEPS = 1e-5

DEBUG = []
PHASES = "ABC"
ALIAS_CHANNELS = False


class Chan:
    __slots__ = ("sem", "n", "name")
    def __init__(self, sem, name):
        self.sem = sem
        self.n = 0
        self.name = name


class Prog:
    ENGINES = ("tensor", "vector", "scalar", "gpsimd", "sync")

    def __init__(self, nc, es):
        self.nc = nc
        self.es = es
        self.ops = {e: [] for e in self.ENGINES}
        self.dumps = []

    def chan(self, name):
        sem = self.es.enter_context(self.nc.semaphore(name))
        return Chan(sem, name)

    def op(self, engine, fn, waits=()):
        self.ops[engine].append([tuple(waits), fn, None, 0])

    def tick(self, engine, ch, by=1):
        rec = self.ops[engine][-1]
        assert rec[2] is None, "op already ticked"
        rec[2] = ch
        rec[3] = by
        ch.n += by
        return (ch, ch.n)

    def dma(self, engine, out, in_, ch, waits=(), **kw):
        self.op(engine, lambda e, out=out, in_=in_, kw=kw: e.dma_start(out=out, in_=in_, **kw),
                waits=waits)
        return self.tick(engine, ch, by=16)

    def emit(self):
        block = self.es.enter_context(self.nc.Block())
        for ename in self.ENGINES:
            ops = self.ops[ename]
            if not ops:
                continue

            def body(eng, ops=ops):
                last_wait = {}
                for waits, fn, inc_ch, inc_by in ops:
                    for (ch, cnt) in waits:
                        if cnt > last_wait.get(ch.name, -1):
                            eng.wait_ge(ch.sem, cnt)
                            last_wait[ch.name] = cnt
                    ins = fn(eng)
                    if inc_ch is not None:
                        ins.then_inc(inc_ch.sem, inc_by)
            getattr(block, ename)(body)


class Arena:
    def __init__(self, nc, name, nbytes):
        nbytes = (nbytes + 63) & ~63
        self.h = nc.alloc_sbuf_tensor(name, [128, nbytes], U8)
        self.nbytes = nbytes

    @staticmethod
    def _shape(ap, shape):
        if len(shape) <= 1:
            return ap
        names = "abcdefg"[: len(shape)]
        spec = "p (" + " ".join(names) + ") -> p " + " ".join(names)
        kw = {names[i]: shape[i] for i in range(1, len(shape))}
        return ap.rearrange(spec, **kw)

    def f32(self, off, *shape):
        n = int(np.prod(shape))
        assert off % 64 == 0 and off + 4 * n <= self.nbytes, (off, shape, self.nbytes)
        return self._shape(self.h.ap().bitcast(F32)[:, off // 4: off // 4 + n], shape)

    def bf16(self, off, *shape):
        n = int(np.prod(shape))
        assert off % 64 == 0 and off + 2 * n <= self.nbytes, (off, shape, self.nbytes)
        return self._shape(self.h.ap().bitcast(BF16)[:, off // 2: off // 2 + n], shape)


def _rot_tables():
    half = np.arange(0, DH, 2, dtype=np.float64)
    inv_freq = 1.0 / 10000.0 ** (half / DH)
    t = np.arange(SEQ, dtype=np.float64)
    fr = t[:, None] * inv_freq[None, :]
    fr = np.concatenate([fr, fr], axis=-1)
    base = (half + 0.4 * DH) / (1.4 * DH)
    power = (t - SEQ // 2) / float(WIN // 2)
    scale = np.concatenate([base[None, :] ** power[:, None]] * 2, axis=-1)
    cos = np.cos(fr)
    sin = np.sin(fr)
    sign = np.concatenate([-np.ones(DH // 2), np.ones(DH // 2)])[None, :]
    Aq = (cos * scale).astype(np.float32)
    Bq = (sign * sin * scale).astype(np.float32)
    Ak = (cos / scale).astype(np.float32)
    Bk = (sign * sin / scale).astype(np.float32)
    return Aq, Bq, Ak, Bk


def _edge_masks():
    i = np.arange(128)[:, None]
    j = np.arange(128)[None, :]
    d0 = np.where(j < i, NEG, 0.0).astype(BF)
    d4 = np.where(j > i, NEG, 0.0).astype(BF)
    return d0, d4


def _w0_masks(is_s0):
    i = np.arange(128)[:, None]
    j = np.arange(128)[None, :]
    tiles = np.zeros((4, 5, 128, 128), np.float32)
    for qi in range(4):
        for bs in range(5):
            jc = qi + bs
            if bs == 0:
                m = np.where(j < i, NEG, 0.0)
            elif bs == 4:
                m = np.where(j > i, NEG, 0.0)
            else:
                m = np.zeros((128, 128), np.float32)
            if is_s0 and jc < 4:
                m = np.full((128, 128), NEG, np.float32)
            tiles[qi, bs] = m
    return tiles.astype(BF)


def stage_inputs(inputs):
    f = np.float32
    x = np.ascontiguousarray(inputs["L0_attn_x"], dtype=f)
    Wt = np.ascontiguousarray(np.asarray(inputs["L0_attn_W"], dtype=f).T)
    dec = np.ascontiguousarray(np.asarray(inputs["L0_attn_decay_logit"], f).reshape(NG, 128).T)
    bias_b = np.ascontiguousarray(np.asarray(inputs["L0_attn_b"], f).reshape(NG, 128).T)
    conv_b = np.ascontiguousarray(np.asarray(inputs["L0_attn_conv_b"], f).reshape(NG, 128).T)
    conv_w = np.ascontiguousarray(
        np.asarray(inputs["L0_attn_conv_w"], f).reshape(NG, 128, 5).transpose(1, 0, 2))
    wqkv = np.ascontiguousarray(np.asarray(inputs["L0_ffn_wqkv"], f))
    wo = np.ascontiguousarray(np.asarray(inputs["L0_ffn_wo"], f))
    w1p = np.zeros((2, D, 2 * NHC * 128), f)
    w1p[:, :, :HID] = np.asarray(inputs["L0_ffn_w1"], f)[:, :, :HID]
    w1p[:, :, NHC * 128: NHC * 128 + HID] = np.asarray(inputs["L0_ffn_w1"], f)[:, :, HID:]
    w2p = np.zeros((2, NHC * 128, 1024), f)
    w2p[:, :HID] = np.asarray(inputs["L0_ffn_w2"], f)
    qsr = np.ascontiguousarray(np.repeat(np.asarray(inputs["L0_ffn_q_scale"], f)[:, None, :], 128, 1))
    ksr = np.ascontiguousarray(np.repeat(np.asarray(inputs["L0_ffn_k_scale"], f)[:, None, :], 128, 1))
    ln_g = np.ascontiguousarray(np.asarray(inputs["L0_ffn_ln_g"], f).reshape(2, NG, 128).transpose(2, 0, 1))
    ln_b = np.ascontiguousarray(np.asarray(inputs["L0_ffn_ln_b"], f).reshape(2, NG, 128).transpose(2, 0, 1))
    b1p = np.zeros((2, 2, NHC * 128), f)
    b1p[:, 0, :HID] = np.asarray(inputs["L0_ffn_b1"], f)[:, :HID]
    b1p[:, 1, :HID] = np.asarray(inputs["L0_ffn_b1"], f)[:, HID:]
    b1s = np.ascontiguousarray(b1p.reshape(2, 2, NHC, 128).transpose(3, 0, 1, 2))
    b2s = np.ascontiguousarray(np.asarray(inputs["L0_ffn_b2"], f).reshape(2, NG, 128).transpose(2, 0, 1))

    Aq, Bq, Ak, Bk = _rot_tables()
    d0, d4 = _edge_masks()
    shared = dict(
        Wt=Wt, dec=dec, bias_b=bias_b, conv_b=conv_b, conv_w=conv_w,
        wqkv=wqkv, wo=wo, w1=w1p, w2=w2p, qsr=qsr, ksr=ksr,
        qsr_shuf=np.ascontiguousarray(qsr[:, :, [32 + i for i in range(32)] + list(range(32))]),
        ksr_shuf=np.ascontiguousarray(ksr[:, :, [32 + i for i in range(32)] + list(range(32))]),
        identf=np.eye(128).astype(f),
        bndmask3=np.ascontiguousarray(np.broadcast_to(
            np.where(np.arange(128)[None, :] < 127, NEG, 0.0).astype(BF), (128, 128))),
        ln_g=ln_g, ln_b=ln_b, b1=b1s, b2=b2s,
        mask_d0=np.ascontiguousarray(d0), mask_d4=np.ascontiguousarray(d4),
        ident=np.eye(128).astype(BF),
        ones_col=np.ones((128, 1), BF),
        ones1_128=np.ones((1, 128), BF),
        ones1_64=np.ones((1, 64), BF),
        on64_d=np.ones((65, 64), BF),
    )

    in_maps = []
    for c in range(8):
        b, s = c // 4, c % 4
        t0 = TOK * s
        xs = np.zeros((D, COLS), f)
        lo, hi = t0 - 2, t0 + TOK + 2
        slo, shi = max(lo, 0), min(hi, SEQ)
        xs[:, slo - lo: shi - lo] = x[b, slo:shi, :].T
        xs = np.ascontiguousarray(xs)

        def tt(tab):
            return np.ascontiguousarray(
                tab[t0:t0 + TOK].reshape(16, 128, DH).transpose(1, 0, 2))
        m_sel = np.zeros((128, 4), f)
        e_sel = np.zeros((128, 4), f)
        for r in range(4):
            if r < s:
                m_sel[:, r] = 1.0
                e_sel[:, r] = float(s - 1 - r)
        gate = np.full((128, 1), 0.0 if s == 0 else 1.0, f)
        m = dict(shared)
        m.update(
            x_in=xs,
            tabAq=tt(Aq), tabBq=tt(Bq), tabAk=tt(Ak), tabBk=tt(Bk),
            m_sel=m_sel, e_sel=e_sel, gate=gate,
            s0neg=np.full((1, 512), NEG if s == 0 else 0.0, BF),
            prev_off=np.array([[(s - 1) % 4]], np.int32),
        )
        in_maps.append(m)
    return in_maps


def build_program(debug=(), phases="ABC"):
    nc = bass.Bass("TRN2", target_bir_lowering=False)
    es = ExitStack()
    P = Prog(nc, es)

    def din(name, shape, dt=F32):
        return nc.dram_tensor(name, shape, dt, kind="ExternalInput").ap()

    x_in = din("x_in", [D, COLS])
    Wt_d = din("Wt", [D, D])
    dec_d = din("dec", [128, NG])
    bias_b_d = din("bias_b", [128, NG])
    conv_b_d = din("conv_b", [128, NG])
    conv_w_d = din("conv_w", [128, NG, 5])
    wqkv_d = din("wqkv", [2, D, 3 * INNER])
    wo_d = din("wo", [2, INNER, D])
    w1_d = din("w1", [2, D, 2 * NHC * 128])
    w2_d = din("w2", [2, NHC * 128, D])
    qsr_d = din("qsr", [2, 128, DH])
    ksr_d = din("ksr", [2, 128, DH])
    ln_g_d = din("ln_g", [128, 2, NG])
    ln_b_d = din("ln_b", [128, 2, NG])
    b1_d = din("b1", [128, 2, 2, NHC])
    b2_d = din("b2", [128, 2, NG])
    qsr_shuf_d = din("qsr_shuf", [2, 128, DH])
    ksr_shuf_d = din("ksr_shuf", [2, 128, DH])
    identf_d = din("identf", [128, 128])
    bndmask3_d = din("bndmask3", [128, 128], BF16)
    s0neg_d = din("s0neg", [1, 512], BF16)
    mask_d0_d = din("mask_d0", [128, 128], BF16)
    mask_d4_d = din("mask_d4", [128, 128], BF16)
    ident_d = din("ident", [128, 128], BF16)
    ones_col_d = din("ones_col", [128, 1], BF16)
    ones1_128_d = din("ones1_128", [1, 128], BF16)
    ones1_64_d = din("ones1_64", [1, 64], BF16)
    on64_d = din("on64_d", [65, 64], BF16)
    tabAq_d = din("tabAq", [128, 16, DH])
    tabBq_d = din("tabBq", [128, 16, DH])
    tabAk_d = din("tabAk", [128, 16, DH])
    tabBk_d = din("tabBk", [128, 16, DH])
    m_sel_d = din("m_sel", [128, 4])
    e_sel_d = din("e_sel", [128, 4])
    gate_d = din("gate", [128, 1])
    prev_off_d = din("prev_off", [1, 1], I32)

    y_out = nc.dram_tensor("y", [D, TOK], F32, kind="ExternalOutput").ap()

    x2_spill = nc.dram_tensor("x2_spill", [D, TOK], F32, kind="Internal").ap()
    cc1_in = nc.dram_tensor("cc1_in", [128, NG], F32, kind="Internal").ap()
    cc1_out = nc.dram_tensor("cc1_out", [4 * 128, NG], F32, kind="Internal").ap()
    cc2_in = nc.dram_tensor("cc2_in", [128, NG], F32, kind="Internal").ap()
    cc2_out = nc.dram_tensor("cc2_out", [4 * 128, NG], F32, kind="Internal").ap()
    R3 = 1552
    cc3_in = [nc.dram_tensor(f"cc3_in{l}", [R3, 1040], BF16, kind="Internal").ap() for l in range(2)]
    cc3_out = [nc.dram_tensor(f"cc3_out{l}", [4 * R3, 1040], BF16, kind="Internal").ap() for l in range(2)]
    cc4_in = nc.dram_tensor("cc4_in", [128, NG], F32, kind="Internal").ap()
    cc4_out = nc.dram_tensor("cc4_out", [4 * 128, NG], F32, kind="Internal").ap()
    RG = [[0, 1, 2, 3], [4, 5, 6, 7]]

    AR_S = Arena(nc, "arS", NG * COLS * 4)
    AR_L = Arena(nc, "arL", NG * COLS * 4)
    AR_C = Arena(nc, "arC", 50496)

    def sb(name, shape, dt=F32):
        return es.enter_context(nc.sbuf_tensor(name, shape, dt))

    ident = sb("t_ident", [128, 128], BF16)
    mask_d0 = sb("t_mask_d0", [128, 128], BF16)
    mask_d4 = sb("t_mask_d4", [128, 128], BF16)
    ones_col = sb("t_ones_col", [128, 1], BF16)
    ones1_128 = sb("t_ones1_128", [1, 128], BF16)
    ones1_64 = sb("t_ones1_64", [1, 64], BF16)
    dec = sb("t_dec", [128, NG])
    smalla = sb("smalla", [128, NG])
    smallb = sb("smallb", [128, NG])
    a_col = sb("a_col", [128, NG])
    om_a = sb("om_a", [128, NG])
    a2k = sb("a2k", [128, NG])
    lnA = sb("lnA", [128, NG])
    coef = sb("coef", [128, 4, NG])
    bcb = sb("bcb", [128, NG])
    conv_w = sb("t_conv_w", [128, NG, 5])
    qsr = sb("t_qsr", [128, 2, DH])
    ksr = sb("t_ksr", [128, 2, DH])
    ln_g = sb("t_ln_g", [128, 2, NG])
    ln_b = sb("t_ln_b", [128, 2, NG])
    b1 = sb("t_b1", [128, 2, 2, NHC])
    b2 = sb("t_b2", [128, 2, NG])
    m_sel = sb("t_m_sel", [128, 4])
    e_sel = sb("t_e_sel", [128, 4])
    gate = sb("t_gate", [128, 1])
    poff = sb("poff", [1, 1], I32)
    hb1 = sb("hb1", [128, NG])
    hb2 = sb("hb2", [128, NG])
    wc_all = sb("wc_all", [128, 4, NG])
    sinit = sb("sinit", [128, NG])
    qsr_shuf = sb("t_qsr_shuf", [128, 2, DH])
    ksr_shuf = sb("t_ksr_shuf", [128, 2, DH])
    identf = sb("t_identf", [128, 128])
    bndmask3 = sb("t_bndmask3", [128, 128], BF16)
    s0neg = sb("t_s0neg", [1, 512], BF16)
    negm = sb("negm", [1, 512], BF16)
    r64 = sb("r64", [65, 1024])
    r64h = sb("r64h", [65, 1024], BF16)
    on64 = sb("on64", [65, 64], BF16)
    rb64 = sb("rb64", [64, 512])
    mst = sb("mst", [128, 4])
    qbndt = sb("qbndt", [128, 4], BF16)
    qfk = sb("qfk", [128, 4, 128], BF16)

    nrm = sb("nrm", [128, 8])
    nrm2 = sb("nrm2", [128, 8])
    rqv = sb("rqv", [128, 8])
    qbf = AR_C.bf16(42368, 512)
    ub = [AR_C.bf16(40256, 8, 129), AR_C.bf16(44352, 8, 129)]
    r_mu = sb("r_mu", [1, 512])
    r_var = sb("r_var", [1, 512])
    r_rstd = sb("r_rstd", [1, 512])
    r_mrs = sb("r_mrs", [1, 512])
    rb_hi = sb("rb_hi", [1, 512], BF16)
    rb_lo = sb("rb_lo", [1, 512], BF16)
    mb_hi = sb("mb_hi", [1, 512], BF16)
    mb_lo = sb("mb_lo", [1, 512], BF16)


    PB = [es.enter_context(nc.psum_tensor(f"PB{i}", [128, 512], F32)) for i in range(4)]
    PS640 = es.enter_context(nc.psum_tensor("PS640", [128, 640], F32))
    PW = es.enter_context(nc.psum_tensor("PW", [65, 512], F32))
    PT = es.enter_context(nc.psum_tensor("PT", [128, 512], BF16))

    cLD = P.chan("cLD")
    cSM = P.chan("cSM")
    cLX = P.chan("cLX")
    cSP = P.chan("cSP")
    cTB = [P.chan("cTB0"), P.chan("cTB1")]
    cCCP = P.chan("cCCP")
    cCCR = P.chan("cCCR")
    cWQ = [P.chan("cWQ0"), P.chan("cWQ1")]
    cWT = [P.chan("cWT0"), P.chan("cWT1")]
    cWO = [P.chan("cWO0"), P.chan("cWO1")]
    cW1 = [P.chan("cW10"), P.chan("cW11")]
    cW2 = [P.chan("cW20"), P.chan("cW21")]
    cLG = P.chan("cLG")
    cST = P.chan("cST")
    if ALIAS_CHANNELS:
        cSM = cLX = cSP = cCCP = cCCR = cLD
        cTB = [cLD, cLD]
        cWQ = [cLG, cLG]
        cWT = [cLD, cLD]
        cWO = [cLG, cLG]
        cW1 = [cLG, cLG]
        cW2 = [cLG, cLG]
    cPE = P.chan("cPE")
    cDVE = P.chan("cDVE")
    cACT = P.chan("cACT")
    cPOOL = P.chan("cPOOL")
    cCC = P.chan("cCC")

    dump_toks = []

    def dbg_guard():
        return [(cST, cST.n)] if debug else []

    def dump(name, view3d, ncols, waits):
        if name not in debug:
            return
        dd = nc.dram_tensor("dbg_" + name, [D, ncols], F32, kind="ExternalOutput").ap()
        t = P.dma("sync", dd.rearrange("(g p) t -> p g t", p=128), view3d, cST, waits=waits)
        dump_toks.append(t)

    vS = AR_S.f32(0, NG, COLS)
    vL = AR_L.f32(0, NG, COLS)

    tok_x = P.dma("sync", vS, x_in.rearrange("(g p) t -> p g t", p=128), cLX)
    small_loads = [
        (dec, dec_d), (bcb, bias_b_d), (conv_w, conv_w_d),
        (qsr, qsr_d.rearrange("l p d -> p l d")), (ksr, ksr_d.rearrange("l p d -> p l d")),
        (ln_g, ln_g_d), (ln_b, ln_b_d), (b1, b1_d), (b2, b2_d),
        (m_sel, m_sel_d), (e_sel, e_sel_d), (gate, gate_d), (poff, prev_off_d),
        (smalla, conv_b_d),
        (ident, ident_d), (mask_d0, mask_d0_d), (mask_d4, mask_d4_d),
        (qsr_shuf, qsr_shuf_d.rearrange("l p d -> p l d")),
        (ksr_shuf, ksr_shuf_d.rearrange("l p d -> p l d")),
        (identf, identf_d), (bndmask3, bndmask3_d), (s0neg, s0neg_d),
        (ones_col, ones_col_d), (ones1_128, ones1_128_d), (ones1_64, ones1_64_d),
    ]
    for dst, src in small_loads:
        tok_small = P.dma("sync", dst[:], src, cSM)
    tok_on64 = P.dma("sync", on64[:, :], on64_d, cSM)
    tok_small = tok_on64
    P.op("vector", lambda e: e.tensor_add(out=bcb[:], in0=bcb[:], in1=smalla[:]),
         waits=[tok_small])
    tok_bcb = P.tick("vector", cDVE)

    def chain(engine, fn, waits, ch):
        P.op(engine, fn, waits=waits)
        return P.tick(engine, ch)

    t = chain("scalar", lambda e: e.activation(out=smalla[:], in_=dec[:], func=AF.Exp,
                                               scale=-1.0), [tok_small, tok_bcb], cACT)
    t = chain("vector", lambda e: e.tensor_scalar(out=smalla[:], in0=smalla[:], scalar1=1.0,
                                                  scalar2=None, op0=ALU.add), [t], cDVE)
    t = chain("vector", lambda e: e.reciprocal(out=a_col[:], in_=smalla[:]), [t], cDVE)
    tok_a = t
    t = chain("vector", lambda e: e.tensor_scalar(out=om_a[:], in0=a_col[:], scalar1=-1.0,
                                                  scalar2=-1.0, op0=ALU.mult,
                                                  op1=ALU.subtract), [t], cDVE)
    tok_oma = t
    cur, oth = a_col, None
    t = chain("vector", lambda e: e.tensor_mul(out=smalla[:], in0=a_col[:], in1=a_col[:]),
              [t], cDVE)
    cur, oth = smalla, smallb
    for i in range(10):
        t = chain("vector", lambda e, cur=cur, oth=oth: e.tensor_mul(
            out=oth[:], in0=cur[:], in1=cur[:]), [t], cDVE)
        cur, oth = oth, cur
    t = chain("vector", lambda e, cur=cur: e.tensor_scalar_max(out=a2k[:], in0=cur[:],
                                                               scalar1=1e-30), [t], cDVE)
    t = chain("scalar", lambda e: e.activation(out=lnA[:], in_=a2k[:], func=AF.Ln), [t], cACT)
    for r in range(4):
        t = chain("vector", lambda e, r=r: e.tensor_scalar(
            out=coef[:, r, :], in0=lnA[:], scalar1=e_sel[:, r:r + 1], scalar2=None,
            op0=ALU.mult), [t], cDVE)
    t = chain("scalar", lambda e: e.activation(out=coef[:], in_=coef[:], func=AF.Exp), [t], cACT)
    for r in range(4):
        t = chain("vector", lambda e, r=r: e.tensor_scalar(
            out=coef[:, r, :], in0=coef[:, r, :], scalar1=m_sel[:, r:r + 1], scalar2=None,
            op0=ALU.mult), [t], cDVE)
    tok_coef = t

    stgA = AR_C.bf16(16384, NG, 512)
    stgB = AR_C.bf16(24768, NG, 512)
    scr1 = AR_C.f32(0, 2048)
    scr2 = AR_C.f32(8192, 2048)
    bc_rb = AR_C.f32(44352, 512)
    bc_mb = AR_C.f32(46400, 512)
    bc_t1 = AR_C.f32(48448, 512)
    pr_s1 = AR_C.f32(16704, 512)
    pr_s2 = AR_C.f32(18752, 512)
    pr_s3 = AR_C.f32(20800, 512)

    ln_state = {"mm": None, "apply": None, "rows": None}

    def layernorm(src3, tiles, apply_ops, waits_src):
        for (c0, w) in tiles:
            cast_waits = list(waits_src)
            if ln_state["mm"] is not None:
                cast_waits.append(ln_state["mm"])
            for g in range(NG):
                P.op("scalar", lambda e, g=g, c0=c0, w=w: e.activation(
                    out=stgA[:, g, 0:w], in_=src3(g, c0, w), func=AF.Identity),
                    waits=cast_waits)
                P.op("scalar", lambda e, g=g, c0=c0, w=w: e.activation(
                    out=stgB[:, g, 0:w], in_=src3(g, c0, w), func=AF.Square))
            tok_cast = P.tick("scalar", cACT)
            mm_waits = [tok_cast]
            if ln_state["rows"] is not None:
                mm_waits.append(ln_state["rows"])
            for g in range(NG):
                P.op("tensor", lambda e, g=g, w=w: e.matmul(
                    PB[0].ap()[0:1, 0:w], ones_col[:, 0:1], stgA[:, g, 0:w],
                    start=(g == 0), stop=(g == NG - 1)), waits=mm_waits)
            for g in range(NG):
                P.op("tensor", lambda e, g=g, w=w: e.matmul(
                    PB[1].ap()[0:1, 0:w], ones_col[:, 0:1], stgB[:, g, 0:w],
                    start=(g == 0), stop=(g == NG - 1)))
            tok_mm = P.tick("tensor", cPE)
            ln_state["mm"] = tok_mm
            P.op("vector", lambda e, w=w: e.tensor_scalar(
                out=r_mu[0:1, 0:w], in0=PB[0].ap()[0:1, 0:w], scalar1=1.0 / D, scalar2=None,
                op0=ALU.mult), waits=[tok_mm])
            P.op("vector", lambda e, w=w: e.tensor_scalar(
                out=r_var[0:1, 0:w], in0=PB[1].ap()[0:1, 0:w], scalar1=1.0 / D, scalar2=None,
                op0=ALU.mult))
            tok_rows = P.tick("vector", cDVE)
            ln_state["rows"] = tok_rows
            P.op("vector", lambda e, w=w: e.tensor_mul(
                out=r_mrs[0:1, 0:w], in0=r_mu[0:1, 0:w], in1=r_mu[0:1, 0:w]))
            P.op("vector", lambda e, w=w: e.scalar_tensor_tensor(
                out=r_var[0:1, 0:w], in0=r_var[0:1, 0:w], scalar=LNEPS, in1=r_mrs[0:1, 0:w],
                op0=ALU.add, op1=ALU.subtract))
            tok_var = P.tick("vector", cDVE)
            P.op("scalar", lambda e, w=w: e.activation(
                out=r_mrs[0:1, 0:w], in_=r_var[0:1, 0:w], func=AF.Ln), waits=[tok_var])
            P.op("scalar", lambda e, w=w: e.activation(
                out=r_rstd[0:1, 0:w], in_=r_mrs[0:1, 0:w], func=AF.Exp, scale=-0.5))
            tok_rstd = P.tick("scalar", cACT)
            P.op("vector", lambda e, w=w: e.tensor_mul(
                out=r_mrs[0:1, 0:w], in0=r_mu[0:1, 0:w], in1=r_rstd[0:1, 0:w]), waits=[tok_rstd])
            tok_mrs = P.tick("vector", cDVE)
            P.op("scalar", lambda e, w=w: e.activation(
                out=rb_hi[0:1, 0:w], in_=r_rstd[0:1, 0:w], func=AF.Identity), waits=[tok_mrs])
            P.op("scalar", lambda e, w=w: e.activation(
                out=mb_hi[0:1, 0:w], in_=r_mrs[0:1, 0:w], func=AF.Identity))
            P.op("scalar", lambda e, w=w: e.activation(
                out=r_mu[0:1, 0:w], in_=rb_hi[0:1, 0:w], func=AF.Identity))
            tok_hi = P.tick("scalar", cACT)
            P.op("vector", lambda e, w=w: e.tensor_sub(
                out=r_mu[0:1, 0:w], in0=r_rstd[0:1, 0:w], in1=r_mu[0:1, 0:w]), waits=[tok_hi])
            tok_lof1 = P.tick("vector", cDVE)
            P.op("scalar", lambda e, w=w: e.activation(
                out=rb_lo[0:1, 0:w], in_=r_mu[0:1, 0:w], func=AF.Identity), waits=[tok_lof1])
            P.op("scalar", lambda e, w=w: e.activation(
                out=r_mu[0:1, 0:w], in_=mb_hi[0:1, 0:w], func=AF.Identity))
            tok_hi2 = P.tick("scalar", cACT)
            P.op("vector", lambda e, w=w: e.tensor_sub(
                out=r_mu[0:1, 0:w], in0=r_mrs[0:1, 0:w], in1=r_mu[0:1, 0:w]), waits=[tok_hi2])
            tok_lof2 = P.tick("vector", cDVE)
            P.op("scalar", lambda e, w=w: e.activation(
                out=mb_lo[0:1, 0:w], in_=r_mu[0:1, 0:w], func=AF.Identity), waits=[tok_lof2])
            tok_lo = P.tick("scalar", cACT)
            bc_waits = [tok_lo]
            if ln_state["apply"] is not None:
                bc_waits.append(ln_state["apply"])
            P.op("tensor", lambda e, w=w: e.matmul(
                PB[2].ap()[:, 0:w], ones1_128[0:1, :], rb_hi[0:1, 0:w], start=True, stop=False),
                waits=bc_waits)
            P.op("tensor", lambda e, w=w: e.matmul(
                PB[2].ap()[:, 0:w], ones1_128[0:1, :], rb_lo[0:1, 0:w], start=False, stop=True))
            P.op("tensor", lambda e, w=w: e.matmul(
                PB[3].ap()[:, 0:w], ones1_128[0:1, :], mb_hi[0:1, 0:w], start=True, stop=False))
            P.op("tensor", lambda e, w=w: e.matmul(
                PB[3].ap()[:, 0:w], ones1_128[0:1, :], mb_lo[0:1, 0:w], start=False, stop=True))
            tok_bc = P.tick("tensor", cPE)
            P.op("scalar", lambda e, w=w: e.activation(
                out=bc_rb[:, 0:w], in_=PB[2].ap()[:, 0:w], func=AF.Identity), waits=[tok_bc])
            P.op("scalar", lambda e, w=w: e.activation(
                out=bc_mb[:, 0:w], in_=PB[3].ap()[:, 0:w], func=AF.Identity))
            tok_bcs = P.tick("scalar", cACT)
            apply_ops(c0, w, tok_bcs)
            ln_state["apply"] = P.tick("vector", cDVE)
        return ln_state["apply"]

    tok_attn_out_done = None
    tok_u0 = None
    tok_spill = None
    if "A" in phases:
        tiles_full = [(0, 512), (512, 512), (1024, 512), (1536, 512), (1540, 512)]

        def apply_ln1(c0, w, tok_bcs):
            for g in range(NG):
                P.op("vector", lambda e, g=g, c0=c0, w=w: e.tensor_mul(
                    out=bc_t1[:, 0:w], in0=vS[:, g, c0:c0 + w], in1=bc_rb[:, 0:w]),
                    waits=[tok_bcs])
                P.op("vector", lambda e, g=g, c0=c0, w=w: e.tensor_sub(
                    out=vL[:, g, c0:c0 + w], in0=bc_t1[:, 0:w], in1=bc_mb[:, 0:w]))

        tok_ln1 = layernorm(lambda g, c0, w: vS[:, g, c0:c0 + w], tiles_full, apply_ln1, [tok_x])
        dump("ln1", vL[:, :, 2:2050], 2048, [tok_ln1])
        if "bcdbg" in debug:
            dbg1 = nc.dram_tensor("dbg_bcdbg", [128, 1024], F32, kind="ExternalOutput").ap()
            dbg2 = nc.dram_tensor("dbg_rows", [1, 2048], F32, kind="ExternalOutput").ap()
            dump_toks.append(P.dma("sync", dbg1[:, 0:512], bc_rb[:], cST, waits=[tok_ln1]))
            dump_toks.append(P.dma("sync", dbg1[:, 512:1024], bc_mb[:], cST))
            dump_toks.append(P.dma("sync", dbg2[0:1, 0:512], r_mu[:], cST))
            dump_toks.append(P.dma("sync", dbg2[0:1, 512:1024], r_var[:], cST))
            dump_toks.append(P.dma("sync", dbg2[0:1, 1024:1536], r_rstd[:], cST))
            dump_toks.append(P.dma("sync", dbg2[0:1, 1536:2048], r_mrs[:], cST))

        P.op("gpsimd", lambda e: e.memset(scr2[:], 1.0))
        tok_ones = P.tick("gpsimd", cPOOL)
        scan_toks = []
        for g in range(NG):
            P.op("vector", lambda e, g=g: e.tensor_scalar(
                out=scr1[:], in0=scr2[:], scalar1=a_col[:, g:g + 1], scalar2=None, op0=ALU.mult),
                waits=[tok_ones, tok_a, tok_ln1])
            tok_ab = P.tick("vector", cDVE)
            P.op("vector", lambda e, g=g: e.tensor_tensor_scan(
                out=vS[:, g, 2:2050], data0=scr1[:], data1=vL[:, g, 2:2050],
                initial=0.0, op0=ALU.mult, op1=ALU.add), waits=[tok_ab])
            scan_toks.append(P.tick("vector", cDVE))
        P.op("vector", lambda e: e.tensor_copy(out=hb1[:], in_=vS[:, :, 2049]), waits=[scan_toks[-1]])
        tok_wc = P.tick("vector", cDVE)
        t1 = P.dma("sync", cc1_in, hb1[:], cCCP, waits=[tok_wc])
        P.op("gpsimd", lambda e: e.collective_compute(
            "AllGather", ALU.bypass, replica_groups=RG,
            ins=[cc1_in.opt()], outs=[cc1_out.opt()]), waits=[t1])
        tok_cc1 = P.tick("gpsimd", cCC)
        t2 = P.dma("gpsimd", wc_all[:], cc1_out.rearrange("(r p) g -> p r g", p=128), cCCR,
                   waits=[tok_cc1])
        ts = chain("vector", lambda e: e.tensor_mul(out=sinit[:], in0=coef[:, 0, :],
                                                    in1=wc_all[:, 0, :]), [t2, tok_coef], cDVE)
        for r in range(1, 4):
            ts = chain("vector", lambda e, r=r: e.tensor_mul(out=smalla[:], in0=coef[:, r, :],
                                                             in1=wc_all[:, r, :]), [ts], cDVE)
            ts = chain("vector", lambda e, r=r: e.tensor_add(out=sinit[:], in0=sinit[:],
                                                             in1=smalla[:]), [ts], cDVE)
        tok_sinit = ts
        if "emadbg" in debug:
            dbge = nc.dram_tensor("dbg_emadbg", [128, 512], F32, kind="ExternalOutput").ap()
            emadump = bc_rb
            P.op("vector", lambda e: e.tensor_copy(out=emadump[:, 0:32], in_=wc_all[:].rearrange("p a b -> p (a b)")), waits=[tok_sinit])
            P.op("vector", lambda e: e.tensor_copy(out=emadump[:, 32:40], in_=sinit[:]))
            P.op("vector", lambda e: e.tensor_copy(out=emadump[:, 40:72], in_=coef[:].rearrange("p a b -> p (a b)")))
            P.op("vector", lambda e: e.tensor_copy(out=emadump[:, 72:80], in_=a2k[:]))
            P.op("vector", lambda e: e.tensor_copy(out=emadump[:, 80:88], in_=a_col[:]))
            P.op("vector", lambda e: e.tensor_copy(out=emadump[:, 88:96], in_=lnA[:]))
            tok_ed = P.tick("vector", cDVE)
            dump_toks.append(P.dma("sync", dbge, emadump[:], cST, waits=[tok_ed]))
        for g in range(NG):
            P.op("vector", lambda e, g=g: e.tensor_scalar(
                out=scr1[:], in0=scr2[:], scalar1=a_col[:, g:g + 1], scalar2=None, op0=ALU.mult),
                waits=[tok_sinit])
            tok_ab = P.tick("vector", cDVE)
            P.op("vector", lambda e, g=g: e.tensor_tensor_scan(
                out=vS[:, g, 2:2050], data0=scr1[:], data1=vL[:, g, 2:2050],
                initial=sinit[:, g:g + 1], op0=ALU.mult, op1=ALU.add), waits=[tok_ab])
            P.tick("vector", cDVE)
        tok_scan2 = (cDVE, cDVE.n)
        dump("S", vS[:, :, 2:2050], 2048, [tok_scan2])

        wtb = [AR_C.bf16(33088, NG, 128), AR_C.bf16(35136, NG, 128)]
        tok_prev_evac = [tok_scan2, tok_scan2]
        tok_wtb_rd = [None, None]
        tok_prev_mm = None
        last_conv = None
        for tt in range(4):
            c0 = 2 + 512 * tt
            cast_w = [tok_scan2] + dbg_guard()
            if tok_prev_mm is not None:
                cast_w.append(tok_prev_mm)
            P.op("scalar", lambda e, c0=c0: e.activation(
                out=stgA[:, :, 0:512], in_=vS[:, :, c0:c0 + 512], func=AF.Identity),
                waits=cast_w)
            tok_scast = P.tick("scalar", cACT)
            for ec in range(NG):
                bi = ec % 2
                tw_w = [] if tok_wtb_rd[bi] is None else [tok_wtb_rd[bi]]
                tw = P.dma("gpsimd", wtb[bi][:],
                           Wt_d.rearrange("(g p) e -> p g e", p=128)[:, :, 128 * ec:128 * (ec + 1)],
                           cWT[bi], waits=tw_w)
                for g in range(NG):
                    P.op("vector", lambda e, bi=bi, g=g: e.tensor_scalar(
                        out=wtb[bi][:, g, :], in0=wtb[bi][:, g, :], scalar1=om_a[:, g:g + 1],
                        scalar2=None, op0=ALU.mult), waits=[tw])
                tok_fold = P.tick("vector", cDVE)
                pb = PB[ec % 2]
                for g in range(NG):
                    P.op("tensor", lambda e, pb=pb, bi=bi, g=g: e.matmul(
                        pb.ap()[:, :], wtb[bi][:, g, :], stgA[:, g, 0:512],
                        start=(g == 0), stop=(g == NG - 1)),
                        waits=[tok_fold, tok_scast, tok_prev_evac[ec % 2]])
                tok_ymm = P.tick("tensor", cPE)
                tok_wtb_rd[bi] = tok_ymm
                cbase = c0 - 2
                P.op("vector", lambda e, ec=ec, cbase=cbase: e.tensor_scalar(
                    out=scr1[0:128, 0:512], in0=vL[:, ec, cbase:cbase + 512],
                    scalar1=conv_w[:, ec, 0:1], scalar2=None, op0=ALU.mult),
                    waits=[tok_ln1])
                srcs = [scr1, scr2, scr1, scr2, scr1]
                for kk in range(1, 5):
                    P.op("vector", lambda e, ec=ec, cbase=cbase, kk=kk, si=srcs[kk - 1], so=srcs[kk]:
                         e.scalar_tensor_tensor(
                             out=so[0:128, 0:512], in0=vL[:, ec, cbase + kk:cbase + kk + 512],
                             scalar=conv_w[:, ec, kk:kk + 1], in1=si[0:128, 0:512],
                             op0=ALU.mult, op1=ALU.add))
                tok_conv = P.tick("vector", cDVE)
                last_conv = tok_conv
                P.op("vector", lambda e, pb=pb, ec=ec, c0=c0: e.scalar_tensor_tensor(
                    out=vS[:, ec, c0:c0 + 512], in0=pb.ap()[:, :], scalar=bcb[:, ec:ec + 1],
                    in1=scr1[0:128, 0:512], op0=ALU.add, op1=ALU.add),
                    waits=[tok_ymm, tok_bcb, tok_conv])
                tok_prev_evac[ec % 2] = P.tick("vector", cDVE)
            tok_prev_mm = tok_ymm
        tok_attn_out_done = tok_prev_evac[1]
        dump("attn_out", vS[:, :, 2:2050], 2048, [tok_attn_out_done])

        tok_xre = P.dma("sync", vL, x_in.rearrange("(g p) t -> p g t", p=128), cLX,
                        waits=[last_conv] + dbg_guard())
        tiles_own = [(2, 512), (514, 512), (1026, 512), (1538, 512)]

        def apply_x2(c0, w, tok_bcs):
            for g in range(NG):
                P.op("vector", lambda e, g=g, c0=c0, w=w: e.tensor_mul(
                    out=bc_t1[:, 0:w], in0=vS[:, g, c0:c0 + w], in1=bc_rb[:, 0:w]),
                    waits=[tok_bcs, tok_xre] + dbg_guard())
                P.op("vector", lambda e, g=g, c0=c0, w=w: e.tensor_sub(
                    out=scr2[0:128, 0:w], in0=bc_t1[:, 0:w], in1=bc_mb[:, 0:w]))
                P.op("vector", lambda e, g=g, c0=c0, w=w: e.tensor_add(
                    out=vS[:, g, c0:c0 + w], in0=scr2[0:128, 0:w], in1=vL[:, g, c0:c0 + w]))

        tok_x2 = layernorm(lambda g, c0, w: vS[:, g, c0:c0 + w], tiles_own, apply_x2, [tok_attn_out_done])
        dump("x2", vS[:, :, 2:2050], 2048, [tok_x2])
        tok_spill = P.dma("sync", x2_spill.rearrange("(g p) t -> p g t", p=128),
                          vS[:, :, 2:2050], cSP, waits=[tok_x2])

        def apply_u0(c0, w, tok_bcs):
            for g in range(NG):
                P.op("vector", lambda e, g=g, c0=c0, w=w: e.tensor_mul(
                    out=bc_t1[:, 0:w], in0=vS[:, g, c0:c0 + w], in1=bc_rb[:, 0:w]),
                    waits=[tok_bcs])
                P.op("vector", lambda e, g=g, c0=c0, w=w: e.tensor_sub(
                    out=vL[:, g, c0:c0 + w], in0=bc_t1[:, 0:w], in1=bc_mb[:, 0:w]))

        tok_u0 = layernorm(lambda g, c0, w: vS[:, g, c0:c0 + w], tiles_own, apply_u0, [tok_x2])
        P.op("gpsimd", lambda e: e.memset(vL[:, :, 0:2], 0.0), waits=[tok_xre])
        P.op("gpsimd", lambda e: e.memset(vL[:, :, 2050:2052], 0.0))
        tok_hz = P.tick("gpsimd", cPOOL)
        P.op("vector", lambda e: e.tensor_copy(out=hb1[:], in_=vL[:, :, 2049]), waits=[tok_u0])
        tok_b2 = P.tick("vector", cDVE)
        t3 = P.dma("sync", cc2_in, hb1[:], cCCP, waits=[tok_b2])
        P.op("gpsimd", lambda e: e.collective_compute(
            "AllGather", ALU.bypass, replica_groups=RG,
            ins=[cc2_in.opt()], outs=[cc2_out.opt()]), waits=[t3])
        tok_cc2 = P.tick("gpsimd", cCC)
        tok_poff = tok_small

        _poff_cache = {}

        def read_prev(e, base_ap, dst_ap, block_elems):
            if id(e) not in _poff_cache:
                _poff_cache[id(e)] = e.value_load(poff[0:1, 0:1])
            v = _poff_cache[id(e)]
            srcp = dataclasses.replace(base_ap, offset=v * block_elems + base_ap.offset)
            return e.dma_start(out=dst_ap, in_=srcp)

        P.op("gpsimd", lambda e: read_prev(e, cc2_out[0:128, :], hb2[:], 128 * NG),
             waits=[tok_cc2, tok_poff])
        tok_hb = P.tick("gpsimd", cCCR, by=16)
        P.op("vector", lambda e: e.tensor_scalar(
            out=vL[:, :, 1], in0=hb2[:], scalar1=gate[:, 0:1], scalar2=None, op0=ALU.mult),
            waits=[tok_hb, tok_hz])
        tok_halo = P.tick("vector", cDVE)
        dump("u0", vL[:, :, 2:2050], 2048, [tok_u0, tok_halo])

    import dataclasses as _dc

    def apx(ap, dims):
        part = list(ap.ap[0])
        return _dc.replace(ap, ap=[part] + [list(d) for d in dims])

    def offx(ap, delta):
        return _dc.replace(ap, offset=ap.offset + delta)

    def qp_ap(fc):
        return AR_S.bf16(4096 * fc, 2048)
    def kp_ap(fc):
        return AR_S.bf16(16384 + 4096 * fc, 2048)
    vqk_flat = AR_S.bf16(0, 32832)
    hln = AR_S.bf16(0, NG, 2048)
    h1t = AR_S.bf16(32768, NHC, 512)
    x2r = AR_S.f32(0, NG, 2048)

    c_v = AR_C.bf16(0, 16, 520)
    c_oa = AR_C.bf16(16704, 8, 512)
    obnd = AR_C.bf16(16704, 8, 128)
    c_aT = AR_C.bf16(24896, 3, 512)
    c_st = [AR_C.f32(27968, 512), AR_C.f32(30016, 512)]
    c_tab = AR_C.f32(32064, 2, 4, 64)
    c_wq = AR_C.bf16(34112, 2, 1536)
    c_wo = AR_C.bf16(40256, 2, 8, 128)
    def khalo_ap(fc):
        return AR_C.bf16(27968 + 2048 * fc, 1024)
    c_w1 = AR_C.bf16(0, 2, 2, NG, 128)
    c_w2 = AR_C.bf16(16384, 2, 512)

    wqkv3 = wqkv_d.rearrange("l (g p) c -> l p g c", p=128)
    w1r = w1_d.rearrange("l (g p) c -> l p g c", p=128)
    wo_r = wo_d.rearrange("l (h dh) e -> l dh h e", dh=64)

    def exchange_boundary(cc_in_t, cc_out_t, tok_ready):
        P.op("vector", lambda e: e.tensor_copy(out=hb1[:], in_=vL[:, :, 2049]),
             waits=[tok_ready])
        twc = P.tick("vector", cDVE)
        td = P.dma("sync", cc_in_t, hb1[:], cCCP, waits=[twc])
        P.op("gpsimd", lambda e: e.collective_compute(
            "AllGather", ALU.bypass, replica_groups=RG,
            ins=[cc_in_t.opt()], outs=[cc_out_t.opt()]), waits=[td])
        tcc = P.tick("gpsimd", cCC)
        P.op("gpsimd", lambda e: read_prev(e, cc_out_t[0:128, :], hb2[:], 128 * NG),
             waits=[tcc])
        th = P.tick("gpsimd", cCCR, by=16)
        P.op("vector", lambda e: e.tensor_scalar(
            out=vL[:, :, 1], in0=hb2[:], scalar1=gate[:, 0:1], scalar2=None, op0=ALU.mult),
            waits=[th])
        return P.tick("vector", cDVE)

    def _slice_cols(ap2, so, n):
        return ap2[:, so:so + n]

    def mm_bank_group(ops, waits):
        n = len(ops)
        for i, f in enumerate(ops):
            P.op("tensor",
                 (lambda e, f=f, s=(i == 0), t=(i == n - 1): f(e, s, t)),
                 waits=(waits if i == 0 else ()))

    sub = set(c for c in phases if c.isdigit()) or {"1", "2", "3", "4", "5"}
    if "B" in phases:
        tok_u = tok_halo

        for l in range(2):
            st = dict(
                ub_rd=[None, None], qkv_ev=[None, None, None], vmemset=None,
                tab_rd=[None, None], bct1=None, qbf_rd=None, pt=None,
                wq=[None, None], stats_rd=None, mrow_rd=None,
                st_rd=[None, None], aT_w=[None, None, None], aT_rd=[None, None, None],
                pw_rd=None, oa_rd=None, kv_dma=None,
            )
            P.op("gpsimd", lambda e: (e.memset(c_v[:, :, :], 1.0) if "v" not in phases
                                      else e.engine_nop()), waits=[tok_u])
            st["vmemset"] = P.tick("gpsimd", cPOOL)

            NTC = 2 if "x" in phases else 16
            for tc in range(NTC):
                bi = tc % 2
                pass
                uw = [tok_u] + ([st["ub_rd"][bi]] if st["ub_rd"][bi] else [])
                P.op("vector", lambda e, tc=tc, bi=bi: (e.tensor_copy(
                    out=ub[bi][:, :, 0:129], in_=vL[:, :, 1 + 128 * tc: 130 + 128 * tc])
                    if "u" not in phases else e.engine_nop()), waits=uw)
                tok_ub = P.tick("vector", cDVE)
                tbi = tc % 2
                for ti, tabd in enumerate([] if "t" in phases else [tabAq_d, tabBq_d, tabAk_d, tabBk_d]):
                    tok_tab = P.dma(
                        "sync", c_tab[:, tbi, ti, :], tabd[:, tc, :], cTB[tbi],
                        waits=([st["tab_rd"][tbi]] if (ti == 0 and st["tab_rd"][tbi]) else []))
                pw0 = [tok_ub]
                for sec in range(3):
                    if st["qkv_ev"][sec] is not None:
                        pw0.append(st["qkv_ev"][sec])
                for g in range(NG):
                    wbuf = g % 2
                    wv = [] if st["wq"][wbuf] is None else [st["wq"][wbuf]]
                    twq = (P.dma("gpsimd", c_wq[:, wbuf, :], wqkv3[l, :, g, :], cWQ[wbuf], waits=wv)
                           if "w" not in phases else (tok_ub))
                    off = 1 if g < 4 else 0
                    if "m" in phases:
                        st["wq"][wbuf] = twq
                        continue
                    for sec, pb in ((0, PB[0]), (1, PB[1]), (2, PB[2])):
                        P.op("tensor", lambda e, g=g, off=off, sec=sec, pb=pb, bi=bi,
                             wbuf=wbuf: e.matmul(
                                 pb.ap()[:, :], ub[bi][:, g, off:off + 128],
                                 c_wq[:, wbuf, 512 * sec: 512 * (sec + 1)],
                                 start=(g == 0), stop=(g == NG - 1)),
                             waits=(pw0 + [twq]) if True else [])
                    tok_mm = P.tick("tensor", cPE)
                    st["wq"][wbuf] = tok_mm
                if "m" in phases:
                    tok_mm = tok_ub
                tok_qkvmm = tok_mm
                st["ub_rd"][bi] = tok_qkvmm
                if "e" in phases:
                    st["qkv_ev"] = [tok_qkvmm] * 3
                    continue
                vdst = apx(c_v[:, tc, 0:520], [[65, 8], [1, 64]])
                P.op("scalar", lambda e, vdst=vdst: e.activation(
                    out=vdst, in_=PB[2].ap()[:, :], func=AF.Identity),
                    waits=[tok_qkvmm, st["vmemset"]])
                P.op("scalar", lambda e: e.activation(out=c_st[0][:, :], in_=PB[0].ap()[:, :],
                                                      func=AF.Identity))
                P.op("scalar", lambda e: e.activation(out=c_st[1][:, :], in_=PB[1].ap()[:, :],
                                                      func=AF.Identity))
                tok_qkev = P.tick("scalar", cACT)
                st["qkv_ev"] = [tok_qkev, tok_qkev, tok_qkev]

                for which in ((0, 1) if "p" not in phases else ()):
                    stg = c_st[which]
                    tabA = c_tab[:, tbi, 2 * which, :]
                    tabB = c_tab[:, tbi, 2 * which + 1, :]
                    scl = (qsr if which == 0 else ksr)[:, l, :]
                    scl_sh = (qsr_shuf if which == 0 else ksr_shuf)[:, l, :]
                    sqw = [tok_qkev]
                    if st["bct1"] is not None:
                        sqw.append(st["bct1"])
                    P.op("scalar", lambda e, stg=stg: e.activation(
                        out=pr_s3[:, :], in_=stg[:, :], func=AF.Square), waits=sqw)
                    tok_sq = P.tick("scalar", cACT)
                    P.op("vector", lambda e: e.reduce_sum(
                        out=nrm[:, :], in_=apx(pr_s3[:, :], [[64, 8], [1, 64]]), axis=AX.X),
                        waits=[tok_sq])
                    t = P.tick("vector", cDVE)
                    t = chain("scalar", lambda e: e.activation(out=nrm2[:], in_=nrm[:],
                                                               func=AF.Ln), [t], cACT)
                    t = chain("scalar", lambda e: e.activation(out=nrm2[:], in_=nrm2[:],
                                                               func=AF.Exp, scale=0.5),
                              [t], cACT)
                    t = chain("vector", lambda e: e.tensor_scalar_max(
                        out=nrm2[:], in0=nrm2[:], scalar1=1e-12), [t], cDVE)
                    t = chain("vector", lambda e: e.reciprocal(out=rqv[:], in_=nrm2[:]),
                              [t], cDVE)
                    t = chain("vector", lambda e, tabA=tabA, scl=scl: e.tensor_tensor(
                        out=pr_s1[:, 0:64], in0=tabA, in1=scl, op=ALU.mult),
                        [t, tok_tab], cDVE)
                    t = chain("vector", lambda e, tabB=tabB, scl_sh=scl_sh: e.tensor_tensor(
                        out=pr_s1[:, 64:128], in0=tabB, in1=scl_sh, op=ALU.mult), [t], cDVE)
                    rq_b = apx(rqv[:, :], [[1, 8], [0, 64]])
                    P.op("vector", lambda e, stg=stg, rq_b=rq_b: e.tensor_tensor(
                        out=apx(pr_s2[:, :], [[64, 8], [1, 64]]),
                        in0=apx(stg[:, :], [[64, 8], [1, 64]]), in1=rq_b, op=ALU.mult),
                        waits=[t])
                    P.op("vector", lambda e: e.tensor_copy(
                        out=apx(pr_s3[:, :], [[64, 8], [1, 32]]),
                        in_=apx(offx(pr_s2[:, :], 32), [[64, 8], [1, 32]])))
                    P.op("vector", lambda e: e.tensor_copy(
                        out=apx(offx(pr_s3[:, :], 32), [[64, 8], [1, 32]]),
                        in_=apx(pr_s2[:, :], [[64, 8], [1, 32]])))
                    tok_pp = P.tick("vector", cDVE)
                    tA_rep = apx(pr_s1[:, 0:64], [[0, 8], [1, 64]])
                    tB_rep = apx(pr_s1[:, 64:128], [[0, 8], [1, 64]])
                    P.op("vector", lambda e, stg=stg, tA_rep=tA_rep: e.tensor_tensor(
                        out=apx(stg[:, :], [[64, 8], [1, 64]]),
                        in0=apx(pr_s2[:, :], [[64, 8], [1, 64]]), in1=tA_rep, op=ALU.mult),
                        waits=[tok_pp])
                    P.op("vector", lambda e, tB_rep=tB_rep: e.tensor_tensor(
                        out=apx(pr_s2[:, :], [[64, 8], [1, 64]]),
                        in0=apx(pr_s3[:, :], [[64, 8], [1, 64]]), in1=tB_rep, op=ALU.mult))
                    P.op("vector", lambda e, stg=stg: e.tensor_add(
                        out=pr_s3[:, :], in0=stg[:, :], in1=pr_s2[:, :]))
                    tok_rot = P.tick("vector", cDVE)
                    st["bct1"] = tok_rot
                    cw = [tok_rot]
                    if st["qbf_rd"] is not None:
                        cw.append(st["qbf_rd"])
                    P.op("scalar", lambda e, which=which: e.activation(
                        out=qbf[:, :], in_=pr_s3[:, :], func=AF.Identity,
                        scale=(QK_SCALE if which == 0 else 1.0)), waits=cw)
                    tok_cast = P.tick("scalar", cACT)
                    tw = [tok_cast]
                    if st["pt"] is not None:
                        tw.append(st["pt"])
                    for fc in range(4):
                        P.op("tensor", lambda e, fc=fc: e.transpose(
                            PT.ap()[:, 128 * fc: 128 * (fc + 1)],
                            qbf[:, 128 * fc: 128 * (fc + 1)], ident[:, :]),
                            waits=tw if fc == 0 else ())
                    tok_tp = P.tick("tensor", cPE)
                    st["qbf_rd"] = tok_tp
                    base = 0 if which == 0 else 8192
                    dst = apx(offx(vqk_flat, base + 128 * tc), [[2048, 4], [1, 128]])
                    P.op("vector", lambda e, dst=dst: e.tensor_copy(
                        out=dst, in_=apx(PT.ap()[:, :], [[128, 4], [1, 128]])),
                        waits=[tok_tp])
                    st["pt"] = P.tick("vector", cDVE)
                st["tab_rd"][tbi] = st["bct1"]
            tok_prep_done = st["pt"] or st["qkv_ev"][0]
            if "2" not in sub:
                tok_u = tok_prep_done
                continue

            pk_toks = []
            for fc in range(4):
                pk_toks.append(P.dma(
                    "sync", cc3_in[l][128 * fc: 128 * (fc + 1), 0:1024],
                    kp_ap(fc)[:, 1024:2048], cCCP, waits=[tok_prep_done]))
            for tcv in range(8, 16):
                pk_toks.append(P.dma(
                    "sync", cc3_in[l][520 + 128 * (tcv - 8): 520 + 128 * (tcv - 7), 0:520],
                    c_v[:, tcv, :], cCCP, waits=[tok_prep_done]))
            for fc in range(4):
                pk_toks.append(P.dma(
                    "sync", cc3_in[l][1544 + fc: 1545 + fc, 0:128],
                    qp_ap(fc)[:, 2047:2048], cCCP, waits=[tok_prep_done]))
            P.op("gpsimd", lambda e, l=l: e.collective_compute(
                "AllGather", ALU.bypass, replica_groups=RG,
                ins=[cc3_in[l].opt()], outs=[cc3_out[l].opt()]), waits=[pk_toks[-1]])
            tok_cc3 = P.tick("gpsimd", cCC)
            blk = R3 * 1040
            for fc in range(4):
                P.op("gpsimd", lambda e, fc=fc, l=l: read_prev(
                    e, cc3_out[l][128 * fc: 128 * (fc + 1), 0:1024], khalo_ap(fc)[:, :], blk),
                    waits=[tok_cc3, tok_prep_done])
                P.tick("gpsimd", cCCR, by=16)
            for fc in range(4):
                P.op("gpsimd", lambda e, fc=fc, l=l: read_prev(
                    e, cc3_out[l][1544 + fc: 1545 + fc, 0:128], qbndt[:, fc:fc + 1], blk),
                    waits=[tok_cc3])
                P.tick("gpsimd", cCCR, by=16)
            tok_halo_kv = (cCCR, cCCR.n)

            if "3" not in sub:
                tok_u = tok_halo_kv
                continue
            def kjc_src(w, jc, fc, ro):
                if w == 0 and jc < 4:
                    return khalo_ap(fc)[ro:ro + 64, 512 + 128 * jc: 640 + 128 * jc]
                base = 512 * (w - 1) if w >= 1 else -512
                return kp_ap(fc)[ro:ro + 64, base + 128 * jc: base + 128 * jc + 128]

            def vjc_src(w, jc):
                if w == 0 and jc < 4:
                    return 12 + jc
                return (4 * (w - 1) + jc) if w >= 1 else (jc - 4)

            def stats_segs(w, qi, fc, ro):
                j0, j1 = 128 * qi, 128 * qi + 640
                segs = []
                if w == 0:
                    if j0 < 512:
                        segs.append(khalo_ap(fc)[ro:ro + 64, 512 + j0: 1024])
                    segs.append(kp_ap(fc)[ro:ro + 64, max(0, j0 - 512): j1 - 512])
                else:
                    b = 512 * (w - 1)
                    segs.append(kp_ap(fc)[ro:ro + 64, b + j0: b + j1])
                return segs

            def attn_evac(h, iw, dst_ap, tok_pv, st, extra=()):
                t = chain("vector", lambda e, iw=iw: e.reciprocal(
                    out=r64[64:65, 0:iw], in_=PW.ap()[64:65, 0:iw]),
                    [tok_pv] + list(extra), cDVE)
                t = chain("scalar", lambda e, iw=iw: e.activation(
                    out=r64h[64:65, 0:iw], in_=r64[64:65, 0:iw], func=AF.Identity), [t], cACT)
                t = chain("scalar", lambda e, iw=iw: e.activation(
                    out=r64[64:65, 512:512 + iw], in_=r64h[64:65, 0:iw], func=AF.Identity),
                    [t], cACT)
                t = chain("vector", lambda e, iw=iw: e.tensor_sub(
                    out=pr_s3[64:65, 0:iw], in0=r64[64:65, 0:iw],
                    in1=r64[64:65, 512:512 + iw]), [t], cDVE)
                t = chain("scalar", lambda e, iw=iw: e.activation(
                    out=r64h[64:65, 512:512 + iw], in_=pr_s3[64:65, 0:iw],
                    func=AF.Identity), [t], cACT)
                bw = [t]
                if st["rb_rd"] is not None:
                    bw.append(st["rb_rd"])
                mm_bank_group([
                    lambda e, s, p, iw=iw: e.matmul(PB[3].ap()[0:64, 0:iw],
                                                    on64[64:65, 0:64],
                                                    r64h[64:65, 0:iw], start=s, stop=p),
                    lambda e, s, p, iw=iw: e.matmul(PB[3].ap()[0:64, 0:iw],
                                                    on64[64:65, 0:64],
                                                    r64h[64:65, 512:512 + iw],
                                                    start=s, stop=p),
                ], bw)
                tok_rb = P.tick("tensor", cPE)
                P.op("scalar", lambda e, iw=iw: e.activation(
                    out=rb64[0:64, 0:iw], in_=PB[3].ap()[0:64, 0:iw], func=AF.Identity),
                    waits=[tok_rb])
                tok_rbs = P.tick("scalar", cACT)
                st["pb3_rd"] = tok_rbs
                P.op("vector", lambda e, iw=iw, dst_ap=dst_ap: e.tensor_tensor(
                    out=dst_ap, in0=PW.ap()[0:64, 0:iw], in1=rb64[0:64, 0:iw], op=ALU.mult),
                    waits=[tok_rbs])
                tok_e = P.tick("vector", cDVE)
                st["pw_rd"] = tok_e
                st["rb_rd"] = tok_e
                return tok_e

            st["rb_rd"] = None
            tok_oa_w = {}
            tok_vhalo = None
            windows = [1, 2, 3, 0]
            for w in windows:
                if w == 0:
                    for tcv in range(8):
                        P.op("gpsimd", lambda e, tcv=tcv, l=l: read_prev(
                            e, cc3_out[l][520 + 128 * tcv: 520 + 128 * (tcv + 1), 0:520],
                            c_v[:, 8 + tcv, :], R3 * 1040),
                            waits=[tok_cc3, st["pw_rd"]])
                        P.tick("gpsimd", cCCR, by=16)
                    tok_vhalo = (cCCR, cCCR.n)
                tok_kv = tok_prep_done if w != 0 else tok_halo_kv
                for h in range(HEADS):
                    fc, ro = h // 2, (h % 2) * 64
                    qw_full = qp_ap(fc)
                    for qi in range(4):
                        pw = [tok_kv]
                        if st["stats_rd"] is not None:
                            pw.append(st["stats_rd"])
                        qslice = qp_ap(fc)[ro:ro + 64, 512 * w + 128 * qi: 512 * w + 128 * qi + 128]
                        segs = stats_segs(w, qi, fc, ro)
                        b4ops, b5ops = [], []
                        col = 0
                        for sg in segs:
                            wid = sg.shape[-1]
                            while wid > 0:
                                n = min(wid, 512 - (col % 512)) if col < 512 else min(wid, 128)
                                n = min(n, 512 - col) if col < 512 else n
                                tgt = (b4ops if col < 512 else b5ops)
                                tgt.append((col, n, sg, sg.shape[-1] - wid))
                                col += n
                                wid -= n
                        ops4 = [
                            (lambda e, s, p, c=c, n=n, sg=sg, so=so, qslice=qslice:
                             e.matmul(PS640.ap()[:, c:c + n], qslice,
                                      _slice_cols(sg, so, n), start=s, stop=p))
                            for (c, n, sg, so) in b4ops]
                        if w == 0:
                            hw = 512 - 128 * qi
                            ops4.append(lambda e, s, p, hw=hw: e.matmul(
                                PS640.ap()[:, 0:hw], ones1_128[0:1, :], s0neg[0:1, 0:hw],
                                start=s, stop=p))
                        mm_bank_group(ops4, pw)
                        ops5 = [
                            (lambda e, s, p, c=c, n=n, sg=sg, so=so, qslice=qslice:
                             e.matmul(PS640.ap()[:, c:c + n], qslice,
                                      _slice_cols(sg, so, n), start=s, stop=p))
                            for (c, n, sg, so) in b5ops]
                        mm_bank_group(ops5, [])
                        tok_smm = P.tick("tensor", cPE)
                        P.op("vector", lambda e, qi=qi: e.tensor_reduce(
                            out=mst[:, qi:qi + 1], in_=PS640.ap()[:, 0:640], axis=AX.X,
                            op=ALU.max, negate=True), waits=[tok_smm])
                        st["stats_rd"] = P.tick("vector", cDVE)
                    mw = [st["stats_rd"]]
                    if st["mrow_rd"] is not None:
                        mw.append(st["mrow_rd"])
                    if st.get("pb3_rd") is not None:
                        mw.append(st["pb3_rd"])
                    for qi in range(4):
                        P.op("tensor", lambda e, qi=qi: e.transpose(
                            PB[3].ap()[0:1, 128 * qi: 128 * qi + 128], mst[:, qi:qi + 1],
                            identf[:, :]), waits=(mw if qi == 0 else ()))
                    tok_mtp = P.tick("tensor", cPE)
                    negw = [tok_mtp]
                    if st.get("negm_rd") is not None:
                        negw.append(st["negm_rd"])
                    P.op("scalar", lambda e: e.activation(
                        out=negm[0:1, 0:512], in_=PB[3].ap()[0:1, 0:512], func=AF.Identity),
                        waits=negw)
                    tok_negm = P.tick("scalar", cACT)
                    st["mrow_rd"] = tok_negm
                    for jc in range(8):
                        i0 = max(0, 128 * (jc - 4))
                        i1 = min(512, 128 * jc + 128)
                        iw = i1 - i0
                        pb = PB[jc % 2]
                        pw = [tok_negm, tok_kv]
                        if st["st_rd"][jc % 2] is not None:
                            pw.append(st["st_rd"][jc % 2])
                        kap = kjc_src(w, jc, fc, ro)
                        qap = qp_ap(fc)[ro:ro + 64, 512 * w + i0: 512 * w + i1]
                        ops = [lambda e, s, p, kap=kap, qap=qap, pb=pb, i0=i0, iw=iw:
                               e.matmul(pb.ap()[:, i0:i0 + iw], kap, qap, start=s, stop=p)]
                        for qi in range(i0 // 128, i1 // 128):
                            ops.append(lambda e, s, p, pb=pb, qi=qi: e.matmul(
                                pb.ap()[:, 128 * qi: 128 * qi + 128], ones1_128[0:1, :],
                                negm[0:1, 128 * qi: 128 * qi + 128], start=s, stop=p))
                        if jc <= 3:
                            qe = jc
                            if i0 <= 128 * qe < i1:
                                ops.append(lambda e, s, p, pb=pb, qe=qe: e.matmul(
                                    pb.ap()[:, 128 * qe: 128 * qe + 128], mask_d0[:, :],
                                    ident[:, :], start=s, stop=p))
                        else:
                            qe = jc - 4
                            if i0 <= 128 * qe < i1:
                                ops.append(lambda e, s, p, pb=pb, qe=qe: e.matmul(
                                    pb.ap()[:, 128 * qe: 128 * qe + 128], mask_d4[:, :],
                                    ident[:, :], start=s, stop=p))
                        if w == 0 and jc < 4:
                            ops.append(lambda e, s, p, pb=pb, i0=i0, iw=iw: e.matmul(
                                pb.ap()[:, i0:i0 + iw], ones1_128[0:1, :],
                                s0neg[0:1, 0:iw], start=s, stop=p))
                        mm_bank_group(ops, pw)
                        tok_stmm = P.tick("tensor", cPE)
                        st["negm_rd"] = tok_stmm
                        slot = jc % 3
                        pw2 = [tok_stmm]
                        if st["aT_rd"][slot] is not None:
                            pw2.append(st["aT_rd"][slot])
                        P.op("scalar", lambda e, pb=pb, i0=i0, iw=iw, slot=slot: e.activation(
                            out=c_aT[:, slot, i0:i0 + iw], in_=pb.ap()[:, i0:i0 + iw],
                            func=AF.Exp), waits=pw2)
                        tok_exp = P.tick("scalar", cACT)
                        st["st_rd"][jc % 2] = tok_exp
                        vt = vjc_src(w, jc)
                        pvw = [tok_exp]
                        if jc == 0 and st["pw_rd"] is not None:
                            pvw.append(st["pw_rd"])
                        if jc == 0 and w == 0:
                            pvw.append(tok_vhalo)
                        P.op("tensor", lambda e, vt=vt, h=h, i0=i0, iw=iw, slot=slot, jc=jc:
                             e.matmul(PW.ap()[:, i0:i0 + iw], c_v[:, vt, 65 * h: 65 * h + 65],
                                      c_aT[:, slot, i0:i0 + iw], start=(jc == 0),
                                      stop=(jc == 7)), waits=pvw)
                        st["aT_rd"][slot] = P.tick("tensor", cPE)
                    tok_pv = st["aT_rd"][7 % 3]
                    dw = [st["oa_wo"]] if st.get("oa_wo") else []
                    attn_evac(h, 512, c_oa[0:64, h, 0:512], tok_pv, st, extra=dw)
                tok_attnw = st["pw_rd"]
                ecw = []
                for ec in range(NG):
                    wobuf = ec % 2
                    wv = [tok_attnw]
                    if st.get(f"wo_rd{wobuf}"):
                        wv.append(st[f"wo_rd{wobuf}"])
                    twl = P.dma("gpsimd", c_wo[0:64, wobuf, :, :],
                                wo_r[l, :, :, 128 * ec: 128 * (ec + 1)], cWO[wobuf], waits=wv)
                    pb = PB[ec % 2]
                    pw = [twl]
                    if st.get(f"wops{ec % 2}"):
                        pw.append(st[f"wops{ec % 2}"])
                    ops = [(lambda e, s, p, h=h2, pb=pb, wobuf=wobuf: e.matmul(
                        pb.ap()[:, :], c_wo[0:64, wobuf, h, :], c_oa[0:64, h, 0:512],
                        start=s, stop=p)) for h2 in range(HEADS)]
                    mm_bank_group(ops, pw)
                    tok_womm = P.tick("tensor", cPE)
                    st[f"wo_rd{wobuf}"] = tok_womm
                    c0 = 2 + 512 * w
                    P.op("vector", lambda e, pb=pb, ec=ec, c0=c0: e.tensor_add(
                        out=vL[:, ec, c0:c0 + 512], in0=pb.ap()[:, :],
                        in1=vL[:, ec, c0:c0 + 512]), waits=[tok_womm] + dbg_guard())
                    st[f"wops{ec % 2}"] = P.tick("vector", cDVE)
                tok_oa_w[w] = st[f"wops{(NG - 1) % 2}"]
                st["oa_wo"] = st[f"wops{(NG - 1) % 2}"]

            if "4" not in sub:
                tok_u = tok_oa_w[0]
                continue
            t = chain("vector", lambda e: e.tensor_copy(
                out=apx(qfk[:, :, :], [[128, 4], [1, 1]]), in_=qbndt[:, :]),
                [tok_halo_kv, tok_oa_w[0]], cDVE)
            for fc in range(4):
                t = chain("vector", lambda e, fc=fc: e.tensor_copy(
                    out=qfk[:, fc, 1:128], in_=qp_ap(fc)[:, 0:127]), [t], cDVE)
            tok_qfk = t
            for h in range(HEADS):
                fc, ro = h // 2, (h % 2) * 64
                qap = qfk[ro:ro + 64, fc, :]
                pw = [tok_qfk]
                if st["stats_rd"] is not None:
                    pw.append(st["stats_rd"])
                mm_bank_group([
                    lambda e, s, p, fc=fc, ro=ro, qap=qap: e.matmul(
                        PS640.ap()[:, 0:512], qap, khalo_ap(fc)[ro:ro + 64, 384:896],
                        start=s, stop=p)], pw)
                mm_bank_group([
                    lambda e, s, p, fc=fc, ro=ro, qap=qap: e.matmul(
                        PS640.ap()[:, 512:640], qap, khalo_ap(fc)[ro:ro + 64, 896:1024],
                        start=s, stop=p)], [])
                tok_smm = P.tick("tensor", cPE)
                P.op("vector", lambda e: e.tensor_reduce(
                    out=mst[:, 0:1], in_=PS640.ap()[:, 0:640], axis=AX.X, op=ALU.max,
                    negate=True), waits=[tok_smm])
                st["stats_rd"] = P.tick("vector", cDVE)
                mw = [st["stats_rd"], st["mrow_rd"]]
                if st.get("pb3_rd") is not None:
                    mw.append(st["pb3_rd"])
                P.op("tensor", lambda e: e.transpose(
                    PB[3].ap()[0:1, 0:128], mst[:, 0:1], identf[:, :]), waits=mw)
                tok_mtp = P.tick("tensor", cPE)
                P.op("scalar", lambda e: e.activation(
                    out=negm[0:1, 0:128], in_=PB[3].ap()[0:1, 0:128], func=AF.Identity),
                    waits=[tok_mtp, st["negm_rd"]] if st.get("negm_rd") else [tok_mtp])
                tok_negm = P.tick("scalar", cACT)
                st["mrow_rd"] = tok_negm
                for jj, jc in enumerate(range(3, 8)):
                    pb = PB[jc % 2]
                    pw = [tok_negm]
                    if st["st_rd"][jc % 2] is not None:
                        pw.append(st["st_rd"][jc % 2])
                    ops = [lambda e, s, p, pb=pb, fc=fc, ro=ro, jc=jc, qap=qap: e.matmul(
                        pb.ap()[:, 0:128], khalo_ap(fc)[ro:ro + 64, 128 * jc: 128 * jc + 128],
                        qap, start=s, stop=p)]
                    ops.append(lambda e, s, p, pb=pb: e.matmul(
                        pb.ap()[:, 0:128], ones1_128[0:1, :], negm[0:1, 0:128],
                        start=s, stop=p))
                    if jc == 3:
                        ops.append(lambda e, s, p, pb=pb: e.matmul(
                            pb.ap()[:, 0:128], bndmask3[:, :], ident[:, :], start=s, stop=p))
                    mm_bank_group(ops, pw)
                    tok_stmm = P.tick("tensor", cPE)
                    st["negm_rd"] = tok_stmm
                    slot = jc % 3
                    pw2 = [tok_stmm]
                    if st["aT_rd"][slot] is not None:
                        pw2.append(st["aT_rd"][slot])
                    P.op("scalar", lambda e, pb=pb, slot=slot: e.activation(
                        out=c_aT[:, slot, 0:128], in_=pb.ap()[:, 0:128], func=AF.Exp),
                        waits=pw2)
                    tok_exp = P.tick("scalar", cACT)
                    st["st_rd"][jc % 2] = tok_exp
                    pvw = [tok_exp]
                    if jj == 0 and st["pw_rd"] is not None:
                        pvw.append(st["pw_rd"])
                    P.op("tensor", lambda e, jc=jc, h=h, slot=slot, jj=jj: e.matmul(
                        PW.ap()[:, 0:128], c_v[:, 8 + jc, 65 * h: 65 * h + 65],
                        c_aT[:, slot, 0:128], start=(jj == 0), stop=(jj == 4)), waits=pvw)
                    st["aT_rd"][slot] = P.tick("tensor", cPE)
                tok_pv = st["aT_rd"][7 % 3]
                attn_evac(h, 128, obnd[0:64, h, 0:128], tok_pv, st)
            tok_bnd = st["pw_rd"]
            for ec in range(NG):
                wobuf = ec % 2
                wv = [tok_bnd]
                if st.get(f"wo_rd{wobuf}"):
                    wv.append(st[f"wo_rd{wobuf}"])
                twl = P.dma("gpsimd", c_wo[0:64, wobuf, :, :],
                            wo_r[l, :, :, 128 * ec: 128 * (ec + 1)], cWO[wobuf], waits=wv)
                pb = PB[ec % 2]
                pw = [twl]
                if st.get(f"wops{ec % 2}"):
                    pw.append(st[f"wops{ec % 2}"])
                ops = [(lambda e, s, p, h=h2, pb=pb, wobuf=wobuf: e.matmul(
                    pb.ap()[:, 0:128], c_wo[0:64, wobuf, h, :], obnd[0:64, h, 0:128],
                    start=s, stop=p)) for h2 in range(HEADS)]
                mm_bank_group(ops, pw)
                tok_womm = P.tick("tensor", cPE)
                st[f"wo_rd{wobuf}"] = tok_womm
                P.op("vector", lambda e, pb=pb, ec=ec: e.scalar_tensor_tensor(
                    out=vL[:, ec, 1:2], in0=pb.ap()[:, 0:1], scalar=gate[:, 0:1],
                    in1=vL[:, ec, 1:2], op0=ALU.mult, op1=ALU.add), waits=[tok_womm] + dbg_guard())
                st[f"wops{ec % 2}"] = P.tick("vector", cDVE)
            tok_xmid = st[f"wops{(NG - 1) % 2}"]
            dump(f"xmid{l}", vL[:, :, 2:2050], 2048, [tok_xmid])

            if "5" not in sub:
                tok_u = tok_xmid
                continue
            def src_shift(g, c0, wdt):
                off = 0 if g < 4 else -1
                return vL[:, g, c0 + off: c0 + off + wdt]

            def apply_ln2(c0, wdt, tok_bcs, l=l):
                for g in range(NG):
                    off = 0 if g < 4 else -1
                    P.op("vector", lambda e, g=g, c0=c0, wdt=wdt, off=off: e.tensor_mul(
                        out=bc_t1[:, 0:wdt], in0=vL[:, g, c0 + off: c0 + off + wdt],
                        in1=bc_rb[:, 0:wdt]), waits=[tok_bcs])
                    P.op("vector", lambda e, g=g, c0=c0, wdt=wdt: e.tensor_sub(
                        out=scr1[0:128, 0:wdt], in0=bc_t1[:, 0:wdt], in1=bc_mb[:, 0:wdt]))
                    tok_s = P.tick("vector", cDVE)
                    P.op("scalar", lambda e, g=g, c0=c0, wdt=wdt, l=l: e.activation(
                        out=hln[:, g, c0 - 2: c0 - 2 + wdt], in_=scr1[0:128, 0:wdt],
                        func=AF.Identity, scale=ln_g[:, l, g:g + 1], bias=ln_b[:, l, g:g + 1]),
                        waits=[tok_s])
                P.op("vector", lambda e: e.engine_nop())

            tok_ln2 = layernorm(src_shift, tiles_own, apply_ln2, [tok_xmid])
            tok_hln = P.tick("scalar", cACT)

            stw = {"w1_rd": [None, None], "w2_rd": [None, None], "h1_rd": {}, "ps": {}}
            tok_u_upd = []
            for tt in range(4):
                tok_h1 = {}
                for hc in range(NHC):
                    wb = hc % 2
                    wv = ([stw["w1_rd"][wb]] if stw["w1_rd"][wb] else []) + [tok_hln]
                    ta = P.dma("gpsimd", c_w1[:, wb, 0, :, :],
                               w1r[l, :, :, 128 * hc: 128 * hc + 128], cW1[wb], waits=wv)
                    tg = P.dma("gpsimd", c_w1[:, wb, 1, :, :],
                               w1r[l, :, :, NHC * 128 + 128 * hc: NHC * 128 + 128 * hc + 128],
                               cW1[wb])
                    pba = PB[2 * (hc % 2)]
                    pbg = PB[2 * (hc % 2) + 1]
                    pw = [tg, tok_hln]
                    if stw["ps"].get(hc % 2):
                        pw.append(stw["ps"][hc % 2])
                    if hc < 2:
                        for b in (2 * (hc % 2), 2 * (hc % 2) + 1):
                            if stw.get(f"w2ps{b}"):
                                pw.append(stw[f"w2ps{b}"])
                    opsa = [(lambda e, s, p, g=g, wb=wb, pba=pba, tt=tt: e.matmul(
                        pba.ap()[:, :], c_w1[:, wb, 0, g, :],
                        hln[:, g, 512 * tt: 512 * tt + 512], start=s, stop=p))
                        for g in range(NG)]
                    mm_bank_group(opsa, pw)
                    opsg = [(lambda e, s, p, g=g, wb=wb, pbg=pbg, tt=tt: e.matmul(
                        pbg.ap()[:, :], c_w1[:, wb, 1, g, :],
                        hln[:, g, 512 * tt: 512 * tt + 512], start=s, stop=p))
                        for g in range(NG)]
                    mm_bank_group(opsg, [])
                    tok_w1mm = P.tick("tensor", cPE)
                    stw["w1_rd"][wb] = tok_w1mm
                    gw = [tok_w1mm]
                    if stw.get("gel_rd"):
                        gw.append(stw["gel_rd"])
                    P.op("scalar", lambda e, pbg=pbg, hc=hc, l=l: e.activation(
                        out=bc_t1[:, :], in_=pbg.ap()[:, :], func=AF.Gelu,
                        bias=b1[:, l, 1, hc:hc + 1]), waits=gw)
                    tok_gel = P.tick("scalar", cACT)
                    hw1 = [tok_gel]
                    if stw["h1_rd"].get(hc):
                        hw1.append(stw["h1_rd"][hc])
                    P.op("vector", lambda e, pba=pba, hc=hc, l=l: e.scalar_tensor_tensor(
                        out=h1t[:, hc, :], in0=pba.ap()[:, :], scalar=b1[:, l, 0, hc:hc + 1],
                        in1=bc_t1[:, :], op0=ALU.add, op1=ALU.mult), waits=hw1)
                    tok_h1[hc] = P.tick("vector", cDVE)
                    stw["gel_rd"] = tok_h1[hc]
                    stw["ps"][hc % 2] = tok_h1[hc]
                for ecg in range(2):
                    pw0 = [tok_h1[NHC - 1]]
                    for b in range(4):
                        if stw.get(f"w2ps{b}"):
                            pw0.append(stw[f"w2ps{b}"])
                    for hc in range(NHC):
                        wb2 = hc % 2
                        wv = [stw["w2_rd"][wb2]] if stw["w2_rd"][wb2] else []
                        tl = P.dma("gpsimd", c_w2[:, wb2, :],
                                   w2_d[l, 128 * hc: 128 * hc + 128,
                                        512 * ecg: 512 * ecg + 512], cW2[wb2],
                                   waits=wv + [tok_h1[hc], tok_hln])
                        for b in range(4):
                            P.op("tensor", lambda e, b=b, wb2=wb2, hc=hc:
                                 e.matmul(PB[b].ap()[:, :],
                                          c_w2[:, wb2, 128 * b: 128 * b + 128],
                                          h1t[:, hc, :], start=(hc == 0),
                                          stop=(hc == NHC - 1)),
                                 waits=([tl] + (pw0 if hc == 0 else [])) if b == 0 else ())
                        stw["w2_rd"][wb2] = P.tick("tensor", cPE)
                    tok_w2mm = stw["w2_rd"][(NHC - 1) % 2]
                    c0 = 2 + 512 * tt
                    for b in range(4):
                        ec = 4 * ecg + b
                        P.op("vector", lambda e, b=b, ec=ec, c0=c0, l=l:
                             e.scalar_tensor_tensor(
                                 out=vL[:, ec, c0:c0 + 512], in0=PB[b].ap()[:, :],
                                 scalar=b2[:, l, ec:ec + 1], in1=vL[:, ec, c0:c0 + 512],
                                 op0=ALU.add, op1=ALU.add),
                             waits=[tok_w2mm] + dbg_guard())
                        stw[f"w2ps{b}"] = P.tick("vector", cDVE)
                        tok_u_upd.append(stw[f"w2ps{b}"])
                for hc in range(NHC):
                    stw["h1_rd"][hc] = stw["w2_rd"][(NHC - 1) % 2]
            tok_layer = tok_u_upd[-1]
            dump(f"y{l}", vL[:, :, 2:2050], 2048, [tok_layer])

            if l == 0:
                tok_u = exchange_boundary(cc4_in, cc4_out, tok_layer)
            else:
                tok_u = tok_layer
            if l == 1:
                tok_x2r = P.dma("sync", x2r, x2_spill.rearrange("(g p) t -> p g t", p=128),
                                cLX, waits=[tok_spill, tok_layer])

    if "C" in phases and "B" in phases:
        def apply_final(c0, wdt, tok_bcs):
            for g in range(NG):
                P.op("vector", lambda e, g=g, c0=c0, wdt=wdt: e.tensor_mul(
                    out=bc_t1[:, 0:wdt], in0=vL[:, g, c0:c0 + wdt], in1=bc_rb[:, 0:wdt]),
                    waits=[tok_bcs, tok_x2r] + dbg_guard())
                P.op("vector", lambda e, g=g, c0=c0, wdt=wdt: e.tensor_sub(
                    out=scr1[0:128, 0:wdt], in0=bc_t1[:, 0:wdt], in1=bc_mb[:, 0:wdt]))
                P.op("vector", lambda e, g=g, c0=c0, wdt=wdt: e.tensor_add(
                    out=vL[:, g, c0:c0 + wdt], in0=scr1[0:128, 0:wdt],
                    in1=x2r[:, g, c0 - 2: c0 - 2 + wdt]))

        tok_fin = layernorm(lambda g, c0, wdt: vL[:, g, c0:c0 + wdt], tiles_own,
                            apply_final, [tok_u])
        tok_y = P.dma("sync", y_out.rearrange("(g p) t -> p g t", p=128),
                      vL[:, :, 2:2050], cST, waits=[tok_fin])
        dump_toks.append(tok_y)
    else:
        tok_y = P.dma("sync", y_out.rearrange("(g p) t -> p g t", p=128),
                      vL[:, :, 2:2050], cST,
                      waits=[tok_u if "B" in phases else (tok_halo if "A" in phases else tok_x)])
        dump_toks.append(tok_y)

    P.op("sync", lambda e: e.nop(), waits=dump_toks + [(cST, cST.n)])
    P.emit()
    es.close()
    return nc


def kernel(**inputs):
    in_maps = stage_inputs(inputs)
    nc = build_program(debug=DEBUG, phases=PHASES)
    res = run_bass_kernel_spmd(nc, in_maps, core_ids=list(range(8)))
    out = np.zeros((2, SEQ, D), np.float32)
    for c in range(8):
        b, s = c // 4, c % 4
        y = np.asarray(res.results[c]["y"], np.float32)
        out[b, TOK * s: TOK * (s + 1), :] = y.T
    return out


# version 07
# speedup vs baseline: 1.4536x; 1.0834x over previous
import dataclasses
import numpy as np
import ml_dtypes
import concourse.bass as bass
import concourse.mybir as mybir
from concourse.bass_utils import run_bass_kernel_spmd
from contextlib import ExitStack

F32 = mybir.dt.float32
BF16 = mybir.dt.bfloat16
I32 = mybir.dt.int32
U8 = mybir.dt.uint8
ALU = mybir.AluOpType
AF = mybir.ActivationFunctionType
DR = mybir.MatmulPerfMode.DoubleRow
AX = mybir.AxisListType
BF = ml_dtypes.bfloat16

D = 1024
NG = 8
TOK = 2048
COLS = 2052
SEQ = 8192
HEADS = 8
DH = 64
WIN = 512
INNER = 512
HID = 2730
NHC = 22
HC_LAST = 42
QK_SCALE = 8.0
NEG = -1e30
LNEPS = 1e-5

DEBUG = []
PHASES = "ABC"
ALIAS_CHANNELS = False


class Chan:
    __slots__ = ("sem", "n", "name")
    def __init__(self, sem, name):
        self.sem = sem
        self.n = 0
        self.name = name


class Prog:
    ENGINES = ("tensor", "vector", "scalar", "gpsimd", "sync")

    def __init__(self, nc, es):
        self.nc = nc
        self.es = es
        self.ops = {e: [] for e in self.ENGINES}
        self.dumps = []

    def chan(self, name):
        sem = self.es.enter_context(self.nc.semaphore(name))
        return Chan(sem, name)

    def op(self, engine, fn, waits=()):
        self.ops[engine].append([tuple(waits), fn, None, 0])

    def tick(self, engine, ch, by=1):
        rec = self.ops[engine][-1]
        assert rec[2] is None, "op already ticked"
        rec[2] = ch
        rec[3] = by
        ch.n += by
        return (ch, ch.n)

    def dma(self, engine, out, in_, ch, waits=(), **kw):
        self.op(engine, lambda e, out=out, in_=in_, kw=kw: e.dma_start(out=out, in_=in_, **kw),
                waits=waits)
        return self.tick(engine, ch, by=16)

    def emit(self):
        block = self.es.enter_context(self.nc.Block())
        for ename in self.ENGINES:
            ops = self.ops[ename]
            if not ops:
                continue

            def body(eng, ops=ops):
                last_wait = {}
                for waits, fn, inc_ch, inc_by in ops:
                    for (ch, cnt) in waits:
                        if cnt > last_wait.get(ch.name, -1):
                            eng.wait_ge(ch.sem, cnt)
                            last_wait[ch.name] = cnt
                    ins = fn(eng)
                    if inc_ch is not None:
                        ins.then_inc(inc_ch.sem, inc_by)
            getattr(block, ename)(body)


class Arena:
    def __init__(self, nc, name, nbytes):
        nbytes = (nbytes + 63) & ~63
        self.h = nc.alloc_sbuf_tensor(name, [128, nbytes], U8)
        self.nbytes = nbytes

    @staticmethod
    def _shape(ap, shape):
        if len(shape) <= 1:
            return ap
        names = "abcdefg"[: len(shape)]
        spec = "p (" + " ".join(names) + ") -> p " + " ".join(names)
        kw = {names[i]: shape[i] for i in range(1, len(shape))}
        return ap.rearrange(spec, **kw)

    def f32(self, off, *shape):
        n = int(np.prod(shape))
        assert off % 64 == 0 and off + 4 * n <= self.nbytes, (off, shape, self.nbytes)
        return self._shape(self.h.ap().bitcast(F32)[:, off // 4: off // 4 + n], shape)

    def bf16(self, off, *shape):
        n = int(np.prod(shape))
        assert off % 64 == 0 and off + 2 * n <= self.nbytes, (off, shape, self.nbytes)
        return self._shape(self.h.ap().bitcast(BF16)[:, off // 2: off // 2 + n], shape)


def _rot_tables():
    half = np.arange(0, DH, 2, dtype=np.float64)
    inv_freq = 1.0 / 10000.0 ** (half / DH)
    t = np.arange(SEQ, dtype=np.float64)
    fr = t[:, None] * inv_freq[None, :]
    fr = np.concatenate([fr, fr], axis=-1)
    base = (half + 0.4 * DH) / (1.4 * DH)
    power = (t - SEQ // 2) / float(WIN // 2)
    scale = np.concatenate([base[None, :] ** power[:, None]] * 2, axis=-1)
    cos = np.cos(fr)
    sin = np.sin(fr)
    sign = np.concatenate([-np.ones(DH // 2), np.ones(DH // 2)])[None, :]
    Aq = (cos * scale).astype(np.float32)
    Bq = (sign * sin * scale).astype(np.float32)
    Ak = (cos / scale).astype(np.float32)
    Bk = (sign * sin / scale).astype(np.float32)
    return Aq, Bq, Ak, Bk


def _edge_masks():
    i = np.arange(128)[:, None]
    j = np.arange(128)[None, :]
    d0 = np.where(j < i, NEG, 0.0).astype(BF)
    d4 = np.where(j > i, NEG, 0.0).astype(BF)
    return d0, d4


def _w0_masks(is_s0):
    i = np.arange(128)[:, None]
    j = np.arange(128)[None, :]
    tiles = np.zeros((4, 5, 128, 128), np.float32)
    for qi in range(4):
        for bs in range(5):
            jc = qi + bs
            if bs == 0:
                m = np.where(j < i, NEG, 0.0)
            elif bs == 4:
                m = np.where(j > i, NEG, 0.0)
            else:
                m = np.zeros((128, 128), np.float32)
            if is_s0 and jc < 4:
                m = np.full((128, 128), NEG, np.float32)
            tiles[qi, bs] = m
    return tiles.astype(BF)


def stage_inputs(inputs):
    f = np.float32
    x = np.ascontiguousarray(inputs["L0_attn_x"], dtype=f)
    Wt = np.ascontiguousarray(np.asarray(inputs["L0_attn_W"], dtype=f).T)
    dec = np.ascontiguousarray(np.asarray(inputs["L0_attn_decay_logit"], f).reshape(NG, 128).T)
    bias_b = np.ascontiguousarray(np.asarray(inputs["L0_attn_b"], f).reshape(NG, 128).T)
    conv_b = np.ascontiguousarray(np.asarray(inputs["L0_attn_conv_b"], f).reshape(NG, 128).T)
    conv_w = np.ascontiguousarray(
        np.asarray(inputs["L0_attn_conv_w"], f).reshape(NG, 128, 5).transpose(1, 0, 2))
    wqkv = np.ascontiguousarray(np.asarray(inputs["L0_ffn_wqkv"], f))
    wo = np.ascontiguousarray(np.asarray(inputs["L0_ffn_wo"], f))
    w1p = np.zeros((2, D, 2 * NHC * 128), f)
    w1p[:, :, :HID] = np.asarray(inputs["L0_ffn_w1"], f)[:, :, :HID]
    w1p[:, :, NHC * 128: NHC * 128 + HID] = np.asarray(inputs["L0_ffn_w1"], f)[:, :, HID:]
    w2p = np.zeros((2, NHC * 128, 1024), f)
    w2p[:, :HID] = np.asarray(inputs["L0_ffn_w2"], f)
    qsr = np.ascontiguousarray(np.repeat(np.asarray(inputs["L0_ffn_q_scale"], f)[:, None, :], 128, 1))
    ksr = np.ascontiguousarray(np.repeat(np.asarray(inputs["L0_ffn_k_scale"], f)[:, None, :], 128, 1))
    ln_g = np.ascontiguousarray(np.asarray(inputs["L0_ffn_ln_g"], f).reshape(2, NG, 128).transpose(2, 0, 1))
    ln_b = np.ascontiguousarray(np.asarray(inputs["L0_ffn_ln_b"], f).reshape(2, NG, 128).transpose(2, 0, 1))
    b1p = np.zeros((2, 2, NHC * 128), f)
    b1p[:, 0, :HID] = np.asarray(inputs["L0_ffn_b1"], f)[:, :HID]
    b1p[:, 1, :HID] = np.asarray(inputs["L0_ffn_b1"], f)[:, HID:]
    b1s = np.ascontiguousarray(b1p.reshape(2, 2, NHC, 128).transpose(3, 0, 1, 2))
    b2s = np.ascontiguousarray(np.asarray(inputs["L0_ffn_b2"], f).reshape(2, NG, 128).transpose(2, 0, 1))

    Aq, Bq, Ak, Bk = _rot_tables()
    d0, d4 = _edge_masks()
    shared = dict(
        Wt=Wt, dec=dec, bias_b=bias_b, conv_b=conv_b, conv_w=conv_w,
        wqkv=wqkv, wo=wo, w1=w1p, w2=w2p, qsr=qsr, ksr=ksr,
        qsr_shuf=np.ascontiguousarray(qsr[:, :, [32 + i for i in range(32)] + list(range(32))]),
        ksr_shuf=np.ascontiguousarray(ksr[:, :, [32 + i for i in range(32)] + list(range(32))]),
        identf=np.eye(128).astype(f),
        bndmask3=np.ascontiguousarray(np.broadcast_to(
            np.where(np.arange(128)[None, :] < 127, NEG, 0.0).astype(BF), (128, 128))),
        ln_g=ln_g, ln_b=ln_b, b1=b1s, b2=b2s,
        mask_d0=np.ascontiguousarray(d0), mask_d4=np.ascontiguousarray(d4),
        ident=np.eye(128).astype(BF),
        ones_col=np.ones((128, 1), BF),
        ones1_128=np.ones((1, 128), BF),
        ones1_64=np.ones((1, 64), BF),
        on64_d=np.ones((65, 64), BF),
    )

    in_maps = []
    for c in range(8):
        b, s = c // 4, c % 4
        t0 = TOK * s
        xs = np.zeros((D, COLS), f)
        lo, hi = t0 - 2, t0 + TOK + 2
        slo, shi = max(lo, 0), min(hi, SEQ)
        xs[:, slo - lo: shi - lo] = x[b, slo:shi, :].T
        xs = np.ascontiguousarray(xs)

        def tt(tab):
            return np.ascontiguousarray(
                tab[t0:t0 + TOK].reshape(16, 128, DH).transpose(1, 0, 2))
        m_sel = np.zeros((128, 4), f)
        e_sel = np.zeros((128, 4), f)
        for r in range(4):
            if r < s:
                m_sel[:, r] = 1.0
                e_sel[:, r] = float(s - 1 - r)
        gate = np.full((128, 1), 0.0 if s == 0 else 1.0, f)
        m = dict(shared)
        m.update(
            x_in=xs,
            tabAq=tt(Aq), tabBq=tt(Bq), tabAk=tt(Ak), tabBk=tt(Bk),
            m_sel=m_sel, e_sel=e_sel, gate=gate,
            s0neg=np.full((1, 512), NEG if s == 0 else 0.0, BF),
            prev_off=np.array([[(s - 1) % 4]], np.int32),
        )
        in_maps.append(m)
    return in_maps


def build_program(debug=(), phases="ABC"):
    nc = bass.Bass("TRN2", target_bir_lowering=False)
    es = ExitStack()
    P = Prog(nc, es)

    def din(name, shape, dt=F32):
        return nc.dram_tensor(name, shape, dt, kind="ExternalInput").ap()

    x_in = din("x_in", [D, COLS])
    Wt_d = din("Wt", [D, D])
    dec_d = din("dec", [128, NG])
    bias_b_d = din("bias_b", [128, NG])
    conv_b_d = din("conv_b", [128, NG])
    conv_w_d = din("conv_w", [128, NG, 5])
    wqkv_d = din("wqkv", [2, D, 3 * INNER])
    wo_d = din("wo", [2, INNER, D])
    w1_d = din("w1", [2, D, 2 * NHC * 128])
    w2_d = din("w2", [2, NHC * 128, D])
    qsr_d = din("qsr", [2, 128, DH])
    ksr_d = din("ksr", [2, 128, DH])
    ln_g_d = din("ln_g", [128, 2, NG])
    ln_b_d = din("ln_b", [128, 2, NG])
    b1_d = din("b1", [128, 2, 2, NHC])
    b2_d = din("b2", [128, 2, NG])
    qsr_shuf_d = din("qsr_shuf", [2, 128, DH])
    ksr_shuf_d = din("ksr_shuf", [2, 128, DH])
    identf_d = din("identf", [128, 128])
    bndmask3_d = din("bndmask3", [128, 128], BF16)
    s0neg_d = din("s0neg", [1, 512], BF16)
    mask_d0_d = din("mask_d0", [128, 128], BF16)
    mask_d4_d = din("mask_d4", [128, 128], BF16)
    ident_d = din("ident", [128, 128], BF16)
    ones_col_d = din("ones_col", [128, 1], BF16)
    ones1_128_d = din("ones1_128", [1, 128], BF16)
    ones1_64_d = din("ones1_64", [1, 64], BF16)
    on64_d = din("on64_d", [65, 64], BF16)
    tabAq_d = din("tabAq", [128, 16, DH])
    tabBq_d = din("tabBq", [128, 16, DH])
    tabAk_d = din("tabAk", [128, 16, DH])
    tabBk_d = din("tabBk", [128, 16, DH])
    m_sel_d = din("m_sel", [128, 4])
    e_sel_d = din("e_sel", [128, 4])
    gate_d = din("gate", [128, 1])
    prev_off_d = din("prev_off", [1, 1], I32)

    y_out = nc.dram_tensor("y", [D, TOK], F32, kind="ExternalOutput").ap()

    x2_spill = nc.dram_tensor("x2_spill", [D, TOK], F32, kind="Internal").ap()
    cc1_in = nc.dram_tensor("cc1_in", [128, NG], F32, kind="Internal").ap()
    cc1_out = nc.dram_tensor("cc1_out", [4 * 128, NG], F32, kind="Internal").ap()
    cc2_in = nc.dram_tensor("cc2_in", [128, NG], F32, kind="Internal").ap()
    cc2_out = nc.dram_tensor("cc2_out", [4 * 128, NG], F32, kind="Internal").ap()
    R3 = 1552
    cc3_in = [nc.dram_tensor(f"cc3_in{l}", [R3, 1040], BF16, kind="Internal").ap() for l in range(2)]
    cc3_out = [nc.dram_tensor(f"cc3_out{l}", [4 * R3, 1040], BF16, kind="Internal").ap() for l in range(2)]
    cc4_in = nc.dram_tensor("cc4_in", [128, NG], F32, kind="Internal").ap()
    cc4_out = nc.dram_tensor("cc4_out", [4 * 128, NG], F32, kind="Internal").ap()
    RG = [[0, 1, 2, 3], [4, 5, 6, 7]]

    AR_S = Arena(nc, "arS", NG * COLS * 4)
    AR_L = Arena(nc, "arL", NG * COLS * 4)
    AR_C = Arena(nc, "arC", 50496)

    def sb(name, shape, dt=F32):
        return es.enter_context(nc.sbuf_tensor(name, shape, dt))

    ident = sb("t_ident", [128, 128], BF16)
    mask_d0 = sb("t_mask_d0", [128, 128], BF16)
    mask_d4 = sb("t_mask_d4", [128, 128], BF16)
    ones_col = sb("t_ones_col", [128, 1], BF16)
    ones1_128 = sb("t_ones1_128", [1, 128], BF16)
    ones1_64 = sb("t_ones1_64", [1, 64], BF16)
    dec = sb("t_dec", [128, NG])
    smalla = sb("smalla", [128, NG])
    smallb = sb("smallb", [128, NG])
    a_col = sb("a_col", [128, NG])
    om_a = sb("om_a", [128, NG])
    a2k = sb("a2k", [128, NG])
    lnA = sb("lnA", [128, NG])
    coef = sb("coef", [128, 4, NG])
    bcb = sb("bcb", [128, NG])
    conv_w = sb("t_conv_w", [128, NG, 5])
    qsr = sb("t_qsr", [128, 2, DH])
    ksr = sb("t_ksr", [128, 2, DH])
    ln_g = sb("t_ln_g", [128, 2, NG])
    ln_b = sb("t_ln_b", [128, 2, NG])
    b1 = sb("t_b1", [128, 2, 2, NHC])
    b2 = sb("t_b2", [128, 2, NG])
    m_sel = sb("t_m_sel", [128, 4])
    e_sel = sb("t_e_sel", [128, 4])
    gate = sb("t_gate", [128, 1])
    poff = sb("poff", [1, 1], I32)
    hb1 = sb("hb1", [128, NG])
    hb2 = sb("hb2", [128, NG])
    wc_all = sb("wc_all", [128, 4, NG])
    sinit = sb("sinit", [128, NG])
    qsr_shuf = sb("t_qsr_shuf", [128, 2, DH])
    ksr_shuf = sb("t_ksr_shuf", [128, 2, DH])
    identf = sb("t_identf", [128, 128])
    bndmask3 = sb("t_bndmask3", [128, 128], BF16)
    s0neg = sb("t_s0neg", [1, 512], BF16)
    negm = sb("negm", [1, 512], BF16)
    r64 = sb("r64", [65, 1024])
    r64h = sb("r64h", [65, 1024], BF16)
    on64 = sb("on64", [65, 64], BF16)
    rb64 = sb("rb64", [64, 512])
    mst = sb("mst", [128, 4])
    qbndt = sb("qbndt", [128, 4], BF16)
    qfk = sb("qfk", [128, 4, 128], BF16)

    nrm = sb("nrm", [128, 8])
    nrm2 = sb("nrm2", [128, 8])
    rqv = sb("rqv", [128, 8])
    qbf = AR_C.bf16(34112, 2, 512)
    ub = [AR_C.bf16(40256, 8, 129), AR_C.bf16(44352, 8, 129)]
    r_mu = sb("r_mu", [1, 512])
    r_var = sb("r_var", [1, 512])
    r_rstd = sb("r_rstd", [1, 512])
    r_mrs = sb("r_mrs", [1, 512])
    rb_hi = sb("rb_hi", [1, 512], BF16)
    rb_lo = sb("rb_lo", [1, 512], BF16)
    mb_hi = sb("mb_hi", [1, 512], BF16)
    mb_lo = sb("mb_lo", [1, 512], BF16)


    PB = [es.enter_context(nc.psum_tensor(f"PB{i}", [128, 512], F32)) for i in range(4)]
    PS640 = es.enter_context(nc.psum_tensor("PS640", [128, 640], F32))
    PW = es.enter_context(nc.psum_tensor("PW", [65, 512], F32))
    PT = es.enter_context(nc.psum_tensor("PT", [128, 512], BF16))

    cLD = P.chan("cLD")
    cSM = P.chan("cSM")
    cLX = P.chan("cLX")
    cSP = P.chan("cSP")
    cTB = [P.chan("cTB0"), P.chan("cTB1")]
    cCCP = P.chan("cCCP")
    cCCR = P.chan("cCCR")
    cWQ = [P.chan("cWQ0"), P.chan("cWQ1")]
    cWT = [P.chan("cWT0"), P.chan("cWT1")]
    cWO = [P.chan("cWO0"), P.chan("cWO1")]
    cW1 = [P.chan("cW10"), P.chan("cW11")]
    cW2 = [P.chan("cW20"), P.chan("cW21")]
    cLG = P.chan("cLG")
    cST = P.chan("cST")
    if ALIAS_CHANNELS:
        cSM = cLX = cSP = cCCP = cCCR = cLD
        cTB = [cLD, cLD]
        cWQ = [cLG, cLG]
        cWT = [cLD, cLD]
        cWO = [cLG, cLG]
        cW1 = [cLG, cLG]
        cW2 = [cLG, cLG]
    cPE = P.chan("cPE")
    cDVE = P.chan("cDVE")
    cACT = P.chan("cACT")
    cPOOL = P.chan("cPOOL")
    cCC = P.chan("cCC")

    dump_toks = []

    def dbg_guard():
        return [(cST, cST.n)] if debug else []

    def dump(name, view3d, ncols, waits):
        if name not in debug:
            return
        dd = nc.dram_tensor("dbg_" + name, [D, ncols], F32, kind="ExternalOutput").ap()
        t = P.dma("sync", dd.rearrange("(g p) t -> p g t", p=128), view3d, cST, waits=waits)
        dump_toks.append(t)

    vS = AR_S.f32(0, NG, COLS)
    vL = AR_L.f32(0, NG, COLS)

    tok_x = P.dma("sync", vS, x_in.rearrange("(g p) t -> p g t", p=128), cLX)
    small_loads = [
        (dec, dec_d), (bcb, bias_b_d), (conv_w, conv_w_d),
        (qsr, qsr_d.rearrange("l p d -> p l d")), (ksr, ksr_d.rearrange("l p d -> p l d")),
        (ln_g, ln_g_d), (ln_b, ln_b_d), (b1, b1_d), (b2, b2_d),
        (m_sel, m_sel_d), (e_sel, e_sel_d), (gate, gate_d), (poff, prev_off_d),
        (smalla, conv_b_d),
        (ident, ident_d), (mask_d0, mask_d0_d), (mask_d4, mask_d4_d),
        (qsr_shuf, qsr_shuf_d.rearrange("l p d -> p l d")),
        (ksr_shuf, ksr_shuf_d.rearrange("l p d -> p l d")),
        (identf, identf_d), (bndmask3, bndmask3_d), (s0neg, s0neg_d),
        (ones_col, ones_col_d), (ones1_128, ones1_128_d), (ones1_64, ones1_64_d),
    ]
    for dst, src in small_loads:
        tok_small = P.dma("sync", dst[:], src, cSM)
    tok_on64 = P.dma("sync", on64[:, :], on64_d, cSM)
    tok_small = tok_on64
    P.op("vector", lambda e: e.tensor_add(out=bcb[:], in0=bcb[:], in1=smalla[:]),
         waits=[tok_small])
    tok_bcb = P.tick("vector", cDVE)

    def chain(engine, fn, waits, ch):
        P.op(engine, fn, waits=waits)
        return P.tick(engine, ch)

    t = chain("scalar", lambda e: e.activation(out=smalla[:], in_=dec[:], func=AF.Exp,
                                               scale=-1.0), [tok_small, tok_bcb], cACT)
    t = chain("vector", lambda e: e.tensor_scalar(out=smalla[:], in0=smalla[:], scalar1=1.0,
                                                  scalar2=None, op0=ALU.add), [t], cDVE)
    t = chain("vector", lambda e: e.reciprocal(out=a_col[:], in_=smalla[:]), [t], cDVE)
    tok_a = t
    t = chain("vector", lambda e: e.tensor_scalar(out=om_a[:], in0=a_col[:], scalar1=-1.0,
                                                  scalar2=-1.0, op0=ALU.mult,
                                                  op1=ALU.subtract), [t], cDVE)
    tok_oma = t
    cur, oth = a_col, None
    t = chain("vector", lambda e: e.tensor_mul(out=smalla[:], in0=a_col[:], in1=a_col[:]),
              [t], cDVE)
    cur, oth = smalla, smallb
    for i in range(10):
        t = chain("vector", lambda e, cur=cur, oth=oth: e.tensor_mul(
            out=oth[:], in0=cur[:], in1=cur[:]), [t], cDVE)
        cur, oth = oth, cur
    t = chain("vector", lambda e, cur=cur: e.tensor_scalar_max(out=a2k[:], in0=cur[:],
                                                               scalar1=1e-30), [t], cDVE)
    t = chain("scalar", lambda e: e.activation(out=lnA[:], in_=a2k[:], func=AF.Ln), [t], cACT)
    for r in range(4):
        t = chain("vector", lambda e, r=r: e.tensor_scalar(
            out=coef[:, r, :], in0=lnA[:], scalar1=e_sel[:, r:r + 1], scalar2=None,
            op0=ALU.mult), [t], cDVE)
    t = chain("scalar", lambda e: e.activation(out=coef[:], in_=coef[:], func=AF.Exp), [t], cACT)
    for r in range(4):
        t = chain("vector", lambda e, r=r: e.tensor_scalar(
            out=coef[:, r, :], in0=coef[:, r, :], scalar1=m_sel[:, r:r + 1], scalar2=None,
            op0=ALU.mult), [t], cDVE)
    tok_coef = t

    stgA = AR_C.bf16(16384, NG, 512)
    stgB = AR_C.bf16(24768, NG, 512)
    scr1 = AR_C.f32(0, 2048)
    scr2 = AR_C.f32(8192, 2048)
    bc_rb = AR_C.f32(44352, 512)
    bc_mb = AR_C.f32(46400, 512)
    bc_t1 = AR_C.f32(48448, 512)
    pr_s1 = AR_C.f32(16704, 512)
    pr_s2 = AR_C.f32(18752, 512)
    pr_s3 = AR_C.f32(20800, 512)

    ln_state = {"mm": None, "apply": None, "rows": None}

    def layernorm(src3, tiles, apply_ops, waits_src):
        for (c0, w) in tiles:
            cast_waits = list(waits_src)
            if ln_state["mm"] is not None:
                cast_waits.append(ln_state["mm"])
            for g in range(NG):
                P.op("scalar", lambda e, g=g, c0=c0, w=w: e.activation(
                    out=stgA[:, g, 0:w], in_=src3(g, c0, w), func=AF.Identity),
                    waits=cast_waits)
                P.op("scalar", lambda e, g=g, c0=c0, w=w: e.activation(
                    out=stgB[:, g, 0:w], in_=src3(g, c0, w), func=AF.Square))
            tok_cast = P.tick("scalar", cACT)
            mm_waits = [tok_cast]
            if ln_state["rows"] is not None:
                mm_waits.append(ln_state["rows"])
            for g in range(NG):
                P.op("tensor", lambda e, g=g, w=w: e.matmul(
                    PB[0].ap()[0:1, 0:w], ones_col[:, 0:1], stgA[:, g, 0:w],
                    start=(g == 0), stop=(g == NG - 1)), waits=mm_waits)
            for g in range(NG):
                P.op("tensor", lambda e, g=g, w=w: e.matmul(
                    PB[1].ap()[0:1, 0:w], ones_col[:, 0:1], stgB[:, g, 0:w],
                    start=(g == 0), stop=(g == NG - 1)))
            tok_mm = P.tick("tensor", cPE)
            ln_state["mm"] = tok_mm
            P.op("vector", lambda e, w=w: e.tensor_scalar(
                out=r_mu[0:1, 0:w], in0=PB[0].ap()[0:1, 0:w], scalar1=1.0 / D, scalar2=None,
                op0=ALU.mult), waits=[tok_mm])
            P.op("vector", lambda e, w=w: e.tensor_scalar(
                out=r_var[0:1, 0:w], in0=PB[1].ap()[0:1, 0:w], scalar1=1.0 / D, scalar2=None,
                op0=ALU.mult))
            tok_rows = P.tick("vector", cDVE)
            ln_state["rows"] = tok_rows
            P.op("vector", lambda e, w=w: e.tensor_mul(
                out=r_mrs[0:1, 0:w], in0=r_mu[0:1, 0:w], in1=r_mu[0:1, 0:w]))
            P.op("vector", lambda e, w=w: e.scalar_tensor_tensor(
                out=r_var[0:1, 0:w], in0=r_var[0:1, 0:w], scalar=LNEPS, in1=r_mrs[0:1, 0:w],
                op0=ALU.add, op1=ALU.subtract))
            tok_var = P.tick("vector", cDVE)
            P.op("scalar", lambda e, w=w: e.activation(
                out=r_mrs[0:1, 0:w], in_=r_var[0:1, 0:w], func=AF.Ln), waits=[tok_var])
            P.op("scalar", lambda e, w=w: e.activation(
                out=r_rstd[0:1, 0:w], in_=r_mrs[0:1, 0:w], func=AF.Exp, scale=-0.5))
            tok_rstd = P.tick("scalar", cACT)
            P.op("vector", lambda e, w=w: e.tensor_mul(
                out=r_mrs[0:1, 0:w], in0=r_mu[0:1, 0:w], in1=r_rstd[0:1, 0:w]), waits=[tok_rstd])
            tok_mrs = P.tick("vector", cDVE)
            P.op("scalar", lambda e, w=w: e.activation(
                out=rb_hi[0:1, 0:w], in_=r_rstd[0:1, 0:w], func=AF.Identity), waits=[tok_mrs])
            P.op("scalar", lambda e, w=w: e.activation(
                out=mb_hi[0:1, 0:w], in_=r_mrs[0:1, 0:w], func=AF.Identity))
            P.op("scalar", lambda e, w=w: e.activation(
                out=r_mu[0:1, 0:w], in_=rb_hi[0:1, 0:w], func=AF.Identity))
            tok_hi = P.tick("scalar", cACT)
            P.op("vector", lambda e, w=w: e.tensor_sub(
                out=r_mu[0:1, 0:w], in0=r_rstd[0:1, 0:w], in1=r_mu[0:1, 0:w]), waits=[tok_hi])
            tok_lof1 = P.tick("vector", cDVE)
            P.op("scalar", lambda e, w=w: e.activation(
                out=rb_lo[0:1, 0:w], in_=r_mu[0:1, 0:w], func=AF.Identity), waits=[tok_lof1])
            P.op("scalar", lambda e, w=w: e.activation(
                out=r_mu[0:1, 0:w], in_=mb_hi[0:1, 0:w], func=AF.Identity))
            tok_hi2 = P.tick("scalar", cACT)
            P.op("vector", lambda e, w=w: e.tensor_sub(
                out=r_mu[0:1, 0:w], in0=r_mrs[0:1, 0:w], in1=r_mu[0:1, 0:w]), waits=[tok_hi2])
            tok_lof2 = P.tick("vector", cDVE)
            P.op("scalar", lambda e, w=w: e.activation(
                out=mb_lo[0:1, 0:w], in_=r_mu[0:1, 0:w], func=AF.Identity), waits=[tok_lof2])
            tok_lo = P.tick("scalar", cACT)
            bc_waits = [tok_lo]
            if ln_state["apply"] is not None:
                bc_waits.append(ln_state["apply"])
            P.op("tensor", lambda e, w=w: e.matmul(
                PB[2].ap()[:, 0:w], ones1_128[0:1, :], rb_hi[0:1, 0:w], start=True, stop=False),
                waits=bc_waits)
            P.op("tensor", lambda e, w=w: e.matmul(
                PB[2].ap()[:, 0:w], ones1_128[0:1, :], rb_lo[0:1, 0:w], start=False, stop=True))
            P.op("tensor", lambda e, w=w: e.matmul(
                PB[3].ap()[:, 0:w], ones1_128[0:1, :], mb_hi[0:1, 0:w], start=True, stop=False))
            P.op("tensor", lambda e, w=w: e.matmul(
                PB[3].ap()[:, 0:w], ones1_128[0:1, :], mb_lo[0:1, 0:w], start=False, stop=True))
            tok_bc = P.tick("tensor", cPE)
            P.op("scalar", lambda e, w=w: e.activation(
                out=bc_rb[:, 0:w], in_=PB[2].ap()[:, 0:w], func=AF.Identity), waits=[tok_bc])
            P.op("scalar", lambda e, w=w: e.activation(
                out=bc_mb[:, 0:w], in_=PB[3].ap()[:, 0:w], func=AF.Identity))
            tok_bcs = P.tick("scalar", cACT)
            apply_ops(c0, w, tok_bcs)
            ln_state["apply"] = P.tick("vector", cDVE)
        return ln_state["apply"]

    tok_attn_out_done = None
    tok_u0 = None
    tok_spill = None
    if "A" in phases:
        tiles_full = [(0, 512), (512, 512), (1024, 512), (1536, 512), (1540, 512)]

        def apply_ln1(c0, w, tok_bcs):
            for g in range(NG):
                P.op("vector", lambda e, g=g, c0=c0, w=w: e.tensor_mul(
                    out=bc_t1[:, 0:w], in0=vS[:, g, c0:c0 + w], in1=bc_rb[:, 0:w]),
                    waits=[tok_bcs])
                P.op("vector", lambda e, g=g, c0=c0, w=w: e.tensor_sub(
                    out=vL[:, g, c0:c0 + w], in0=bc_t1[:, 0:w], in1=bc_mb[:, 0:w]))

        tok_ln1 = layernorm(lambda g, c0, w: vS[:, g, c0:c0 + w], tiles_full, apply_ln1, [tok_x])
        dump("ln1", vL[:, :, 2:2050], 2048, [tok_ln1])
        if "bcdbg" in debug:
            dbg1 = nc.dram_tensor("dbg_bcdbg", [128, 1024], F32, kind="ExternalOutput").ap()
            dbg2 = nc.dram_tensor("dbg_rows", [1, 2048], F32, kind="ExternalOutput").ap()
            dump_toks.append(P.dma("sync", dbg1[:, 0:512], bc_rb[:], cST, waits=[tok_ln1]))
            dump_toks.append(P.dma("sync", dbg1[:, 512:1024], bc_mb[:], cST))
            dump_toks.append(P.dma("sync", dbg2[0:1, 0:512], r_mu[:], cST))
            dump_toks.append(P.dma("sync", dbg2[0:1, 512:1024], r_var[:], cST))
            dump_toks.append(P.dma("sync", dbg2[0:1, 1024:1536], r_rstd[:], cST))
            dump_toks.append(P.dma("sync", dbg2[0:1, 1536:2048], r_mrs[:], cST))

        P.op("gpsimd", lambda e: e.memset(scr2[:], 1.0))
        tok_ones = P.tick("gpsimd", cPOOL)
        scan_toks = []
        for g in range(NG):
            P.op("vector", lambda e, g=g: e.tensor_scalar(
                out=scr1[:], in0=scr2[:], scalar1=a_col[:, g:g + 1], scalar2=None, op0=ALU.mult),
                waits=[tok_ones, tok_a, tok_ln1])
            tok_ab = P.tick("vector", cDVE)
            P.op("vector", lambda e, g=g: e.tensor_tensor_scan(
                out=vS[:, g, 2:2050], data0=scr1[:], data1=vL[:, g, 2:2050],
                initial=0.0, op0=ALU.mult, op1=ALU.add), waits=[tok_ab])
            scan_toks.append(P.tick("vector", cDVE))
        P.op("vector", lambda e: e.tensor_copy(out=hb1[:], in_=vS[:, :, 2049]), waits=[scan_toks[-1]])
        tok_wc = P.tick("vector", cDVE)
        t1 = P.dma("sync", cc1_in, hb1[:], cCCP, waits=[tok_wc])
        P.op("gpsimd", lambda e: e.collective_compute(
            "AllGather", ALU.bypass, replica_groups=RG,
            ins=[cc1_in.opt()], outs=[cc1_out.opt()]), waits=[t1])
        tok_cc1 = P.tick("gpsimd", cCC)
        t2 = P.dma("gpsimd", wc_all[:], cc1_out.rearrange("(r p) g -> p r g", p=128), cCCR,
                   waits=[tok_cc1])
        ts = chain("vector", lambda e: e.tensor_mul(out=sinit[:], in0=coef[:, 0, :],
                                                    in1=wc_all[:, 0, :]), [t2, tok_coef], cDVE)
        for r in range(1, 4):
            ts = chain("vector", lambda e, r=r: e.tensor_mul(out=smalla[:], in0=coef[:, r, :],
                                                             in1=wc_all[:, r, :]), [ts], cDVE)
            ts = chain("vector", lambda e, r=r: e.tensor_add(out=sinit[:], in0=sinit[:],
                                                             in1=smalla[:]), [ts], cDVE)
        tok_sinit = ts
        if "emadbg" in debug:
            dbge = nc.dram_tensor("dbg_emadbg", [128, 512], F32, kind="ExternalOutput").ap()
            emadump = bc_rb
            P.op("vector", lambda e: e.tensor_copy(out=emadump[:, 0:32], in_=wc_all[:].rearrange("p a b -> p (a b)")), waits=[tok_sinit])
            P.op("vector", lambda e: e.tensor_copy(out=emadump[:, 32:40], in_=sinit[:]))
            P.op("vector", lambda e: e.tensor_copy(out=emadump[:, 40:72], in_=coef[:].rearrange("p a b -> p (a b)")))
            P.op("vector", lambda e: e.tensor_copy(out=emadump[:, 72:80], in_=a2k[:]))
            P.op("vector", lambda e: e.tensor_copy(out=emadump[:, 80:88], in_=a_col[:]))
            P.op("vector", lambda e: e.tensor_copy(out=emadump[:, 88:96], in_=lnA[:]))
            tok_ed = P.tick("vector", cDVE)
            dump_toks.append(P.dma("sync", dbge, emadump[:], cST, waits=[tok_ed]))
        for g in range(NG):
            P.op("vector", lambda e, g=g: e.tensor_scalar(
                out=scr1[:], in0=scr2[:], scalar1=a_col[:, g:g + 1], scalar2=None, op0=ALU.mult),
                waits=[tok_sinit])
            tok_ab = P.tick("vector", cDVE)
            P.op("vector", lambda e, g=g: e.tensor_tensor_scan(
                out=vS[:, g, 2:2050], data0=scr1[:], data1=vL[:, g, 2:2050],
                initial=sinit[:, g:g + 1], op0=ALU.mult, op1=ALU.add), waits=[tok_ab])
            P.tick("vector", cDVE)
        tok_scan2 = (cDVE, cDVE.n)
        dump("S", vS[:, :, 2:2050], 2048, [tok_scan2])

        wtb = [AR_C.bf16(33088, NG, 128), AR_C.bf16(35136, NG, 128)]
        tok_prev_evac = [tok_scan2, tok_scan2]
        tok_wtb_rd = [None, None]
        tok_prev_mm = None
        last_conv = None
        for tt in range(4):
            c0 = 2 + 512 * tt
            cast_w = [tok_scan2] + dbg_guard()
            if tok_prev_mm is not None:
                cast_w.append(tok_prev_mm)
            P.op("scalar", lambda e, c0=c0: e.activation(
                out=stgA[:, :, 0:512], in_=vS[:, :, c0:c0 + 512], func=AF.Identity),
                waits=cast_w)
            tok_scast = P.tick("scalar", cACT)
            for ec in range(NG):
                bi = ec % 2
                tw_w = [] if tok_wtb_rd[bi] is None else [tok_wtb_rd[bi]]
                tw = P.dma("gpsimd", wtb[bi][:],
                           Wt_d.rearrange("(g p) e -> p g e", p=128)[:, :, 128 * ec:128 * (ec + 1)],
                           cWT[bi], waits=tw_w)
                for g in range(NG):
                    P.op("vector", lambda e, bi=bi, g=g: e.tensor_scalar(
                        out=wtb[bi][:, g, :], in0=wtb[bi][:, g, :], scalar1=om_a[:, g:g + 1],
                        scalar2=None, op0=ALU.mult), waits=[tw])
                tok_fold = P.tick("vector", cDVE)
                pb = PB[ec % 2]
                for g in range(NG):
                    P.op("tensor", lambda e, pb=pb, bi=bi, g=g: e.matmul(
                        pb.ap()[:, :], wtb[bi][:, g, :], stgA[:, g, 0:512],
                        start=(g == 0), stop=(g == NG - 1)),
                        waits=[tok_fold, tok_scast, tok_prev_evac[ec % 2]])
                tok_ymm = P.tick("tensor", cPE)
                tok_wtb_rd[bi] = tok_ymm
                cbase = c0 - 2
                P.op("vector", lambda e, ec=ec, cbase=cbase: e.tensor_scalar(
                    out=scr1[0:128, 0:512], in0=vL[:, ec, cbase:cbase + 512],
                    scalar1=conv_w[:, ec, 0:1], scalar2=None, op0=ALU.mult),
                    waits=[tok_ln1])
                srcs = [scr1, scr2, scr1, scr2, scr1]
                for kk in range(1, 5):
                    P.op("vector", lambda e, ec=ec, cbase=cbase, kk=kk, si=srcs[kk - 1], so=srcs[kk]:
                         e.scalar_tensor_tensor(
                             out=so[0:128, 0:512], in0=vL[:, ec, cbase + kk:cbase + kk + 512],
                             scalar=conv_w[:, ec, kk:kk + 1], in1=si[0:128, 0:512],
                             op0=ALU.mult, op1=ALU.add))
                tok_conv = P.tick("vector", cDVE)
                last_conv = tok_conv
                P.op("vector", lambda e, pb=pb, ec=ec, c0=c0: e.scalar_tensor_tensor(
                    out=vS[:, ec, c0:c0 + 512], in0=pb.ap()[:, :], scalar=bcb[:, ec:ec + 1],
                    in1=scr1[0:128, 0:512], op0=ALU.add, op1=ALU.add),
                    waits=[tok_ymm, tok_bcb, tok_conv])
                tok_prev_evac[ec % 2] = P.tick("vector", cDVE)
            tok_prev_mm = tok_ymm
        tok_attn_out_done = tok_prev_evac[1]
        dump("attn_out", vS[:, :, 2:2050], 2048, [tok_attn_out_done])

        tok_xre = P.dma("sync", vL, x_in.rearrange("(g p) t -> p g t", p=128), cLX,
                        waits=[last_conv] + dbg_guard())
        tiles_own = [(2, 512), (514, 512), (1026, 512), (1538, 512)]

        def apply_x2(c0, w, tok_bcs):
            for g in range(NG):
                P.op("vector", lambda e, g=g, c0=c0, w=w: e.tensor_mul(
                    out=bc_t1[:, 0:w], in0=vS[:, g, c0:c0 + w], in1=bc_rb[:, 0:w]),
                    waits=[tok_bcs, tok_xre] + dbg_guard())
                P.op("vector", lambda e, g=g, c0=c0, w=w: e.tensor_sub(
                    out=scr2[0:128, 0:w], in0=bc_t1[:, 0:w], in1=bc_mb[:, 0:w]))
                P.op("vector", lambda e, g=g, c0=c0, w=w: e.tensor_add(
                    out=vS[:, g, c0:c0 + w], in0=scr2[0:128, 0:w], in1=vL[:, g, c0:c0 + w]))

        tok_x2 = layernorm(lambda g, c0, w: vS[:, g, c0:c0 + w], tiles_own, apply_x2, [tok_attn_out_done])
        dump("x2", vS[:, :, 2:2050], 2048, [tok_x2])
        tok_spill = P.dma("sync", x2_spill.rearrange("(g p) t -> p g t", p=128),
                          vS[:, :, 2:2050], cSP, waits=[tok_x2])

        def apply_u0(c0, w, tok_bcs):
            for g in range(NG):
                P.op("vector", lambda e, g=g, c0=c0, w=w: e.tensor_mul(
                    out=bc_t1[:, 0:w], in0=vS[:, g, c0:c0 + w], in1=bc_rb[:, 0:w]),
                    waits=[tok_bcs])
                P.op("vector", lambda e, g=g, c0=c0, w=w: e.tensor_sub(
                    out=vL[:, g, c0:c0 + w], in0=bc_t1[:, 0:w], in1=bc_mb[:, 0:w]))

        tok_u0 = layernorm(lambda g, c0, w: vS[:, g, c0:c0 + w], tiles_own, apply_u0, [tok_x2])
        P.op("gpsimd", lambda e: e.memset(vL[:, :, 0:2], 0.0), waits=[tok_xre])
        P.op("gpsimd", lambda e: e.memset(vL[:, :, 2050:2052], 0.0))
        tok_hz = P.tick("gpsimd", cPOOL)
        P.op("vector", lambda e: e.tensor_copy(out=hb1[:], in_=vL[:, :, 2049]), waits=[tok_u0])
        tok_b2 = P.tick("vector", cDVE)
        t3 = P.dma("sync", cc2_in, hb1[:], cCCP, waits=[tok_b2])
        P.op("gpsimd", lambda e: e.collective_compute(
            "AllGather", ALU.bypass, replica_groups=RG,
            ins=[cc2_in.opt()], outs=[cc2_out.opt()]), waits=[t3])
        tok_cc2 = P.tick("gpsimd", cCC)
        tok_poff = tok_small

        _poff_cache = {}

        def read_prev(e, base_ap, dst_ap, block_elems):
            if id(e) not in _poff_cache:
                _poff_cache[id(e)] = e.value_load(poff[0:1, 0:1])
            v = _poff_cache[id(e)]
            srcp = dataclasses.replace(base_ap, offset=v * block_elems + base_ap.offset)
            return e.dma_start(out=dst_ap, in_=srcp)

        P.op("gpsimd", lambda e: read_prev(e, cc2_out[0:128, :], hb2[:], 128 * NG),
             waits=[tok_cc2, tok_poff])
        tok_hb = P.tick("gpsimd", cCCR, by=16)
        P.op("vector", lambda e: e.tensor_scalar(
            out=vL[:, :, 1], in0=hb2[:], scalar1=gate[:, 0:1], scalar2=None, op0=ALU.mult),
            waits=[tok_hb, tok_hz])
        tok_halo = P.tick("vector", cDVE)
        dump("u0", vL[:, :, 2:2050], 2048, [tok_u0, tok_halo])

    import dataclasses as _dc

    def apx(ap, dims):
        part = list(ap.ap[0])
        return _dc.replace(ap, ap=[part] + [list(d) for d in dims])

    def offx(ap, delta):
        return _dc.replace(ap, offset=ap.offset + delta)

    def qp_ap(fc):
        return AR_S.bf16(4096 * fc, 2048)
    def kp_ap(fc):
        return AR_S.bf16(16384 + 4096 * fc, 2048)
    vqk_flat = AR_S.bf16(0, 32832)
    hln = AR_S.bf16(0, NG, 2048)
    h1t = AR_S.bf16(32768, NHC, 512)
    x2r = AR_S.f32(0, NG, 2048)

    c_v = AR_C.bf16(0, 16, 520)
    c_oa = AR_C.bf16(16704, 8, 512)
    obnd = AR_C.bf16(16704, 8, 128)
    c_aT = AR_C.bf16(24896, 3, 512)
    c_st = [AR_C.f32(27968, 512), AR_C.f32(30016, 512)]
    c_tab = AR_C.f32(32064, 2, 4, 64)
    c_wq = AR_C.bf16(34112, 2, 1536)
    c_wo = AR_C.bf16(40256, 2, 8, 128)
    def khalo_ap(fc):
        return AR_C.bf16(27968 + 2048 * fc, 1024)
    c_w1 = AR_C.bf16(0, 2, 2, NG, 128)
    c_w2 = AR_C.bf16(16384, 2, 512)

    wqkv3 = wqkv_d.rearrange("l (g p) c -> l p g c", p=128)
    c_wqf = AR_S.bf16(32768, NG, 1536)
    w1r = w1_d.rearrange("l (g p) c -> l p g c", p=128)
    wo_r = wo_d.rearrange("l (h dh) e -> l dh h e", dh=64)

    def exchange_boundary(cc_in_t, cc_out_t, tok_ready):
        P.op("vector", lambda e: e.tensor_copy(out=hb1[:], in_=vL[:, :, 2049]),
             waits=[tok_ready])
        twc = P.tick("vector", cDVE)
        td = P.dma("sync", cc_in_t, hb1[:], cCCP, waits=[twc])
        P.op("gpsimd", lambda e: e.collective_compute(
            "AllGather", ALU.bypass, replica_groups=RG,
            ins=[cc_in_t.opt()], outs=[cc_out_t.opt()]), waits=[td])
        tcc = P.tick("gpsimd", cCC)
        P.op("gpsimd", lambda e: read_prev(e, cc_out_t[0:128, :], hb2[:], 128 * NG),
             waits=[tcc])
        th = P.tick("gpsimd", cCCR, by=16)
        P.op("vector", lambda e: e.tensor_scalar(
            out=vL[:, :, 1], in0=hb2[:], scalar1=gate[:, 0:1], scalar2=None, op0=ALU.mult),
            waits=[th])
        return P.tick("vector", cDVE)

    def _slice_cols(ap2, so, n):
        return ap2[:, so:so + n]

    def mm_bank_group(ops, waits):
        n = len(ops)
        for i, f in enumerate(ops):
            P.op("tensor",
                 (lambda e, f=f, s=(i == 0), t=(i == n - 1): f(e, s, t)),
                 waits=(waits if i == 0 else ()))

    sub = set(c for c in phases if c.isdigit()) or {"1", "2", "3", "4", "5"}
    if "B" in phases:
        tok_u = tok_halo

        for l in range(2):
            st = dict(
                ub_rd=[None, None], qkv_ev=[None, None, None], vmemset=None,
                tab_rd=[None, None], bct1=None, qbf_rd=None, pt=None,
                wq=[None, None], stats_rd=None, mrow_rd=None,
                st_rd=[None, None], aT_w=[None, None, None], aT_rd=[None, None, None],
                pw_rd=None, oa_rd=None, kv_dma=None,
            )
            P.op("gpsimd", lambda e: (e.memset(c_v[:, :, :], 1.0) if "v" not in phases
                                      else e.engine_nop()), waits=[tok_u])
            st["vmemset"] = P.tick("gpsimd", cPOOL)
            wq_toks = []
            for g in range(NG):
                wq_toks.append(P.dma("gpsimd", c_wqf[:, g, :], wqkv3[l, :, g, :],
                                     cWQ[g % 2], waits=[tok_u] if g < 2 else []))

            pending_tp = []

            def emit_lagged_transpose():
                while pending_tp:
                    pending_tp.pop(0)()
            NTC = 2 if "x" in phases else 16
            for tc in range(NTC):
                bi = tc % 2
                pass
                uw = [tok_u] + ([st["ub_rd"][bi]] if st["ub_rd"][bi] else [])
                P.op("vector", lambda e, tc=tc, bi=bi: (e.tensor_copy(
                    out=ub[bi][:, :, 0:129], in_=vL[:, :, 1 + 128 * tc: 130 + 128 * tc])
                    if "u" not in phases else e.engine_nop()), waits=uw)
                tok_ub = P.tick("vector", cDVE)
                tbi = tc % 2
                for ti, tabd in enumerate([] if "t" in phases else [tabAq_d, tabBq_d, tabAk_d, tabBk_d]):
                    tok_tab = P.dma(
                        "sync", c_tab[:, tbi, ti, :], tabd[:, tc, :], cTB[tbi],
                        waits=([st["tab_rd"][tbi]] if (ti == 0 and st["tab_rd"][tbi]) else []))
                pw0 = [tok_ub, wq_toks[-1]]
                for sec in range(3):
                    if st["qkv_ev"][sec] is not None:
                        pw0.append(st["qkv_ev"][sec])
                for g in range(NG):
                    off = 1 if g < 4 else 0
                    for sec, pb in ((0, PB[0]), (1, PB[1]), (2, PB[2])):
                        P.op("tensor", lambda e, g=g, off=off, sec=sec, pb=pb, bi=bi:
                             e.matmul(
                                 pb.ap()[:, :], ub[bi][:, g, off:off + 128],
                                 c_wqf[:, g, 512 * sec: 512 * (sec + 1)],
                                 start=(g == 0), stop=(g == NG - 1)),
                             waits=(pw0 if g == 0 else ()))
                tok_mm = P.tick("tensor", cPE)
                tok_qkvmm = tok_mm
                emit_lagged_transpose()
                st["ub_rd"][bi] = tok_qkvmm
                if "e" in phases:
                    st["qkv_ev"] = [tok_qkvmm] * 3
                    continue
                vdst = apx(c_v[:, tc, 0:520], [[65, 8], [1, 64]])
                P.op("scalar", lambda e, vdst=vdst: e.activation(
                    out=vdst, in_=PB[2].ap()[:, :], func=AF.Identity),
                    waits=[tok_qkvmm, st["vmemset"]])
                P.op("scalar", lambda e: e.activation(out=c_st[0][:, :], in_=PB[0].ap()[:, :],
                                                      func=AF.Identity))
                P.op("scalar", lambda e: e.activation(out=c_st[1][:, :], in_=PB[1].ap()[:, :],
                                                      func=AF.Identity))
                tok_qkev = P.tick("scalar", cACT)
                st["qkv_ev"] = [tok_qkev, tok_qkev, tok_qkev]

                for which in ((0, 1) if "p" not in phases else ()):
                    stg = c_st[which]
                    tabA = c_tab[:, tbi, 2 * which, :]
                    tabB = c_tab[:, tbi, 2 * which + 1, :]
                    scl = (qsr if which == 0 else ksr)[:, l, :]
                    scl_sh = (qsr_shuf if which == 0 else ksr_shuf)[:, l, :]
                    sqw = [tok_qkev]
                    if st["bct1"] is not None:
                        sqw.append(st["bct1"])
                    P.op("scalar", lambda e, stg=stg: e.activation(
                        out=pr_s3[:, :], in_=stg[:, :], func=AF.Square), waits=sqw)
                    tok_sq = P.tick("scalar", cACT)
                    P.op("vector", lambda e: e.reduce_sum(
                        out=nrm[:, :], in_=apx(pr_s3[:, :], [[64, 8], [1, 64]]), axis=AX.X),
                        waits=[tok_sq])
                    t = P.tick("vector", cDVE)
                    t = chain("scalar", lambda e: e.activation(out=nrm2[:], in_=nrm[:],
                                                               func=AF.Ln), [t], cACT)
                    t = chain("scalar", lambda e: e.activation(out=nrm2[:], in_=nrm2[:],
                                                               func=AF.Exp, scale=0.5),
                              [t], cACT)
                    t = chain("vector", lambda e: e.tensor_scalar_max(
                        out=nrm2[:], in0=nrm2[:], scalar1=1e-12), [t], cDVE)
                    t = chain("vector", lambda e: e.reciprocal(out=rqv[:], in_=nrm2[:]),
                              [t], cDVE)
                    t = chain("vector", lambda e, tabA=tabA, scl=scl: e.tensor_tensor(
                        out=pr_s1[:, 0:64], in0=tabA, in1=scl, op=ALU.mult),
                        [t, tok_tab], cDVE)
                    t = chain("vector", lambda e, tabB=tabB, scl_sh=scl_sh: e.tensor_tensor(
                        out=pr_s1[:, 64:128], in0=tabB, in1=scl_sh, op=ALU.mult), [t], cDVE)
                    rq_b = apx(rqv[:, :], [[1, 8], [0, 64]])
                    P.op("vector", lambda e, stg=stg, rq_b=rq_b: e.tensor_tensor(
                        out=apx(pr_s2[:, :], [[64, 8], [1, 64]]),
                        in0=apx(stg[:, :], [[64, 8], [1, 64]]), in1=rq_b, op=ALU.mult),
                        waits=[t])
                    P.op("vector", lambda e: e.tensor_copy(
                        out=apx(pr_s3[:, :], [[64, 8], [1, 32]]),
                        in_=apx(offx(pr_s2[:, :], 32), [[64, 8], [1, 32]])))
                    P.op("vector", lambda e: e.tensor_copy(
                        out=apx(offx(pr_s3[:, :], 32), [[64, 8], [1, 32]]),
                        in_=apx(pr_s2[:, :], [[64, 8], [1, 32]])))
                    tok_pp = P.tick("vector", cDVE)
                    tA_rep = apx(pr_s1[:, 0:64], [[0, 8], [1, 64]])
                    tB_rep = apx(pr_s1[:, 64:128], [[0, 8], [1, 64]])
                    P.op("vector", lambda e, stg=stg, tA_rep=tA_rep: e.tensor_tensor(
                        out=apx(stg[:, :], [[64, 8], [1, 64]]),
                        in0=apx(pr_s2[:, :], [[64, 8], [1, 64]]), in1=tA_rep, op=ALU.mult),
                        waits=[tok_pp])
                    P.op("vector", lambda e, tB_rep=tB_rep: e.tensor_tensor(
                        out=apx(pr_s2[:, :], [[64, 8], [1, 64]]),
                        in0=apx(pr_s3[:, :], [[64, 8], [1, 64]]), in1=tB_rep, op=ALU.mult))
                    P.op("vector", lambda e, stg=stg: e.tensor_add(
                        out=pr_s3[:, :], in0=stg[:, :], in1=pr_s2[:, :]))
                    tok_rot = P.tick("vector", cDVE)
                    st["bct1"] = tok_rot
                    cw = [tok_rot]
                    if st["qbf_rd"] is not None:
                        cw.append(st["qbf_rd"])
                    qb2 = which
                    P.op("scalar", lambda e, which=which, qb2=qb2: e.activation(
                        out=qbf[:, qb2, :], in_=pr_s3[:, :], func=AF.Identity,
                        scale=(QK_SCALE if which == 0 else 1.0)), waits=cw)
                    tok_cast = P.tick("scalar", cACT)

                    def mk_tp(tok_cast=tok_cast, qb2=qb2, tc=tc, which=which):
                        def emit():
                            tw = [tok_cast]
                            if st["pt"] is not None:
                                tw.append(st["pt"])
                            for fc in range(4):
                                P.op("tensor", lambda e, fc=fc: e.transpose(
                                    PT.ap()[:, 128 * fc: 128 * (fc + 1)],
                                    qbf[:, qb2, 128 * fc: 128 * (fc + 1)], ident[:, :]),
                                    waits=tw if fc == 0 else ())
                            tok_tp = P.tick("tensor", cPE)
                            st["qbf_rd"] = tok_tp
                            base = 0 if which == 0 else 8192
                            dst = apx(offx(vqk_flat, base + 128 * tc), [[2048, 4], [1, 128]])
                            P.op("vector", lambda e, dst=dst: e.tensor_copy(
                                out=dst, in_=apx(PT.ap()[:, :], [[128, 4], [1, 128]])),
                                waits=[tok_tp])
                            st["pt"] = P.tick("vector", cDVE)
                        return emit
                    pending_tp.append(mk_tp())
                st["tab_rd"][tbi] = st["bct1"]
            emit_lagged_transpose()
            tok_prep_done = st["pt"] or st["qkv_ev"][0]
            if "2" not in sub:
                tok_u = tok_prep_done
                continue

            pk_toks = []
            for fc in range(4):
                pk_toks.append(P.dma(
                    "sync", cc3_in[l][128 * fc: 128 * (fc + 1), 0:1024],
                    kp_ap(fc)[:, 1024:2048], cCCP, waits=[tok_prep_done]))
            for tcv in range(8, 16):
                pk_toks.append(P.dma(
                    "sync", cc3_in[l][520 + 128 * (tcv - 8): 520 + 128 * (tcv - 7), 0:520],
                    c_v[:, tcv, :], cCCP, waits=[tok_prep_done]))
            for fc in range(4):
                pk_toks.append(P.dma(
                    "sync", cc3_in[l][1544 + fc: 1545 + fc, 0:128],
                    qp_ap(fc)[:, 2047:2048], cCCP, waits=[tok_prep_done]))
            P.op("gpsimd", lambda e, l=l: e.collective_compute(
                "AllGather", ALU.bypass, replica_groups=RG,
                ins=[cc3_in[l].opt()], outs=[cc3_out[l].opt()]), waits=[pk_toks[-1]])
            tok_cc3 = P.tick("gpsimd", cCC)
            blk = R3 * 1040
            for fc in range(4):
                P.op("gpsimd", lambda e, fc=fc, l=l: read_prev(
                    e, cc3_out[l][128 * fc: 128 * (fc + 1), 0:1024], khalo_ap(fc)[:, :], blk),
                    waits=[tok_cc3, tok_prep_done])
                P.tick("gpsimd", cCCR, by=16)
            for fc in range(4):
                P.op("gpsimd", lambda e, fc=fc, l=l: read_prev(
                    e, cc3_out[l][1544 + fc: 1545 + fc, 0:128], qbndt[:, fc:fc + 1], blk),
                    waits=[tok_cc3])
                P.tick("gpsimd", cCCR, by=16)
            tok_halo_kv = (cCCR, cCCR.n)

            if "3" not in sub:
                tok_u = tok_halo_kv
                continue
            def kjc_src(w, jc, fc, ro):
                if w == 0 and jc < 4:
                    return khalo_ap(fc)[ro:ro + 64, 512 + 128 * jc: 640 + 128 * jc]
                base = 512 * (w - 1) if w >= 1 else -512
                return kp_ap(fc)[ro:ro + 64, base + 128 * jc: base + 128 * jc + 128]

            def vjc_src(w, jc):
                if w == 0 and jc < 4:
                    return 12 + jc
                return (4 * (w - 1) + jc) if w >= 1 else (jc - 4)

            def stats_segs(w, qi, fc, ro):
                j0, j1 = 128 * qi, 128 * qi + 640
                segs = []
                if w == 0:
                    if j0 < 512:
                        segs.append(khalo_ap(fc)[ro:ro + 64, 512 + j0: 1024])
                    segs.append(kp_ap(fc)[ro:ro + 64, max(0, j0 - 512): j1 - 512])
                else:
                    b = 512 * (w - 1)
                    segs.append(kp_ap(fc)[ro:ro + 64, b + j0: b + j1])
                return segs

            def attn_evac(h, iw, dst_ap, tok_pv, st, extra=()):
                t = chain("vector", lambda e, iw=iw: e.reciprocal(
                    out=r64[64:65, 0:iw], in_=PW.ap()[64:65, 0:iw]),
                    [tok_pv] + list(extra), cDVE)
                t = chain("scalar", lambda e, iw=iw: e.activation(
                    out=r64h[64:65, 0:iw], in_=r64[64:65, 0:iw], func=AF.Identity), [t], cACT)
                t = chain("scalar", lambda e, iw=iw: e.activation(
                    out=r64[64:65, 512:512 + iw], in_=r64h[64:65, 0:iw], func=AF.Identity),
                    [t], cACT)
                t = chain("vector", lambda e, iw=iw: e.tensor_sub(
                    out=pr_s3[64:65, 0:iw], in0=r64[64:65, 0:iw],
                    in1=r64[64:65, 512:512 + iw]), [t], cDVE)
                t = chain("scalar", lambda e, iw=iw: e.activation(
                    out=r64h[64:65, 512:512 + iw], in_=pr_s3[64:65, 0:iw],
                    func=AF.Identity), [t], cACT)
                bw = [t]
                if st["rb_rd"] is not None:
                    bw.append(st["rb_rd"])
                mm_bank_group([
                    lambda e, s, p, iw=iw: e.matmul(PB[3].ap()[0:64, 0:iw],
                                                    on64[64:65, 0:64],
                                                    r64h[64:65, 0:iw], start=s, stop=p),
                    lambda e, s, p, iw=iw: e.matmul(PB[3].ap()[0:64, 0:iw],
                                                    on64[64:65, 0:64],
                                                    r64h[64:65, 512:512 + iw],
                                                    start=s, stop=p),
                ], bw)
                tok_rb = P.tick("tensor", cPE)
                P.op("scalar", lambda e, iw=iw: e.activation(
                    out=rb64[0:64, 0:iw], in_=PB[3].ap()[0:64, 0:iw], func=AF.Identity),
                    waits=[tok_rb])
                tok_rbs = P.tick("scalar", cACT)
                st["pb3_rd"] = tok_rbs
                P.op("vector", lambda e, iw=iw, dst_ap=dst_ap: e.tensor_tensor(
                    out=dst_ap, in0=PW.ap()[0:64, 0:iw], in1=rb64[0:64, 0:iw], op=ALU.mult),
                    waits=[tok_rbs])
                tok_e = P.tick("vector", cDVE)
                st["pw_rd"] = tok_e
                st["rb_rd"] = tok_e
                return tok_e

            st["rb_rd"] = None
            tok_oa_w = {}
            tok_vhalo = None
            windows = [1, 2, 3, 0]
            for w in windows:
                if w == 0:
                    for tcv in range(8):
                        P.op("gpsimd", lambda e, tcv=tcv, l=l: read_prev(
                            e, cc3_out[l][520 + 128 * tcv: 520 + 128 * (tcv + 1), 0:520],
                            c_v[:, 8 + tcv, :], R3 * 1040),
                            waits=[tok_cc3, st["pw_rd"]])
                        P.tick("gpsimd", cCCR, by=16)
                    tok_vhalo = (cCCR, cCCR.n)
                tok_kv = tok_prep_done if w != 0 else tok_halo_kv
                for h in range(HEADS):
                    fc, ro = h // 2, (h % 2) * 64
                    qw_full = qp_ap(fc)
                    for qi in range(4):
                        pw = [tok_kv]
                        if st["stats_rd"] is not None:
                            pw.append(st["stats_rd"])
                        qslice = qp_ap(fc)[ro:ro + 64, 512 * w + 128 * qi: 512 * w + 128 * qi + 128]
                        segs = stats_segs(w, qi, fc, ro)
                        b4ops, b5ops = [], []
                        col = 0
                        for sg in segs:
                            wid = sg.shape[-1]
                            while wid > 0:
                                n = min(wid, 512 - (col % 512)) if col < 512 else min(wid, 128)
                                n = min(n, 512 - col) if col < 512 else n
                                tgt = (b4ops if col < 512 else b5ops)
                                tgt.append((col, n, sg, sg.shape[-1] - wid))
                                col += n
                                wid -= n
                        ops4 = [
                            (lambda e, s, p, c=c, n=n, sg=sg, so=so, qslice=qslice:
                             e.matmul(PS640.ap()[:, c:c + n], qslice,
                                      _slice_cols(sg, so, n), start=s, stop=p))
                            for (c, n, sg, so) in b4ops]
                        if w == 0:
                            hw = 512 - 128 * qi
                            ops4.append(lambda e, s, p, hw=hw: e.matmul(
                                PS640.ap()[:, 0:hw], ones1_128[0:1, :], s0neg[0:1, 0:hw],
                                start=s, stop=p))
                        mm_bank_group(ops4, pw)
                        ops5 = [
                            (lambda e, s, p, c=c, n=n, sg=sg, so=so, qslice=qslice:
                             e.matmul(PS640.ap()[:, c:c + n], qslice,
                                      _slice_cols(sg, so, n), start=s, stop=p))
                            for (c, n, sg, so) in b5ops]
                        mm_bank_group(ops5, [])
                        tok_smm = P.tick("tensor", cPE)
                        P.op("vector", lambda e, qi=qi: e.tensor_reduce(
                            out=mst[:, qi:qi + 1], in_=PS640.ap()[:, 0:640], axis=AX.X,
                            op=ALU.max, negate=True), waits=[tok_smm])
                        st["stats_rd"] = P.tick("vector", cDVE)
                    mw = [st["stats_rd"]]
                    if st["mrow_rd"] is not None:
                        mw.append(st["mrow_rd"])
                    if st.get("pb3_rd") is not None:
                        mw.append(st["pb3_rd"])
                    for qi in range(4):
                        P.op("tensor", lambda e, qi=qi: e.transpose(
                            PB[3].ap()[0:1, 128 * qi: 128 * qi + 128], mst[:, qi:qi + 1],
                            identf[:, :]), waits=(mw if qi == 0 else ()))
                    tok_mtp = P.tick("tensor", cPE)
                    negw = [tok_mtp]
                    if st.get("negm_rd") is not None:
                        negw.append(st["negm_rd"])
                    P.op("scalar", lambda e: e.activation(
                        out=negm[0:1, 0:512], in_=PB[3].ap()[0:1, 0:512], func=AF.Identity),
                        waits=negw)
                    tok_negm = P.tick("scalar", cACT)
                    st["mrow_rd"] = tok_negm
                    for jc in range(8):
                        i0 = max(0, 128 * (jc - 4))
                        i1 = min(512, 128 * jc + 128)
                        iw = i1 - i0
                        pb = PB[jc % 2]
                        pw = [tok_negm, tok_kv]
                        if st["st_rd"][jc % 2] is not None:
                            pw.append(st["st_rd"][jc % 2])
                        kap = kjc_src(w, jc, fc, ro)
                        qap = qp_ap(fc)[ro:ro + 64, 512 * w + i0: 512 * w + i1]
                        ops = [lambda e, s, p, kap=kap, qap=qap, pb=pb, i0=i0, iw=iw:
                               e.matmul(pb.ap()[:, i0:i0 + iw], kap, qap, start=s, stop=p)]
                        for qi in range(i0 // 128, i1 // 128):
                            ops.append(lambda e, s, p, pb=pb, qi=qi: e.matmul(
                                pb.ap()[:, 128 * qi: 128 * qi + 128], ones1_128[0:1, :],
                                negm[0:1, 128 * qi: 128 * qi + 128], start=s, stop=p))
                        if jc <= 3:
                            qe = jc
                            if i0 <= 128 * qe < i1:
                                ops.append(lambda e, s, p, pb=pb, qe=qe: e.matmul(
                                    pb.ap()[:, 128 * qe: 128 * qe + 128], mask_d0[:, :],
                                    ident[:, :], start=s, stop=p))
                        else:
                            qe = jc - 4
                            if i0 <= 128 * qe < i1:
                                ops.append(lambda e, s, p, pb=pb, qe=qe: e.matmul(
                                    pb.ap()[:, 128 * qe: 128 * qe + 128], mask_d4[:, :],
                                    ident[:, :], start=s, stop=p))
                        if w == 0 and jc < 4:
                            ops.append(lambda e, s, p, pb=pb, i0=i0, iw=iw: e.matmul(
                                pb.ap()[:, i0:i0 + iw], ones1_128[0:1, :],
                                s0neg[0:1, 0:iw], start=s, stop=p))
                        mm_bank_group(ops, pw)
                        tok_stmm = P.tick("tensor", cPE)
                        st["negm_rd"] = tok_stmm
                        slot = jc % 3
                        pw2 = [tok_stmm]
                        if st["aT_rd"][slot] is not None:
                            pw2.append(st["aT_rd"][slot])
                        P.op("scalar", lambda e, pb=pb, i0=i0, iw=iw, slot=slot: e.activation(
                            out=c_aT[:, slot, i0:i0 + iw], in_=pb.ap()[:, i0:i0 + iw],
                            func=AF.Exp), waits=pw2)
                        tok_exp = P.tick("scalar", cACT)
                        st["st_rd"][jc % 2] = tok_exp
                        vt = vjc_src(w, jc)
                        pvw = [tok_exp]
                        if jc == 0 and st["pw_rd"] is not None:
                            pvw.append(st["pw_rd"])
                        if jc == 0 and w == 0:
                            pvw.append(tok_vhalo)
                        P.op("tensor", lambda e, vt=vt, h=h, i0=i0, iw=iw, slot=slot, jc=jc:
                             e.matmul(PW.ap()[:, i0:i0 + iw], c_v[:, vt, 65 * h: 65 * h + 65],
                                      c_aT[:, slot, i0:i0 + iw], start=(jc == 0),
                                      stop=(jc == 7)), waits=pvw)
                        st["aT_rd"][slot] = P.tick("tensor", cPE)
                    tok_pv = st["aT_rd"][7 % 3]
                    dw = [st["oa_wo"]] if st.get("oa_wo") else []
                    attn_evac(h, 512, c_oa[0:64, h, 0:512], tok_pv, st, extra=dw)
                tok_attnw = st["pw_rd"]
                ecw = []
                for ec in range(NG):
                    wobuf = ec % 2
                    wv = [tok_attnw]
                    if st.get(f"wo_rd{wobuf}"):
                        wv.append(st[f"wo_rd{wobuf}"])
                    twl = P.dma("gpsimd", c_wo[0:64, wobuf, :, :],
                                wo_r[l, :, :, 128 * ec: 128 * (ec + 1)], cWO[wobuf], waits=wv)
                    pb = PB[ec % 2]
                    pw = [twl]
                    if st.get(f"wops{ec % 2}"):
                        pw.append(st[f"wops{ec % 2}"])
                    ops = [(lambda e, s, p, h=h2, pb=pb, wobuf=wobuf: e.matmul(
                        pb.ap()[:, :], c_wo[0:64, wobuf, h, :], c_oa[0:64, h, 0:512],
                        start=s, stop=p)) for h2 in range(HEADS)]
                    mm_bank_group(ops, pw)
                    tok_womm = P.tick("tensor", cPE)
                    st[f"wo_rd{wobuf}"] = tok_womm
                    c0 = 2 + 512 * w
                    P.op("vector", lambda e, pb=pb, ec=ec, c0=c0: e.tensor_add(
                        out=vL[:, ec, c0:c0 + 512], in0=pb.ap()[:, :],
                        in1=vL[:, ec, c0:c0 + 512]), waits=[tok_womm] + dbg_guard())
                    st[f"wops{ec % 2}"] = P.tick("vector", cDVE)
                tok_oa_w[w] = st[f"wops{(NG - 1) % 2}"]
                st["oa_wo"] = st[f"wops{(NG - 1) % 2}"]

            if "4" not in sub:
                tok_u = tok_oa_w[0]
                continue
            t = chain("vector", lambda e: e.tensor_copy(
                out=apx(qfk[:, :, :], [[128, 4], [1, 1]]), in_=qbndt[:, :]),
                [tok_halo_kv, tok_oa_w[0]], cDVE)
            for fc in range(4):
                t = chain("vector", lambda e, fc=fc: e.tensor_copy(
                    out=qfk[:, fc, 1:128], in_=qp_ap(fc)[:, 0:127]), [t], cDVE)
            tok_qfk = t
            for h in range(HEADS):
                fc, ro = h // 2, (h % 2) * 64
                qap = qfk[ro:ro + 64, fc, :]
                pw = [tok_qfk]
                if st["stats_rd"] is not None:
                    pw.append(st["stats_rd"])
                mm_bank_group([
                    lambda e, s, p, fc=fc, ro=ro, qap=qap: e.matmul(
                        PS640.ap()[:, 0:512], qap, khalo_ap(fc)[ro:ro + 64, 384:896],
                        start=s, stop=p)], pw)
                mm_bank_group([
                    lambda e, s, p, fc=fc, ro=ro, qap=qap: e.matmul(
                        PS640.ap()[:, 512:640], qap, khalo_ap(fc)[ro:ro + 64, 896:1024],
                        start=s, stop=p)], [])
                tok_smm = P.tick("tensor", cPE)
                P.op("vector", lambda e: e.tensor_reduce(
                    out=mst[:, 0:1], in_=PS640.ap()[:, 0:640], axis=AX.X, op=ALU.max,
                    negate=True), waits=[tok_smm])
                st["stats_rd"] = P.tick("vector", cDVE)
                mw = [st["stats_rd"], st["mrow_rd"]]
                if st.get("pb3_rd") is not None:
                    mw.append(st["pb3_rd"])
                P.op("tensor", lambda e: e.transpose(
                    PB[3].ap()[0:1, 0:128], mst[:, 0:1], identf[:, :]), waits=mw)
                tok_mtp = P.tick("tensor", cPE)
                P.op("scalar", lambda e: e.activation(
                    out=negm[0:1, 0:128], in_=PB[3].ap()[0:1, 0:128], func=AF.Identity),
                    waits=[tok_mtp, st["negm_rd"]] if st.get("negm_rd") else [tok_mtp])
                tok_negm = P.tick("scalar", cACT)
                st["mrow_rd"] = tok_negm
                for jj, jc in enumerate(range(3, 8)):
                    pb = PB[jc % 2]
                    pw = [tok_negm]
                    if st["st_rd"][jc % 2] is not None:
                        pw.append(st["st_rd"][jc % 2])
                    ops = [lambda e, s, p, pb=pb, fc=fc, ro=ro, jc=jc, qap=qap: e.matmul(
                        pb.ap()[:, 0:128], khalo_ap(fc)[ro:ro + 64, 128 * jc: 128 * jc + 128],
                        qap, start=s, stop=p)]
                    ops.append(lambda e, s, p, pb=pb: e.matmul(
                        pb.ap()[:, 0:128], ones1_128[0:1, :], negm[0:1, 0:128],
                        start=s, stop=p))
                    if jc == 3:
                        ops.append(lambda e, s, p, pb=pb: e.matmul(
                            pb.ap()[:, 0:128], bndmask3[:, :], ident[:, :], start=s, stop=p))
                    mm_bank_group(ops, pw)
                    tok_stmm = P.tick("tensor", cPE)
                    st["negm_rd"] = tok_stmm
                    slot = jc % 3
                    pw2 = [tok_stmm]
                    if st["aT_rd"][slot] is not None:
                        pw2.append(st["aT_rd"][slot])
                    P.op("scalar", lambda e, pb=pb, slot=slot: e.activation(
                        out=c_aT[:, slot, 0:128], in_=pb.ap()[:, 0:128], func=AF.Exp),
                        waits=pw2)
                    tok_exp = P.tick("scalar", cACT)
                    st["st_rd"][jc % 2] = tok_exp
                    pvw = [tok_exp]
                    if jj == 0 and st["pw_rd"] is not None:
                        pvw.append(st["pw_rd"])
                    P.op("tensor", lambda e, jc=jc, h=h, slot=slot, jj=jj: e.matmul(
                        PW.ap()[:, 0:128], c_v[:, 8 + jc, 65 * h: 65 * h + 65],
                        c_aT[:, slot, 0:128], start=(jj == 0), stop=(jj == 4)), waits=pvw)
                    st["aT_rd"][slot] = P.tick("tensor", cPE)
                tok_pv = st["aT_rd"][7 % 3]
                attn_evac(h, 128, obnd[0:64, h, 0:128], tok_pv, st)
            tok_bnd = st["pw_rd"]
            for ec in range(NG):
                wobuf = ec % 2
                wv = [tok_bnd]
                if st.get(f"wo_rd{wobuf}"):
                    wv.append(st[f"wo_rd{wobuf}"])
                twl = P.dma("gpsimd", c_wo[0:64, wobuf, :, :],
                            wo_r[l, :, :, 128 * ec: 128 * (ec + 1)], cWO[wobuf], waits=wv)
                pb = PB[ec % 2]
                pw = [twl]
                if st.get(f"wops{ec % 2}"):
                    pw.append(st[f"wops{ec % 2}"])
                ops = [(lambda e, s, p, h=h2, pb=pb, wobuf=wobuf: e.matmul(
                    pb.ap()[:, 0:128], c_wo[0:64, wobuf, h, :], obnd[0:64, h, 0:128],
                    start=s, stop=p)) for h2 in range(HEADS)]
                mm_bank_group(ops, pw)
                tok_womm = P.tick("tensor", cPE)
                st[f"wo_rd{wobuf}"] = tok_womm
                P.op("vector", lambda e, pb=pb, ec=ec: e.scalar_tensor_tensor(
                    out=vL[:, ec, 1:2], in0=pb.ap()[:, 0:1], scalar=gate[:, 0:1],
                    in1=vL[:, ec, 1:2], op0=ALU.mult, op1=ALU.add), waits=[tok_womm] + dbg_guard())
                st[f"wops{ec % 2}"] = P.tick("vector", cDVE)
            tok_xmid = st[f"wops{(NG - 1) % 2}"]
            dump(f"xmid{l}", vL[:, :, 2:2050], 2048, [tok_xmid])

            if "5" not in sub:
                tok_u = tok_xmid
                continue
            def src_shift(g, c0, wdt):
                off = 0 if g < 4 else -1
                return vL[:, g, c0 + off: c0 + off + wdt]

            def apply_ln2(c0, wdt, tok_bcs, l=l):
                for g in range(NG):
                    off = 0 if g < 4 else -1
                    P.op("vector", lambda e, g=g, c0=c0, wdt=wdt, off=off: e.tensor_mul(
                        out=bc_t1[:, 0:wdt], in0=vL[:, g, c0 + off: c0 + off + wdt],
                        in1=bc_rb[:, 0:wdt]), waits=[tok_bcs])
                    P.op("vector", lambda e, g=g, c0=c0, wdt=wdt: e.tensor_sub(
                        out=scr1[0:128, 0:wdt], in0=bc_t1[:, 0:wdt], in1=bc_mb[:, 0:wdt]))
                    tok_s = P.tick("vector", cDVE)
                    P.op("scalar", lambda e, g=g, c0=c0, wdt=wdt, l=l: e.activation(
                        out=hln[:, g, c0 - 2: c0 - 2 + wdt], in_=scr1[0:128, 0:wdt],
                        func=AF.Identity, scale=ln_g[:, l, g:g + 1], bias=ln_b[:, l, g:g + 1]),
                        waits=[tok_s])
                P.op("vector", lambda e: e.engine_nop())

            tok_ln2 = layernorm(src_shift, tiles_own, apply_ln2, [tok_xmid])
            tok_hln = P.tick("scalar", cACT)

            stw = {"w1_rd": [None, None], "w2_rd": [None, None], "h1_rd": {}, "ps": {}}
            tok_u_upd = []
            for tt in range(4):
                tok_h1 = {}
                for hc in range(NHC):
                    wb = hc % 2
                    wv = ([stw["w1_rd"][wb]] if stw["w1_rd"][wb] else []) + [tok_hln]
                    ta = P.dma("gpsimd", c_w1[:, wb, 0, :, :],
                               w1r[l, :, :, 128 * hc: 128 * hc + 128], cW1[wb], waits=wv)
                    tg = P.dma("gpsimd", c_w1[:, wb, 1, :, :],
                               w1r[l, :, :, NHC * 128 + 128 * hc: NHC * 128 + 128 * hc + 128],
                               cW1[wb])
                    pba = PB[2 * (hc % 2)]
                    pbg = PB[2 * (hc % 2) + 1]
                    pw = [tg, tok_hln]
                    if stw["ps"].get(hc % 2):
                        pw.append(stw["ps"][hc % 2])
                    if hc < 2:
                        for b in (2 * (hc % 2), 2 * (hc % 2) + 1):
                            if stw.get(f"w2ps{b}"):
                                pw.append(stw[f"w2ps{b}"])
                    opsa = [(lambda e, s, p, g=g, wb=wb, pba=pba, tt=tt: e.matmul(
                        pba.ap()[:, :], c_w1[:, wb, 0, g, :],
                        hln[:, g, 512 * tt: 512 * tt + 512], start=s, stop=p))
                        for g in range(NG)]
                    mm_bank_group(opsa, pw)
                    opsg = [(lambda e, s, p, g=g, wb=wb, pbg=pbg, tt=tt: e.matmul(
                        pbg.ap()[:, :], c_w1[:, wb, 1, g, :],
                        hln[:, g, 512 * tt: 512 * tt + 512], start=s, stop=p))
                        for g in range(NG)]
                    mm_bank_group(opsg, [])
                    tok_w1mm = P.tick("tensor", cPE)
                    stw["w1_rd"][wb] = tok_w1mm
                    gw = [tok_w1mm]
                    if stw.get("gel_rd"):
                        gw.append(stw["gel_rd"])
                    P.op("scalar", lambda e, pbg=pbg, hc=hc, l=l: e.activation(
                        out=bc_t1[:, :], in_=pbg.ap()[:, :], func=AF.Gelu,
                        bias=b1[:, l, 1, hc:hc + 1]), waits=gw)
                    tok_gel = P.tick("scalar", cACT)
                    hw1 = [tok_gel]
                    if stw["h1_rd"].get(hc):
                        hw1.append(stw["h1_rd"][hc])
                    P.op("vector", lambda e, pba=pba, hc=hc, l=l: e.scalar_tensor_tensor(
                        out=h1t[:, hc, :], in0=pba.ap()[:, :], scalar=b1[:, l, 0, hc:hc + 1],
                        in1=bc_t1[:, :], op0=ALU.add, op1=ALU.mult), waits=hw1)
                    tok_h1[hc] = P.tick("vector", cDVE)
                    stw["gel_rd"] = tok_h1[hc]
                    stw["ps"][hc % 2] = tok_h1[hc]
                for ecg in range(2):
                    pw0 = [tok_h1[NHC - 1]]
                    for b in range(4):
                        if stw.get(f"w2ps{b}"):
                            pw0.append(stw[f"w2ps{b}"])
                    for hc in range(NHC):
                        wb2 = hc % 2
                        wv = [stw["w2_rd"][wb2]] if stw["w2_rd"][wb2] else []
                        tl = P.dma("gpsimd", c_w2[:, wb2, :],
                                   w2_d[l, 128 * hc: 128 * hc + 128,
                                        512 * ecg: 512 * ecg + 512], cW2[wb2],
                                   waits=wv + [tok_h1[hc], tok_hln])
                        for b in range(4):
                            P.op("tensor", lambda e, b=b, wb2=wb2, hc=hc:
                                 e.matmul(PB[b].ap()[:, :],
                                          c_w2[:, wb2, 128 * b: 128 * b + 128],
                                          h1t[:, hc, :], start=(hc == 0),
                                          stop=(hc == NHC - 1)),
                                 waits=([tl] + (pw0 if hc == 0 else [])) if b == 0 else ())
                        stw["w2_rd"][wb2] = P.tick("tensor", cPE)
                    tok_w2mm = stw["w2_rd"][(NHC - 1) % 2]
                    c0 = 2 + 512 * tt
                    for b in range(4):
                        ec = 4 * ecg + b
                        P.op("vector", lambda e, b=b, ec=ec, c0=c0, l=l:
                             e.scalar_tensor_tensor(
                                 out=vL[:, ec, c0:c0 + 512], in0=PB[b].ap()[:, :],
                                 scalar=b2[:, l, ec:ec + 1], in1=vL[:, ec, c0:c0 + 512],
                                 op0=ALU.add, op1=ALU.add),
                             waits=[tok_w2mm] + dbg_guard())
                        stw[f"w2ps{b}"] = P.tick("vector", cDVE)
                        tok_u_upd.append(stw[f"w2ps{b}"])
                for hc in range(NHC):
                    stw["h1_rd"][hc] = stw["w2_rd"][(NHC - 1) % 2]
            tok_layer = tok_u_upd[-1]
            dump(f"y{l}", vL[:, :, 2:2050], 2048, [tok_layer])

            if l == 0:
                tok_u = exchange_boundary(cc4_in, cc4_out, tok_layer)
            else:
                tok_u = tok_layer
            if l == 1:
                tok_x2r = P.dma("sync", x2r, x2_spill.rearrange("(g p) t -> p g t", p=128),
                                cLX, waits=[tok_spill, tok_layer])

    if "C" in phases and "B" in phases:
        def apply_final(c0, wdt, tok_bcs):
            for g in range(NG):
                P.op("vector", lambda e, g=g, c0=c0, wdt=wdt: e.tensor_mul(
                    out=bc_t1[:, 0:wdt], in0=vL[:, g, c0:c0 + wdt], in1=bc_rb[:, 0:wdt]),
                    waits=[tok_bcs, tok_x2r] + dbg_guard())
                P.op("vector", lambda e, g=g, c0=c0, wdt=wdt: e.tensor_sub(
                    out=scr1[0:128, 0:wdt], in0=bc_t1[:, 0:wdt], in1=bc_mb[:, 0:wdt]))
                P.op("vector", lambda e, g=g, c0=c0, wdt=wdt: e.tensor_add(
                    out=vL[:, g, c0:c0 + wdt], in0=scr1[0:128, 0:wdt],
                    in1=x2r[:, g, c0 - 2: c0 - 2 + wdt]))

        tok_fin = layernorm(lambda g, c0, wdt: vL[:, g, c0:c0 + wdt], tiles_own,
                            apply_final, [tok_u])
        tok_y = P.dma("sync", y_out.rearrange("(g p) t -> p g t", p=128),
                      vL[:, :, 2:2050], cST, waits=[tok_fin])
        dump_toks.append(tok_y)
    else:
        tok_y = P.dma("sync", y_out.rearrange("(g p) t -> p g t", p=128),
                      vL[:, :, 2:2050], cST,
                      waits=[tok_u if "B" in phases else (tok_halo if "A" in phases else tok_x)])
        dump_toks.append(tok_y)

    P.op("sync", lambda e: e.nop(), waits=dump_toks + [(cST, cST.n)])
    P.emit()
    es.close()
    return nc


def kernel(**inputs):
    in_maps = stage_inputs(inputs)
    nc = build_program(debug=DEBUG, phases=PHASES)
    res = run_bass_kernel_spmd(nc, in_maps, core_ids=list(range(8)))
    out = np.zeros((2, SEQ, D), np.float32)
    for c in range(8):
        b, s = c // 4, c % 4
        y = np.asarray(res.results[c]["y"], np.float32)
        out[b, TOK * s: TOK * (s + 1), :] = y.T
    return out


# version 08
# speedup vs baseline: 1.7082x; 1.1751x over previous
import dataclasses
import numpy as np
import ml_dtypes
import concourse.bass as bass
import concourse.mybir as mybir
from concourse.bass_utils import run_bass_kernel_spmd
from contextlib import ExitStack

F32 = mybir.dt.float32
BF16 = mybir.dt.bfloat16
I32 = mybir.dt.int32
U8 = mybir.dt.uint8
ALU = mybir.AluOpType
AF = mybir.ActivationFunctionType
DR = mybir.MatmulPerfMode.DoubleRow
AX = mybir.AxisListType
BF = ml_dtypes.bfloat16

D = 1024
NG = 8
TOK = 2048
COLS = 2052
SEQ = 8192
HEADS = 8
DH = 64
WIN = 512
INNER = 512
HID = 2730
NHC = 22
HC_LAST = 42
QK_SCALE = 8.0
NEG = -1e30
LNEPS = 1e-5

DEBUG = []
PHASES = "ABC"
ALIAS_CHANNELS = False


class Chan:
    __slots__ = ("sem", "n", "name")
    def __init__(self, sem, name):
        self.sem = sem
        self.n = 0
        self.name = name


class Prog:
    ENGINES = ("tensor", "vector", "scalar", "gpsimd", "sync")

    def __init__(self, nc, es):
        self.nc = nc
        self.es = es
        self.ops = {e: [] for e in self.ENGINES}
        self.dumps = []

    def chan(self, name):
        sem = self.es.enter_context(self.nc.semaphore(name))
        return Chan(sem, name)

    def op(self, engine, fn, waits=()):
        self.ops[engine].append([tuple(waits), fn, None, 0])

    def tick(self, engine, ch, by=1):
        rec = self.ops[engine][-1]
        assert rec[2] is None, "op already ticked"
        rec[2] = ch
        rec[3] = by
        ch.n += by
        return (ch, ch.n)

    def dma(self, engine, out, in_, ch, waits=(), **kw):
        self.op(engine, lambda e, out=out, in_=in_, kw=kw: e.dma_start(out=out, in_=in_, **kw),
                waits=waits)
        return self.tick(engine, ch, by=16)

    def emit(self):
        block = self.es.enter_context(self.nc.Block())
        for ename in self.ENGINES:
            ops = self.ops[ename]
            if not ops:
                continue

            def body(eng, ops=ops):
                last_wait = {}
                for waits, fn, inc_ch, inc_by in ops:
                    for (ch, cnt) in waits:
                        if cnt > last_wait.get(ch.name, -1):
                            eng.wait_ge(ch.sem, cnt)
                            last_wait[ch.name] = cnt
                    ins = fn(eng)
                    if inc_ch is not None:
                        ins.then_inc(inc_ch.sem, inc_by)
            getattr(block, ename)(body)


class Arena:
    def __init__(self, nc, name, nbytes):
        nbytes = (nbytes + 63) & ~63
        self.h = nc.alloc_sbuf_tensor(name, [128, nbytes], U8)
        self.nbytes = nbytes

    @staticmethod
    def _shape(ap, shape):
        if len(shape) <= 1:
            return ap
        names = "abcdefg"[: len(shape)]
        spec = "p (" + " ".join(names) + ") -> p " + " ".join(names)
        kw = {names[i]: shape[i] for i in range(1, len(shape))}
        return ap.rearrange(spec, **kw)

    def f32(self, off, *shape):
        n = int(np.prod(shape))
        assert off % 64 == 0 and off + 4 * n <= self.nbytes, (off, shape, self.nbytes)
        return self._shape(self.h.ap().bitcast(F32)[:, off // 4: off // 4 + n], shape)

    def bf16(self, off, *shape):
        n = int(np.prod(shape))
        assert off % 64 == 0 and off + 2 * n <= self.nbytes, (off, shape, self.nbytes)
        return self._shape(self.h.ap().bitcast(BF16)[:, off // 2: off // 2 + n], shape)


def _rot_tables():
    half = np.arange(0, DH, 2, dtype=np.float64)
    inv_freq = 1.0 / 10000.0 ** (half / DH)
    t = np.arange(SEQ, dtype=np.float64)
    fr = t[:, None] * inv_freq[None, :]
    fr = np.concatenate([fr, fr], axis=-1)
    base = (half + 0.4 * DH) / (1.4 * DH)
    power = (t - SEQ // 2) / float(WIN // 2)
    scale = np.concatenate([base[None, :] ** power[:, None]] * 2, axis=-1)
    cos = np.cos(fr)
    sin = np.sin(fr)
    sign = np.concatenate([-np.ones(DH // 2), np.ones(DH // 2)])[None, :]
    Aq = (cos * scale).astype(np.float32)
    Bq = (sign * sin * scale).astype(np.float32)
    Ak = (cos / scale).astype(np.float32)
    Bk = (sign * sin / scale).astype(np.float32)
    return Aq, Bq, Ak, Bk


def _edge_masks():
    i = np.arange(128)[:, None]
    j = np.arange(128)[None, :]
    d0 = np.where(j < i, NEG, 0.0).astype(BF)
    d4 = np.where(j > i, NEG, 0.0).astype(BF)
    return d0, d4


def _w0_masks(is_s0):
    i = np.arange(128)[:, None]
    j = np.arange(128)[None, :]
    tiles = np.zeros((4, 5, 128, 128), np.float32)
    for qi in range(4):
        for bs in range(5):
            jc = qi + bs
            if bs == 0:
                m = np.where(j < i, NEG, 0.0)
            elif bs == 4:
                m = np.where(j > i, NEG, 0.0)
            else:
                m = np.zeros((128, 128), np.float32)
            if is_s0 and jc < 4:
                m = np.full((128, 128), NEG, np.float32)
            tiles[qi, bs] = m
    return tiles.astype(BF)


def stage_inputs(inputs):
    f = np.float32
    x = np.ascontiguousarray(inputs["L0_attn_x"], dtype=f)
    Wt = np.ascontiguousarray(np.asarray(inputs["L0_attn_W"], dtype=f).T)
    dec = np.ascontiguousarray(np.asarray(inputs["L0_attn_decay_logit"], f).reshape(NG, 128).T)
    bias_b = np.ascontiguousarray(np.asarray(inputs["L0_attn_b"], f).reshape(NG, 128).T)
    conv_b = np.ascontiguousarray(np.asarray(inputs["L0_attn_conv_b"], f).reshape(NG, 128).T)
    conv_w = np.ascontiguousarray(
        np.asarray(inputs["L0_attn_conv_w"], f).reshape(NG, 128, 5).transpose(1, 0, 2))
    wqkv = np.ascontiguousarray(np.asarray(inputs["L0_ffn_wqkv"], f))
    wo = np.ascontiguousarray(np.asarray(inputs["L0_ffn_wo"], f))
    w1p = np.zeros((2, D, 2 * NHC * 128), f)
    w1p[:, :, :HID] = np.asarray(inputs["L0_ffn_w1"], f)[:, :, :HID]
    w1p[:, :, NHC * 128: NHC * 128 + HID] = np.asarray(inputs["L0_ffn_w1"], f)[:, :, HID:]
    w2p = np.zeros((2, NHC * 128, 1024), f)
    w2p[:, :HID] = np.asarray(inputs["L0_ffn_w2"], f)
    qsr = np.ascontiguousarray(np.repeat(np.asarray(inputs["L0_ffn_q_scale"], f)[:, None, :], 128, 1))
    ksr = np.ascontiguousarray(np.repeat(np.asarray(inputs["L0_ffn_k_scale"], f)[:, None, :], 128, 1))
    ln_g = np.ascontiguousarray(np.asarray(inputs["L0_ffn_ln_g"], f).reshape(2, NG, 128).transpose(2, 0, 1))
    ln_b = np.ascontiguousarray(np.asarray(inputs["L0_ffn_ln_b"], f).reshape(2, NG, 128).transpose(2, 0, 1))
    b1p = np.zeros((2, 2, NHC * 128), f)
    b1p[:, 0, :HID] = np.asarray(inputs["L0_ffn_b1"], f)[:, :HID]
    b1p[:, 1, :HID] = np.asarray(inputs["L0_ffn_b1"], f)[:, HID:]
    b1s = np.ascontiguousarray(b1p.reshape(2, 2, NHC, 128).transpose(3, 0, 1, 2))
    b2s = np.ascontiguousarray(np.asarray(inputs["L0_ffn_b2"], f).reshape(2, NG, 128).transpose(2, 0, 1))

    Aq, Bq, Ak, Bk = _rot_tables()
    d0, d4 = _edge_masks()
    shared = dict(
        Wt=Wt, dec=dec, bias_b=bias_b, conv_b=conv_b, conv_w=conv_w,
        wqkv=wqkv, wo=wo, w1=w1p, w2=w2p, qsr=qsr, ksr=ksr,
        qsr_shuf=np.ascontiguousarray(qsr[:, :, [32 + i for i in range(32)] + list(range(32))]),
        ksr_shuf=np.ascontiguousarray(ksr[:, :, [32 + i for i in range(32)] + list(range(32))]),
        identf=np.eye(128).astype(f),
        bndmask3=np.ascontiguousarray(np.broadcast_to(
            np.where(np.arange(128)[None, :] < 127, NEG, 0.0).astype(BF), (128, 128))),
        ln_g=ln_g, ln_b=ln_b, b1=b1s, b2=b2s,
        mask_d0=np.ascontiguousarray(d0), mask_d4=np.ascontiguousarray(d4),
        ident=np.eye(128).astype(BF),
        ones_col=np.ones((128, 1), BF),
        ones1_128=np.ones((1, 128), BF),
        ones1_64=np.ones((1, 64), BF),
        on64_d=np.ones((65, 64), BF),
    )

    in_maps = []
    for c in range(8):
        b, s = c // 4, c % 4
        t0 = TOK * s
        xs = np.zeros((D, COLS), f)
        lo, hi = t0 - 2, t0 + TOK + 2
        slo, shi = max(lo, 0), min(hi, SEQ)
        xs[:, slo - lo: shi - lo] = x[b, slo:shi, :].T
        xs = np.ascontiguousarray(xs)

        def tt(tab):
            return np.ascontiguousarray(
                tab[t0:t0 + TOK].reshape(16, 128, DH).transpose(1, 0, 2))
        m_sel = np.zeros((128, 4), f)
        e_sel = np.zeros((128, 4), f)
        for r in range(4):
            if r < s:
                m_sel[:, r] = 1.0
                e_sel[:, r] = float(s - 1 - r)
        gate = np.full((128, 1), 0.0 if s == 0 else 1.0, f)
        m = dict(shared)
        m.update(
            x_in=xs,
            tabAq=tt(Aq), tabBq=tt(Bq), tabAk=tt(Ak), tabBk=tt(Bk),
            m_sel=m_sel, e_sel=e_sel, gate=gate,
            s0neg=np.full((1, 512), NEG if s == 0 else 0.0, BF),
            prev_off=np.array([[(s - 1) % 4]], np.int32),
        )
        in_maps.append(m)
    return in_maps


def build_program(debug=(), phases="ABC"):
    nc = bass.Bass("TRN2", target_bir_lowering=False)
    es = ExitStack()
    P = Prog(nc, es)

    def din(name, shape, dt=F32):
        return nc.dram_tensor(name, shape, dt, kind="ExternalInput").ap()

    x_in = din("x_in", [D, COLS])
    Wt_d = din("Wt", [D, D])
    dec_d = din("dec", [128, NG])
    bias_b_d = din("bias_b", [128, NG])
    conv_b_d = din("conv_b", [128, NG])
    conv_w_d = din("conv_w", [128, NG, 5])
    wqkv_d = din("wqkv", [2, D, 3 * INNER])
    wo_d = din("wo", [2, INNER, D])
    w1_d = din("w1", [2, D, 2 * NHC * 128])
    w2_d = din("w2", [2, NHC * 128, D])
    qsr_d = din("qsr", [2, 128, DH])
    ksr_d = din("ksr", [2, 128, DH])
    ln_g_d = din("ln_g", [128, 2, NG])
    ln_b_d = din("ln_b", [128, 2, NG])
    b1_d = din("b1", [128, 2, 2, NHC])
    b2_d = din("b2", [128, 2, NG])
    qsr_shuf_d = din("qsr_shuf", [2, 128, DH])
    ksr_shuf_d = din("ksr_shuf", [2, 128, DH])
    identf_d = din("identf", [128, 128])
    bndmask3_d = din("bndmask3", [128, 128], BF16)
    s0neg_d = din("s0neg", [1, 512], BF16)
    mask_d0_d = din("mask_d0", [128, 128], BF16)
    mask_d4_d = din("mask_d4", [128, 128], BF16)
    ident_d = din("ident", [128, 128], BF16)
    ones_col_d = din("ones_col", [128, 1], BF16)
    ones1_128_d = din("ones1_128", [1, 128], BF16)
    ones1_64_d = din("ones1_64", [1, 64], BF16)
    on64_d = din("on64_d", [65, 64], BF16)
    tabAq_d = din("tabAq", [128, 16, DH])
    tabBq_d = din("tabBq", [128, 16, DH])
    tabAk_d = din("tabAk", [128, 16, DH])
    tabBk_d = din("tabBk", [128, 16, DH])
    m_sel_d = din("m_sel", [128, 4])
    e_sel_d = din("e_sel", [128, 4])
    gate_d = din("gate", [128, 1])
    prev_off_d = din("prev_off", [1, 1], I32)

    y_out = nc.dram_tensor("y", [D, TOK], F32, kind="ExternalOutput").ap()

    x2_spill = nc.dram_tensor("x2_spill", [D, TOK], F32, kind="Internal").ap()
    cc1_in = nc.dram_tensor("cc1_in", [128, NG], F32, kind="Internal").ap()
    cc1_out = nc.dram_tensor("cc1_out", [4 * 128, NG], F32, kind="Internal").ap()
    cc2_in = nc.dram_tensor("cc2_in", [128, NG], F32, kind="Internal").ap()
    cc2_out = nc.dram_tensor("cc2_out", [4 * 128, NG], F32, kind="Internal").ap()
    R3 = 1552
    cc3_in = [nc.dram_tensor(f"cc3_in{l}", [R3, 1040], BF16, kind="Internal").ap() for l in range(2)]
    cc3_out = [nc.dram_tensor(f"cc3_out{l}", [4 * R3, 1040], BF16, kind="Internal").ap() for l in range(2)]
    cc4_in = nc.dram_tensor("cc4_in", [128, NG], F32, kind="Internal").ap()
    cc4_out = nc.dram_tensor("cc4_out", [4 * 128, NG], F32, kind="Internal").ap()
    RG = [[0, 1, 2, 3], [4, 5, 6, 7]]

    AR_S = Arena(nc, "arS", NG * COLS * 4)
    AR_L = Arena(nc, "arL", NG * COLS * 4)
    AR_C = Arena(nc, "arC", 50496)

    def sb(name, shape, dt=F32):
        return es.enter_context(nc.sbuf_tensor(name, shape, dt))

    ident = sb("t_ident", [128, 128], BF16)
    mask_d0 = sb("t_mask_d0", [128, 128], BF16)
    mask_d4 = sb("t_mask_d4", [128, 128], BF16)
    ones_col = sb("t_ones_col", [128, 1], BF16)
    ones1_128 = sb("t_ones1_128", [1, 128], BF16)
    ones1_64 = sb("t_ones1_64", [1, 64], BF16)
    dec = sb("t_dec", [128, NG])
    smalla = sb("smalla", [128, NG])
    smallb = sb("smallb", [128, NG])
    a_col = sb("a_col", [128, NG])
    om_a = sb("om_a", [128, NG])
    a2k = sb("a2k", [128, NG])
    lnA = sb("lnA", [128, NG])
    coef = sb("coef", [128, 4, NG])
    bcb = sb("bcb", [128, NG])
    conv_w = sb("t_conv_w", [128, NG, 5])
    qsr = sb("t_qsr", [128, 2, DH])
    ksr = sb("t_ksr", [128, 2, DH])
    ln_g = sb("t_ln_g", [128, 2, NG])
    ln_b = sb("t_ln_b", [128, 2, NG])
    b1 = sb("t_b1", [128, 2, 2, NHC])
    b2 = sb("t_b2", [128, 2, NG])
    m_sel = sb("t_m_sel", [128, 4])
    e_sel = sb("t_e_sel", [128, 4])
    gate = sb("t_gate", [128, 1])
    poff = sb("poff", [1, 1], I32)
    hb1 = sb("hb1", [128, NG])
    hb2 = sb("hb2", [128, NG])
    wc_all = sb("wc_all", [128, 4, NG])
    sinit = sb("sinit", [128, NG])
    qsr_shuf = sb("t_qsr_shuf", [128, 2, DH])
    ksr_shuf = sb("t_ksr_shuf", [128, 2, DH])
    identf = sb("t_identf", [128, 128])
    bndmask3 = sb("t_bndmask3", [128, 128], BF16)
    s0neg = sb("t_s0neg", [1, 512], BF16)
    negm = sb("negm", [1, 512], BF16)
    r64 = sb("r64", [65, 1024])
    r64h = sb("r64h", [65, 1024], BF16)
    on64 = sb("on64", [65, 64], BF16)
    rb64 = sb("rb64", [64, 512])
    mst = sb("mst", [128, 4])
    qbndt = sb("qbndt", [128, 4], BF16)
    qfk = sb("qfk", [128, 4, 128], BF16)

    nrm = sb("nrm", [128, 8])
    nrm2 = sb("nrm2", [128, 8])
    rqv = sb("rqv", [128, 8])
    qbf = AR_C.bf16(34112, 2, 512)
    ub = [AR_C.bf16(40256, 8, 129), AR_C.bf16(44352, 8, 129)]
    r_mu = sb("r_mu", [1, 512])
    r_var = sb("r_var", [1, 512])
    r_rstd = sb("r_rstd", [1, 512])
    r_mrs = sb("r_mrs", [1, 512])
    rb_hi = sb("rb_hi", [1, 512], BF16)
    rb_lo = sb("rb_lo", [1, 512], BF16)
    mb_hi = sb("mb_hi", [1, 512], BF16)
    mb_lo = sb("mb_lo", [1, 512], BF16)


    PB = [es.enter_context(nc.psum_tensor(f"PB{i}", [128, 512], F32)) for i in range(4)]
    PS640 = es.enter_context(nc.psum_tensor("PS640", [128, 640], F32))
    PW = es.enter_context(nc.psum_tensor("PW", [65, 512], F32))
    PT = es.enter_context(nc.psum_tensor("PT", [128, 512], BF16))

    cLD = P.chan("cLD")
    cSM = P.chan("cSM")
    cLX = P.chan("cLX")
    cSP = P.chan("cSP")
    cTB = [P.chan("cTB0"), P.chan("cTB1")]
    cCCP = P.chan("cCCP")
    cCCR = P.chan("cCCR")
    cWQ = [P.chan("cWQ0"), P.chan("cWQ1")]
    cWT = [P.chan("cWT0"), P.chan("cWT1")]
    cWO = [P.chan("cWO0"), P.chan("cWO1")]
    cW1 = [P.chan(f"cW1{i}") for i in range(4)]
    cW2 = [P.chan(f"cW2{i}") for i in range(4)]
    cLG = P.chan("cLG")
    cST = P.chan("cST")
    if ALIAS_CHANNELS:
        cSM = cLX = cSP = cCCP = cCCR = cLD
        cTB = [cLD, cLD]
        cWQ = [cLG, cLG]
        cWT = [cLD, cLD]
        cWO = [cLG, cLG]
        cW1 = [cLG] * 4
        cW2 = [cLG] * 4
    cPE = P.chan("cPE")
    cDVE = P.chan("cDVE")
    cACT = P.chan("cACT")
    cPOOL = P.chan("cPOOL")
    cCC = P.chan("cCC")

    dump_toks = []

    def dbg_guard():
        return [(cST, cST.n)] if debug else []

    def dump(name, view3d, ncols, waits):
        if name not in debug:
            return
        dd = nc.dram_tensor("dbg_" + name, [D, ncols], F32, kind="ExternalOutput").ap()
        t = P.dma("sync", dd.rearrange("(g p) t -> p g t", p=128), view3d, cST, waits=waits)
        dump_toks.append(t)

    vS = AR_S.f32(0, NG, COLS)
    vL = AR_L.f32(0, NG, COLS)

    tok_x = P.dma("sync", vS, x_in.rearrange("(g p) t -> p g t", p=128), cLX)
    small_loads = [
        (dec, dec_d), (bcb, bias_b_d), (conv_w, conv_w_d),
        (qsr, qsr_d.rearrange("l p d -> p l d")), (ksr, ksr_d.rearrange("l p d -> p l d")),
        (ln_g, ln_g_d), (ln_b, ln_b_d), (b1, b1_d), (b2, b2_d),
        (m_sel, m_sel_d), (e_sel, e_sel_d), (gate, gate_d), (poff, prev_off_d),
        (smalla, conv_b_d),
        (ident, ident_d), (mask_d0, mask_d0_d), (mask_d4, mask_d4_d),
        (qsr_shuf, qsr_shuf_d.rearrange("l p d -> p l d")),
        (ksr_shuf, ksr_shuf_d.rearrange("l p d -> p l d")),
        (identf, identf_d), (bndmask3, bndmask3_d), (s0neg, s0neg_d),
        (ones_col, ones_col_d), (ones1_128, ones1_128_d), (ones1_64, ones1_64_d),
    ]
    for dst, src in small_loads:
        tok_small = P.dma("sync", dst[:], src, cSM)
    tok_on64 = P.dma("sync", on64[:, :], on64_d, cSM)
    tok_small = tok_on64
    P.op("vector", lambda e: e.tensor_add(out=bcb[:], in0=bcb[:], in1=smalla[:]),
         waits=[tok_small])
    tok_bcb = P.tick("vector", cDVE)

    def chain(engine, fn, waits, ch):
        P.op(engine, fn, waits=waits)
        return P.tick(engine, ch)

    t = chain("scalar", lambda e: e.activation(out=smalla[:], in_=dec[:], func=AF.Exp,
                                               scale=-1.0), [tok_small, tok_bcb], cACT)
    t = chain("vector", lambda e: e.tensor_scalar(out=smalla[:], in0=smalla[:], scalar1=1.0,
                                                  scalar2=None, op0=ALU.add), [t], cDVE)
    t = chain("vector", lambda e: e.reciprocal(out=a_col[:], in_=smalla[:]), [t], cDVE)
    tok_a = t
    t = chain("vector", lambda e: e.tensor_scalar(out=om_a[:], in0=a_col[:], scalar1=-1.0,
                                                  scalar2=-1.0, op0=ALU.mult,
                                                  op1=ALU.subtract), [t], cDVE)
    tok_oma = t
    cur, oth = a_col, None
    t = chain("vector", lambda e: e.tensor_mul(out=smalla[:], in0=a_col[:], in1=a_col[:]),
              [t], cDVE)
    cur, oth = smalla, smallb
    for i in range(10):
        t = chain("vector", lambda e, cur=cur, oth=oth: e.tensor_mul(
            out=oth[:], in0=cur[:], in1=cur[:]), [t], cDVE)
        cur, oth = oth, cur
    t = chain("vector", lambda e, cur=cur: e.tensor_scalar_max(out=a2k[:], in0=cur[:],
                                                               scalar1=1e-30), [t], cDVE)
    t = chain("scalar", lambda e: e.activation(out=lnA[:], in_=a2k[:], func=AF.Ln), [t], cACT)
    for r in range(4):
        t = chain("vector", lambda e, r=r: e.tensor_scalar(
            out=coef[:, r, :], in0=lnA[:], scalar1=e_sel[:, r:r + 1], scalar2=None,
            op0=ALU.mult), [t], cDVE)
    t = chain("scalar", lambda e: e.activation(out=coef[:], in_=coef[:], func=AF.Exp), [t], cACT)
    for r in range(4):
        t = chain("vector", lambda e, r=r: e.tensor_scalar(
            out=coef[:, r, :], in0=coef[:, r, :], scalar1=m_sel[:, r:r + 1], scalar2=None,
            op0=ALU.mult), [t], cDVE)
    tok_coef = t

    stgA = AR_C.bf16(16384, NG, 512)
    stgB = AR_C.bf16(24768, NG, 512)
    scr1 = AR_C.f32(0, 2048)
    scr2 = AR_C.f32(8192, 2048)
    bc_rb = AR_C.f32(44352, 512)
    bc_mb = AR_C.f32(46400, 512)
    bc_t1 = AR_C.f32(48448, 512)
    pr_s1 = AR_C.f32(16704, 512)
    pr_s2 = AR_C.f32(18752, 512)
    pr_s3 = AR_C.f32(20800, 512)

    ln_state = {"mm": None, "apply": None, "rows": None}

    def layernorm(src3, tiles, apply_ops, waits_src):
        for (c0, w) in tiles:
            cast_waits = list(waits_src)
            if ln_state["mm"] is not None:
                cast_waits.append(ln_state["mm"])
            for g in range(NG):
                P.op("scalar", lambda e, g=g, c0=c0, w=w: e.activation(
                    out=stgA[:, g, 0:w], in_=src3(g, c0, w), func=AF.Identity),
                    waits=cast_waits)
                P.op("scalar", lambda e, g=g, c0=c0, w=w: e.activation(
                    out=stgB[:, g, 0:w], in_=src3(g, c0, w), func=AF.Square))
            tok_cast = P.tick("scalar", cACT)
            mm_waits = [tok_cast]
            if ln_state["rows"] is not None:
                mm_waits.append(ln_state["rows"])
            for g in range(NG):
                P.op("tensor", lambda e, g=g, w=w: e.matmul(
                    PB[0].ap()[0:1, 0:w], ones_col[:, 0:1], stgA[:, g, 0:w],
                    start=(g == 0), stop=(g == NG - 1)), waits=mm_waits)
            for g in range(NG):
                P.op("tensor", lambda e, g=g, w=w: e.matmul(
                    PB[1].ap()[0:1, 0:w], ones_col[:, 0:1], stgB[:, g, 0:w],
                    start=(g == 0), stop=(g == NG - 1)))
            tok_mm = P.tick("tensor", cPE)
            ln_state["mm"] = tok_mm
            P.op("vector", lambda e, w=w: e.tensor_scalar(
                out=r_mu[0:1, 0:w], in0=PB[0].ap()[0:1, 0:w], scalar1=1.0 / D, scalar2=None,
                op0=ALU.mult), waits=[tok_mm])
            P.op("vector", lambda e, w=w: e.tensor_scalar(
                out=r_var[0:1, 0:w], in0=PB[1].ap()[0:1, 0:w], scalar1=1.0 / D, scalar2=None,
                op0=ALU.mult))
            tok_rows = P.tick("vector", cDVE)
            ln_state["rows"] = tok_rows
            P.op("vector", lambda e, w=w: e.tensor_mul(
                out=r_mrs[0:1, 0:w], in0=r_mu[0:1, 0:w], in1=r_mu[0:1, 0:w]))
            P.op("vector", lambda e, w=w: e.scalar_tensor_tensor(
                out=r_var[0:1, 0:w], in0=r_var[0:1, 0:w], scalar=LNEPS, in1=r_mrs[0:1, 0:w],
                op0=ALU.add, op1=ALU.subtract))
            tok_var = P.tick("vector", cDVE)
            P.op("scalar", lambda e, w=w: e.activation(
                out=r_mrs[0:1, 0:w], in_=r_var[0:1, 0:w], func=AF.Ln), waits=[tok_var])
            P.op("scalar", lambda e, w=w: e.activation(
                out=r_rstd[0:1, 0:w], in_=r_mrs[0:1, 0:w], func=AF.Exp, scale=-0.5))
            tok_rstd = P.tick("scalar", cACT)
            P.op("vector", lambda e, w=w: e.tensor_mul(
                out=r_mrs[0:1, 0:w], in0=r_mu[0:1, 0:w], in1=r_rstd[0:1, 0:w]), waits=[tok_rstd])
            tok_mrs = P.tick("vector", cDVE)
            P.op("scalar", lambda e, w=w: e.activation(
                out=rb_hi[0:1, 0:w], in_=r_rstd[0:1, 0:w], func=AF.Identity), waits=[tok_mrs])
            P.op("scalar", lambda e, w=w: e.activation(
                out=mb_hi[0:1, 0:w], in_=r_mrs[0:1, 0:w], func=AF.Identity))
            P.op("scalar", lambda e, w=w: e.activation(
                out=r_mu[0:1, 0:w], in_=rb_hi[0:1, 0:w], func=AF.Identity))
            tok_hi = P.tick("scalar", cACT)
            P.op("vector", lambda e, w=w: e.tensor_sub(
                out=r_mu[0:1, 0:w], in0=r_rstd[0:1, 0:w], in1=r_mu[0:1, 0:w]), waits=[tok_hi])
            tok_lof1 = P.tick("vector", cDVE)
            P.op("scalar", lambda e, w=w: e.activation(
                out=rb_lo[0:1, 0:w], in_=r_mu[0:1, 0:w], func=AF.Identity), waits=[tok_lof1])
            P.op("scalar", lambda e, w=w: e.activation(
                out=r_mu[0:1, 0:w], in_=mb_hi[0:1, 0:w], func=AF.Identity))
            tok_hi2 = P.tick("scalar", cACT)
            P.op("vector", lambda e, w=w: e.tensor_sub(
                out=r_mu[0:1, 0:w], in0=r_mrs[0:1, 0:w], in1=r_mu[0:1, 0:w]), waits=[tok_hi2])
            tok_lof2 = P.tick("vector", cDVE)
            P.op("scalar", lambda e, w=w: e.activation(
                out=mb_lo[0:1, 0:w], in_=r_mu[0:1, 0:w], func=AF.Identity), waits=[tok_lof2])
            tok_lo = P.tick("scalar", cACT)
            bc_waits = [tok_lo]
            if ln_state["apply"] is not None:
                bc_waits.append(ln_state["apply"])
            P.op("tensor", lambda e, w=w: e.matmul(
                PB[2].ap()[:, 0:w], ones1_128[0:1, :], rb_hi[0:1, 0:w], start=True, stop=False),
                waits=bc_waits)
            P.op("tensor", lambda e, w=w: e.matmul(
                PB[2].ap()[:, 0:w], ones1_128[0:1, :], rb_lo[0:1, 0:w], start=False, stop=True))
            P.op("tensor", lambda e, w=w: e.matmul(
                PB[3].ap()[:, 0:w], ones1_128[0:1, :], mb_hi[0:1, 0:w], start=True, stop=False))
            P.op("tensor", lambda e, w=w: e.matmul(
                PB[3].ap()[:, 0:w], ones1_128[0:1, :], mb_lo[0:1, 0:w], start=False, stop=True))
            tok_bc = P.tick("tensor", cPE)
            P.op("scalar", lambda e, w=w: e.activation(
                out=bc_rb[:, 0:w], in_=PB[2].ap()[:, 0:w], func=AF.Identity), waits=[tok_bc])
            P.op("scalar", lambda e, w=w: e.activation(
                out=bc_mb[:, 0:w], in_=PB[3].ap()[:, 0:w], func=AF.Identity))
            tok_bcs = P.tick("scalar", cACT)
            apply_ops(c0, w, tok_bcs)
            ln_state["apply"] = P.tick("vector", cDVE)
        return ln_state["apply"]

    tok_attn_out_done = None
    tok_u0 = None
    tok_spill = None
    if "A" in phases:
        tiles_full = [(0, 512), (512, 512), (1024, 512), (1536, 512), (1540, 512)]

        def apply_ln1(c0, w, tok_bcs):
            for g in range(NG):
                P.op("vector", lambda e, g=g, c0=c0, w=w: e.tensor_mul(
                    out=bc_t1[:, 0:w], in0=vS[:, g, c0:c0 + w], in1=bc_rb[:, 0:w]),
                    waits=[tok_bcs])
                P.op("vector", lambda e, g=g, c0=c0, w=w: e.tensor_sub(
                    out=vL[:, g, c0:c0 + w], in0=bc_t1[:, 0:w], in1=bc_mb[:, 0:w]))

        tok_ln1 = layernorm(lambda g, c0, w: vS[:, g, c0:c0 + w], tiles_full, apply_ln1, [tok_x])
        dump("ln1", vL[:, :, 2:2050], 2048, [tok_ln1])
        if "bcdbg" in debug:
            dbg1 = nc.dram_tensor("dbg_bcdbg", [128, 1024], F32, kind="ExternalOutput").ap()
            dbg2 = nc.dram_tensor("dbg_rows", [1, 2048], F32, kind="ExternalOutput").ap()
            dump_toks.append(P.dma("sync", dbg1[:, 0:512], bc_rb[:], cST, waits=[tok_ln1]))
            dump_toks.append(P.dma("sync", dbg1[:, 512:1024], bc_mb[:], cST))
            dump_toks.append(P.dma("sync", dbg2[0:1, 0:512], r_mu[:], cST))
            dump_toks.append(P.dma("sync", dbg2[0:1, 512:1024], r_var[:], cST))
            dump_toks.append(P.dma("sync", dbg2[0:1, 1024:1536], r_rstd[:], cST))
            dump_toks.append(P.dma("sync", dbg2[0:1, 1536:2048], r_mrs[:], cST))

        P.op("gpsimd", lambda e: e.memset(scr2[:], 1.0))
        tok_ones = P.tick("gpsimd", cPOOL)
        scan_toks = []
        for g in range(NG):
            P.op("vector", lambda e, g=g: e.tensor_scalar(
                out=scr1[:], in0=scr2[:], scalar1=a_col[:, g:g + 1], scalar2=None, op0=ALU.mult),
                waits=[tok_ones, tok_a, tok_ln1])
            tok_ab = P.tick("vector", cDVE)
            P.op("vector", lambda e, g=g: e.tensor_tensor_scan(
                out=vS[:, g, 2:2050], data0=scr1[:], data1=vL[:, g, 2:2050],
                initial=0.0, op0=ALU.mult, op1=ALU.add), waits=[tok_ab])
            scan_toks.append(P.tick("vector", cDVE))
        P.op("vector", lambda e: e.tensor_copy(out=hb1[:], in_=vS[:, :, 2049]), waits=[scan_toks[-1]])
        tok_wc = P.tick("vector", cDVE)
        t1 = P.dma("sync", cc1_in, hb1[:], cCCP, waits=[tok_wc])
        P.op("gpsimd", lambda e: e.collective_compute(
            "AllGather", ALU.bypass, replica_groups=RG,
            ins=[cc1_in.opt()], outs=[cc1_out.opt()]), waits=[t1])
        tok_cc1 = P.tick("gpsimd", cCC)
        t2 = P.dma("gpsimd", wc_all[:], cc1_out.rearrange("(r p) g -> p r g", p=128), cCCR,
                   waits=[tok_cc1])
        ts = chain("vector", lambda e: e.tensor_mul(out=sinit[:], in0=coef[:, 0, :],
                                                    in1=wc_all[:, 0, :]), [t2, tok_coef], cDVE)
        for r in range(1, 4):
            ts = chain("vector", lambda e, r=r: e.tensor_mul(out=smalla[:], in0=coef[:, r, :],
                                                             in1=wc_all[:, r, :]), [ts], cDVE)
            ts = chain("vector", lambda e, r=r: e.tensor_add(out=sinit[:], in0=sinit[:],
                                                             in1=smalla[:]), [ts], cDVE)
        tok_sinit = ts
        if "emadbg" in debug:
            dbge = nc.dram_tensor("dbg_emadbg", [128, 512], F32, kind="ExternalOutput").ap()
            emadump = bc_rb
            P.op("vector", lambda e: e.tensor_copy(out=emadump[:, 0:32], in_=wc_all[:].rearrange("p a b -> p (a b)")), waits=[tok_sinit])
            P.op("vector", lambda e: e.tensor_copy(out=emadump[:, 32:40], in_=sinit[:]))
            P.op("vector", lambda e: e.tensor_copy(out=emadump[:, 40:72], in_=coef[:].rearrange("p a b -> p (a b)")))
            P.op("vector", lambda e: e.tensor_copy(out=emadump[:, 72:80], in_=a2k[:]))
            P.op("vector", lambda e: e.tensor_copy(out=emadump[:, 80:88], in_=a_col[:]))
            P.op("vector", lambda e: e.tensor_copy(out=emadump[:, 88:96], in_=lnA[:]))
            tok_ed = P.tick("vector", cDVE)
            dump_toks.append(P.dma("sync", dbge, emadump[:], cST, waits=[tok_ed]))
        for g in range(NG):
            P.op("vector", lambda e, g=g: e.tensor_scalar(
                out=scr1[:], in0=scr2[:], scalar1=a_col[:, g:g + 1], scalar2=None, op0=ALU.mult),
                waits=[tok_sinit])
            tok_ab = P.tick("vector", cDVE)
            P.op("vector", lambda e, g=g: e.tensor_tensor_scan(
                out=vS[:, g, 2:2050], data0=scr1[:], data1=vL[:, g, 2:2050],
                initial=sinit[:, g:g + 1], op0=ALU.mult, op1=ALU.add), waits=[tok_ab])
            P.tick("vector", cDVE)
        tok_scan2 = (cDVE, cDVE.n)
        dump("S", vS[:, :, 2:2050], 2048, [tok_scan2])

        wtb = [AR_C.bf16(33088, NG, 128), AR_C.bf16(35136, NG, 128)]
        tok_prev_evac = [tok_scan2, tok_scan2]
        tok_wtb_rd = [None, None]
        tok_prev_mm = None
        last_conv = None
        for tt in range(4):
            c0 = 2 + 512 * tt
            cast_w = [tok_scan2] + dbg_guard()
            if tok_prev_mm is not None:
                cast_w.append(tok_prev_mm)
            P.op("scalar", lambda e, c0=c0: e.activation(
                out=stgA[:, :, 0:512], in_=vS[:, :, c0:c0 + 512], func=AF.Identity),
                waits=cast_w)
            tok_scast = P.tick("scalar", cACT)
            for ec in range(NG):
                bi = ec % 2
                tw_w = [] if tok_wtb_rd[bi] is None else [tok_wtb_rd[bi]]
                tw = P.dma("gpsimd", wtb[bi][:],
                           Wt_d.rearrange("(g p) e -> p g e", p=128)[:, :, 128 * ec:128 * (ec + 1)],
                           cWT[bi], waits=tw_w)
                for g in range(NG):
                    P.op("vector", lambda e, bi=bi, g=g: e.tensor_scalar(
                        out=wtb[bi][:, g, :], in0=wtb[bi][:, g, :], scalar1=om_a[:, g:g + 1],
                        scalar2=None, op0=ALU.mult), waits=[tw])
                tok_fold = P.tick("vector", cDVE)
                pb = PB[ec % 2]
                for g in range(NG):
                    P.op("tensor", lambda e, pb=pb, bi=bi, g=g: e.matmul(
                        pb.ap()[:, :], wtb[bi][:, g, :], stgA[:, g, 0:512],
                        start=(g == 0), stop=(g == NG - 1)),
                        waits=[tok_fold, tok_scast, tok_prev_evac[ec % 2]])
                tok_ymm = P.tick("tensor", cPE)
                tok_wtb_rd[bi] = tok_ymm
                cbase = c0 - 2
                P.op("vector", lambda e, ec=ec, cbase=cbase: e.tensor_scalar(
                    out=scr1[0:128, 0:512], in0=vL[:, ec, cbase:cbase + 512],
                    scalar1=conv_w[:, ec, 0:1], scalar2=None, op0=ALU.mult),
                    waits=[tok_ln1])
                srcs = [scr1, scr2, scr1, scr2, scr1]
                for kk in range(1, 5):
                    P.op("vector", lambda e, ec=ec, cbase=cbase, kk=kk, si=srcs[kk - 1], so=srcs[kk]:
                         e.scalar_tensor_tensor(
                             out=so[0:128, 0:512], in0=vL[:, ec, cbase + kk:cbase + kk + 512],
                             scalar=conv_w[:, ec, kk:kk + 1], in1=si[0:128, 0:512],
                             op0=ALU.mult, op1=ALU.add))
                tok_conv = P.tick("vector", cDVE)
                last_conv = tok_conv
                P.op("vector", lambda e, pb=pb, ec=ec, c0=c0: e.scalar_tensor_tensor(
                    out=vS[:, ec, c0:c0 + 512], in0=pb.ap()[:, :], scalar=bcb[:, ec:ec + 1],
                    in1=scr1[0:128, 0:512], op0=ALU.add, op1=ALU.add),
                    waits=[tok_ymm, tok_bcb, tok_conv])
                tok_prev_evac[ec % 2] = P.tick("vector", cDVE)
            tok_prev_mm = tok_ymm
        tok_attn_out_done = tok_prev_evac[1]
        dump("attn_out", vS[:, :, 2:2050], 2048, [tok_attn_out_done])

        tok_xre = P.dma("sync", vL, x_in.rearrange("(g p) t -> p g t", p=128), cLX,
                        waits=[last_conv] + dbg_guard())
        tiles_own = [(2, 512), (514, 512), (1026, 512), (1538, 512)]

        def apply_x2(c0, w, tok_bcs):
            for g in range(NG):
                P.op("vector", lambda e, g=g, c0=c0, w=w: e.tensor_mul(
                    out=bc_t1[:, 0:w], in0=vS[:, g, c0:c0 + w], in1=bc_rb[:, 0:w]),
                    waits=[tok_bcs, tok_xre] + dbg_guard())
                P.op("vector", lambda e, g=g, c0=c0, w=w: e.tensor_sub(
                    out=scr2[0:128, 0:w], in0=bc_t1[:, 0:w], in1=bc_mb[:, 0:w]))
                P.op("vector", lambda e, g=g, c0=c0, w=w: e.tensor_add(
                    out=vS[:, g, c0:c0 + w], in0=scr2[0:128, 0:w], in1=vL[:, g, c0:c0 + w]))

        tok_x2 = layernorm(lambda g, c0, w: vS[:, g, c0:c0 + w], tiles_own, apply_x2, [tok_attn_out_done])
        dump("x2", vS[:, :, 2:2050], 2048, [tok_x2])
        tok_spill = P.dma("sync", x2_spill.rearrange("(g p) t -> p g t", p=128),
                          vS[:, :, 2:2050], cSP, waits=[tok_x2])

        def apply_u0(c0, w, tok_bcs):
            for g in range(NG):
                P.op("vector", lambda e, g=g, c0=c0, w=w: e.tensor_mul(
                    out=bc_t1[:, 0:w], in0=vS[:, g, c0:c0 + w], in1=bc_rb[:, 0:w]),
                    waits=[tok_bcs])
                P.op("vector", lambda e, g=g, c0=c0, w=w: e.tensor_sub(
                    out=vL[:, g, c0:c0 + w], in0=bc_t1[:, 0:w], in1=bc_mb[:, 0:w]))

        tok_u0 = layernorm(lambda g, c0, w: vS[:, g, c0:c0 + w], tiles_own, apply_u0, [tok_x2])
        P.op("gpsimd", lambda e: e.memset(vL[:, :, 0:2], 0.0), waits=[tok_xre])
        P.op("gpsimd", lambda e: e.memset(vL[:, :, 2050:2052], 0.0))
        tok_hz = P.tick("gpsimd", cPOOL)
        P.op("vector", lambda e: e.tensor_copy(out=hb1[:], in_=vL[:, :, 2049]), waits=[tok_u0])
        tok_b2 = P.tick("vector", cDVE)
        t3 = P.dma("sync", cc2_in, hb1[:], cCCP, waits=[tok_b2])
        P.op("gpsimd", lambda e: e.collective_compute(
            "AllGather", ALU.bypass, replica_groups=RG,
            ins=[cc2_in.opt()], outs=[cc2_out.opt()]), waits=[t3])
        tok_cc2 = P.tick("gpsimd", cCC)
        tok_poff = tok_small

        _poff_cache = {}

        def read_prev(e, base_ap, dst_ap, block_elems):
            if id(e) not in _poff_cache:
                _poff_cache[id(e)] = e.value_load(poff[0:1, 0:1])
            v = _poff_cache[id(e)]
            srcp = dataclasses.replace(base_ap, offset=v * block_elems + base_ap.offset)
            return e.dma_start(out=dst_ap, in_=srcp)

        P.op("gpsimd", lambda e: read_prev(e, cc2_out[0:128, :], hb2[:], 128 * NG),
             waits=[tok_cc2, tok_poff])
        tok_hb = P.tick("gpsimd", cCCR, by=16)
        P.op("vector", lambda e: e.tensor_scalar(
            out=vL[:, :, 1], in0=hb2[:], scalar1=gate[:, 0:1], scalar2=None, op0=ALU.mult),
            waits=[tok_hb, tok_hz])
        tok_halo = P.tick("vector", cDVE)
        dump("u0", vL[:, :, 2:2050], 2048, [tok_u0, tok_halo])

    import dataclasses as _dc

    def apx(ap, dims):
        part = list(ap.ap[0])
        return _dc.replace(ap, ap=[part] + [list(d) for d in dims])

    def offx(ap, delta):
        return _dc.replace(ap, offset=ap.offset + delta)

    def qp_ap(fc):
        return AR_S.bf16(4096 * fc, 2048)
    def kp_ap(fc):
        return AR_S.bf16(16384 + 4096 * fc, 2048)
    vqk_flat = AR_S.bf16(0, 32832)
    hln = AR_S.bf16(0, NG, 2048)
    h1t = AR_S.bf16(32768, NHC, 512)
    x2r = AR_S.f32(0, NG, 2048)

    c_v = AR_C.bf16(0, 16, 520)
    c_oa = AR_C.bf16(16704, 8, 512)
    obnd = AR_C.bf16(16704, 8, 128)
    c_aT = AR_C.bf16(24896, 3, 512)
    c_st = [AR_C.f32(27968, 512), AR_C.f32(30016, 512)]
    c_tab = AR_C.f32(32064, 2, 4, 64)
    c_wq = AR_C.bf16(34112, 2, 1536)
    c_wo = AR_C.bf16(40256, 2, 8, 128)
    def khalo_ap(fc):
        return AR_C.bf16(27968 + 2048 * fc, 1024)
    c_w1 = AR_C.bf16(0, 4, 2, NG, 128)
    c_w2 = AR_C.bf16(16384, 4, 512)

    wqkv3 = wqkv_d.rearrange("l (g p) c -> l p g c", p=128)
    c_wqf = AR_S.bf16(32768, NG, 1536)
    w1r = w1_d.rearrange("l (g p) c -> l p g c", p=128)
    wo_r = wo_d.rearrange("l (h dh) e -> l dh h e", dh=64)

    def exchange_boundary(cc_in_t, cc_out_t, tok_ready):
        P.op("vector", lambda e: e.tensor_copy(out=hb1[:], in_=vL[:, :, 2049]),
             waits=[tok_ready])
        twc = P.tick("vector", cDVE)
        td = P.dma("sync", cc_in_t, hb1[:], cCCP, waits=[twc])
        P.op("gpsimd", lambda e: e.collective_compute(
            "AllGather", ALU.bypass, replica_groups=RG,
            ins=[cc_in_t.opt()], outs=[cc_out_t.opt()]), waits=[td])
        tcc = P.tick("gpsimd", cCC)
        P.op("gpsimd", lambda e: read_prev(e, cc_out_t[0:128, :], hb2[:], 128 * NG),
             waits=[tcc])
        th = P.tick("gpsimd", cCCR, by=16)
        P.op("vector", lambda e: e.tensor_scalar(
            out=vL[:, :, 1], in0=hb2[:], scalar1=gate[:, 0:1], scalar2=None, op0=ALU.mult),
            waits=[th])
        return P.tick("vector", cDVE)

    def _slice_cols(ap2, so, n):
        return ap2[:, so:so + n]

    def mm_bank_group(ops, waits):
        n = len(ops)
        for i, f in enumerate(ops):
            P.op("tensor",
                 (lambda e, f=f, s=(i == 0), t=(i == n - 1): f(e, s, t)),
                 waits=(waits if i == 0 else ()))

    sub = set(c for c in phases if c.isdigit()) or {"1", "2", "3", "4", "5"}
    if "B" in phases:
        tok_u = tok_halo

        for l in range(2):
            st = dict(
                ub_rd=[None, None], ub_tok=[None, None],
                qkv_ev=[None, None, None], vmemset=None,
                tab_rd=[None, None], bct1=None, qbf_rd=None, pt=None,
                wq=[None, None], stats_rd=None, mrow_rd=None,
                st_rd=[None, None], aT_w=[None, None, None], aT_rd=[None, None, None],
                pw_rd=None, oa_rd=None, kv_dma=None,
            )
            P.op("gpsimd", lambda e: (e.memset(c_v[:, :, :], 1.0) if "v" not in phases
                                      else e.engine_nop()), waits=[tok_u])
            st["vmemset"] = P.tick("gpsimd", cPOOL)
            wq_toks = []
            for g in range(NG):
                wq_toks.append(P.dma("gpsimd", c_wqf[:, g, :], wqkv3[l, :, g, :],
                                     cWQ[g % 2], waits=[tok_u] if g < 2 else []))

            pending_tp = []

            def emit_lagged_transpose():
                while pending_tp:
                    pending_tp.pop(0)()
            NTC = 2 if "x" in phases else 16
            for tc in range(NTC):
                bi = tc % 2
                pass
                if tc == 0:
                    uw = [tok_u] + ([st["ub_rd"][bi]] if st["ub_rd"][bi] else [])
                    P.op("vector", lambda e, tc=tc, bi=bi: e.tensor_copy(
                        out=ub[bi][:, :, 0:129], in_=vL[:, :, 1 + 128 * tc: 130 + 128 * tc]),
                        waits=uw)
                    st["ub_tok"][bi] = P.tick("vector", cDVE)
                tok_ub = st["ub_tok"][bi]
                tbi = tc % 2
                for ti, tabd in enumerate([] if "t" in phases else [tabAq_d, tabBq_d, tabAk_d, tabBk_d]):
                    tok_tab = P.dma(
                        "sync", c_tab[:, tbi, ti, :], tabd[:, tc, :], cTB[tbi],
                        waits=([st["tab_rd"][tbi]] if (ti == 0 and st["tab_rd"][tbi]) else []))
                pw0 = [tok_ub, wq_toks[-1]]
                for sec in range(3):
                    if st["qkv_ev"][sec] is not None:
                        pw0.append(st["qkv_ev"][sec])
                for g in range(NG):
                    off = 1 if g < 4 else 0
                    for sec, pb in ((0, PB[0]), (1, PB[1]), (2, PB[2])):
                        P.op("tensor", lambda e, g=g, off=off, sec=sec, pb=pb, bi=bi:
                             e.matmul(
                                 pb.ap()[:, :], ub[bi][:, g, off:off + 128],
                                 c_wqf[:, g, 512 * sec: 512 * (sec + 1)],
                                 start=(g == 0), stop=(g == NG - 1)),
                             waits=(pw0 if g == 0 else ()))
                tok_mm = P.tick("tensor", cPE)
                tok_qkvmm = tok_mm
                st["ub_rd"][bi] = tok_qkvmm
                if tc + 1 < NTC:
                    nbi = (tc + 1) % 2
                    uw = [tok_u] + ([st["ub_rd"][nbi]] if st["ub_rd"][nbi] else [])
                    P.op("vector", lambda e, tc=tc, nbi=nbi: e.tensor_copy(
                        out=ub[nbi][:, :, 0:129],
                        in_=vL[:, :, 1 + 128 * (tc + 1): 130 + 128 * (tc + 1)]),
                        waits=uw)
                    st["ub_tok"][nbi] = P.tick("vector", cDVE)
                emit_lagged_transpose()
                if "e" in phases:
                    st["qkv_ev"] = [tok_qkvmm] * 3
                    continue
                vdst = apx(c_v[:, tc, 0:520], [[65, 8], [1, 64]])
                P.op("scalar", lambda e, vdst=vdst: e.activation(
                    out=vdst, in_=PB[2].ap()[:, :], func=AF.Identity),
                    waits=[tok_qkvmm, st["vmemset"]])
                P.op("scalar", lambda e: e.activation(out=c_st[0][:, :], in_=PB[0].ap()[:, :],
                                                      func=AF.Identity))
                P.op("scalar", lambda e: e.activation(out=c_st[1][:, :], in_=PB[1].ap()[:, :],
                                                      func=AF.Identity))
                tok_qkev = P.tick("scalar", cACT)
                st["qkv_ev"] = [tok_qkev, tok_qkev, tok_qkev]

                for which in ((0, 1) if "p" not in phases else ()):
                    stg = c_st[which]
                    tabA = c_tab[:, tbi, 2 * which, :]
                    tabB = c_tab[:, tbi, 2 * which + 1, :]
                    scl = (qsr if which == 0 else ksr)[:, l, :]
                    scl_sh = (qsr_shuf if which == 0 else ksr_shuf)[:, l, :]
                    sqw = [tok_qkev]
                    if st["bct1"] is not None:
                        sqw.append(st["bct1"])
                    P.op("scalar", lambda e, stg=stg: e.activation(
                        out=pr_s3[:, :], in_=stg[:, :], func=AF.Square), waits=sqw)
                    tok_sq = P.tick("scalar", cACT)
                    P.op("vector", lambda e: e.reduce_sum(
                        out=nrm[:, :], in_=apx(pr_s3[:, :], [[64, 8], [1, 64]]), axis=AX.X),
                        waits=[tok_sq])
                    t = P.tick("vector", cDVE)
                    t = chain("scalar", lambda e: e.activation(out=nrm2[:], in_=nrm[:],
                                                               func=AF.Ln), [t], cACT)
                    t = chain("scalar", lambda e: e.activation(out=nrm2[:], in_=nrm2[:],
                                                               func=AF.Exp, scale=0.5),
                              [t], cACT)
                    t = chain("vector", lambda e: e.tensor_scalar_max(
                        out=nrm2[:], in0=nrm2[:], scalar1=1e-12), [t], cDVE)
                    t = chain("vector", lambda e: e.reciprocal(out=rqv[:], in_=nrm2[:]),
                              [t], cDVE)
                    t = chain("vector", lambda e, tabA=tabA, scl=scl: e.tensor_tensor(
                        out=pr_s1[:, 0:64], in0=tabA, in1=scl, op=ALU.mult),
                        [t, tok_tab], cDVE)
                    t = chain("vector", lambda e, tabB=tabB, scl_sh=scl_sh: e.tensor_tensor(
                        out=pr_s1[:, 64:128], in0=tabB, in1=scl_sh, op=ALU.mult), [t], cDVE)
                    rq_b = apx(rqv[:, :], [[1, 8], [0, 64]])
                    P.op("vector", lambda e, stg=stg, rq_b=rq_b: e.tensor_tensor(
                        out=apx(pr_s2[:, :], [[64, 8], [1, 64]]),
                        in0=apx(stg[:, :], [[64, 8], [1, 64]]), in1=rq_b, op=ALU.mult),
                        waits=[t])
                    P.op("vector", lambda e: e.tensor_copy(
                        out=apx(pr_s3[:, :], [[64, 8], [1, 32]]),
                        in_=apx(offx(pr_s2[:, :], 32), [[64, 8], [1, 32]])))
                    P.op("vector", lambda e: e.tensor_copy(
                        out=apx(offx(pr_s3[:, :], 32), [[64, 8], [1, 32]]),
                        in_=apx(pr_s2[:, :], [[64, 8], [1, 32]])))
                    tok_pp = P.tick("vector", cDVE)
                    tA_rep = apx(pr_s1[:, 0:64], [[0, 8], [1, 64]])
                    tB_rep = apx(pr_s1[:, 64:128], [[0, 8], [1, 64]])
                    P.op("vector", lambda e, stg=stg, tA_rep=tA_rep: e.tensor_tensor(
                        out=apx(stg[:, :], [[64, 8], [1, 64]]),
                        in0=apx(pr_s2[:, :], [[64, 8], [1, 64]]), in1=tA_rep, op=ALU.mult),
                        waits=[tok_pp])
                    P.op("vector", lambda e, tB_rep=tB_rep: e.tensor_tensor(
                        out=apx(pr_s2[:, :], [[64, 8], [1, 64]]),
                        in0=apx(pr_s3[:, :], [[64, 8], [1, 64]]), in1=tB_rep, op=ALU.mult))
                    P.op("vector", lambda e, stg=stg: e.tensor_add(
                        out=pr_s3[:, :], in0=stg[:, :], in1=pr_s2[:, :]))
                    tok_rot = P.tick("vector", cDVE)
                    st["bct1"] = tok_rot
                    cw = [tok_rot]
                    if st["qbf_rd"] is not None:
                        cw.append(st["qbf_rd"])
                    qb2 = which
                    P.op("scalar", lambda e, which=which, qb2=qb2: e.activation(
                        out=qbf[:, qb2, :], in_=pr_s3[:, :], func=AF.Identity,
                        scale=(QK_SCALE if which == 0 else 1.0)), waits=cw)
                    tok_cast = P.tick("scalar", cACT)

                    def mk_tp(tok_cast=tok_cast, qb2=qb2, tc=tc, which=which):
                        def emit():
                            tw = [tok_cast]
                            if st["pt"] is not None:
                                tw.append(st["pt"])
                            for fc in range(4):
                                P.op("tensor", lambda e, fc=fc: e.transpose(
                                    PT.ap()[:, 128 * fc: 128 * (fc + 1)],
                                    qbf[:, qb2, 128 * fc: 128 * (fc + 1)], ident[:, :]),
                                    waits=tw if fc == 0 else ())
                            tok_tp = P.tick("tensor", cPE)
                            st["qbf_rd"] = tok_tp
                            base = 0 if which == 0 else 8192
                            dst = apx(offx(vqk_flat, base + 128 * tc), [[2048, 4], [1, 128]])
                            P.op("vector", lambda e, dst=dst: e.tensor_copy(
                                out=dst, in_=apx(PT.ap()[:, :], [[128, 4], [1, 128]])),
                                waits=[tok_tp])
                            st["pt"] = P.tick("vector", cDVE)
                        return emit
                    pending_tp.append(mk_tp())
                st["tab_rd"][tbi] = st["bct1"]
            emit_lagged_transpose()
            tok_prep_done = st["pt"] or st["qkv_ev"][0]
            if "2" not in sub:
                tok_u = tok_prep_done
                continue

            pk_toks = []
            for fc in range(4):
                pk_toks.append(P.dma(
                    "sync", cc3_in[l][128 * fc: 128 * (fc + 1), 0:1024],
                    kp_ap(fc)[:, 1024:2048], cCCP, waits=[tok_prep_done]))
            for tcv in range(8, 16):
                pk_toks.append(P.dma(
                    "sync", cc3_in[l][520 + 128 * (tcv - 8): 520 + 128 * (tcv - 7), 0:520],
                    c_v[:, tcv, :], cCCP, waits=[tok_prep_done]))
            for fc in range(4):
                pk_toks.append(P.dma(
                    "sync", cc3_in[l][1544 + fc: 1545 + fc, 0:128],
                    qp_ap(fc)[:, 2047:2048], cCCP, waits=[tok_prep_done]))
            P.op("gpsimd", lambda e, l=l: e.collective_compute(
                "AllGather", ALU.bypass, replica_groups=RG,
                ins=[cc3_in[l].opt()], outs=[cc3_out[l].opt()]), waits=[pk_toks[-1]])
            tok_cc3 = P.tick("gpsimd", cCC)
            blk = R3 * 1040
            for fc in range(4):
                P.op("gpsimd", lambda e, fc=fc, l=l: read_prev(
                    e, cc3_out[l][128 * fc: 128 * (fc + 1), 0:1024], khalo_ap(fc)[:, :], blk),
                    waits=[tok_cc3, tok_prep_done])
                P.tick("gpsimd", cCCR, by=16)
            for fc in range(4):
                P.op("gpsimd", lambda e, fc=fc, l=l: read_prev(
                    e, cc3_out[l][1544 + fc: 1545 + fc, 0:128], qbndt[:, fc:fc + 1], blk),
                    waits=[tok_cc3])
                P.tick("gpsimd", cCCR, by=16)
            tok_halo_kv = (cCCR, cCCR.n)

            if "3" not in sub:
                tok_u = tok_halo_kv
                continue
            def kjc_src(w, jc, fc, ro):
                if w == 0 and jc < 4:
                    return khalo_ap(fc)[ro:ro + 64, 512 + 128 * jc: 640 + 128 * jc]
                base = 512 * (w - 1) if w >= 1 else -512
                return kp_ap(fc)[ro:ro + 64, base + 128 * jc: base + 128 * jc + 128]

            def vjc_src(w, jc):
                if w == 0 and jc < 4:
                    return 12 + jc
                return (4 * (w - 1) + jc) if w >= 1 else (jc - 4)

            def stats_segs(w, qi, fc, ro):
                j0, j1 = 128 * qi, 128 * qi + 640
                segs = []
                if w == 0:
                    if j0 < 512:
                        segs.append(khalo_ap(fc)[ro:ro + 64, 512 + j0: 1024])
                    segs.append(kp_ap(fc)[ro:ro + 64, max(0, j0 - 512): j1 - 512])
                else:
                    b = 512 * (w - 1)
                    segs.append(kp_ap(fc)[ro:ro + 64, b + j0: b + j1])
                return segs

            def attn_evac(h, iw, dst_ap, tok_pv, st, extra=()):
                t = chain("vector", lambda e, iw=iw: e.reciprocal(
                    out=r64[64:65, 0:iw], in_=PW.ap()[64:65, 0:iw]),
                    [tok_pv] + list(extra), cDVE)
                t = chain("scalar", lambda e, iw=iw: e.activation(
                    out=r64h[64:65, 0:iw], in_=r64[64:65, 0:iw], func=AF.Identity), [t], cACT)
                t = chain("scalar", lambda e, iw=iw: e.activation(
                    out=r64[64:65, 512:512 + iw], in_=r64h[64:65, 0:iw], func=AF.Identity),
                    [t], cACT)
                t = chain("vector", lambda e, iw=iw: e.tensor_sub(
                    out=pr_s3[64:65, 0:iw], in0=r64[64:65, 0:iw],
                    in1=r64[64:65, 512:512 + iw]), [t], cDVE)
                t = chain("scalar", lambda e, iw=iw: e.activation(
                    out=r64h[64:65, 512:512 + iw], in_=pr_s3[64:65, 0:iw],
                    func=AF.Identity), [t], cACT)
                bw = [t]
                if st["rb_rd"] is not None:
                    bw.append(st["rb_rd"])
                mm_bank_group([
                    lambda e, s, p, iw=iw: e.matmul(PB[3].ap()[0:64, 0:iw],
                                                    on64[64:65, 0:64],
                                                    r64h[64:65, 0:iw], start=s, stop=p),
                    lambda e, s, p, iw=iw: e.matmul(PB[3].ap()[0:64, 0:iw],
                                                    on64[64:65, 0:64],
                                                    r64h[64:65, 512:512 + iw],
                                                    start=s, stop=p),
                ], bw)
                tok_rb = P.tick("tensor", cPE)
                P.op("scalar", lambda e, iw=iw: e.activation(
                    out=rb64[0:64, 0:iw], in_=PB[3].ap()[0:64, 0:iw], func=AF.Identity),
                    waits=[tok_rb])
                tok_rbs = P.tick("scalar", cACT)
                st["pb3_rd"] = tok_rbs
                P.op("vector", lambda e, iw=iw, dst_ap=dst_ap: e.tensor_tensor(
                    out=dst_ap, in0=PW.ap()[0:64, 0:iw], in1=rb64[0:64, 0:iw], op=ALU.mult),
                    waits=[tok_rbs])
                tok_e = P.tick("vector", cDVE)
                st["pw_rd"] = tok_e
                st["rb_rd"] = tok_e
                return tok_e

            st["rb_rd"] = None
            tok_oa_w = {}
            tok_vhalo = None
            windows = [1, 2, 3, 0]
            for w in windows:
                if w == 0:
                    for tcv in range(8):
                        P.op("gpsimd", lambda e, tcv=tcv, l=l: read_prev(
                            e, cc3_out[l][520 + 128 * tcv: 520 + 128 * (tcv + 1), 0:520],
                            c_v[:, 8 + tcv, :], R3 * 1040),
                            waits=[tok_cc3, st["pw_rd"]])
                        P.tick("gpsimd", cCCR, by=16)
                    tok_vhalo = (cCCR, cCCR.n)
                tok_kv = tok_prep_done if w != 0 else tok_halo_kv
                for h in range(HEADS):
                    fc, ro = h // 2, (h % 2) * 64
                    qw_full = qp_ap(fc)
                    for qi in range(4):
                        pw = [tok_kv]
                        if st["stats_rd"] is not None:
                            pw.append(st["stats_rd"])
                        qslice = qp_ap(fc)[ro:ro + 64, 512 * w + 128 * qi: 512 * w + 128 * qi + 128]
                        segs = stats_segs(w, qi, fc, ro)
                        b4ops, b5ops = [], []
                        col = 0
                        for sg in segs:
                            wid = sg.shape[-1]
                            while wid > 0:
                                n = min(wid, 512 - (col % 512)) if col < 512 else min(wid, 128)
                                n = min(n, 512 - col) if col < 512 else n
                                tgt = (b4ops if col < 512 else b5ops)
                                tgt.append((col, n, sg, sg.shape[-1] - wid))
                                col += n
                                wid -= n
                        ops4 = [
                            (lambda e, s, p, c=c, n=n, sg=sg, so=so, qslice=qslice:
                             e.matmul(PS640.ap()[:, c:c + n], qslice,
                                      _slice_cols(sg, so, n), start=s, stop=p))
                            for (c, n, sg, so) in b4ops]
                        if w == 0:
                            hw = 512 - 128 * qi
                            ops4.append(lambda e, s, p, hw=hw: e.matmul(
                                PS640.ap()[:, 0:hw], ones1_128[0:1, :], s0neg[0:1, 0:hw],
                                start=s, stop=p))
                        mm_bank_group(ops4, pw)
                        ops5 = [
                            (lambda e, s, p, c=c, n=n, sg=sg, so=so, qslice=qslice:
                             e.matmul(PS640.ap()[:, c:c + n], qslice,
                                      _slice_cols(sg, so, n), start=s, stop=p))
                            for (c, n, sg, so) in b5ops]
                        mm_bank_group(ops5, [])
                        tok_smm = P.tick("tensor", cPE)
                        P.op("vector", lambda e, qi=qi: e.tensor_reduce(
                            out=mst[:, qi:qi + 1], in_=PS640.ap()[:, 0:640], axis=AX.X,
                            op=ALU.max, negate=True), waits=[tok_smm])
                        st["stats_rd"] = P.tick("vector", cDVE)
                    mw = [st["stats_rd"]]
                    if st["mrow_rd"] is not None:
                        mw.append(st["mrow_rd"])
                    if st.get("pb3_rd") is not None:
                        mw.append(st["pb3_rd"])
                    for qi in range(4):
                        P.op("tensor", lambda e, qi=qi: e.transpose(
                            PB[3].ap()[0:1, 128 * qi: 128 * qi + 128], mst[:, qi:qi + 1],
                            identf[:, :]), waits=(mw if qi == 0 else ()))
                    tok_mtp = P.tick("tensor", cPE)
                    negw = [tok_mtp]
                    if st.get("negm_rd") is not None:
                        negw.append(st["negm_rd"])
                    P.op("scalar", lambda e: e.activation(
                        out=negm[0:1, 0:512], in_=PB[3].ap()[0:1, 0:512], func=AF.Identity),
                        waits=negw)
                    tok_negm = P.tick("scalar", cACT)
                    st["mrow_rd"] = tok_negm
                    for jc in range(8):
                        i0 = max(0, 128 * (jc - 4))
                        i1 = min(512, 128 * jc + 128)
                        iw = i1 - i0
                        pb = PB[jc % 2]
                        pw = [tok_negm, tok_kv]
                        if st["st_rd"][jc % 2] is not None:
                            pw.append(st["st_rd"][jc % 2])
                        kap = kjc_src(w, jc, fc, ro)
                        qap = qp_ap(fc)[ro:ro + 64, 512 * w + i0: 512 * w + i1]
                        ops = [lambda e, s, p, kap=kap, qap=qap, pb=pb, i0=i0, iw=iw:
                               e.matmul(pb.ap()[:, i0:i0 + iw], kap, qap, start=s, stop=p)]
                        for qi in range(i0 // 128, i1 // 128):
                            ops.append(lambda e, s, p, pb=pb, qi=qi: e.matmul(
                                pb.ap()[:, 128 * qi: 128 * qi + 128], ones1_128[0:1, :],
                                negm[0:1, 128 * qi: 128 * qi + 128], start=s, stop=p))
                        if jc <= 3:
                            qe = jc
                            if i0 <= 128 * qe < i1:
                                ops.append(lambda e, s, p, pb=pb, qe=qe: e.matmul(
                                    pb.ap()[:, 128 * qe: 128 * qe + 128], mask_d0[:, :],
                                    ident[:, :], start=s, stop=p))
                        else:
                            qe = jc - 4
                            if i0 <= 128 * qe < i1:
                                ops.append(lambda e, s, p, pb=pb, qe=qe: e.matmul(
                                    pb.ap()[:, 128 * qe: 128 * qe + 128], mask_d4[:, :],
                                    ident[:, :], start=s, stop=p))
                        if w == 0 and jc < 4:
                            ops.append(lambda e, s, p, pb=pb, i0=i0, iw=iw: e.matmul(
                                pb.ap()[:, i0:i0 + iw], ones1_128[0:1, :],
                                s0neg[0:1, 0:iw], start=s, stop=p))
                        mm_bank_group(ops, pw)
                        tok_stmm = P.tick("tensor", cPE)
                        st["negm_rd"] = tok_stmm
                        slot = jc % 3
                        pw2 = [tok_stmm]
                        if st["aT_rd"][slot] is not None:
                            pw2.append(st["aT_rd"][slot])
                        P.op("scalar", lambda e, pb=pb, i0=i0, iw=iw, slot=slot: e.activation(
                            out=c_aT[:, slot, i0:i0 + iw], in_=pb.ap()[:, i0:i0 + iw],
                            func=AF.Exp), waits=pw2)
                        tok_exp = P.tick("scalar", cACT)
                        st["st_rd"][jc % 2] = tok_exp
                        vt = vjc_src(w, jc)
                        pvw = [tok_exp]
                        if jc == 0 and st["pw_rd"] is not None:
                            pvw.append(st["pw_rd"])
                        if jc == 0 and w == 0:
                            pvw.append(tok_vhalo)
                        P.op("tensor", lambda e, vt=vt, h=h, i0=i0, iw=iw, slot=slot, jc=jc:
                             e.matmul(PW.ap()[:, i0:i0 + iw], c_v[:, vt, 65 * h: 65 * h + 65],
                                      c_aT[:, slot, i0:i0 + iw], start=(jc == 0),
                                      stop=(jc == 7)), waits=pvw)
                        st["aT_rd"][slot] = P.tick("tensor", cPE)
                    tok_pv = st["aT_rd"][7 % 3]
                    dw = [st["oa_wo"]] if st.get("oa_wo") else []
                    attn_evac(h, 512, c_oa[0:64, h, 0:512], tok_pv, st, extra=dw)
                tok_attnw = st["pw_rd"]
                ecw = []
                for ec in range(NG):
                    wobuf = ec % 2
                    wv = [tok_attnw]
                    if st.get(f"wo_rd{wobuf}"):
                        wv.append(st[f"wo_rd{wobuf}"])
                    twl = P.dma("gpsimd", c_wo[0:64, wobuf, :, :],
                                wo_r[l, :, :, 128 * ec: 128 * (ec + 1)], cWO[wobuf], waits=wv)
                    pb = PB[ec % 2]
                    pw = [twl]
                    if st.get(f"wops{ec % 2}"):
                        pw.append(st[f"wops{ec % 2}"])
                    ops = [(lambda e, s, p, h=h2, pb=pb, wobuf=wobuf: e.matmul(
                        pb.ap()[:, :], c_wo[0:64, wobuf, h, :], c_oa[0:64, h, 0:512],
                        start=s, stop=p)) for h2 in range(HEADS)]
                    mm_bank_group(ops, pw)
                    tok_womm = P.tick("tensor", cPE)
                    st[f"wo_rd{wobuf}"] = tok_womm
                    c0 = 2 + 512 * w
                    P.op("vector", lambda e, pb=pb, ec=ec, c0=c0: e.tensor_add(
                        out=vL[:, ec, c0:c0 + 512], in0=pb.ap()[:, :],
                        in1=vL[:, ec, c0:c0 + 512]), waits=[tok_womm] + dbg_guard())
                    st[f"wops{ec % 2}"] = P.tick("vector", cDVE)
                tok_oa_w[w] = st[f"wops{(NG - 1) % 2}"]
                st["oa_wo"] = st[f"wops{(NG - 1) % 2}"]

            if "4" not in sub:
                tok_u = tok_oa_w[0]
                continue
            t = chain("vector", lambda e: e.tensor_copy(
                out=apx(qfk[:, :, :], [[128, 4], [1, 1]]), in_=qbndt[:, :]),
                [tok_halo_kv, tok_oa_w[0]], cDVE)
            for fc in range(4):
                t = chain("vector", lambda e, fc=fc: e.tensor_copy(
                    out=qfk[:, fc, 1:128], in_=qp_ap(fc)[:, 0:127]), [t], cDVE)
            tok_qfk = t
            for h in range(HEADS):
                fc, ro = h // 2, (h % 2) * 64
                qap = qfk[ro:ro + 64, fc, :]
                pw = [tok_qfk]
                if st["stats_rd"] is not None:
                    pw.append(st["stats_rd"])
                mm_bank_group([
                    lambda e, s, p, fc=fc, ro=ro, qap=qap: e.matmul(
                        PS640.ap()[:, 0:512], qap, khalo_ap(fc)[ro:ro + 64, 384:896],
                        start=s, stop=p)], pw)
                mm_bank_group([
                    lambda e, s, p, fc=fc, ro=ro, qap=qap: e.matmul(
                        PS640.ap()[:, 512:640], qap, khalo_ap(fc)[ro:ro + 64, 896:1024],
                        start=s, stop=p)], [])
                tok_smm = P.tick("tensor", cPE)
                P.op("vector", lambda e: e.tensor_reduce(
                    out=mst[:, 0:1], in_=PS640.ap()[:, 0:640], axis=AX.X, op=ALU.max,
                    negate=True), waits=[tok_smm])
                st["stats_rd"] = P.tick("vector", cDVE)
                mw = [st["stats_rd"], st["mrow_rd"]]
                if st.get("pb3_rd") is not None:
                    mw.append(st["pb3_rd"])
                P.op("tensor", lambda e: e.transpose(
                    PB[3].ap()[0:1, 0:128], mst[:, 0:1], identf[:, :]), waits=mw)
                tok_mtp = P.tick("tensor", cPE)
                P.op("scalar", lambda e: e.activation(
                    out=negm[0:1, 0:128], in_=PB[3].ap()[0:1, 0:128], func=AF.Identity),
                    waits=[tok_mtp, st["negm_rd"]] if st.get("negm_rd") else [tok_mtp])
                tok_negm = P.tick("scalar", cACT)
                st["mrow_rd"] = tok_negm
                for jj, jc in enumerate(range(3, 8)):
                    pb = PB[jc % 2]
                    pw = [tok_negm]
                    if st["st_rd"][jc % 2] is not None:
                        pw.append(st["st_rd"][jc % 2])
                    ops = [lambda e, s, p, pb=pb, fc=fc, ro=ro, jc=jc, qap=qap: e.matmul(
                        pb.ap()[:, 0:128], khalo_ap(fc)[ro:ro + 64, 128 * jc: 128 * jc + 128],
                        qap, start=s, stop=p)]
                    ops.append(lambda e, s, p, pb=pb: e.matmul(
                        pb.ap()[:, 0:128], ones1_128[0:1, :], negm[0:1, 0:128],
                        start=s, stop=p))
                    if jc == 3:
                        ops.append(lambda e, s, p, pb=pb: e.matmul(
                            pb.ap()[:, 0:128], bndmask3[:, :], ident[:, :], start=s, stop=p))
                    mm_bank_group(ops, pw)
                    tok_stmm = P.tick("tensor", cPE)
                    st["negm_rd"] = tok_stmm
                    slot = jc % 3
                    pw2 = [tok_stmm]
                    if st["aT_rd"][slot] is not None:
                        pw2.append(st["aT_rd"][slot])
                    P.op("scalar", lambda e, pb=pb, slot=slot: e.activation(
                        out=c_aT[:, slot, 0:128], in_=pb.ap()[:, 0:128], func=AF.Exp),
                        waits=pw2)
                    tok_exp = P.tick("scalar", cACT)
                    st["st_rd"][jc % 2] = tok_exp
                    pvw = [tok_exp]
                    if jj == 0 and st["pw_rd"] is not None:
                        pvw.append(st["pw_rd"])
                    P.op("tensor", lambda e, jc=jc, h=h, slot=slot, jj=jj: e.matmul(
                        PW.ap()[:, 0:128], c_v[:, 8 + jc, 65 * h: 65 * h + 65],
                        c_aT[:, slot, 0:128], start=(jj == 0), stop=(jj == 4)), waits=pvw)
                    st["aT_rd"][slot] = P.tick("tensor", cPE)
                tok_pv = st["aT_rd"][7 % 3]
                attn_evac(h, 128, obnd[0:64, h, 0:128], tok_pv, st)
            tok_bnd = st["pw_rd"]
            for ec in range(NG):
                wobuf = ec % 2
                wv = [tok_bnd]
                if st.get(f"wo_rd{wobuf}"):
                    wv.append(st[f"wo_rd{wobuf}"])
                twl = P.dma("gpsimd", c_wo[0:64, wobuf, :, :],
                            wo_r[l, :, :, 128 * ec: 128 * (ec + 1)], cWO[wobuf], waits=wv)
                pb = PB[ec % 2]
                pw = [twl]
                if st.get(f"wops{ec % 2}"):
                    pw.append(st[f"wops{ec % 2}"])
                ops = [(lambda e, s, p, h=h2, pb=pb, wobuf=wobuf: e.matmul(
                    pb.ap()[:, 0:128], c_wo[0:64, wobuf, h, :], obnd[0:64, h, 0:128],
                    start=s, stop=p)) for h2 in range(HEADS)]
                mm_bank_group(ops, pw)
                tok_womm = P.tick("tensor", cPE)
                st[f"wo_rd{wobuf}"] = tok_womm
                P.op("vector", lambda e, pb=pb, ec=ec: e.scalar_tensor_tensor(
                    out=vL[:, ec, 1:2], in0=pb.ap()[:, 0:1], scalar=gate[:, 0:1],
                    in1=vL[:, ec, 1:2], op0=ALU.mult, op1=ALU.add), waits=[tok_womm] + dbg_guard())
                st[f"wops{ec % 2}"] = P.tick("vector", cDVE)
            tok_xmid = st[f"wops{(NG - 1) % 2}"]
            dump(f"xmid{l}", vL[:, :, 2:2050], 2048, [tok_xmid])

            if "5" not in sub:
                tok_u = tok_xmid
                continue
            def src_shift(g, c0, wdt):
                off = 0 if g < 4 else -1
                return vL[:, g, c0 + off: c0 + off + wdt]

            def apply_ln2(c0, wdt, tok_bcs, l=l):
                for g in range(NG):
                    off = 0 if g < 4 else -1
                    P.op("vector", lambda e, g=g, c0=c0, wdt=wdt, off=off: e.tensor_mul(
                        out=bc_t1[:, 0:wdt], in0=vL[:, g, c0 + off: c0 + off + wdt],
                        in1=bc_rb[:, 0:wdt]), waits=[tok_bcs])
                    P.op("vector", lambda e, g=g, c0=c0, wdt=wdt: e.tensor_sub(
                        out=scr1[0:128, 0:wdt], in0=bc_t1[:, 0:wdt], in1=bc_mb[:, 0:wdt]))
                    tok_s = P.tick("vector", cDVE)
                    P.op("scalar", lambda e, g=g, c0=c0, wdt=wdt, l=l: e.activation(
                        out=hln[:, g, c0 - 2: c0 - 2 + wdt], in_=scr1[0:128, 0:wdt],
                        func=AF.Identity, scale=ln_g[:, l, g:g + 1], bias=ln_b[:, l, g:g + 1]),
                        waits=[tok_s])
                P.op("vector", lambda e: e.engine_nop())

            tok_ln2 = layernorm(src_shift, tiles_own, apply_ln2, [tok_xmid])
            tok_hln = P.tick("scalar", cACT)

            stw = {"w1_rd": [None] * 4, "w2_rd": [None] * 4, "h1_rd": {}, "ps": {}}
            tok_u_upd = []
            for tt in range(4):
                tok_h1 = {}
                for hc in range(NHC):
                    wb = hc % 4
                    wv = ([stw["w1_rd"][wb]] if stw["w1_rd"][wb] else []) + [tok_hln]
                    ta = P.dma("gpsimd", c_w1[:, wb, 0, :, :],
                               w1r[l, :, :, 128 * hc: 128 * hc + 128], cW1[wb], waits=wv)
                    tg = P.dma("gpsimd", c_w1[:, wb, 1, :, :],
                               w1r[l, :, :, NHC * 128 + 128 * hc: NHC * 128 + 128 * hc + 128],
                               cW1[wb])
                    pba = PB[2 * (hc % 2)]
                    pbg = PB[2 * (hc % 2) + 1]
                    pw = [tg, tok_hln]
                    if stw["ps"].get(hc % 2):
                        pw.append(stw["ps"][hc % 2])
                    if hc < 2:
                        for b in (2 * (hc % 2), 2 * (hc % 2) + 1):
                            if stw.get(f"w2ps{b}"):
                                pw.append(stw[f"w2ps{b}"])
                    opsa = [(lambda e, s, p, g=g, wb=wb, pba=pba, tt=tt: e.matmul(
                        pba.ap()[:, :], c_w1[:, wb, 0, g, :],
                        hln[:, g, 512 * tt: 512 * tt + 512], start=s, stop=p))
                        for g in range(NG)]
                    mm_bank_group(opsa, pw)
                    opsg = [(lambda e, s, p, g=g, wb=wb, pbg=pbg, tt=tt: e.matmul(
                        pbg.ap()[:, :], c_w1[:, wb, 1, g, :],
                        hln[:, g, 512 * tt: 512 * tt + 512], start=s, stop=p))
                        for g in range(NG)]
                    mm_bank_group(opsg, [])
                    tok_w1mm = P.tick("tensor", cPE)
                    stw["w1_rd"][wb] = tok_w1mm
                    gw = [tok_w1mm]
                    if stw.get("gel_rd"):
                        gw.append(stw["gel_rd"])
                    P.op("scalar", lambda e, pbg=pbg, hc=hc, l=l: e.activation(
                        out=bc_t1[:, :], in_=pbg.ap()[:, :], func=AF.Gelu,
                        bias=b1[:, l, 1, hc:hc + 1]), waits=gw)
                    tok_gel = P.tick("scalar", cACT)
                    hw1 = [tok_gel]
                    if stw["h1_rd"].get(hc):
                        hw1.append(stw["h1_rd"][hc])
                    P.op("vector", lambda e, pba=pba, hc=hc, l=l: e.scalar_tensor_tensor(
                        out=h1t[:, hc, :], in0=pba.ap()[:, :], scalar=b1[:, l, 0, hc:hc + 1],
                        in1=bc_t1[:, :], op0=ALU.add, op1=ALU.mult), waits=hw1)
                    tok_h1[hc] = P.tick("vector", cDVE)
                    stw["gel_rd"] = tok_h1[hc]
                    stw["ps"][hc % 2] = tok_h1[hc]
                for ecg in range(2):
                    pw0 = [tok_h1[NHC - 1]]
                    for b in range(4):
                        if stw.get(f"w2ps{b}"):
                            pw0.append(stw[f"w2ps{b}"])
                    for hc in range(NHC):
                        wb2 = hc % 4
                        wv = [stw["w2_rd"][wb2]] if stw["w2_rd"][wb2] else []
                        tl = P.dma("gpsimd", c_w2[:, wb2, :],
                                   w2_d[l, 128 * hc: 128 * hc + 128,
                                        512 * ecg: 512 * ecg + 512], cW2[wb2],
                                   waits=wv + [tok_h1[hc], tok_hln])
                        for b in range(4):
                            P.op("tensor", lambda e, b=b, wb2=wb2, hc=hc:
                                 e.matmul(PB[b].ap()[:, :],
                                          c_w2[:, wb2, 128 * b: 128 * b + 128],
                                          h1t[:, hc, :], start=(hc == 0),
                                          stop=(hc == NHC - 1)),
                                 waits=([tl] + (pw0 if hc == 0 else [])) if b == 0 else ())
                        stw["w2_rd"][wb2] = P.tick("tensor", cPE)
                    tok_w2mm = stw["w2_rd"][(NHC - 1) % 2]
                    c0 = 2 + 512 * tt
                    for b in range(4):
                        ec = 4 * ecg + b
                        P.op("vector", lambda e, b=b, ec=ec, c0=c0, l=l:
                             e.scalar_tensor_tensor(
                                 out=vL[:, ec, c0:c0 + 512], in0=PB[b].ap()[:, :],
                                 scalar=b2[:, l, ec:ec + 1], in1=vL[:, ec, c0:c0 + 512],
                                 op0=ALU.add, op1=ALU.add),
                             waits=[tok_w2mm] + dbg_guard())
                        stw[f"w2ps{b}"] = P.tick("vector", cDVE)
                        tok_u_upd.append(stw[f"w2ps{b}"])
                for hc in range(NHC):
                    stw["h1_rd"][hc] = stw["w2_rd"][(NHC - 1) % 2]
            tok_layer = tok_u_upd[-1]
            dump(f"y{l}", vL[:, :, 2:2050], 2048, [tok_layer])

            if l == 0:
                tok_u = exchange_boundary(cc4_in, cc4_out, tok_layer)
            else:
                tok_u = tok_layer
            if l == 1:
                tok_x2r = P.dma("sync", x2r, x2_spill.rearrange("(g p) t -> p g t", p=128),
                                cLX, waits=[tok_spill, tok_layer])

    if "C" in phases and "B" in phases:
        def apply_final(c0, wdt, tok_bcs):
            for g in range(NG):
                P.op("vector", lambda e, g=g, c0=c0, wdt=wdt: e.tensor_mul(
                    out=bc_t1[:, 0:wdt], in0=vL[:, g, c0:c0 + wdt], in1=bc_rb[:, 0:wdt]),
                    waits=[tok_bcs, tok_x2r] + dbg_guard())
                P.op("vector", lambda e, g=g, c0=c0, wdt=wdt: e.tensor_sub(
                    out=scr1[0:128, 0:wdt], in0=bc_t1[:, 0:wdt], in1=bc_mb[:, 0:wdt]))
                P.op("vector", lambda e, g=g, c0=c0, wdt=wdt: e.tensor_add(
                    out=vL[:, g, c0:c0 + wdt], in0=scr1[0:128, 0:wdt],
                    in1=x2r[:, g, c0 - 2: c0 - 2 + wdt]))

        tok_fin = layernorm(lambda g, c0, wdt: vL[:, g, c0:c0 + wdt], tiles_own,
                            apply_final, [tok_u])
        tok_y = P.dma("sync", y_out.rearrange("(g p) t -> p g t", p=128),
                      vL[:, :, 2:2050], cST, waits=[tok_fin])
        dump_toks.append(tok_y)
    else:
        tok_y = P.dma("sync", y_out.rearrange("(g p) t -> p g t", p=128),
                      vL[:, :, 2:2050], cST,
                      waits=[tok_u if "B" in phases else (tok_halo if "A" in phases else tok_x)])
        dump_toks.append(tok_y)

    P.op("sync", lambda e: e.nop(), waits=dump_toks + [(cST, cST.n)])
    P.emit()
    es.close()
    return nc


def kernel(**inputs):
    in_maps = stage_inputs(inputs)
    nc = build_program(debug=DEBUG, phases=PHASES)
    res = run_bass_kernel_spmd(nc, in_maps, core_ids=list(range(8)))
    out = np.zeros((2, SEQ, D), np.float32)
    for c in range(8):
        b, s = c // 4, c % 4
        y = np.asarray(res.results[c]["y"], np.float32)
        out[b, TOK * s: TOK * (s + 1), :] = y.T
    return out
